# version 11
# baseline (speedup 1.0000x reference)
import numpy as np
from contextlib import ExitStack
import concourse.bass as bass
import concourse.mybir as mybir
from concourse.bass_utils import run_bass_kernel_spmd

F32 = mybir.dt.float32
F32R = mybir.dt.float32r
BF16 = mybir.dt.bfloat16
AF = mybir.ActivationFunctionType
ALU = mybir.AluOpType

NCORE = 8
T = 4096
TL = 2048
D = 1024
DI = 2048
NS = 16


class Ctr:
    def __init__(self, h):
        self.h = h
        self.n = 0
    def inc(self, instr, n=1):
        instr.then_inc(self.h, n)
        self.n += n
        return self.n


DETECT_RACES = True


class Prog:
    def __init__(self):
        self.nc = bass.Bass("TRN2", target_bir_lowering=False,
                            detect_race_conditions=DETECT_RACES)
        self.es = ExitStack()
        self.stack = [self.es]
        self.pre = ""
        self.io = {}
    def begin_stage(self, pre):
        self.stack.append(ExitStack())
        self.pre = pre
    def end_stage(self):
        self.stack.pop().close()
        self.pre = ""
    def sb(self, name, shape, dt):
        return self.stack[-1].enter_context(self.nc.sbuf_tensor(self.pre + name, shape, dt))
    def ps(self, name, shape, dt=F32):
        return self.stack[-1].enter_context(self.nc.psum_tensor(self.pre + name, shape, dt))
    def sem(self, name):
        return Ctr(self.es.enter_context(self.nc.semaphore(self.pre + name)))
    def din(self, name, shape, dt=F32):
        if name in self.io:
            return self.io[name]
        return self.nc.dram_tensor(self.pre + name, list(shape), dt, kind="ExternalInput").ap()
    def dout(self, name, shape, dt=F32):
        if name in self.io:
            return self.io[name]
        return self.nc.dram_tensor(self.pre + name, list(shape), dt, kind="ExternalOutput").ap()
    def close(self):
        self.es.close()
        return self.nc


def fr(ap):
    return ap.bitcast(F32R)



def build_l0a(Tn=T, TT=512, p=None, sems=None):
    NT = Tn // TT
    standalone = p is None
    if standalone:
        p = Prog()
    nc = p.nc
    x = p.din("x", [8, 128, Tn], F32R)
    w_in_x = p.din("w_in_x", [1024, 1024], F32R)
    w_in_z = p.din("w_in_z", [1024, 1024], F32R)
    wxp_dt = p.din("wxp_dt", [1024, 128], F32R)
    wxp_b = p.din("wxp_b", [1024, 16], F32R)
    wxp_c = p.din("wxp_c", [1024, 16], F32R)
    consts = p.din("consts", [128, 8 * 7])
    o_xa = p.dout("xa", [8, 128, Tn], F32R)
    o_z = p.dout("z", [8, 128, Tn])
    o_dtp = p.dout("dtp", [128, Tn])
    o_bp = p.dout("bp", [16, Tn])
    o_cp = p.dout("cp", [16, Tn])

    wix = p.sb("wix", [128, 8 * 1024], F32R)
    wiz = p.sb("wiz", [128, 8 * 1024], F32R)
    wdt = p.sb("wdt", [128, 8 * 128], F32R)
    wbb = p.sb("wbb", [128, 8 * 16], F32R)
    wcc = p.sb("wcc", [128, 8 * 16], F32R)
    cst = p.sb("cst", [128, 8 * 7], F32)
    xb = p.sb("xb", [128, 2 * 8 * TT], F32R)
    xpe = [p.sb(f"xpe{c}", [128, TT + 3], F32) for c in range(8)]
    convo = p.sb("convo", [128, 8 * TT], F32)
    xaa = p.sb("xaa", [128, 8 * TT], F32R)
    zz = p.sb("zz", [128, 8 * TT], F32)
    dtv = p.sb("dtv", [128, TT], F32)
    bv = p.sb("bv", [16, TT], F32)
    cv = p.sb("cv", [16, TT], F32)
    px = [p.ps(f"px{i}", [128, TT]) for i in range(2)]
    pz = [p.ps(f"pz{i}", [128, TT]) for i in range(2)]
    pdt = p.ps("pdt", [128, TT])
    pbc = p.ps("pbc", [16, TT])
    pcc = p.ps("pcc", [16, TT])

    if sems is None:
        ld, mm, dv, ac, st = (p.sem("ld"), p.sem("mm"), p.sem("dv"), p.sem("ac"), p.sem("st"))
    else:
        ld, mm, dv, ac, st = sems["ld"], sems["mm"], sems["dv"], sems["ac"], sems["st"]
    ldE, mmE, dvE, acE, stE = ld.n, mm.n, dv.n, ac.n, st.n

    with nc.Block() as blk:
        @blk.sync
        def _(q):
            for kc in range(8):
                q.dma_start(out=wix[:, kc * 1024:(kc + 1) * 1024],
                            in_=w_in_x[kc * 128:(kc + 1) * 128, :]).then_inc(ld.h, 16); ld.n += 16
                q.dma_start(out=wiz[:, kc * 1024:(kc + 1) * 1024],
                            in_=w_in_z[kc * 128:(kc + 1) * 128, :]).then_inc(ld.h, 16); ld.n += 16
                q.dma_start(out=wdt[:, kc * 128:(kc + 1) * 128],
                            in_=wxp_dt[kc * 128:(kc + 1) * 128, :]).then_inc(ld.h, 16); ld.n += 16
                q.dma_start(out=wbb[:, kc * 16:(kc + 1) * 16],
                            in_=wxp_b[kc * 128:(kc + 1) * 128, :]).then_inc(ld.h, 16); ld.n += 16
                q.dma_start(out=wcc[:, kc * 16:(kc + 1) * 16],
                            in_=wxp_c[kc * 128:(kc + 1) * 128, :]).then_inc(ld.h, 16); ld.n += 16
            q.dma_start(out=cst[:], in_=consts).then_inc(ld.h, 16); ld.n += 16

        @blk.vector
        def _(v):
            for c in range(8):
                v.memset(xpe[c][:, 0:3], 0.0)

    wload = ld.n

    def xcol(c, j):
        return cst[:, c * 7 + j:c * 7 + j + 1]

    MM_PER, DV_PER, AC_PER = 19, 17, 8

    for tt in range(NT):
        mm0, dv0, ac0 = mmE + MM_PER * tt, dvE + DV_PER * tt, acE + AC_PER * tt
        mark_mm_x = lambda mc: mm0 + 2 * mc + 1
        mark_mm_z = lambda mc: mm0 + 2 * mc + 2
        mark_mm_dt = mm0 + 17
        mark_mm_b = mm0 + 18
        mark_mm_c = mm0 + 19
        mark_dv_evac = lambda mc: dv0 + 2 * mc + 1
        mark_dv_conv = lambda mc: dv0 + 2 * mc + 2
        mark_dv_dtbc = dv0 + 17
        mark_ac = lambda mc: ac0 + mc + 1
        b = tt % 2

        with nc.Block() as blk:
            @blk.sync
            def _(q):
                tiles = [0, 1] if (tt == 0 and NT > 1) else ([0] if tt == 0 else ([tt + 1] if tt + 1 < NT else []))
                for ti in tiles:
                    bb = ti % 2
                    for c in range(8):
                        q.dma_start(out=xb[:, (bb * 8 + c) * TT:(bb * 8 + c + 1) * TT],
                                    in_=x[c, :, ti * TT:(ti + 1) * TT]).then_inc(ld.h, 16); ld.n += 16

            @blk.tensor
            def _(t):
                t.wait_ge(ld.h, wload + 16 * 8 * min(tt + 1, NT))
                for mc in range(8):
                    bank = mc % 2
                    if mc >= 2:
                        t.wait_ge(dv.h, mark_dv_evac(mc - 2))
                    for kc in range(8):
                        last = t.matmul(out=px[bank][:],
                                        lhsT=wix[:, kc * 1024 + mc * 128: kc * 1024 + (mc + 1) * 128],
                                        rhs=xb[:, (b * 8 + kc) * TT:(b * 8 + kc + 1) * TT],
                                        start=(kc == 0), stop=(kc == 7))
                    mm.inc(last)
                    for kc in range(8):
                        last = t.matmul(out=pz[bank][:],
                                        lhsT=wiz[:, kc * 1024 + mc * 128: kc * 1024 + (mc + 1) * 128],
                                        rhs=xb[:, (b * 8 + kc) * TT:(b * 8 + kc + 1) * TT],
                                        start=(kc == 0), stop=(kc == 7))
                    mm.inc(last)
                t.wait_ge(ac.h, ac0 + 8)
                for kc in range(8):
                    last = t.matmul(out=pdt[:], lhsT=wdt[:, kc * 128:(kc + 1) * 128],
                                    rhs=xaa[:, kc * TT:(kc + 1) * TT],
                                    start=(kc == 0), stop=(kc == 7))
                mm.inc(last)
                for kc in range(8):
                    last = t.matmul(out=pbc[:], lhsT=wbb[:, kc * 16:(kc + 1) * 16],
                                    rhs=xaa[:, kc * TT:(kc + 1) * TT],
                                    start=(kc == 0), stop=(kc == 7))
                mm.inc(last)
                for kc in range(8):
                    last = t.matmul(out=pcc[:], lhsT=wcc[:, kc * 16:(kc + 1) * 16],
                                    rhs=xaa[:, kc * TT:(kc + 1) * TT],
                                    start=(kc == 0), stop=(kc == 7))
                mm.inc(last)

            @blk.vector
            def _(v):
                for mc in range(8):
                    bank = mc % 2
                    v.wait_ge(mm.h, mark_mm_x(mc))
                    v.tensor_single_scalar(out=xpe[mc][:, 3:3 + TT], in_=px[bank][:],
                                           scalar=xcol(mc, 0), op=ALU.add)
                    v.wait_ge(mm.h, mark_mm_z(mc))
                    dv.inc(v.tensor_single_scalar(out=zz[:, mc * TT:(mc + 1) * TT], in_=pz[bank][:],
                                                  scalar=xcol(mc, 1), op=ALU.add))
                    e = xpe[mc]
                    v.tensor_single_scalar(out=convo[:, mc * TT:(mc + 1) * TT],
                                           in_=e[:, 0:TT], scalar=xcol(mc, 3), op=ALU.mult)
                    for k in range(1, 4):
                        v.scalar_tensor_tensor(out=convo[:, mc * TT:(mc + 1) * TT],
                                               in0=e[:, k:k + TT], scalar=xcol(mc, 3 + k),
                                               in1=convo[:, mc * TT:(mc + 1) * TT],
                                               op0=ALU.mult, op1=ALU.add)
                    dv.inc(v.tensor_copy(out=e[:, 0:3], in_=e[:, TT:TT + 3]))
                v.wait_ge(mm.h, mark_mm_dt)
                v.tensor_copy(out=dtv[:], in_=pdt[:])
                v.wait_ge(mm.h, mark_mm_c)
                v.tensor_copy(out=bv[:], in_=pbc[:])
                dv.inc(v.tensor_copy(out=cv[:], in_=pcc[:]))

            @blk.scalar
            def _(a):
                for mc in range(8):
                    a.wait_ge(dv.h, mark_dv_conv(mc))
                    ac.inc(a.activation(out=xaa[:, mc * TT:(mc + 1) * TT],
                                        in_=convo[:, mc * TT:(mc + 1) * TT],
                                        func=AF.Silu, bias=xcol(mc, 2)))

            @blk.gpsimd
            def _(g):
                for mc in range(8):
                    g.wait_ge(ac.h, mark_ac(mc))
                    g.dma_start(out=o_xa[mc, :, tt * TT:(tt + 1) * TT],
                                in_=xaa[:, mc * TT:(mc + 1) * TT]).then_inc(st.h, 16); st.n += 16
                    g.wait_ge(dv.h, mark_dv_evac(mc))
                    g.dma_start(out=o_z[mc, :, tt * TT:(tt + 1) * TT],
                                in_=zz[:, mc * TT:(mc + 1) * TT]).then_inc(st.h, 16); st.n += 16
                g.wait_ge(dv.h, mark_dv_dtbc)
                g.dma_start(out=o_dtp[:, tt * TT:(tt + 1) * TT], in_=dtv[:]).then_inc(st.h, 16); st.n += 16
                g.dma_start(out=o_bp[:, tt * TT:(tt + 1) * TT], in_=bv[:]).then_inc(st.h, 16); st.n += 16
                g.dma_start(out=o_cp[:, tt * TT:(tt + 1) * TT], in_=cv[:]).then_inc(st.h, 16); st.n += 16
                g.wait_ge(st.h, st.n)

        assert mm.n == mmE + MM_PER * (tt + 1) and dv.n == dvE + DV_PER * (tt + 1) and ac.n == acE + AC_PER * (tt + 1)

    with nc.Block() as blk:
        @blk.gpsimd
        def _(g):
            g.wait_ge(st.h, st.n)
    if standalone:
        return p.close()
    return None



def build_l0b(Tn=T, TTs=512, p=None, sems=None, summed=False, cc_cb=None):
    NSUP = Tn // TTs
    standalone = p is None
    if standalone:
        p = Prog()
    nc = p.nc
    xa_i = p.din("xa", [8, 128, Tn], F32R)
    z_i = p.din("z", [8, 128, Tn])
    if summed:
        dtps_i = p.din("dts", [128, Tn])
        bps_i = p.din("bs", [16, Tn])
        cps_i = p.din("cs", [16, Tn])
        dtpp_i = bpp_i = cpp_i = None
    else:
        dtps_i = p.din("dtps", [128, Tn])
        dtpp_i = p.din("dtpp", [128, Tn])
        bps_i = p.din("bps", [16, Tn]); bpp_i = p.din("bpp", [16, Tn])
        cps_i = p.din("cps", [16, Tn]); cpp_i = p.din("cpp", [16, Tn])
    wdt_i = p.din("wdti", [128, 1024], F32R)
    wout_i = p.din("wouti", [1024, 1024], F32R)
    alog_i = p.din("alogi", [128, 128])
    cst2_i = p.din("cst2i", [128, 17])
    bxpbr_i = p.din("bxpbr", [128, 16])
    bxpcr_i = p.din("bxpcr", [128, 16])
    id128_i = p.din("id128", [128, 128], F32R)
    o_part = p.dout("opart", [Tn // TTs, 8, 128, TTs])

    wdt = p.sb("s_wdt", [128, 1024], F32R)
    wout = p.sb("wout", [128, 8 * 1024], BF16)
    dtint = p.sb("dtint", [128, Tn], F32R)
    alog = p.sb("s_alog", [128, 128], F32)
    aneg = p.sb("aneg", [128, 128], F32)
    ascr = p.sb("ascr", [128, 128], F32)
    cst2 = p.sb("s_cst2", [128, 17], F32)
    bxpbr = p.sb("s_bxpbr", [128, 16], F32)
    bxpcr = p.sb("s_bxpcr", [128, 16], F32)
    hst = p.sb("hst", [128, 128], F32)
    xas = p.sb("xas", [128, 8 * TTs], F32)
    zs = p.sb("zs", [128, 8 * TTs], F32)
    dt_all = p.sb("dt_all", [128, 8 * TTs], F32)
    dtxa = p.sb("dtxa", [128, 8 * TTs], F32)
    sp1 = p.sb("sp1", [128, TTs], F32)
    sp2 = p.sb("sp2", [128, TTs], F32)
    bb1 = p.sb("bb1", [128, 2 * TTs], F32)
    bb2 = p.sb("bb2", [128, 2 * TTs], F32)
    cb1 = p.sb("cb1", [128, 2 * TTs], F32)
    cb2 = p.sb("cb2", [128, 2 * TTs], F32)
    bbuf = p.sb("bbuf", [128, 2 * TTs], F32)
    cbuf = p.sb("cbuf", [128, 2 * TTs], F32)
    dabuf = p.sb("dabuf", [128, 4 * TTs], F32)
    dxbuf = p.sb("dxbuf", [128, 4 * TTs], F32)
    hbuf = p.sb("hbuf", [128, 4 * TTs], F32)
    hcbuf = p.sb("hcbuf", [128, 4 * TTs], BF16)
    id128 = p.sb("s_id128", [128, 128], BF16)
    yacc = p.sb("yacc", [128, 8 * TTs], F32)
    ybuf = p.sb("ybuf", [128, TTs], F32)
    zsc = p.sb("zsc", [128, TTs], F32)
    gated = p.sb("gated", [128, 8 * TTs], BF16)
    ovac = p.sb("ovac", [128, 2 * TTs], F32)
    pdt = [p.ps(f"pdt{i}", [128, TTs]) for i in range(2)]
    pout = [p.ps(f"pout{i}", [128, TTs]) for i in range(2)]
    pacc = [p.ps(f"pacc{i}", [128, TTs]) for i in range(4)]

    if sems is None:
        ld, gl, mm = p.sem("ld"), p.sem("gl"), p.sem("mm")
        dv, ac, pl, st = p.sem("dv"), p.sem("ac"), p.sem("pl"), p.sem("st")
    else:
        ld, gl, mm, dv, ac, pl, st = (sems["ld"], sems["gl"], sems["mm"], sems["dv"],
                                      sems["ac"], sems["pl"], sems["st"])
    ldE, stE = ld.n, st.n

    with nc.Block() as blk:
        @blk.gpsimd
        def _(g0):
            for kc in range(8):
                g0.dma_start(out=wout[:, kc * 1024:(kc + 1) * 1024],
                             in_=wout_i[kc * 128:(kc + 1) * 128, :]).then_inc(ld.h, 16); ld.n += 16
            g0.dma_start(out=id128[:], in_=id128_i).then_inc(ld.h, 16); ld.n += 16

        @blk.sync
        def _(q):
            pairs = [(wdt, wdt_i), (xas, dtps_i), (alog, alog_i),
                     (cst2, cst2_i), (bxpbr, bxpbr_i), (bxpcr, bxpcr_i)]
            if not summed:
                pairs.insert(2, (zs, dtpp_i))
            for t_, s_ in pairs:
                q.dma_start(out=t_[:], in_=s_).then_inc(ld.h, 16); ld.n += 16

        ld_init = ld.n

        @blk.vector
        def _(v):
            v.wait_ge(ld.h, ld_init)
            v.memset(hst[:], 0.0)
            if summed:
                dv.inc(v.tensor_single_scalar(out=dtint[:], in_=xas[:],
                                              scalar=cst2[:, 16:17], op=ALU.add))
            else:
                dv.inc(v.scalar_tensor_tensor(out=dtint[:], in0=xas[:], scalar=cst2[:, 16:17],
                                              in1=zs[:], op0=ALU.add, op1=ALU.add))

        @blk.scalar
        def _(a):
            a.wait_ge(ld.h, ld_init)
            ac.inc(a.activation(out=ascr[:], in_=alog[:], func=AF.Exp))

    with nc.Block() as blk:
        @blk.vector
        def _(v):
            v.tensor_single_scalar(out=aneg[:], in_=ascr[:], scalar=-1.0, op=ALU.mult)

    DV0 = dv.n
    AC0 = ac.n
    PL0 = pl.n
    MM0 = mm.n
    GLE = gl.n
    DV_PER, AC_PER, PL_PER, MM_PER = 384, 272, 128, 80
    GL_PER = 8 * 16
    NBC = 2 if summed else 4
    LD_SUP = (8 + 16 * NBC) * 16
    LD0 = ld.n
    NPE = 4

    def marks(ts):
        dv0 = DV0 + DV_PER * ts
        ac0 = AC0 + AC_PER * ts
        pl0 = PL0 + PL_PER * ts
        mm0 = MM0 + MM_PER * ts
        dvm = {}
        n = dv0
        for c in range(8):
            n += 1; dvm[("dtadd", c)] = n
            n += 1; dvm[("dtxa", c)] = n
        for s in range(16):
            n += 1; dvm[("sttB", s)] = n
            n += 1; dvm[("sttC", s)] = n
            for c in range(8):
                i = s * 8 + c
                n += 1; dvm[("scan", i)] = n
                n += 1; dvm[("hc", i)] = n
                if c >= NPE:
                    n += 1; dvm[("yacc", i)] = n
        for c in range(8):
            n += 1; dvm[("gated", c)] = n
        for mc in range(8):
            n += 1; dvm[("evac", mc)] = n
        assert n - dv0 == DV_PER
        plm = {}
        for i in range(128):
            plm[("dbx", i)] = pl0 + i + 1
        mmm = {}
        q = mm0
        for c in range(8):
            q += 1; mmm[("dtmm", c)] = q
        for s in range(16):
            for c in range(NPE):
                q += 1; mmm[("yaccmm", s * 8 + c)] = q
        for mc in range(8):
            q += 1; mmm[("omm", mc)] = q
        assert q - mm0 == MM_PER
        return {
            "dv": dvm,
            "pl": plm,
            "mm": mmm,
            "relu": lambda c: ac0 + c + 1,
            "da": lambda i: ac0 + 8 + i + max(0, i - 4) + 1,
            "hst": lambda i: ac0 + 8 + min(i + 5, 128) + i + 1,
            "zsil": lambda c: ac0 + 264 + c + 1,
            "ld_bc": lambda s: LD0 + LD_SUP * ts + 16 * (8 + NBC * s + NBC),
            "gl_xa": GLE + GL_PER * (ts + 1),
        }

    for ts in range(NSUP):
        M = marks(ts)
        tsl = slice(ts * TTs, (ts + 1) * TTs)

        with nc.Block() as blk:
            @blk.sync
            def _(q):
                for c in range(8):
                    q.dma_start(out=zs[:, c * TTs:(c + 1) * TTs],
                                in_=z_i[c, :, tsl]).then_inc(ld.h, 16); ld.n += 16
                for s in range(16):
                    k = s % 2
                    if s >= 2:
                        q.wait_ge(dv.h, M["dv"][("sttC", s - 2)])
                    srcs = ((bb1, bps_i), (cb1, cps_i)) if summed else \
                           ((bb1, bps_i), (bb2, bpp_i), (cb1, cps_i), (cb2, cpp_i))
                    for dst, srcr in srcs:
                        q.dma_start(out=dst[:, k * TTs:(k + 1) * TTs],
                                    in_=srcr[s:s + 1, tsl].partition_broadcast(128)).then_inc(ld.h, 16); ld.n += 16

            @blk.tensor
            def _(t):
                for c in range(8):
                    if c >= 2:
                        t.wait_ge(ac.h, M["relu"](c - 2))
                    mm.inc(t.matmul(out=pdt[c % 2][:], lhsT=wdt[:, c * 128:(c + 1) * 128],
                                    rhs=dtint[:, tsl], start=True, stop=True))
                for s in range(16):
                    for c in range(NPE):
                        i = s * 8 + c
                        kk = i % 4
                        t.wait_ge(dv.h, M["dv"][("hc", i)])
                        mm.inc(t.matmul(out=pacc[c][:], lhsT=id128[:],
                                        rhs=hcbuf[:, kk * TTs:(kk + 1) * TTs],
                                        start=(s == 0), stop=(s == 15)))
                for mc in range(8):
                    t.wait_ge(dv.h, M["dv"][("gated", 7)])
                    if mc >= 2:
                        t.wait_ge(dv.h, M["dv"][("evac", mc - 2)])
                    for kc in range(8):
                        last = t.matmul(out=pout[mc % 2][:],
                                        lhsT=wout[:, kc * 1024 + mc * 128:kc * 1024 + (mc + 1) * 128],
                                        rhs=gated[:, kc * TTs:(kc + 1) * TTs],
                                        start=(kc == 0), stop=(kc == 7))
                    mm.inc(last)

            @blk.scalar
            def _(a):
                for c in range(8):
                    a.wait_ge(mm.h, M["mm"][("dtmm", c)])
                    if c >= 1:
                        a.wait_ge(dv.h, M["dv"][("dtadd", c - 1)])
                    a.activation(out=sp1[:], in_=pdt[c % 2][:], func=AF.Abs, bias=cst2[:, 2 * c:2 * c + 1])
                    a.activation(out=sp1[:], in_=sp1[:], func=AF.Exp, scale=-1.0)
                    a.activation(out=sp1[:], in_=sp1[:], func=AF.Ln, bias=1.0)
                    ac.inc(a.activation(out=sp2[:], in_=pdt[c % 2][:], func=AF.Relu,
                                        bias=cst2[:, 2 * c:2 * c + 1]))
                def emit_hst(i):
                    s_, c_ = i // 8, i % 8
                    ac.inc(a.activation(out=hst[:, c_ * 16 + s_:c_ * 16 + s_ + 1],
                                        in_=hbuf[:, (i % 4) * TTs + TTs - 1:(i % 4) * TTs + TTs],
                                        func=AF.Copy))
                for i in range(128):
                    s, c = i // 8, i % 8
                    a.wait_ge(dv.h, M["dv"][("dtadd", c)])
                    if i >= 4:
                        a.wait_ge(dv.h, M["dv"][("scan", i - 4)])
                    ac.inc(a.activation(out=dabuf[:, (i % 4) * TTs:(i % 4 + 1) * TTs],
                                        in_=dt_all[:, c * TTs:(c + 1) * TTs], func=AF.Exp,
                                        scale=aneg[:, c * 16 + s:c * 16 + s + 1]))
                    if i >= 4:
                        emit_hst(i - 4)
                for i in range(124, 128):
                    a.wait_ge(dv.h, M["dv"][("scan", i)])
                    emit_hst(i)
                for c in range(8):
                    if c >= 1:
                        a.wait_ge(dv.h, M["dv"][("gated", c - 1)])
                    ac.inc(a.activation(out=zsc[:], in_=zs[:, c * TTs:(c + 1) * TTs], func=AF.Silu))
                for mc in range(8):
                    a.wait_ge(dv.h, M["dv"][("evac", mc)])
                    a.dma_start(out=o_part[ts, mc],
                                in_=ovac[:, (mc % 2) * TTs:(mc % 2 + 1) * TTs]).then_inc(st.h, 16); st.n += 16
                a.wait_ge(st.h, st.n)

            @blk.vector
            def _(v):
                v.wait_ge(gl.h, M["gl_xa"])
                for c in range(8):
                    v.wait_ge(ac.h, M["relu"](c))
                    dv.inc(v.tensor_add(out=dt_all[:, c * TTs:(c + 1) * TTs], in0=sp1[:], in1=sp2[:]))
                    dv.inc(v.tensor_mul(out=dtxa[:, c * TTs:(c + 1) * TTs],
                                        in0=dt_all[:, c * TTs:(c + 1) * TTs],
                                        in1=xas[:, c * TTs:(c + 1) * TTs]))
                for s in range(16):
                    k = s % 2
                    v.wait_ge(ld.h, M["ld_bc"](s))
                    if summed:
                        dv.inc(v.tensor_single_scalar(out=bbuf[:, k * TTs:(k + 1) * TTs],
                                                      in_=bb1[:, k * TTs:(k + 1) * TTs],
                                                      scalar=bxpbr[:, s:s + 1], op=ALU.add))
                        dv.inc(v.tensor_single_scalar(out=cbuf[:, k * TTs:(k + 1) * TTs],
                                                      in_=cb1[:, k * TTs:(k + 1) * TTs],
                                                      scalar=bxpcr[:, s:s + 1], op=ALU.add))
                    else:
                        dv.inc(v.scalar_tensor_tensor(out=bbuf[:, k * TTs:(k + 1) * TTs],
                                                      in0=bb1[:, k * TTs:(k + 1) * TTs],
                                                      scalar=bxpbr[:, s:s + 1],
                                                      in1=bb2[:, k * TTs:(k + 1) * TTs],
                                                      op0=ALU.add, op1=ALU.add))
                        dv.inc(v.scalar_tensor_tensor(out=cbuf[:, k * TTs:(k + 1) * TTs],
                                                      in0=cb1[:, k * TTs:(k + 1) * TTs],
                                                      scalar=bxpcr[:, s:s + 1],
                                                      in1=cb2[:, k * TTs:(k + 1) * TTs],
                                                      op0=ALU.add, op1=ALU.add))
                    for c in range(8):
                        i = s * 8 + c
                        kk = i % 4
                        v.wait_ge(ac.h, M["hst"](i - 4) if i >= 4 else M["da"](i))
                        v.wait_ge(pl.h, M["pl"][("dbx", i)])
                        dv.inc(v.tensor_tensor_scan(out=hbuf[:, kk * TTs:(kk + 1) * TTs],
                                                    data0=dabuf[:, kk * TTs:(kk + 1) * TTs],
                                                    data1=dxbuf[:, kk * TTs:(kk + 1) * TTs],
                                                    initial=hst[:, c * 16 + s:c * 16 + s + 1],
                                                    op0=ALU.mult, op1=ALU.add))
                        if i >= 4 and (i - 4) % 8 < NPE:
                            v.wait_ge(mm.h, M["mm"][("yaccmm", i - 4)])
                        dv.inc(v.tensor_mul(out=hcbuf[:, kk * TTs:(kk + 1) * TTs],
                                            in0=hbuf[:, kk * TTs:(kk + 1) * TTs],
                                            in1=cbuf[:, k * TTs:(k + 1) * TTs]))
                        if c >= NPE:
                            if s == 0:
                                dv.inc(v.tensor_copy(out=yacc[:, c * TTs:(c + 1) * TTs],
                                                     in_=hcbuf[:, kk * TTs:(kk + 1) * TTs]))
                            else:
                                dv.inc(v.tensor_add(out=yacc[:, c * TTs:(c + 1) * TTs],
                                                    in0=yacc[:, c * TTs:(c + 1) * TTs],
                                                    in1=hcbuf[:, kk * TTs:(kk + 1) * TTs]))
                for c in range(8):
                    if c < NPE:
                        v.wait_ge(mm.h, M["mm"][("yaccmm", 15 * 8 + c)])
                        yin = pacc[c][:]
                    else:
                        yin = yacc[:, c * TTs:(c + 1) * TTs]
                    v.scalar_tensor_tensor(out=ybuf[:], in0=xas[:, c * TTs:(c + 1) * TTs],
                                           scalar=cst2[:, 2 * c + 1:2 * c + 2],
                                           in1=yin,
                                           op0=ALU.mult, op1=ALU.add)
                    v.wait_ge(ac.h, M["zsil"](c))
                    dv.inc(v.tensor_mul(out=gated[:, c * TTs:(c + 1) * TTs],
                                        in0=ybuf[:], in1=zsc[:]))
                for mc in range(8):
                    v.wait_ge(mm.h, M["mm"][("omm", mc)])
                    if mc >= 2:
                        v.wait_ge(st.h, stE + 16 * (8 * ts + mc - 1))
                    dv.inc(v.tensor_copy(out=ovac[:, (mc % 2) * TTs:(mc % 2 + 1) * TTs],
                                         in_=pout[mc % 2][:]))

            @blk.gpsimd
            def _(g):
                g.wait_ge(dv.h, DV0 + DV_PER * ts)
                for c in range(8):
                    g.dma_start(out=xas[:, c * TTs:(c + 1) * TTs],
                                in_=xa_i[c, :, tsl]).then_inc(gl.h, 16); gl.n += 16
                for i in range(128):
                    s_, c_ = i // 8, i % 8
                    g.wait_ge(dv.h, M["dv"][("sttB", s_)])
                    g.wait_ge(dv.h, M["dv"][("dtxa", c_)])
                    if i >= 4:
                        g.wait_ge(dv.h, M["dv"][("scan", i - 4)])
                    pl.inc(g.tensor_mul(out=dxbuf[:, (i % 4) * TTs:(i % 4 + 1) * TTs],
                                        in0=dtxa[:, c_ * TTs:(c_ + 1) * TTs],
                                        in1=bbuf[:, (s_ % 2) * TTs:(s_ % 2 + 1) * TTs]))

        assert dv.n == DV0 + DV_PER * (ts + 1), (dv.n, ts)
        assert ac.n == AC0 + AC_PER * (ts + 1)
        assert pl.n == PL0 + PL_PER * (ts + 1)
        assert mm.n == MM0 + MM_PER * (ts + 1)
        if cc_cb is not None:
            with nc.Block() as blk:
                @blk.gpsimd
                def _(g):
                    cc_cb(g, ts)

    with nc.Block() as blk:
        @blk.gpsimd
        def _(g):
            g.wait_ge(st.h, st.n)
    if standalone:
        return p.close()
    return None



class LnScratch:
    def __init__(self, p, N, tag="", ps_pool=None):
        self.N = N
        rows = p.sb(f"ln_rows{tag}", [1, 3 * N], F32)
        rows2 = p.sb(f"ln_rows2{tag}", [1, 2 * N], F32R)
        self.st1 = rows[0:1, 0:N]
        self.st2 = rows[0:1, N:2 * N]
        self.var = rows[0:1, 2 * N:3 * N]
        self.mu = rows2[0:1, 0:N]
        self.rstd = rows2[0:1, N:2 * N]
        self.murep = p.sb(f"ln_murep{tag}", [128, N], F32)
        self.rrep = p.sb(f"ln_rrep{tag}", [128, N], F32)
        self.sq = p.sb(f"ln_sq{tag}", [128, 2 * N], F32R)
        self.xn = p.sb(f"ln_xn{tag}", [128, N], F32)
        if ps_pool is None:
            self.psS = p.ps(f"ln_psS{tag}", [1, N])[:]
            self.psQ = p.ps(f"ln_psQ{tag}", [1, N])[:]
            self.psM = p.ps(f"ln_psM{tag}", [128, N])[:]
            self.psR = p.ps(f"ln_psR{tag}", [128, N])[:]
        else:
            self.psS = ps_pool[0][0:1, :]
            self.psQ = ps_pool[1][0:1, :]
            self.psM = ps_pool[2][:]
            self.psR = ps_pool[3][:]


def emit_ln(nc, S, ones128, ones1, xin, xout, eps_ap, sems, gamma=None, beta=None,
            store=None, st=None, extra_adds=None):
    mm, dv, ac = sems
    N = S.N
    mm0, dv0, ac0 = mm.n, dv.n, ac.n
    with nc.Block() as blk:
        @blk.tensor
        def _(t):
            for c in range(8):
                last = t.matmul(out=S.psS, lhsT=ones128[:], rhs=xin(c),
                                start=(c == 0), stop=(c == 7))
            mm.inc(last)
            for c in range(8):
                t.wait_ge(ac.h, ac0 + c + 1)
                mm.inc(t.matmul(out=S.psQ, lhsT=ones128[:],
                                rhs=S.sq[:, (c % 2) * N:(c % 2 + 1) * N],
                                start=(c == 0), stop=(c == 7)))
            t.wait_ge(dv.h, dv0 + 3)
            mm.inc(t.matmul(out=S.psM, lhsT=ones1[0:1, :], rhs=S.mu, start=True, stop=True))
            t.wait_ge(ac.h, ac0 + 10)
            mm.inc(t.matmul(out=S.psR, lhsT=ones1[0:1, :], rhs=S.rstd, start=True, stop=True))

        @blk.scalar
        def _(a):
            for c in range(8):
                if c >= 2:
                    a.wait_ge(mm.h, mm0 + 2 + (c - 2))
                ac.inc(a.activation(out=S.sq[:, (c % 2) * N:(c % 2 + 1) * N], in_=xin(c), func=AF.Square))
            a.wait_ge(dv.h, dv0 + 5)
            ac.inc(a.activation(out=S.var, in_=S.var, func=AF.Ln, bias=eps_ap))
            a.wait_ge(ac.h, ac0 + 9)
            ac.inc(a.activation(out=S.rstd, in_=S.var, func=AF.Exp, scale=-0.5))
            if store is not None:
                for c in range(8):
                    a.wait_ge(dv.h, dv0 + 8 + c)
                    dram_ap, sb_ap = store(c)
                    a.dma_start(out=dram_ap, in_=sb_ap).then_inc(st.h, 16); st.n += 16
                a.wait_ge(st.h, st.n)

        @blk.vector
        def _(v):
            v.wait_ge(mm.h, mm0 + 9)
            dv.inc(v.tensor_copy(out=S.st1, in_=S.psS))
            dv.inc(v.tensor_copy(out=S.st2, in_=S.psQ))
            v.wait_ge(dv.h, dv0 + 2)
            dv.inc(v.tensor_single_scalar(out=S.mu, in_=S.st1, scalar=1.0 / 1024.0, op=ALU.mult))
            v.wait_ge(dv.h, dv0 + 3)
            dv.inc(v.tensor_mul(out=S.st1, in0=S.mu, in1=S.mu))
            v.wait_ge(dv.h, dv0 + 4)
            dv.inc(v.scalar_tensor_tensor(out=S.var, in0=S.st2, scalar=1.0 / 1024.0,
                                          in1=S.st1, op0=ALU.mult, op1=ALU.subtract))
            v.wait_ge(mm.h, mm0 + 10)
            dv.inc(v.tensor_copy(out=S.murep[:], in_=S.psM))
            v.wait_ge(mm.h, mm0 + 11)
            dv.inc(v.tensor_copy(out=S.rrep[:], in_=S.psR))
            v.wait_ge(dv.h, dv0 + 7)
            for c in range(8):
                v.tensor_sub(out=S.xn[:], in0=xin(c), in1=S.murep[:])
                if gamma is None:
                    dv.inc(v.tensor_mul(out=xout(c), in0=S.xn[:], in1=S.rrep[:]))
                else:
                    v.tensor_mul(out=S.xn[:], in0=S.xn[:], in1=S.rrep[:])
                    dv.inc(v.tensor_scalar(out=xout(c), in0=S.xn[:], scalar1=gamma(c),
                                           scalar2=beta(c), op0=ALU.mult, op1=ALU.add))



def build_f0(TLn=TL, TTs=256, upto='', p=None, sems=None, heads=4,
             summed_in=False, partial_out=False, cc_cb=None):
    NSUP = TLn // TTs
    H8 = heads * 8
    KGN = heads * 4
    standalone = p is None
    if standalone:
        p = Prog()
    nc = p.nc
    if summed_in:
        pe_i = p.din("osum", [TLn // TTs, 8, 128, TTs])
        po_i = None
    else:
        pe_i = p.din("pe", [8, 128, TLn])
        po_i = p.din("po", [8, 128, TLn])
    x_i = p.din("x", [8, 128, TLn])
    bout_i = p.din("bouti", [128, 8])
    rm_i = p.din("rmbd", [H8, 128, 128], F32R)
    rup_i = p.din("rupbd", [128, 16 * heads * 128], F32R)
    wc_i = p.din("wci", [heads * 4096, 1024])
    ones128_i = p.din("ones128", [128, 1], F32R)
    ones1_i = p.din("ones1", [128, 128], F32R)
    if partial_out:
        o_cp = p.dout("cpart", [TLn // TTs, 8, 128, TTs])
        o_x1 = p.dout("x1o", [8, 128, TLn])
        o_x2 = None
    else:
        o_x2 = p.dout("x2", [8, 128, TLn])

    rm = p.sb("rm", [128, H8 * 128], F32R)
    rup = p.sb("rup", [128, 16 * heads * 128], F32R)
    wck = p.sb("wck", [128, 2 * 8 * 1024], BF16)
    bout = p.sb("s_bout", [128, 8], F32)
    ones128 = p.sb("s_ones128", [128, 1], F32R)
    ones1 = p.sb("s_ones1", [128, 128], F32R)
    pe = p.sb("s_pe", [128, 8 * TTs], F32)
    po = None if summed_in else p.sb("s_po", [128, 8 * TTs], F32)
    xx = p.sb("s_xx", [128, 8 * TTs], F32)
    lnin = p.sb("lnin", [128, 8 * TTs], F32R)
    x1 = p.sb("s_x1", [128, 8 * TTs], F32R)
    x2 = None if partial_out else p.sb("s_x2", [128, 8 * TTs], F32)
    yr = p.sb("yr", [128, H8 * TTs], F32R)
    hact = p.sb("hact", [128, 2 * 8 * TTs], BF16)
    acc = p.sb("s_acc", [128, 8 * TTs], F32)
    eps5 = p.sb("eps5", [1, 1], F32)
    ps1 = [p.ps(f"ps1{i}", [128, TTs]) for i in range(2)]
    ps2 = [p.ps(f"ps2{i}", [128, TTs]) for i in range(2)]
    pso = [p.ps(f"pso{i}", [128, TTs]) for i in range(2)]
    S = LnScratch(p, TTs, tag=p.pre + "F0", ps_pool=[ps1[0], ps1[1], ps2[0], ps2[1]])

    if sems is None:
        ld, gl, mm = p.sem("ld"), p.sem("gl"), p.sem("mm")
        dv, ac, st, pl = p.sem("dv"), p.sem("ac"), p.sem("st"), p.sem("pl")
    else:
        ld, gl, mm, dv, ac, st, pl = (sems["ld"], sems["gl"], sems["mm"], sems["dv"],
                                      sems["ac"], sems["st"], sems["pl"])
    ldE, stE = ld.n, st.n

    with nc.Block() as blk:
        @blk.sync
        def _(q):
            for i in range(H8):
                q.dma_start(out=rm[:, i * 128:(i + 1) * 128],
                            in_=rm_i[i]).then_inc(ld.h, 16); ld.n += 16
            for t_, s_ in ((rup, rup_i), (bout, bout_i), (ones128, ones128_i), (ones1, ones1_i)):
                q.dma_start(out=t_[:], in_=s_).then_inc(ld.h, 16); ld.n += 16

        @blk.vector
        def _(v):
            v.memset(eps5[:], 1e-5)
    ld_init = ld.n

    NLD = 16 if summed_in else 24
    for ts in range(NSUP):
        tsl = slice(ts * TTs, (ts + 1) * TTs)
        with nc.Block() as blk:
            pl_a0 = pl.n

            @blk.sync
            def _(q):
                for c in range(8):
                    q.dma_start(out=pe[:, c * TTs:(c + 1) * TTs], in_=pe_i[ts, c]).then_inc(ld.h, 16); ld.n += 16
                    if not summed_in:
                        q.dma_start(out=po[:, c * TTs:(c + 1) * TTs], in_=po_i[c, :, tsl]).then_inc(ld.h, 16); ld.n += 16
                    q.dma_start(out=xx[:, c * TTs:(c + 1) * TTs], in_=x_i[c, :, tsl]).then_inc(ld.h, 16); ld.n += 16

            ld_all = ld_init + NLD * 16 * (ts + 1)

            if summed_in:
                @blk.vector
                def _(v):
                    v.wait_ge(ld.h, ld_all)
                    for c in range(8):
                        dv.inc(v.scalar_tensor_tensor(out=lnin[:, c * TTs:(c + 1) * TTs],
                                                      in0=xx[:, c * TTs:(c + 1) * TTs],
                                                      scalar=bout[:, c:c + 1],
                                                      in1=pe[:, c * TTs:(c + 1) * TTs],
                                                      op0=ALU.add, op1=ALU.add))
            else:
                @blk.gpsimd
                def _(g):
                    g.wait_ge(ld.h, ld_all)
                    for c in range(8):
                        pl.inc(g.tensor_add(out=acc[:, c * TTs:(c + 1) * TTs],
                                            in0=pe[:, c * TTs:(c + 1) * TTs],
                                            in1=po[:, c * TTs:(c + 1) * TTs]))

                @blk.vector
                def _(v):
                    for c in range(8):
                        v.wait_ge(pl.h, pl_a0 + c + 1)
                        dv.inc(v.scalar_tensor_tensor(out=lnin[:, c * TTs:(c + 1) * TTs],
                                                      in0=xx[:, c * TTs:(c + 1) * TTs],
                                                      scalar=bout[:, c:c + 1],
                                                      in1=acc[:, c * TTs:(c + 1) * TTs],
                                                      op0=ALU.add, op1=ALU.add))
        emit_ln(nc, S, ones128, ones1,
                lambda c: lnin[:, c * TTs:(c + 1) * TTs],
                lambda c: x1[:, c * TTs:(c + 1) * TTs], eps5[0:1, 0:1], (mm, dv, ac),
                store=(lambda c: (o_x1[c, :, tsl], x1[:, c * TTs:(c + 1) * TTs].bitcast(F32)))
                if partial_out else None,
                st=st if partial_out else None)
        if upto == "a":
            with nc.Block() as blk:
                @blk.scalar
                def _(a):
                    for c in range(8):
                        srcT = x1
                        a.dma_start(out=o_x2[c, :, tsl],
                                    in_=srcT[:, c * TTs:(c + 1) * TTs].bitcast(F32)).then_inc(st.h, 16); st.n += 16
                    a.wait_ge(st.h, st.n)
            continue
        with nc.Block() as blk:
            mm0, dv0, ac0, gl0 = mm.n, dv.n, ac.n, gl.n
            s1m = lambda i: mm0 + i + 1
            s2m = lambda k: mm0 + H8 + (k // 8) * 16 + (k % 8) + 1
            cpm = lambda kg, mc: mm0 + H8 + kg * 16 + 8 + mc + 1
            s1e = lambda i: ac0 + i + 1
            gel = lambda k: ac0 + H8 + (k // 8) * 8 + (k % 8) + 1
            aad = lambda kg, mc: dv0 + kg * 8 + mc + 1

            @blk.gpsimd
            def _(g):
                for kg in range(KGN):
                    if kg >= 2:
                        g.wait_ge(mm.h, cpm(kg - 2, 7))
                    for j in range(8):
                        g.dma_start(out=wck[:, (kg % 2) * 8192 + j * 1024:(kg % 2) * 8192 + (j + 1) * 1024],
                                    in_=wc_i[kg * 1024 + j * 128:kg * 1024 + (j + 1) * 128, :]).then_inc(gl.h, 16); gl.n += 16

            @blk.tensor
            def _(t):
                t.wait_ge(ld.h, ld_init)
                for i in range(H8):
                    h, cc = i // 8, i % 8
                    if i >= 2:
                        t.wait_ge(ac.h, s1e(i - 2))
                    mm.inc(t.matmul(out=ps1[i % 2][:], lhsT=rm[:, i * 128:(i + 1) * 128],
                                    rhs=x1[:, cc * TTs:(cc + 1) * TTs], start=True, stop=True))
                for kg in range(KGN):
                    t.wait_ge(gl.h, gl0 + 16 * 8 * (kg + 1))
                    for j in range(8):
                        k = kg * 8 + j
                        h, rem = k // 32, k % 32
                        cc, pj = rem // 4, rem % 4
                        t.wait_ge(ac.h, s1e(h * 8 + cc))
                        if k >= 2:
                            t.wait_ge(ac.h, gel(k - 2))
                        q2 = (h * 8 + cc) * 2 + (pj % 2)
                        mm.inc(t.matmul(out=ps2[k % 2][:],
                                        lhsT=rup[(pj // 2) * 64:(pj // 2) * 64 + 64, q2 * 128:(q2 + 1) * 128],
                                        rhs=yr[(pj // 2) * 64:(pj // 2) * 64 + 64,
                                               (h * 8 + cc) * TTs:(h * 8 + cc + 1) * TTs],
                                        start=True, stop=True))
                    t.wait_ge(ac.h, gel(kg * 8 + 7))
                    for mc in range(8):
                        if kg >= 1 or mc >= 2:
                            pkg, pmc = (kg, mc - 2) if mc >= 2 else (kg - 1, mc + 6)
                            t.wait_ge(dv.h, aad(pkg, pmc))
                        for j in range(8):
                            last = t.matmul(out=pso[(kg * 8 + mc) % 2][:],
                                            lhsT=wck[:, (kg % 2) * 8192 + j * 1024 + mc * 128:(kg % 2) * 8192 + j * 1024 + (mc + 1) * 128],
                                            rhs=hact[:, ((kg % 2) * 8 + j) * TTs:((kg % 2) * 8 + j + 1) * TTs],
                                            start=(j == 0), stop=(j == 7))
                        mm.inc(last)

            @blk.scalar
            def _(a):
                for i in range(H8):
                    a.wait_ge(mm.h, s1m(i))
                    ac.inc(a.activation(out=yr[:, i * TTs:(i + 1) * TTs], in_=ps1[i % 2][:], func=AF.Copy))
                for k in range(KGN * 8):
                    kg, j = k // 8, k % 8
                    a.wait_ge(mm.h, s2m(k))
                    if k >= 16:
                        a.wait_ge(mm.h, cpm(kg - 2, 7))
                    ac.inc(a.activation(out=hact[:, ((kg % 2) * 8 + j) * TTs:((kg % 2) * 8 + j + 1) * TTs],
                                        in_=ps2[k % 2][:], func=AF.Gelu))

            @blk.vector
            def _(v):
                for kg in range(KGN):
                    for mc in range(8):
                        v.wait_ge(mm.h, cpm(kg, mc))
                        if kg == 0:
                            dv.inc(v.tensor_copy(out=acc[:, mc * TTs:(mc + 1) * TTs],
                                                 in_=pso[(kg * 8 + mc) % 2][:]))
                        else:
                            dv.inc(v.tensor_add(out=acc[:, mc * TTs:(mc + 1) * TTs],
                                                in0=acc[:, mc * TTs:(mc + 1) * TTs],
                                                in1=pso[(kg * 8 + mc) % 2][:]))
        if partial_out:
            with nc.Block() as blk:
                @blk.scalar
                def _(a):
                    for c in range(8):
                        a.dma_start(out=o_cp[ts, c],
                                    in_=acc[:, c * TTs:(c + 1) * TTs]).then_inc(st.h, 16); st.n += 16
                    a.wait_ge(st.h, st.n)
            if cc_cb is not None:
                with nc.Block() as blk:
                    @blk.gpsimd
                    def _(g):
                        cc_cb(g, ts)
            continue
        if upto == "b":
            with nc.Block() as blk:
                @blk.scalar
                def _(a):
                    a.wait_ge(dv.h, dv.n)
                    for c in range(8):
                        a.dma_start(out=o_x2[c, :, tsl],
                                    in_=acc[:, c * TTs:(c + 1) * TTs]).then_inc(st.h, 16); st.n += 16
                    a.wait_ge(st.h, st.n)
            continue
        with nc.Block() as blk:
            @blk.vector
            def _(v):
                for c in range(8):
                    dv.inc(v.tensor_add(out=lnin[:, c * TTs:(c + 1) * TTs],
                                        in0=acc[:, c * TTs:(c + 1) * TTs],
                                        in1=x1[:, c * TTs:(c + 1) * TTs]))
        if upto == "c1":
            with nc.Block() as blk:
                @blk.scalar
                def _(a):
                    for c in range(8):
                        a.dma_start(out=o_x2[c, :, tsl],
                                    in_=lnin[:, c * TTs:(c + 1) * TTs].bitcast(F32)).then_inc(st.h, 16); st.n += 16
                    a.wait_ge(st.h, st.n)
            continue
        emit_ln(nc, S, ones128, ones1,
                lambda c: lnin[:, c * TTs:(c + 1) * TTs],
                lambda c: x2[:, c * TTs:(c + 1) * TTs], eps5[0:1, 0:1], (mm, dv, ac),
                store=lambda c: (o_x2[c, :, tsl], x2[:, c * TTs:(c + 1) * TTs]), st=st)

    with nc.Block() as blk:
        @blk.gpsimd
        def _(g):
            g.wait_ge(st.h, st.n)
    if standalone:
        return p.close()
    return None


def host_f0_inputs(inp, l0b_res):
    x = np.asarray(inp["L0_attn_x"], np.float32)
    b_out = np.asarray(inp["L0_attn_b_out"], np.float32)
    rm = np.asarray(inp["L0_ffn_rail_maker"], np.float32)
    rup = np.asarray(inp["L0_ffn_rail_up_proj"], np.float32)
    wc = np.asarray(inp["L0_ffn_w_cproj"], np.float32)
    rmbd = np.zeros((32, 128, 128), np.float32)
    for h in range(4):
        for cc in range(8):
            for j in range(8):
                r = cc * 8 + j
                rmbd[h * 8 + cc, 16 * j:16 * j + 16, 16 * j:16 * j + 16] = rm[h, r]
    rupbd = np.zeros((128, 64 * 128), np.float32)
    for h in range(4):
        for pp in range(32):
            cc, pj = pp // 4, pp % 4
            q = (h * 8 + cc) * 2 + (pj % 2)
            ro = (pj // 2) * 64
            sub = (pj % 2) * 32
            blk = np.zeros((64, 128), np.float32)
            for rr in range(2):
                rail = 2 * pp + rr
                blk[sub + 16 * rr:sub + 16 * rr + 16, 64 * rr:64 * rr + 64] = rup[h, rail].T
            rupbd[ro:ro + 64, q * 128:(q + 1) * 128] = blk
    wct = np.ascontiguousarray(wc.T)
    maps = []
    for c in range(NCORE):
        bidx, hh = c // 2, c % 2
        tok = slice(hh * TL, (hh + 1) * TL)
        xfm = np.ascontiguousarray(x[bidx, tok].T.reshape(8, 128, TL))
        maps.append({
            "pe": l0b_res[2 * bidx]["opart"][:, :, tok].copy(),
            "po": l0b_res[2 * bidx + 1]["opart"][:, :, tok].copy(),
            "x": xfm,
            "bouti": np.ascontiguousarray(b_out.reshape(8, 128).T),
            "rmbd": rmbd,
            "rupbd": rupbd,
            "wci": wct,
            "ones128": np.ones((128, 1), np.float32),
            "ones1": np.ones((128, 128), np.float32),
        })
    return maps



def build_l1a(Tn=T, TTs=256, p=None, sems=None, sel_in=False, cc_cb=None):
    NSUP = Tn // TTs
    GR = 8
    standalone = p is None
    if standalone:
        p = Prog()
    nc = p.nc
    if sel_in:
        x2_i = p.din("x2", [8, 128, Tn])
        x2r_i = p.din("x2r", [8, 128, Tn])
        rsel_i = p.din("revsel", [128, 2])
    else:
        x2_i = p.din("x2f", [8, 128, Tn])
    wixz_i = p.din("wixzi", [1024, 4096])
    wxp_i = p.din("wxpi", [2048, 96], F32R)
    wdt_i = p.din("wdti", [64, 2048], F32R)
    wout_i = p.din("wouti", [2048, 1024])
    alog_i = p.din("alogi", [128, 256])
    cstL_i = p.din("cstLi", [128, 128])
    ones128_i = p.din("ones128", [128, 1], F32R)
    ones1_i = p.din("ones1", [128, 128], F32R)
    id128L_i = p.din("id128L", [128, 128], F32R)
    o_mo = p.dout("mo", [Tn // TTs, 8, 128, TTs])
    sc_b = nc.dram_tensor(p.pre + "l1a_scb", [16, Tn], F32, kind="Internal").ap()
    sc_c = nc.dram_tensor(p.pre + "l1a_scc", [16, Tn], F32, kind="Internal").ap()

    wmc = p.sb("wmc", [128, 2 * 1024], BF16)
    wob = p.sb("wob", [128, 2 * 1024], BF16)
    wxp = p.sb("l_wxp", [128, 16 * 96], F32R)
    wdt = p.sb("l_wdt", [64, 2048], BF16)
    ones128 = p.sb("l_ones128", [128, 1], F32R)
    ones1 = p.sb("l_ones1", [128, 128], F32R)
    x2t = p.sb("l_x2t", [128, 8 * TTs], F32R)
    hh_ = p.sb("l_hh", [128, 8 * TTs], BF16)
    halos = p.sb("l_halos", [128, 16 * 3], F32)
    xs_scr = p.sb("l_xsscr", [128, TTs + 3], F32)
    convo = p.sb("l_convo", [128, TTs], F32)
    xa = p.sb("l_xa", [128, 16 * TTs], F32R)
    zb = p.sb("l_zb", [128, 16 * TTs], BF16)
    dtr = p.sb("l_dtr", [64, TTs], BF16)
    bsb = p.sb("l_bsb", [16, TTs], F32)
    csb = p.sb("l_csb", [16, TTs], F32)
    dt_g = p.sb("l_dtg", [128, GR * TTs], F32)
    dtxa_g = p.sb("l_dtxag", [128, GR * TTs], F32)
    brep = p.sb("l_brep", [128, 2 * TTs], F32)
    crep = p.sb("l_crep", [128, 2 * TTs], F32)
    dab = p.sb("l_dab", [128, 4 * TTs], F32)
    dxb = p.sb("l_dxb", [128, 4 * TTs], F32)
    hbu = p.sb("l_hbu", [128, 4 * TTs], F32)
    hcb = p.sb("l_hcb", [128, 4 * TTs], BF16)
    id128L = p.sb("l_id128", [128, 128], BF16)
    yacc = None
    ybuf = p.sb("l_ybuf", [128, TTs], F32)
    gated = None
    ovac = p.sb("l_ovac", [128, TTs], F32)
    hst = p.sb("l_hst", [128, 256], F32)
    yacc = x2t
    gated = zb
    sp1 = p.sb("l_sp1", [128, TTs], F32)
    sp2 = p.sb("l_sp2", [128, TTs], F32)
    zsc = sp2
    if sel_in:
        selscr = sp1
    alog = p.sb("l_alog", [128, 256], F32)
    aneg = alog
    cst = p.sb("l_cst", [128, 128], F32)
    eps5 = p.sb("l_eps5", [1, 1], F32)
    pa = [p.ps(f"l_pa{i}", [128, TTs]) for i in range(8)]
    S = LnScratch(p, TTs, tag="L", ps_pool=[pa[4], pa[5], pa[6], pa[7]])

    if sel_in:
        x2rv = p.sb("l_x2rv", [128, 2 * TTs], F32)
        rsel = p.sb("l_rsel", [128, 2], F32)
    if sems is None:
        ld, gl, mm = p.sem("ld"), p.sem("gl"), p.sem("mm")
        dv, ac, pl, st = p.sem("dv"), p.sem("ac"), p.sem("pl"), p.sem("st")
    else:
        ld, gl, mm, dv, ac, pl, st = (sems["ld"], sems["gl"], sems["mm"], sems["dv"],
                                      sems["ac"], sems["pl"], sems["st"])
    stE = st.n

    def ccol(j):
        return cst[:, j:j + 1]

    with nc.Block() as blk:
        @blk.sync
        def _(q):
            for kc in range(16):
                q.dma_start(out=wxp[:, kc * 96:(kc + 1) * 96],
                            in_=wxp_i[kc * 128:(kc + 1) * 128, :]).then_inc(ld.h, 16); ld.n += 16
            pairs2 = [(alog, alog_i),
                      (cst, cstL_i), (ones128, ones128_i), (ones1, ones1_i)]
            if sel_in:
                pairs2.append((rsel, rsel_i))
            for t_, s_ in pairs2:
                q.dma_start(out=t_[:], in_=s_).then_inc(ld.h, 16); ld.n += 16

        @blk.gpsimd
        def _(g):
            g.dma_start(out=wdt[:], in_=wdt_i).then_inc(ld.h, 16); ld.n += 16
            g.dma_start(out=id128L[:], in_=id128L_i).then_inc(ld.h, 16); ld.n += 16
            g.wait_ge(ld.h, ld.n)

        @blk.vector
        def _(v):
            v.memset(eps5[:], 1e-5)
            v.memset(hst[:], 0.0)
            v.memset(halos[:], 0.0)

        @blk.scalar
        def _(a):
            a.wait_ge(ld.h, ld.n)
            a.activation(out=aneg[:], in_=alog[:], func=AF.Exp)
    with nc.Block() as blk:
        @blk.vector
        def _(v):
            v.tensor_single_scalar(out=aneg[:], in_=aneg[:], scalar=-1.0, op=ALU.mult)

    for ts in range(NSUP):
        tsl = slice(ts * TTs, (ts + 1) * TTs)
        with nc.Block() as blk:
            if not sel_in:
                @blk.gpsimd
                def _(g):
                    for c in range(8):
                        g.dma_start(out=x2t[:, c * TTs:(c + 1) * TTs],
                                    in_=x2_i[c, :, tsl]).then_inc(gl.h, 16); gl.n += 16
                    g.wait_ge(gl.h, gl.n)
            else:
                ld0s, gl0s, dv0s = ld.n, gl.n, dv.n

                @blk.sync
                def _(q):
                    q.wait_ge(dv.h, dv0s)
                    for c in range(8):
                        if c >= 2:
                            q.wait_ge(dv.h, dv0s + (c - 2) + 1)
                        q.dma_start(out=x2rv[:, (c % 2) * TTs:(c % 2 + 1) * TTs],
                                    in_=x2r_i[c, :, tsl]).then_inc(ld.h, 16); ld.n += 16

                @blk.gpsimd
                def _(g):
                    g.wait_ge(dv.h, dv0s)
                    for c in range(8):
                        g.dma_start(out=x2t[:, c * TTs:(c + 1) * TTs],
                                    in_=x2_i[c, :, tsl]).then_inc(gl.h, 16); gl.n += 16

                @blk.vector
                def _(v):
                    for c in range(8):
                        v.wait_ge(ld.h, ld0s + 16 * (c + 1))
                        v.wait_ge(gl.h, gl0s + 16 * (c + 1))
                        v.tensor_single_scalar(out=selscr[:], in_=x2rv[:, (c % 2) * TTs:(c % 2 + 1) * TTs],
                                               scalar=rsel[:, 1:2], op=ALU.mult)
                        dv.inc(v.scalar_tensor_tensor(out=x2t[:, c * TTs:(c + 1) * TTs],
                                                      in0=x2t[:, c * TTs:(c + 1) * TTs],
                                                      scalar=rsel[:, 0:1], in1=selscr[:],
                                                      op0=ALU.mult, op1=ALU.add))
        emit_ln(nc, S, ones128, ones1,
                lambda c: x2t[:, c * TTs:(c + 1) * TTs],
                lambda c: hh_[:, c * TTs:(c + 1) * TTs], eps5[0:1, 0:1], (mm, dv, ac),
                gamma=lambda c: cst[:, 112 + c:113 + c], beta=lambda c: cst[:, 120 + c:121 + c])
        with nc.Block() as blk:
            mm0, dv0, ac0, gl0 = mm.n, dv.n, ac.n, gl.n
            m_ipz = lambda j: mm0 + j + 1
            m_ipx = lambda c: mm0 + 16 + c + 1
            m_xse = lambda c: dv0 + 2 * c + 1
            m_conv = lambda c: dv0 + 2 * c + 2
            m_zv = lambda j: ac0 + j + 1
            m_xa = lambda c: ac0 + 16 + c + 1
            wcol = lambda m: (16 + m) * 128 if m < 16 else (m - 16) * 128

            @blk.gpsimd
            def _(g):
                for m in range(32):
                    if m >= 2:
                        g.wait_ge(mm.h, mm0 + (m - 2) + 1)
                    src = bass.AP(tensor=wixz_i.tensor, offset=wcol(m),
                                  ap=[[4096, 128], [128 * 4096, 8], [1, 128]])
                    g.dma_start(out=wmc[:, (m % 2) * 1024:(m % 2) * 1024 + 1024],
                                in_=src).then_inc(gl.h, 16); gl.n += 16

            @blk.tensor
            def _(t):
                for j in range(16):
                    bank = j % 2
                    t.wait_ge(gl.h, gl0 + 16 * (j + 1))
                    if j >= 2:
                        t.wait_ge(ac.h, m_zv(j - 2))
                    for kc in range(8):
                        last = t.matmul(out=pa[bank][:],
                                        lhsT=wmc[:, (j % 2) * 1024 + kc * 128:(j % 2) * 1024 + (kc + 1) * 128],
                                        rhs=hh_[:, kc * TTs:(kc + 1) * TTs],
                                        start=(kc == 0), stop=(kc == 7))
                    mm.inc(last)
                for c in range(16):
                    m = 16 + c
                    bank = c % 2
                    t.wait_ge(gl.h, gl0 + 16 * (m + 1))
                    if c >= 2:
                        t.wait_ge(dv.h, m_xse(c - 2))
                    else:
                        t.wait_ge(ac.h, m_zv(14 + c))
                    for kc in range(8):
                        last = t.matmul(out=pa[bank][:],
                                        lhsT=wmc[:, (m % 2) * 1024 + kc * 128:(m % 2) * 1024 + (kc + 1) * 128],
                                        rhs=hh_[:, kc * TTs:(kc + 1) * TTs],
                                        start=(kc == 0), stop=(kc == 7))
                    mm.inc(last)

            @blk.vector
            def _(v):
                for c in range(16):
                    bank = c % 2
                    v.wait_ge(mm.h, m_ipx(c))
                    if c >= 1:
                        v.wait_ge(ac.h, m_xa(c - 1))
                    v.tensor_copy(out=xs_scr[:, 0:3], in_=halos[:, c * 3:(c + 1) * 3])
                    dv.inc(v.tensor_copy(out=xs_scr[:, 3:3 + TTs], in_=pa[bank][:]))
                    v.wait_ge(dv.h, m_xse(c))
                    v.tensor_copy(out=halos[:, c * 3:(c + 1) * 3], in_=xs_scr[:, TTs:TTs + 3])
                    v.tensor_single_scalar(out=convo[:], in_=xs_scr[:, 0:TTs],
                                           scalar=ccol(6 * c + 1), op=ALU.mult)
                    for k in range(1, 4):
                        kwv = {"out": convo[:], "in0": xs_scr[:, k:k + TTs],
                               "scalar": ccol(6 * c + 1 + k), "in1": convo[:],
                               "op0": ALU.mult, "op1": ALU.add}
                        if k < 3:
                            v.scalar_tensor_tensor(**kwv)
                        else:
                            dv.inc(v.scalar_tensor_tensor(**kwv))

            @blk.scalar
            def _(a):
                for j in range(16):
                    a.wait_ge(mm.h, m_ipz(j))
                    ac.inc(a.activation(out=zb[:, j * TTs:(j + 1) * TTs],
                                        in_=pa[j % 2][:], func=AF.Copy))
                for c in range(16):
                    a.wait_ge(dv.h, m_conv(c))
                    ac.inc(a.activation(out=xa[:, c * TTs:(c + 1) * TTs], in_=convo[:],
                                        func=AF.Silu, bias=ccol(6 * c)))

        with nc.Block() as blk:
            mm0, dv0 = mm.n, dv.n

            @blk.tensor
            def _(t):
                for kc in range(16):
                    last = t.matmul(out=pa[0][0:64, :], lhsT=wxp[:, kc * 96:kc * 96 + 64],
                                    rhs=xa[:, kc * TTs:(kc + 1) * TTs],
                                    start=(kc == 0), stop=(kc == 15))
                mm.inc(last)
                for kc in range(16):
                    last = t.matmul(out=pa[1][0:16, :], lhsT=wxp[:, kc * 96 + 64:kc * 96 + 80],
                                    rhs=xa[:, kc * TTs:(kc + 1) * TTs],
                                    start=(kc == 0), stop=(kc == 15))
                mm.inc(last)
                for kc in range(16):
                    last = t.matmul(out=pa[2][0:16, :], lhsT=wxp[:, kc * 96 + 80:kc * 96 + 96],
                                    rhs=xa[:, kc * TTs:(kc + 1) * TTs],
                                    start=(kc == 0), stop=(kc == 15))
                mm.inc(last)

            @blk.vector
            def _(v):
                v.wait_ge(mm.h, mm0 + 1)
                dv.inc(v.tensor_copy(out=dtr[:], in_=pa[0][0:64, :]))
                v.wait_ge(mm.h, mm0 + 2)
                dv.inc(v.tensor_copy(out=bsb[:], in_=pa[1][0:16, :]))
                v.wait_ge(mm.h, mm0 + 3)
                dv.inc(v.tensor_copy(out=csb[:], in_=pa[2][0:16, :]))

            @blk.scalar
            def _(a):
                a.wait_ge(dv.h, dv0 + 3)
                a.dma_start(out=sc_b[:, tsl], in_=bsb[:]).then_inc(st.h, 16); st.n += 16
                a.dma_start(out=sc_c[:, tsl], in_=csb[:]).then_inc(st.h, 16); st.n += 16
                a.wait_ge(st.h, st.n)

        for g_ in range(2):
            with nc.Block() as blk:
                mm0, dv0, ac0, pl0, ld0 = mm.n, dv.n, ac.n, pl.n, ld.n
                m_dtmm = lambda c8: mm0 + c8 + 1
                m_relu = lambda c8: ac0 + c8 + 1
                m_ldbc = lambda s: ld0 + 16 * (2 * s + 2)
                NPE = 6
                pbank = [pa[0], pa[1], pa[2], pa[5], pa[6], pa[7]]
                m_da = lambda i: ac0 + 8 + i + max(0, i - 4) + 1
                m_hst = lambda i: ac0 + 8 + min(i + 5, 128) + i + 1
                m_zsil = lambda c8: ac0 + 264 + c8 + 1
                dvm = {}
                n_ = dv0
                for c8 in range(8):
                    n_ += 1; dvm[("dtadd", c8)] = n_
                    n_ += 1; dvm[("dtxa", c8)] = n_
                for s in range(16):
                    for c8 in range(8):
                        i = s * 8 + c8
                        n_ += 1; dvm[("scan", i)] = n_
                        n_ += 1; dvm[("hc", i)] = n_
                        if c8 >= NPE:
                            n_ += 1; dvm[("yacc", i)] = n_
                for c8 in range(8):
                    n_ += 1; dvm[("gated", c8)] = n_
                plm = {}
                for i in range(128):
                    plm[("dbx", i)] = pl0 + i + 1
                mmm = {}
                q_ = mm0
                for c8 in range(8):
                    q_ += 1; mmm[("dtmm", c8)] = q_
                for s in range(16):
                    for c8 in range(NPE):
                        q_ += 1; mmm[("yaccmm", s * 8 + c8)] = q_

                @blk.sync
                def _(q):
                    for s in range(16):
                        k = s % 2
                        if s >= 2:
                            q.wait_ge(dv.h, dvm[("hc", (s - 2) * 8 + 7)])
                        q.dma_start(out=brep[:, k * TTs:(k + 1) * TTs],
                                    in_=sc_b[s:s + 1, tsl].partition_broadcast(128)).then_inc(ld.h, 16); ld.n += 16
                        q.dma_start(out=crep[:, k * TTs:(k + 1) * TTs],
                                    in_=sc_c[s:s + 1, tsl].partition_broadcast(128)).then_inc(ld.h, 16); ld.n += 16

                @blk.tensor
                def _(t):
                    for c8 in range(8):
                        c = g_ * 8 + c8
                        if c8 >= 2:
                            t.wait_ge(ac.h, m_relu(c8 - 2))
                        mm.inc(t.matmul(out=pa[3 + c8 % 2][:], lhsT=wdt[:, c * 128:(c + 1) * 128],
                                        rhs=dtr[:], start=True, stop=True))
                    for s in range(16):
                        for c8 in range(NPE):
                            i = s * 8 + c8
                            kk = i % 4
                            t.wait_ge(dv.h, dvm[("hc", i)])
                            mm.inc(t.matmul(out=pbank[c8][:], lhsT=id128L[:],
                                            rhs=hcb[:, kk * TTs:(kk + 1) * TTs],
                                            start=(s == 0), stop=(s == 15)))

                @blk.scalar
                def _(a):
                    for c8 in range(8):
                        c = g_ * 8 + c8
                        a.wait_ge(mm.h, m_dtmm(c8))
                        if c8 >= 1:
                            a.wait_ge(dv.h, dvm[("dtadd", c8 - 1)])
                        a.activation(out=sp1[:], in_=pa[3 + c8 % 2][:], func=AF.Abs, bias=cst[:, 96 + c:97 + c])
                        a.activation(out=sp1[:], in_=sp1[:], func=AF.Exp, scale=-1.0)
                        a.activation(out=sp1[:], in_=sp1[:], func=AF.Ln, bias=1.0)
                        ac.inc(a.activation(out=sp2[:], in_=pa[3 + c8 % 2][:], func=AF.Relu,
                                            bias=cst[:, 96 + c:97 + c]))
                    def emit_hst(i):
                        s_, c8_ = i // 8, i % 8
                        c_ = g_ * 8 + c8_
                        ac.inc(a.activation(out=hst[:, c_ * 16 + s_:c_ * 16 + s_ + 1],
                                            in_=hbu[:, (i % 4) * TTs + TTs - 1:(i % 4) * TTs + TTs],
                                            func=AF.Copy))
                    for i in range(128):
                        s, c8 = i // 8, i % 8
                        c = g_ * 8 + c8
                        a.wait_ge(dv.h, dvm[("dtadd", c8)])
                        if i >= 4:
                            a.wait_ge(dv.h, dvm[("scan", i - 4)])
                        ac.inc(a.activation(out=dab[:, (i % 4) * TTs:(i % 4 + 1) * TTs],
                                            in_=dt_g[:, c8 * TTs:(c8 + 1) * TTs], func=AF.Exp,
                                            scale=aneg[:, c * 16 + s:c * 16 + s + 1]))
                        if i >= 4:
                            emit_hst(i - 4)
                    for i in range(124, 128):
                        a.wait_ge(dv.h, dvm[("scan", i)])
                        emit_hst(i)
                    for c8 in range(8):
                        c = g_ * 8 + c8
                        if c8 >= 1:
                            a.wait_ge(dv.h, dvm[("gated", c8 - 1)])
                        ac.inc(a.activation(out=zsc[:], in_=zb[:, c * TTs:(c + 1) * TTs], func=AF.Silu))

                @blk.vector
                def _(v):
                    for c8 in range(8):
                        c = g_ * 8 + c8
                        v.wait_ge(ac.h, m_relu(c8))
                        dv.inc(v.tensor_add(out=dt_g[:, c8 * TTs:(c8 + 1) * TTs], in0=sp1[:], in1=sp2[:]))
                        dv.inc(v.tensor_mul(out=dtxa_g[:, c8 * TTs:(c8 + 1) * TTs],
                                            in0=dt_g[:, c8 * TTs:(c8 + 1) * TTs],
                                            in1=xa[:, c * TTs:(c + 1) * TTs]))
                    for s in range(16):
                        k = s % 2
                        for c8 in range(8):
                            c = g_ * 8 + c8
                            i = s * 8 + c8
                            kk = i % 4
                            v.wait_ge(ac.h, m_hst(i - 4) if i >= 4 else m_da(i))
                            v.wait_ge(pl.h, plm[("dbx", i)])
                            dv.inc(v.tensor_tensor_scan(out=hbu[:, kk * TTs:(kk + 1) * TTs],
                                                        data0=dab[:, kk * TTs:(kk + 1) * TTs],
                                                        data1=dxb[:, kk * TTs:(kk + 1) * TTs],
                                                        initial=hst[:, c * 16 + s:c * 16 + s + 1],
                                                        op0=ALU.mult, op1=ALU.add))
                            if i >= 4 and (i - 4) % 8 < NPE:
                                v.wait_ge(mm.h, mmm[("yaccmm", i - 4)])
                            dv.inc(v.tensor_mul(out=hcb[:, kk * TTs:(kk + 1) * TTs],
                                                in0=hbu[:, kk * TTs:(kk + 1) * TTs],
                                                in1=crep[:, k * TTs:(k + 1) * TTs]))
                            if c8 >= NPE:
                                if s == 0:
                                    dv.inc(v.tensor_copy(out=yacc[:, c8 * TTs:(c8 + 1) * TTs],
                                                         in_=hcb[:, kk * TTs:(kk + 1) * TTs]))
                                else:
                                    dv.inc(v.tensor_add(out=yacc[:, c8 * TTs:(c8 + 1) * TTs],
                                                        in0=yacc[:, c8 * TTs:(c8 + 1) * TTs],
                                                        in1=hcb[:, kk * TTs:(kk + 1) * TTs]))
                    for c8 in range(8):
                        c = g_ * 8 + c8
                        if c8 < NPE:
                            v.wait_ge(mm.h, mmm[("yaccmm", 15 * 8 + c8)])
                            yin = pbank[c8][:]
                        else:
                            yin = yacc[:, c8 * TTs:(c8 + 1) * TTs]
                        v.scalar_tensor_tensor(out=ybuf[:], in0=xa[:, c * TTs:(c + 1) * TTs],
                                               scalar=ccol(6 * c + 5),
                                               in1=yin,
                                               op0=ALU.mult, op1=ALU.add)
                        v.wait_ge(ac.h, m_zsil(c8))
                        dv.inc(v.tensor_mul(out=gated[:, c * TTs:(c + 1) * TTs],
                                            in0=ybuf[:], in1=zsc[:]))

                @blk.gpsimd
                def _(g):
                    for i in range(128):
                        s_, c8_ = i // 8, i % 8
                        g.wait_ge(ld.h, m_ldbc(s_))
                        g.wait_ge(dv.h, dvm[("dtxa", c8_)])
                        if i >= 4:
                            g.wait_ge(dv.h, dvm[("scan", i - 4)])
                        pl.inc(g.tensor_mul(out=dxb[:, (i % 4) * TTs:(i % 4 + 1) * TTs],
                                            in0=dtxa_g[:, c8_ * TTs:(c8_ + 1) * TTs],
                                            in1=brep[:, (s_ % 2) * TTs:(s_ % 2 + 1) * TTs]))

        with nc.Block() as blk:
            mm0, dv0, gl0, st5 = mm.n, dv.n, gl.n, st.n
            m_okc = lambda kc: mm0 + kc + 1
            m_oev = lambda mc: dv0 + mc + 1

            @blk.gpsimd
            def _(g):
                for kc in range(16):
                    if kc >= 2:
                        g.wait_ge(mm.h, m_okc(kc - 2))
                    g.dma_start(out=wob[:, (kc % 2) * 1024:(kc % 2 + 1) * 1024],
                                in_=wout_i[kc * 128:(kc + 1) * 128, :]).then_inc(gl.h, 16); gl.n += 16

            @blk.tensor
            def _(t):
                for kc in range(16):
                    t.wait_ge(gl.h, gl0 + 16 * (kc + 1))
                    for mc in range(8):
                        last = t.matmul(out=pa[mc][:],
                                        lhsT=wob[:, (kc % 2) * 1024 + mc * 128:(kc % 2) * 1024 + (mc + 1) * 128],
                                        rhs=gated[:, kc * TTs:(kc + 1) * TTs],
                                        start=(kc == 0), stop=(kc == 15))
                    mm.inc(last)

            @blk.vector
            def _(v):
                v.wait_ge(mm.h, m_okc(15))
                for mc in range(8):
                    if mc >= 1:
                        v.wait_ge(st.h, st5 + 16 * mc)
                    dv.inc(v.tensor_copy(out=ovac[:], in_=pa[mc][:]))

            @blk.scalar
            def _(a):
                for mc in range(8):
                    a.wait_ge(dv.h, m_oev(mc))
                    a.dma_start(out=o_mo[ts, mc],
                                in_=ovac[:]).then_inc(st.h, 16); st.n += 16
                a.wait_ge(st.h, st.n)

        if cc_cb is not None:
            with nc.Block() as blk:
                @blk.gpsimd
                def _(g):
                    cc_cb(g, ts)

    with nc.Block() as blk:
        @blk.gpsimd
        def _(g):
            g.wait_ge(st.h, st.n)
    if standalone:
        return p.close()
    return None


def host_l1a_inputs(inp, x2_full):
    maps = []
    ones128 = np.ones((128, 1), np.float32)
    ones1 = np.ones((128, 128), np.float32)
    selm = np.zeros((16, 16, 128), np.float32)
    for s in range(16):
        selm[s, s, :] = 1.0
    selm = np.ascontiguousarray(selm.transpose(1, 0, 2).reshape(16, 2048))
    ng = np.asarray(inp["L1_attn_norm_g"], np.float32)
    nb = np.asarray(inp["L1_attn_norm_b"], np.float32)
    for c in range(NCORE):
        bidx, hh = c // 2, c % 2
        mk = "m1" if hh == 0 else "m2"
        in_w = np.asarray(inp[f"L1_attn_{mk}_in_w"], np.float32)
        conv_w = np.asarray(inp[f"L1_attn_{mk}_conv_w"], np.float32)
        conv_b = np.asarray(inp[f"L1_attn_{mk}_conv_b"], np.float32)
        xproj_w = np.asarray(inp[f"L1_attn_{mk}_xproj_w"], np.float32)
        dt_w = np.asarray(inp[f"L1_attn_{mk}_dt_w"], np.float32)
        dt_b = np.asarray(inp[f"L1_attn_{mk}_dt_b"], np.float32)
        Alog = np.asarray(inp[f"L1_attn_{mk}_Alog"], np.float32)
        Dp = np.asarray(inp[f"L1_attn_{mk}_D"], np.float32)
        out_w = np.asarray(inp[f"L1_attn_{mk}_out_w"], np.float32)
        xv = x2_full[bidx]
        if hh == 1:
            xv = xv[::-1]
        cstL = np.zeros((128, 128), np.float32)
        cb16 = conv_b.reshape(16, 128)
        d16 = Dp.reshape(16, 128)
        for ch in range(16):
            cstL[:, 6 * ch] = cb16[ch]
            for k in range(4):
                cstL[:, 6 * ch + 1 + k] = conv_w[:, k].reshape(16, 128)[ch]
            cstL[:, 6 * ch + 5] = d16[ch]
        cstL[:, 96:112] = dt_b.reshape(16, 128).T
        cstL[:, 112:120] = ng.reshape(8, 128).T
        cstL[:, 120:128] = nb.reshape(8, 128).T
        maps.append({
            "x2f": np.ascontiguousarray(xv.T.reshape(8, 128, T)),
            "wixzi": np.ascontiguousarray(in_w.T),
            "wxpi": np.ascontiguousarray(xproj_w.T),
            "wdti": np.ascontiguousarray(dt_w.T),
            "wouti": np.ascontiguousarray(out_w.T),
            "alogi": np.ascontiguousarray(Alog.reshape(16, 128, 16).transpose(1, 0, 2).reshape(128, 256)),
            "cstLi": cstL,
            "ones128": ones128,
            "ones1": ones1,
            "id128L": np.eye(128, dtype=np.float32),
        })
    return maps



def build_l1b(TLn=TL, TTs=256, p=None, sems=None, summed_in=False, kc_n=32, cc_cb=None,
              partial_out=False):
    NSUP = TLn // TTs
    kg_n = kc_n // 8
    standalone = p is None
    if standalone:
        p = Prog()
    nc = p.nc
    x2_i = p.din("x2h", [8, 128, TLn])
    if summed_in:
        fw_i = p.din("mosum", [TLn // TTs, 8, 128, TTs])
        bw_i = None
    else:
        fw_i = p.din("fwdh", [8, 128, TLn])
        bw_i = p.din("bwdh", [8, 128, TLn])
    ff1_i = p.din("ff1wi", [1024, kc_n * 128])
    ff2_i = p.din("ff2wi", [kc_n * 128, 1024])
    cstF_i = p.din("cstFi", [128, 56])
    ones128_i = p.din("ones128", [128, 1], F32R)
    ones1_i = p.din("ones1", [128, 128], F32R)
    if partial_out:
        o_ffp = p.dout("ffp", [TLn // TTs, 8, 128, TTs])
        o_xm = p.dout("xmo", [8, 128, TLn])
        o_x3 = None
    else:
        o_x3 = p.dout("x3", [8, 128, TLn])

    ff1w = p.sb("b_ff1w", [128, 8 * kc_n * 128], BF16)
    ff2w = p.sb("b_ff2w", [128, kc_n * 1024], BF16)
    cstF = p.sb("b_cst", [128, 56], F32)
    ones128 = p.sb("b_ones128", [128, 1], F32R)
    ones1 = p.sb("b_ones1", [128, 128], F32R)
    x2t = p.sb("b_x2t", [128, 8 * TTs], F32)
    fwt = p.sb("b_fwt", [128, 8 * TTs], F32R)
    bwt = None if summed_in else p.sb("b_bwt", [128, 8 * TTs], F32)
    xm = p.sb("b_xm", [128, 8 * TTs], F32R)
    hf = p.sb("b_hf", [128, 8 * TTs], BF16)
    h1 = p.sb("b_h1", [128, 8 * TTs], BF16)
    acc = p.sb("b_acc", [128, 8 * TTs], F32)
    x3t = p.sb("b_x3t", [128, 8 * TTs], F32)
    scr = p.sb("b_scr", [128, TTs], F32)
    eps5 = p.sb("b_eps5", [1, 1], F32)
    pp = [p.ps(f"b_pp{i}", [128, TTs]) for i in range(4)]
    S = LnScratch(p, TTs, tag="B", ps_pool=None)

    if sems is None:
        ld, gl, mm = p.sem("ld"), p.sem("gl"), p.sem("mm")
        dv, ac, pl, st = p.sem("dv"), p.sem("ac"), p.sem("pl"), p.sem("st")
    else:
        ld, gl, mm, dv, ac, pl, st = (sems["ld"], sems["gl"], sems["mm"], sems["dv"],
                                      sems["ac"], sems["pl"], sems["st"])
    stE = st.n
    W1C = kc_n * 128

    with nc.Block() as blk:
        @blk.sync
        def _(q):
            for t_, s_ in ((cstF, cstF_i), (ones128, ones128_i), (ones1, ones1_i)):
                q.dma_start(out=t_[:], in_=s_).then_inc(ld.h, 16); ld.n += 16

        @blk.gpsimd
        def _(g):
            for kc in range(8):
                g.dma_start(out=ff1w[:, kc * W1C:(kc + 1) * W1C],
                            in_=ff1_i[kc * 128:(kc + 1) * 128, :]).then_inc(gl.h, 16); gl.n += 16
            for kc in range(kc_n):
                g.dma_start(out=ff2w[:, kc * 1024:(kc + 1) * 1024],
                            in_=ff2_i[kc * 128:(kc + 1) * 128, :]).then_inc(gl.h, 16); gl.n += 16
            g.wait_ge(gl.h, gl.n)
            g.wait_ge(ld.h, ld.n)

        @blk.vector
        def _(v):
            v.memset(eps5[:], 1e-5)

    for ts in range(NSUP):
        tsl = slice(ts * TTs, (ts + 1) * TTs)
        with nc.Block() as blk:
            dv0, ac0, pl0 = dv.n, ac.n, pl.n
            ld0 = ld.n

            @blk.sync
            def _(q):
                for c in range(8):
                    q.dma_start(out=x2t[:, c * TTs:(c + 1) * TTs], in_=x2_i[c, :, tsl]).then_inc(ld.h, 16); ld.n += 16
                    if not summed_in:
                        q.dma_start(out=bwt[:, c * TTs:(c + 1) * TTs], in_=bw_i[c, :, tsl]).then_inc(ld.h, 16); ld.n += 16

            @blk.gpsimd
            def _(g):
                gl0_ = gl.n
                for c in range(8):
                    g.dma_start(out=fwt[:, c * TTs:(c + 1) * TTs], in_=fw_i[ts, c]).then_inc(gl.h, 16); gl.n += 16
                g.wait_ge(ld.h, ld0 + 16 * (16 if not summed_in else 8))
                g.wait_ge(gl.h, gl0_ + 8 * 16)
                if not summed_in:
                    for c in range(8):
                        pl.inc(g.tensor_add(out=acc[:, c * TTs:(c + 1) * TTs],
                                            in0=fwt[:, c * TTs:(c + 1) * TTs],
                                            in1=bwt[:, c * TTs:(c + 1) * TTs]))
                else:
                    for c in range(8):
                        pl.inc(g.tensor_copy(out=acc[:, c * TTs:(c + 1) * TTs],
                                             in_=fwt[:, c * TTs:(c + 1) * TTs]))

            @blk.vector
            def _(v):
                for c in range(8):
                    v.wait_ge(pl.h, pl0 + c + 1)
                    if c >= 1:
                        v.wait_ge(ac.h, ac0 + c)
                    dv.inc(v.tensor_add(out=scr[:], in0=acc[:, c * TTs:(c + 1) * TTs],
                                        in1=x2t[:, c * TTs:(c + 1) * TTs]))
                for c in range(8):
                    v.wait_ge(ac.h, ac0 + c + 1)
                    dv.inc(v.tensor_add(out=xm[:, c * TTs:(c + 1) * TTs],
                                        in0=x2t[:, c * TTs:(c + 1) * TTs],
                                        in1=x3t[:, c * TTs:(c + 1) * TTs]))

            @blk.scalar
            def _(a):
                for c in range(8):
                    a.wait_ge(dv.h, dv0 + c + 1)
                    ac.inc(a.activation(out=x3t[:, c * TTs:(c + 1) * TTs], in_=scr[:], func=AF.Gelu))
        emit_ln(nc, S, ones128, ones1,
                lambda c: xm[:, c * TTs:(c + 1) * TTs],
                lambda c: hf[:, c * TTs:(c + 1) * TTs], eps5[0:1, 0:1], (mm, dv, ac),
                gamma=lambda c: cstF[:, c:c + 1], beta=lambda c: cstF[:, 8 + c:9 + c])
        with nc.Block() as blk:
            mm0, dv0, ac0 = mm.n, dv.n, ac.n
            m_f1 = lambda kg, j: mm0 + kg * 16 + j + 1
            m_f2 = lambda kg, mc: mm0 + kg * 16 + 8 + mc + 1
            m_gel = lambda kg, j: ac0 + kg * 8 + j + 1
            m_aad = lambda kg, mc: dv0 + kg * 8 + mc + 1

            @blk.tensor
            def _(t):
                for kg in range(kg_n):
                    for j in range(8):
                        kc = kg * 8 + j
                        if kc >= 2:
                            t.wait_ge(ac.h, m_gel((kc - 2) // 8, (kc - 2) % 8))
                        for k8 in range(8):
                            last = t.matmul(out=pp[kc % 2][:],
                                            lhsT=ff1w[:, k8 * W1C + kc * 128:k8 * W1C + (kc + 1) * 128],
                                            rhs=hf[:, k8 * TTs:(k8 + 1) * TTs],
                                            start=(k8 == 0), stop=(k8 == 7))
                        mm.inc(last)
                    t.wait_ge(ac.h, m_gel(kg, 7))
                    for mc in range(8):
                        if kg >= 1 or mc >= 2:
                            pkg, pmc = (kg, mc - 2) if mc >= 2 else (kg - 1, mc + 6)
                            t.wait_ge(dv.h, m_aad(pkg, pmc))
                        for j in range(8):
                            last = t.matmul(out=pp[2 + (kg * 8 + mc) % 2][:],
                                            lhsT=ff2w[:, (kg * 8 + j) * 1024 + mc * 128:(kg * 8 + j) * 1024 + (mc + 1) * 128],
                                            rhs=h1[:, j * TTs:(j + 1) * TTs],
                                            start=(j == 0), stop=(j == 7))
                        mm.inc(last)

            @blk.scalar
            def _(a):
                for kg in range(kg_n):
                    for j in range(8):
                        kc = kg * 8 + j
                        a.wait_ge(mm.h, m_f1(kg, j))
                        if kc >= 8:
                            a.wait_ge(mm.h, m_f2(kc // 8 - 1, 7))
                        ac.inc(a.activation(out=h1[:, j * TTs:(j + 1) * TTs], in_=pp[kc % 2][:],
                                            func=AF.Gelu, bias=cstF[:, 16 + kc:17 + kc]))

            @blk.vector
            def _(v):
                for kg in range(kg_n):
                    for mc in range(8):
                        v.wait_ge(mm.h, m_f2(kg, mc))
                        if kg == 0:
                            dv.inc(v.tensor_copy(out=acc[:, mc * TTs:(mc + 1) * TTs],
                                                 in_=pp[2 + (kg * 8 + mc) % 2][:]))
                        else:
                            dv.inc(v.tensor_add(out=acc[:, mc * TTs:(mc + 1) * TTs],
                                                in0=acc[:, mc * TTs:(mc + 1) * TTs],
                                                in1=pp[2 + (kg * 8 + mc) % 2][:]))
        if partial_out:
            with nc.Block() as blk:
                @blk.scalar
                def _(a):
                    a.wait_ge(dv.h, dv.n)
                    for c in range(8):
                        a.dma_start(out=o_ffp[ts, c],
                                    in_=acc[:, c * TTs:(c + 1) * TTs]).then_inc(st.h, 16); st.n += 16
                        a.dma_start(out=o_xm[c, :, tsl],
                                    in_=xm[:, c * TTs:(c + 1) * TTs].bitcast(F32)).then_inc(st.h, 16); st.n += 16
                    a.wait_ge(st.h, st.n)
            if cc_cb is not None:
                with nc.Block() as blk:
                    @blk.gpsimd
                    def _(g):
                        cc_cb(g, ts)
            continue
        with nc.Block() as blk:
            @blk.vector
            def _(v):
                for c in range(8):
                    dv.inc(v.scalar_tensor_tensor(out=scr[:], in0=acc[:, c * TTs:(c + 1) * TTs],
                                                  scalar=cstF[:, 48 + c:49 + c],
                                                  in1=xm[:, c * TTs:(c + 1) * TTs],
                                                  op0=ALU.add, op1=ALU.add))
                    v.wait_ge(dv.h, dv.n)
                    dv.inc(v.tensor_add(out=fwt[:, c * TTs:(c + 1) * TTs], in0=scr[:],
                                        in1=x2t[:, c * TTs:(c + 1) * TTs]))
        emit_ln(nc, S, ones128, ones1,
                lambda c: fwt[:, c * TTs:(c + 1) * TTs],
                lambda c: x3t[:, c * TTs:(c + 1) * TTs], eps5[0:1, 0:1], (mm, dv, ac),
                store=lambda c: (o_x3[c, :, tsl], x3t[:, c * TTs:(c + 1) * TTs]), st=st)

    with nc.Block() as blk:
        @blk.gpsimd
        def _(g):
            g.wait_ge(st.h, st.n)
    if standalone:
        return p.close()
    return None


def host_l1b_inputs(inp, x2_full, l1a_res):
    ff1w = np.asarray(inp["L1_attn_ff1_w"], np.float32)
    ff2w = np.asarray(inp["L1_attn_ff2_w"], np.float32)
    ff1b = np.asarray(inp["L1_attn_ff1_b"], np.float32)
    ff2b = np.asarray(inp["L1_attn_ff2_b"], np.float32)
    fg = np.asarray(inp["L1_attn_ffn_g"], np.float32)
    fb = np.asarray(inp["L1_attn_ffn_b"], np.float32)
    cstF = np.zeros((128, 56), np.float32)
    cstF[:, 0:8] = fg.reshape(8, 128).T
    cstF[:, 8:16] = fb.reshape(8, 128).T
    cstF[:, 16:48] = ff1b.reshape(32, 128).T
    cstF[:, 48:56] = ff2b.reshape(8, 128).T
    maps = []
    for c in range(NCORE):
        bidx, hh = c // 2, c % 2
        tok = slice(hh * TL, (hh + 1) * TL)
        fwd = l1a_res[2 * bidx]["mo"]
        bwd = l1a_res[2 * bidx + 1]["mo"]
        maps.append({
            "x2h": np.ascontiguousarray(x2_full[bidx, tok].T.reshape(8, 128, TL)),
            "fwdh": np.ascontiguousarray(fwd[:, :, tok]),
            "bwdh": np.ascontiguousarray(bwd[:, :, tok]),
            "ff1wi": np.ascontiguousarray(ff1w.T),
            "ff2wi": np.ascontiguousarray(ff2w.T),
            "cstFi": cstF,
            "ones128": np.ones((128, 1), np.float32),
            "ones1": np.ones((128, 128), np.float32),
        })
    return maps



def build_f1(TLn=TL, TTs=256, p=None, sems=None, mc_n=32, partial_out=False, TH=None, cc_cb=None):
    if TH is None:
        TH = TLn // 4
    NTH = TLn // TH
    NTT = TH // TTs
    standalone = p is None
    if standalone:
        p = Prog()
    nc = p.nc
    x3_i = p.din("x3h", [8, 128, TLn])
    wr_i = p.din("wri", [1024, 4], F32R)
    w1_i = p.din("w1t", [mc_n, 4, 8, 128, 128])
    b1_i = p.din("b1i", [4, mc_n * 128], F32R)
    wo_i = p.din("woi", [mc_n * 128, 1024])
    cstM_i = p.din("cstMi", [128, 24])
    ones128_i = p.din("ones128", [128, 1], F32R)
    ones1_i = p.din("ones1", [128, 128], F32R)
    if partial_out:
        o_wop = p.dout("wop", [TLn // TH, 8, 128, TH])
        o_x4 = None
    else:
        o_x4 = p.dout("x4", [8, 128, TLn])

    wr = p.sb("f_wr", [128, 8 * 4], F32R)
    b1 = p.sb("f_b1", [4, mc_n * 128], BF16)
    w1b = p.sb("f_w1b", [128, 2 * 4096], BF16)
    wob = p.sb("f_wob", [128, 2 * 1024], BF16)
    cstM = p.sb("f_cst", [128, 24], F32)
    ones128 = p.sb("f_ones128", [128, 1], F32R)
    ones1 = p.sb("f_ones1", [128, 128], F32R)
    x3t = p.sb("f_x3t", [128, 8 * TH], F32R)
    xme = p.sb("f_xme", [128, 32 * TH], BF16)
    acc = p.sb("f_acc", [128, 8 * TH], F32)
    lg = p.sb("f_lg", [4, TTs], F32)
    le = [p.sb(f"f_le{e}", [1, TTs], F32) for e in range(4)]
    cm1 = p.sb("f_cm1", [1, TTs], F32)
    cm2 = p.sb("f_cm2", [1, TTs], F32)
    me = [p.sb(f"f_me{e}", [1, TTs], F32R) for e in range(4)]
    oh = p.sb("f_oh", [4, TH], BF16)
    mrep = p.sb("f_mrep", [128, 2 * TTs], F32)
    inter = p.sb("f_inter", [128, 2 * TTs], BF16)
    moe = p.sb("f_moe", [128, TTs], F32R)
    lnin = p.sb("f_lnin", [128, 8 * TTs], F32R)
    x4t = p.sb("f_x4t", [128, 8 * TTs], F32)
    eps12 = p.sb("f_eps12", [1, 1], F32)
    eps5 = p.sb("f_eps5", [1, 1], F32)
    pq = [p.ps(f"f_pq{i}", [128, TTs]) for i in range(4)]
    S = LnScratch(p, TTs, tag="F", ps_pool=None)

    if sems is None:
        ld, gl, mm = p.sem("ld"), p.sem("gl"), p.sem("mm")
        dv, ac, pl, st = p.sem("dv"), p.sem("ac"), p.sem("pl"), p.sem("st")
    else:
        ld, gl, mm, dv, ac, pl, st = (sems["ld"], sems["gl"], sems["mm"], sems["dv"],
                                      sems["ac"], sems["pl"], sems["st"])
    stE = st.n

    with nc.Block() as blk:
        @blk.sync
        def _(q):
            for kc in range(8):
                q.dma_start(out=wr[:, kc * 4:(kc + 1) * 4],
                            in_=wr_i[kc * 128:(kc + 1) * 128, :]).then_inc(ld.h, 16); ld.n += 16
            for t_, s_ in ((cstM, cstM_i), (ones128, ones128_i), (ones1, ones1_i)):
                q.dma_start(out=t_[:], in_=s_).then_inc(ld.h, 16); ld.n += 16

        @blk.vector
        def _(v):
            v.memset(eps12[:], 1e-12)
            v.memset(eps5[:], 1e-5)

        @blk.gpsimd
        def _(g):
            g.dma_start(out=b1[:], in_=b1_i).then_inc(gl.h, 16); gl.n += 16
            g.wait_ge(gl.h, gl.n)
            g.wait_ge(ld.h, ld.n)

    for th in range(NTH):
        thl = slice(th * TH, (th + 1) * TH)
        with nc.Block() as blk:
            gl0, mm0, dv0, pl0 = gl.n, mm.n, dv.n, pl.n
            DV_TT, MM_TT, PL_TT = 41, 5, 128
            m_evlg = lambda tt: dv0 + DV_TT * tt + 1
            m_me = lambda tt, e: dv0 + DV_TT * tt + 2 + e
            m_mrep = lambda tt, e: dv0 + DV_TT * tt + 6 + 9 * e
            m_xm = lambda tt, e, k8: dv0 + DV_TT * tt + 6 + 9 * e + 1 + k8
            m_rt = lambda tt: mm0 + MM_TT * tt + 1
            m_rep = lambda tt, e: mm0 + MM_TT * tt + 2 + e

            @blk.gpsimd
            def _(g):
                for c in range(8):
                    g.dma_start(out=x3t[:, c * TH:(c + 1) * TH],
                                in_=x3_i[c, :, thl]).then_inc(gl.h, 16); gl.n += 16
                g.wait_ge(gl.h, gl.n)
                for tt in range(NTT):
                    g.wait_ge(dv.h, m_evlg(tt))
                    for e in range(4):
                        g.dma_start(out=le[e][:], in_=lg[e:e + 1, :]).then_inc(pl.h, 16); pl.n += 16
                    g.wait_ge(dv.h, m_me(tt, 3))
                    for e in range(4):
                        g.dma_start(out=oh[e:e + 1, tt * TTs:(tt + 1) * TTs],
                                    in_=me[e][:]).then_inc(pl.h, 16); pl.n += 16

            @blk.tensor
            def _(t):
                t.wait_ge(gl.h, gl0 + 8 * 16)
                for tt in range(NTT):
                    if tt >= 1:
                        t.wait_ge(dv.h, m_evlg(tt - 1))
                    for kc in range(8):
                        last = t.matmul(out=pq[0][0:4, :], lhsT=wr[:, kc * 4:(kc + 1) * 4],
                                        rhs=x3t[:, kc * TH + tt * TTs:kc * TH + tt * TTs + TTs],
                                        start=(kc == 0), stop=(kc == 7))
                    mm.inc(last)
                    for e in range(4):
                        t.wait_ge(dv.h, m_me(tt, e))
                        gi = tt * 4 + e
                        if gi >= 2:
                            pt, pe_ = (gi - 2) // 4, (gi - 2) % 4
                            t.wait_ge(dv.h, m_mrep(pt, pe_))
                        mm.inc(t.matmul(out=pq[1 + e % 2][:], lhsT=ones1[0:1, :], rhs=me[e][:],
                                        start=True, stop=True))

            @blk.vector
            def _(v):
                for tt in range(NTT):
                    v.wait_ge(mm.h, m_rt(tt))
                    if tt >= 1:
                        v.wait_ge(pl.h, pl0 + PL_TT * (tt - 1) + 128)
                    dv.inc(v.tensor_copy(out=lg[:], in_=pq[0][0:4, :]))
                    v.wait_ge(pl.h, pl0 + PL_TT * tt + 64)
                    for e in range(4):
                        first = True
                        for o in range(4):
                            if o == e:
                                continue
                            op = ALU.is_ge if o > e else ALU.is_gt
                            if first:
                                v.tensor_tensor(out=cm1[:], in0=le[e][:], in1=le[o][:], op=op)
                            else:
                                v.tensor_tensor(out=cm2[:], in0=le[e][:], in1=le[o][:], op=op)
                                v.tensor_mul(out=cm1[:], in0=cm1[:], in1=cm2[:])
                            first = False
                        dv.inc(v.tensor_copy(out=me[e][:], in_=cm1[:]))
                    for e in range(4):
                        v.wait_ge(mm.h, m_rep(tt, e))
                        dv.inc(v.tensor_copy(out=mrep[:, (e % 2) * TTs:(e % 2 + 1) * TTs],
                                             in_=pq[1 + e % 2][:]))
                        for k8 in range(8):
                            dv.inc(v.tensor_mul(
                                out=xme[:, (e * 8 + k8) * TH + tt * TTs:(e * 8 + k8) * TH + tt * TTs + TTs],
                                in0=x3t[:, k8 * TH + tt * TTs:k8 * TH + tt * TTs + TTs],
                                in1=mrep[:, (e % 2) * TTs:(e % 2 + 1) * TTs]))
        with nc.Block() as blk:
            gl0, mm0, dv0, ac0 = gl.n, mm.n, dv.n, ac.n
            MMC = NTT * 9
            m_w1i = lambda Mc, tt: mm0 + Mc * MMC + tt + 1
            m_woi = lambda Mc, tt, mc: mm0 + Mc * MMC + NTT + tt * 8 + mc + 1
            m_gel = lambda Mc, tt: ac0 + Mc * NTT + tt + 1
            m_aad = lambda Mc, tt, mc: dv0 + Mc * NTT * 8 + tt * 8 + mc + 1

            @blk.gpsimd
            def _(g):
                for Mc in range(mc_n):
                    if Mc >= 2:
                        g.wait_ge(mm.h, m_w1i(Mc - 1, 0))
                    g.dma_start(out=w1b[:, (Mc % 2) * 4096:(Mc % 2 + 1) * 4096],
                                in_=w1_i[Mc].rearrange("e k p m -> p e k m")).then_inc(gl.h, 16); gl.n += 16
                    g.dma_start(out=wob[:, (Mc % 2) * 1024:(Mc % 2 + 1) * 1024],
                                in_=wo_i[Mc * 128:(Mc + 1) * 128, :]).then_inc(gl.h, 16); gl.n += 16

            @blk.tensor
            def _(t):
                for Mc in range(mc_n):
                    t.wait_ge(gl.h, gl0 + 32 * (Mc + 1))
                    for tt in range(NTT):
                        bank = (Mc * NTT + tt) % 2
                        if Mc * NTT + tt >= 2:
                            pk = Mc * NTT + tt - 2
                            t.wait_ge(ac.h, m_gel(pk // NTT, pk % NTT))
                        for e in range(4):
                            for k8 in range(8):
                                t.matmul(out=pq[bank][:],
                                         lhsT=w1b[:, (Mc % 2) * 4096 + e * 1024 + k8 * 128:(Mc % 2) * 4096 + e * 1024 + (k8 + 1) * 128],
                                         rhs=xme[:, (e * 8 + k8) * TH + tt * TTs:(e * 8 + k8) * TH + tt * TTs + TTs],
                                         start=(e == 0 and k8 == 0), stop=False)
                        last = t.matmul(out=pq[bank][:], lhsT=b1[:, Mc * 128:(Mc + 1) * 128],
                                        rhs=oh[:, tt * TTs:(tt + 1) * TTs], start=False, stop=True)
                        mm.inc(last)
                    for tt in range(NTT):
                        t.wait_ge(ac.h, m_gel(Mc, tt))
                        for mc in range(8):
                            i2 = tt * 8 + mc
                            if Mc * NTT * 8 + i2 >= 2:
                                pk = Mc * NTT * 8 + i2 - 2
                                t.wait_ge(dv.h, m_aad(pk // (NTT * 8), (pk % (NTT * 8)) // 8, pk % 8))
                            mm.inc(t.matmul(out=pq[2 + i2 % 2][:],
                                            lhsT=wob[:, (Mc % 2) * 1024 + mc * 128:(Mc % 2) * 1024 + (mc + 1) * 128],
                                            rhs=inter[:, (tt % 2) * TTs:(tt % 2 + 1) * TTs],
                                            start=True, stop=True))

            @blk.scalar
            def _(a):
                for Mc in range(mc_n):
                    for tt in range(NTT):
                        a.wait_ge(mm.h, m_w1i(Mc, tt))
                        if Mc * NTT + tt >= 2:
                            pk = Mc * NTT + tt - 2
                            a.wait_ge(mm.h, m_woi(pk // NTT, pk % NTT, 7))
                        ac.inc(a.activation(out=inter[:, (tt % 2) * TTs:(tt % 2 + 1) * TTs],
                                            in_=pq[(Mc * NTT + tt) % 2][:], func=AF.Gelu))

            @blk.vector
            def _(v):
                for Mc in range(mc_n):
                    for tt in range(NTT):
                        for mc in range(8):
                            i2 = tt * 8 + mc
                            v.wait_ge(mm.h, m_woi(Mc, tt, mc))
                            dst = acc[:, mc * TH + tt * TTs:mc * TH + tt * TTs + TTs]
                            if Mc == 0:
                                dv.inc(v.tensor_copy(out=dst, in_=pq[2 + i2 % 2][:]))
                            else:
                                dv.inc(v.tensor_add(out=dst, in0=dst, in1=pq[2 + i2 % 2][:]))
        if partial_out:
            with nc.Block() as blk:
                @blk.scalar
                def _(a):
                    a.wait_ge(dv.h, dv.n)
                    for c in range(8):
                        a.dma_start(out=o_wop[th, c],
                                    in_=acc[:, c * TH:(c + 1) * TH]).then_inc(st.h, 16); st.n += 16
                    a.wait_ge(st.h, st.n)
            if cc_cb is not None:
                with nc.Block() as blk:
                    @blk.gpsimd
                    def _(g):
                        cc_cb(g, th)
            continue
        for tt in range(NTT):
            ttl = slice(th * TH + tt * TTs, th * TH + (tt + 1) * TTs)
            with nc.Block() as blk:
                @blk.vector
                def _(v):
                    for c in range(8):
                        dv.inc(v.scalar_tensor_tensor(
                            out=lnin[:, c * TTs:(c + 1) * TTs],
                            in0=acc[:, c * TH + tt * TTs:c * TH + tt * TTs + TTs],
                            scalar=cstM[:, c:c + 1],
                            in1=x3t[:, c * TH + tt * TTs:c * TH + tt * TTs + TTs],
                            op0=ALU.add, op1=ALU.add))
            emit_ln(nc, S, ones128, ones1,
                    lambda c: lnin[:, c * TTs:(c + 1) * TTs],
                    lambda c: x4t[:, c * TTs:(c + 1) * TTs], eps12[0:1, 0:1], (mm, dv, ac),
                    gamma=lambda c: cstM[:, 8 + c:9 + c], beta=lambda c: cstM[:, 16 + c:17 + c])
            with nc.Block() as blk:
                @blk.vector
                def _(v):
                    for c in range(8):
                        dv.inc(v.tensor_add(out=lnin[:, c * TTs:(c + 1) * TTs],
                                            in0=x4t[:, c * TTs:(c + 1) * TTs].bitcast(F32R),
                                            in1=x3t[:, c * TH + tt * TTs:c * TH + tt * TTs + TTs]))
            emit_ln(nc, S, ones128, ones1,
                    lambda c: lnin[:, c * TTs:(c + 1) * TTs],
                    lambda c: x4t[:, c * TTs:(c + 1) * TTs], eps5[0:1, 0:1], (mm, dv, ac),
                    store=lambda c: (o_x4[c, :, ttl], x4t[:, c * TTs:(c + 1) * TTs]), st=st)

    with nc.Block() as blk:
        @blk.gpsimd
        def _(g):
            g.wait_ge(st.h, st.n)
    if standalone:
        return p.close()
    return None


def host_f1_inputs(inp, x3_res):
    Wr = np.asarray(inp["L1_ffn_W_router"], np.float32)
    W1 = np.asarray(inp["L1_ffn_W1"], np.float32)
    b1 = np.asarray(inp["L1_ffn_b1"], np.float32)
    Wo = np.asarray(inp["L1_ffn_Wo"], np.float32)
    bo = np.asarray(inp["L1_ffn_bo"], np.float32)
    lng = np.asarray(inp["L1_ffn_ln_gamma"], np.float32)
    lnb = np.asarray(inp["L1_ffn_ln_beta"], np.float32)
    w1t = np.ascontiguousarray(W1.reshape(4, 8, 128, 32, 128).transpose(3, 0, 1, 2, 4))
    cstM = np.zeros((128, 24), np.float32)
    cstM[:, 0:8] = bo.reshape(8, 128).T
    cstM[:, 8:16] = lng.reshape(8, 128).T
    cstM[:, 16:24] = lnb.reshape(8, 128).T
    maps = []
    for c in range(NCORE):
        maps.append({
            "x3h": x3_res[c]["x3"],
            "wri": np.ascontiguousarray(Wr),
            "w1t": w1t,
            "b1i": np.ascontiguousarray(b1),
            "woi": np.ascontiguousarray(Wo),
            "cstMi": cstM,
            "ones128": np.ones((128, 1), np.float32),
            "ones1": np.ones((128, 128), np.float32),
        })
    return maps



def build_f0b(p, sems, Tn=T, TTs=256):
    nc = p.nc
    x1_i = p.din("x1", [8, 128, Tn])
    cs_i = p.din("csum", [Tn // TTs, 8, 128, TTs])
    o_x2 = p.dout("x2", [8, 128, Tn])
    o_x2r = p.dout("x2r", [8, 128, Tn])
    ones128_i = p.din("ones128g", [128, 1], F32R)
    ones1_i = p.din("ones1g", [128, 128], F32R)
    ld, gl, mm, dv, ac, st = (sems["ld"], sems["gl"], sems["mm"], sems["dv"],
                              sems["ac"], sems["st"])
    x1t = p.sb("x1t", [128, 8 * TTs], F32)
    cst_ = p.sb("cstt", [128, 8 * TTs], F32)
    lnin = p.sb("lnin", [128, 8 * TTs], F32R)
    x2t = p.sb("x2t", [128, 8 * TTs], F32)
    x2rv = p.sb("x2rv", [128, TTs], F32)
    ones128 = p.sb("ones128", [128, 1], F32R)
    ones1 = p.sb("ones1", [128, 128], F32R)
    eps5 = p.sb("eps5", [1, 1], F32)
    pp = [p.ps(f"pp{i}", [128, TTs]) for i in range(4)]
    S = LnScratch(p, TTs, tag=p.pre, ps_pool=pp)
    with nc.Block() as blk:
        @blk.sync
        def _(q):
            q.dma_start(out=ones128[:], in_=ones128_i).then_inc(ld.h, 16); ld.n += 16
            q.dma_start(out=ones1[:], in_=ones1_i).then_inc(ld.h, 16); ld.n += 16

        @blk.vector
        def _(v):
            v.memset(eps5[:], 1e-5)
    NSUP = Tn // TTs
    for ts in range(NSUP):
        tsl = slice(ts * TTs, (ts + 1) * TTs)
        with nc.Block() as blk:
            ld0 = ld.n

            @blk.sync
            def _(q):
                for c in range(8):
                    q.dma_start(out=x1t[:, c * TTs:(c + 1) * TTs], in_=x1_i[c, :, tsl]).then_inc(ld.h, 16); ld.n += 16
                    q.dma_start(out=cst_[:, c * TTs:(c + 1) * TTs], in_=cs_i[ts, c]).then_inc(ld.h, 16); ld.n += 16

            @blk.vector
            def _(v):
                v.wait_ge(ld.h, ld0 + 16 * 16)
                for c in range(8):
                    dv.inc(v.tensor_add(out=lnin[:, c * TTs:(c + 1) * TTs],
                                        in0=x1t[:, c * TTs:(c + 1) * TTs],
                                        in1=cst_[:, c * TTs:(c + 1) * TTs]))
        emit_ln(nc, S, ones128, ones1,
                lambda c: lnin[:, c * TTs:(c + 1) * TTs],
                lambda c: x2t[:, c * TTs:(c + 1) * TTs], eps5[0:1, 0:1], (mm, dv, ac),
                store=lambda c: (o_x2[c, :, tsl], x2t[:, c * TTs:(c + 1) * TTs]), st=st)
        with nc.Block() as blk:
            dv0 = dv.n

            @blk.vector
            def _(v):
                for c in range(8):
                    if c >= 1:
                        v.wait_ge(st.h, st.n)
                    dv.inc(v.tensor_copy(
                        out=x2rv[:],
                        in_=bass.AP(x2t, c * TTs + TTs - 1, [[8 * TTs, 128], [-1, TTs]])))
                    v.wait_ge(dv.h, dv.n)
                    v.dma_start if False else None

            @blk.scalar
            def _(a):
                for c in range(8):
                    a.wait_ge(dv.h, dv0 + c + 1)
                    a.dma_start(out=o_x2r[c, :, Tn - (ts + 1) * TTs:Tn - ts * TTs],
                                in_=x2rv[:]).then_inc(st.h, 16); st.n += 16
                    a.wait_ge(st.h, st.n)
    with nc.Block() as blk:
        @blk.gpsimd
        def _(g):
            g.wait_ge(st.h, st.n)


def build_l1bb(p, sems, Tn=T, TTs=256):
    nc = p.nc
    xm_i = p.din("xm", [8, 128, Tn])
    ff_i = p.din("ffsum", [Tn // TTs, 8, 128, TTs])
    x2_i = p.din("x2j", [8, 128, Tn])
    cstD_i = p.din("cstD", [128, 8])
    ones128_i = p.din("ones128j", [128, 1], F32R)
    ones1_i = p.din("ones1j", [128, 128], F32R)
    o_x3 = p.dout("x3", [8, 128, Tn])
    ld, mm, dv, ac, st = sems["ld"], sems["mm"], sems["dv"], sems["ac"], sems["st"]
    xmt = p.sb("xmt", [128, 8 * TTs], F32)
    fft = p.sb("fft", [128, 8 * TTs], F32)
    x2t = p.sb("x2tj", [128, 8 * TTs], F32)
    scr = p.sb("scrj", [128, TTs], F32)
    lnin = p.sb("lninj", [128, 8 * TTs], F32R)
    x3t = p.sb("x3tj", [128, 8 * TTs], F32)
    cstD = p.sb("cstDs", [128, 8], F32)
    ones128 = p.sb("ones128s", [128, 1], F32R)
    ones1 = p.sb("ones1s", [128, 128], F32R)
    eps5 = p.sb("eps5j", [1, 1], F32)
    pp = [p.ps(f"ppj{i}", [128, TTs]) for i in range(4)]
    S = LnScratch(p, TTs, tag=p.pre, ps_pool=pp)
    with nc.Block() as blk:
        @blk.sync
        def _(q):
            for t_, s_ in ((cstD, cstD_i), (ones128, ones128_i), (ones1, ones1_i)):
                q.dma_start(out=t_[:], in_=s_).then_inc(ld.h, 16); ld.n += 16

        @blk.vector
        def _(v):
            v.memset(eps5[:], 1e-5)
    NSUP = Tn // TTs
    for ts in range(NSUP):
        tsl = slice(ts * TTs, (ts + 1) * TTs)
        with nc.Block() as blk:
            ld0 = ld.n

            @blk.sync
            def _(q):
                for c in range(8):
                    q.dma_start(out=xmt[:, c * TTs:(c + 1) * TTs], in_=xm_i[c, :, tsl]).then_inc(ld.h, 16); ld.n += 16
                    q.dma_start(out=fft[:, c * TTs:(c + 1) * TTs], in_=ff_i[ts, c]).then_inc(ld.h, 16); ld.n += 16
                    q.dma_start(out=x2t[:, c * TTs:(c + 1) * TTs], in_=x2_i[c, :, tsl]).then_inc(ld.h, 16); ld.n += 16

            @blk.vector
            def _(v):
                v.wait_ge(ld.h, ld0 + 24 * 16)
                for c in range(8):
                    dv.inc(v.scalar_tensor_tensor(out=scr[:], in0=fft[:, c * TTs:(c + 1) * TTs],
                                                  scalar=cstD[:, c:c + 1],
                                                  in1=xmt[:, c * TTs:(c + 1) * TTs],
                                                  op0=ALU.add, op1=ALU.add))
                    v.wait_ge(dv.h, dv.n)
                    dv.inc(v.tensor_add(out=lnin[:, c * TTs:(c + 1) * TTs], in0=scr[:],
                                        in1=x2t[:, c * TTs:(c + 1) * TTs]))
        emit_ln(nc, S, ones128, ones1,
                lambda c: lnin[:, c * TTs:(c + 1) * TTs],
                lambda c: x3t[:, c * TTs:(c + 1) * TTs], eps5[0:1, 0:1], (mm, dv, ac),
                store=lambda c: (o_x3[c, :, tsl], x3t[:, c * TTs:(c + 1) * TTs]), st=st)
    with nc.Block() as blk:
        @blk.gpsimd
        def _(g):
            g.wait_ge(st.h, st.n)


def build_f1b(p, sems, Tn=T, TTs=256):
    nc = p.nc
    wo_i = p.din("wosum", [Tn // TTs, 8, 128, TTs])
    x3_i = p.din("x3l", [8, 128, Tn])
    cstM_i = p.din("cstM2", [128, 24])
    ones128_i = p.din("ones128l", [128, 1], F32R)
    ones1_i = p.din("ones1l", [128, 128], F32R)
    o_x4 = p.dout("x4", [8, 128, Tn])
    ld, gl, mm, dv, ac, st = (sems["ld"], sems["gl"], sems["mm"], sems["dv"],
                              sems["ac"], sems["st"])
    wot = p.sb("wot", [128, 8 * TTs], F32)
    x3t = p.sb("x3tl", [128, 8 * TTs], F32R)
    lnin = p.sb("lninl", [128, 8 * TTs], F32R)
    moe = p.sb("moel", [128, 8 * TTs], F32R)
    x4t = p.sb("x4tl", [128, 8 * TTs], F32)
    cstM = p.sb("cstMs", [128, 24], F32)
    ones128 = p.sb("ones128t", [128, 1], F32R)
    ones1 = p.sb("ones1t", [128, 128], F32R)
    eps12 = p.sb("eps12l", [1, 1], F32)
    eps5 = p.sb("eps5l", [1, 1], F32)
    pp = [p.ps(f"ppl{i}", [128, TTs]) for i in range(4)]
    S = LnScratch(p, TTs, tag=p.pre, ps_pool=pp)
    with nc.Block() as blk:
        @blk.sync
        def _(q):
            for t_, s_ in ((cstM, cstM_i), (ones128, ones128_i), (ones1, ones1_i)):
                q.dma_start(out=t_[:], in_=s_).then_inc(ld.h, 16); ld.n += 16

        @blk.vector
        def _(v):
            v.memset(eps12[:], 1e-12)
            v.memset(eps5[:], 1e-5)
    NSUP = Tn // TTs
    for ts in range(NSUP):
        tsl = slice(ts * TTs, (ts + 1) * TTs)
        with nc.Block() as blk:
            ld0, gl0 = ld.n, gl.n

            @blk.sync
            def _(q):
                for c in range(8):
                    q.dma_start(out=wot[:, c * TTs:(c + 1) * TTs], in_=wo_i[ts, c]).then_inc(ld.h, 16); ld.n += 16

            @blk.gpsimd
            def _(g):
                for c in range(8):
                    g.dma_start(out=x3t[:, c * TTs:(c + 1) * TTs], in_=x3_i[c, :, tsl]).then_inc(gl.h, 16); gl.n += 16

            @blk.vector
            def _(v):
                v.wait_ge(ld.h, ld0 + 8 * 16)
                v.wait_ge(gl.h, gl0 + 8 * 16)
                for c in range(8):
                    dv.inc(v.scalar_tensor_tensor(out=lnin[:, c * TTs:(c + 1) * TTs],
                                                  in0=wot[:, c * TTs:(c + 1) * TTs],
                                                  scalar=cstM[:, c:c + 1],
                                                  in1=x3t[:, c * TTs:(c + 1) * TTs],
                                                  op0=ALU.add, op1=ALU.add))
        emit_ln(nc, S, ones128, ones1,
                lambda c: lnin[:, c * TTs:(c + 1) * TTs],
                lambda c: moe[:, c * TTs:(c + 1) * TTs], eps12[0:1, 0:1], (mm, dv, ac),
                gamma=lambda c: cstM[:, 8 + c:9 + c], beta=lambda c: cstM[:, 16 + c:17 + c])
        with nc.Block() as blk:
            @blk.vector
            def _(v):
                for c in range(8):
                    dv.inc(v.tensor_add(out=lnin[:, c * TTs:(c + 1) * TTs],
                                        in0=moe[:, c * TTs:(c + 1) * TTs],
                                        in1=x3t[:, c * TTs:(c + 1) * TTs]))
        emit_ln(nc, S, ones128, ones1,
                lambda c: lnin[:, c * TTs:(c + 1) * TTs],
                lambda c: x4t[:, c * TTs:(c + 1) * TTs], eps5[0:1, 0:1], (mm, dv, ac),
                store=lambda c: (o_x4[c, :, tsl], x4t[:, c * TTs:(c + 1) * TTs]), st=st)
    with nc.Block() as blk:
        @blk.gpsimd
        def _(g):
            g.wait_ge(st.h, st.n)


PAIRS = [[0, 1], [2, 3], [4, 5], [6, 7]]


def build_fused():
    p = Prog()
    nc = p.nc
    sems = {k: p.sem(k) for k in ("ld", "gl", "mm", "dv", "ac", "pl", "st")}
    gx = nc.dram_tensor("x", [8, 128, T], F32, kind="ExternalInput").ap()
    gxr = nc.dram_tensor("xr", [8, 128, T], F32R, kind="ExternalInput").ap()
    out_x4 = nc.dram_tensor("x4", [8, 128, T], F32, kind="ExternalOutput").ap()
    dram = lambda n, sh, dt=F32: nc.dram_tensor(n, list(sh), dt, kind="Internal").ap()
    g_xa = dram("g_xa", [8, 128, T], F32R)
    g_z = dram("g_z", [8, 128, T])
    g_dtp = dram("g_dtp", [128, T]); g_bp = dram("g_bp", [16, T]); g_cp = dram("g_cp", [16, T])
    g_dts = dram("g_dts", [128, T]); g_bs = dram("g_bs", [16, T]); g_cs = dram("g_cs", [16, T])
    g_op = dram("g_op", [8, 8, 128, 512]); g_os = dram("g_os", [8, 8, 128, 512])
    g_cpart = dram("g_cpart", [8, 8, 128, 512]); g_csum = dram("g_csum", [8, 8, 128, 512])
    g_x1 = dram("g_x1", [8, 128, T])
    g_x2 = dram("g_x2", [8, 128, T]); g_x2r = dram("g_x2r", [8, 128, T])
    g_mo = dram("g_mo", [8, 8, 128, 512]); g_mos = dram("g_mos", [8, 8, 128, 512])
    g_ffp = dram("g_ffp", [8, 8, 128, 512]); g_ffs = dram("g_ffs", [8, 8, 128, 512])
    g_xm = dram("g_xm", [8, 128, T])
    g_x3 = dram("g_x3", [8, 128, T])
    g_wop = dram("g_wop", [8, 8, 128, 512]); g_wos = dram("g_wos", [8, 8, 128, 512])

    cc_k = [0]

    def collective(items):
        with nc.Block() as blk:
            @blk.gpsimd
            def _(g):
                for s_, d_ in items:
                    cs = p.sem(f"cc{cc_k[0]}"); cc_k[0] += 1
                    g.collective_compute("AllReduce", ALU.add, replica_groups=PAIRS,
                                         ins=[s_], outs=[d_]).then_inc(cs.h)
                    g.wait_ge(cs.h, 1)

    def mk_cc(items, stc, base, per, TTc=512):
        css = []

        def cb(g, ts):
            if ts % 2 == 0:
                return
            k2 = ts // 2
            g.wait_ge(stc.h, base + per * (ts + 1))
            for s_, d_ in items:
                cs = p.sem(f"cc{cc_k[0]}"); cc_k[0] += 1
                g.collective_compute("AllReduce", ALU.add, replica_groups=PAIRS,
                                     ins=[s_[2 * k2:2 * k2 + 2]],
                                     outs=[d_[2 * k2:2 * k2 + 2]]).then_inc(cs.h)
                css.append(cs)

        def fin():
            with nc.Block() as blk:
                @blk.gpsimd
                def _(g):
                    for cs in css:
                        g.wait_ge(cs.h, 1)
        return cb, fin

    scopes = ExitStack()
    scopes.enter_context(nc.named_scope("s_l0a"))
    p.begin_stage("a_")
    p.io = {"x": gxr, "xa": g_xa, "z": g_z, "dtp": g_dtp, "bp": g_bp, "cp": g_cp}
    build_l0a(p=p, sems=sems)
    p.end_stage(); p.io = {}
    scopes.close(); scopes = ExitStack(); scopes.enter_context(nc.named_scope("c_1"))
    collective([(g_dtp, g_dts), (g_bp, g_bs), (g_cp, g_cs)])
    scopes.close(); scopes = ExitStack(); scopes.enter_context(nc.named_scope("s_l0b"))

    p.begin_stage("b_")
    p.io = {"xa": g_xa, "z": g_z, "dts": g_dts, "bs": g_bs, "cs": g_cs, "opart": g_op}
    cb2, fin2 = mk_cc([(g_op, g_os)], sems["st"], sems["st"].n, 128)
    build_l0b(p=p, sems=sems, summed=True, cc_cb=cb2)
    p.end_stage(); p.io = {}
    scopes.close(); scopes = ExitStack(); scopes.enter_context(nc.named_scope("c_2"))
    fin2()
    scopes.close(); scopes = ExitStack(); scopes.enter_context(nc.named_scope("s_f0a"))

    p.begin_stage("f_")
    p.io = {"osum": g_os, "x": gx, "cpart": g_cpart, "x1o": g_x1}
    cb3, fin3 = mk_cc([(g_cpart, g_csum)], sems["st"], sems["st"].n, 256)
    build_f0(TLn=T, TTs=512, p=p, sems=sems, heads=2, summed_in=True, partial_out=True,
             cc_cb=cb3)
    p.end_stage(); p.io = {}
    scopes.close(); scopes = ExitStack(); scopes.enter_context(nc.named_scope("c_3"))
    fin3()
    scopes.close(); scopes = ExitStack(); scopes.enter_context(nc.named_scope("s_f0b"))

    p.begin_stage("g_")
    p.io = {"x1": g_x1, "csum": g_csum, "x2": g_x2, "x2r": g_x2r}
    build_f0b(p, sems, TTs=512)
    p.end_stage(); p.io = {}

    scopes.close(); scopes = ExitStack(); scopes.enter_context(nc.named_scope("s_l1a"))
    p.begin_stage("h_")
    p.io = {"x2": g_x2, "x2r": g_x2r, "mo": g_mo}
    cb4, fin4 = mk_cc([(g_mo, g_mos)], sems["st"], sems["st"].n, 160)
    build_l1a(TTs=512, p=p, sems=sems, sel_in=True, cc_cb=cb4)
    p.end_stage(); p.io = {}
    scopes.close(); scopes = ExitStack(); scopes.enter_context(nc.named_scope("c_4"))
    fin4()
    scopes.close(); scopes = ExitStack(); scopes.enter_context(nc.named_scope("s_l1ba"))

    p.begin_stage("i_")
    p.io = {"x2h": g_x2, "mosum": g_mos, "ffp": g_ffp, "xmo": g_xm}
    cb5, fin5 = mk_cc([(g_ffp, g_ffs)], sems["st"], sems["st"].n, 256)
    build_l1b(TLn=T, TTs=512, p=p, sems=sems, summed_in=True, kc_n=16, partial_out=True,
              cc_cb=cb5)
    p.end_stage(); p.io = {}
    scopes.close(); scopes = ExitStack(); scopes.enter_context(nc.named_scope("c_5"))
    fin5()
    scopes.close(); scopes = ExitStack(); scopes.enter_context(nc.named_scope("s_l1bb_f1a"))

    p.begin_stage("j_")
    p.io = {"xm": g_xm, "ffsum": g_ffs, "x2j": g_x2, "x3": g_x3}
    build_l1bb(p, sems, TTs=512)
    p.end_stage(); p.io = {}

    p.begin_stage("k_")
    p.io = {"x3h": g_x3, "wop": g_wop}
    cb6, fin6 = mk_cc([(g_wop, g_wos)], sems["st"], sems["st"].n, 128)
    build_f1(TLn=T, TTs=512, p=p, sems=sems, mc_n=16, partial_out=True, TH=512,
             cc_cb=cb6)
    p.end_stage(); p.io = {}
    scopes.close(); scopes = ExitStack(); scopes.enter_context(nc.named_scope("c_6"))
    fin6()
    scopes.close(); scopes = ExitStack(); scopes.enter_context(nc.named_scope("s_f1b"))

    p.begin_stage("l_")
    p.io = {"wosum": g_wos, "x3l": g_x3, "x4": out_x4}
    build_f1b(p, sems, TTs=512)
    p.end_stage(); p.io = {}
    scopes.close()
    return p.close()


def host_fused_inputs(inp):
    x = np.asarray(inp["L0_attn_x"], np.float32)
    m_a = host_l0a_inputs(inp)
    dummy = [{k: None for k in ()} for _ in range(NCORE)]
    W_dt = np.asarray(inp["L0_attn_W_dt"], np.float32)
    W_out = np.asarray(inp["L0_attn_W_out"], np.float32)
    A_log = np.asarray(inp["L0_attn_A_log"], np.float32)
    b_dt = np.asarray(inp["L0_attn_b_dt"], np.float32)
    D_ssm = np.asarray(inp["L0_attn_D_ssm"], np.float32)
    b_xp = np.asarray(inp["L0_attn_b_xp"], np.float32)
    b_out = np.asarray(inp["L0_attn_b_out"], np.float32)
    rm = np.asarray(inp["L0_ffn_rail_maker"], np.float32)
    rup = np.asarray(inp["L0_ffn_rail_up_proj"], np.float32)
    wc = np.asarray(inp["L0_ffn_w_cproj"], np.float32)
    wct = np.ascontiguousarray(wc.T)
    ff1w = np.asarray(inp["L1_attn_ff1_w"], np.float32)
    ff2w = np.asarray(inp["L1_attn_ff2_w"], np.float32)
    ff1b = np.asarray(inp["L1_attn_ff1_b"], np.float32)
    ff2b = np.asarray(inp["L1_attn_ff2_b"], np.float32)
    fg = np.asarray(inp["L1_attn_ffn_g"], np.float32)
    fb = np.asarray(inp["L1_attn_ffn_b"], np.float32)
    Wr = np.asarray(inp["L1_ffn_W_router"], np.float32)
    W1 = np.asarray(inp["L1_ffn_W1"], np.float32)
    b1v = np.asarray(inp["L1_ffn_b1"], np.float32)
    Wo = np.asarray(inp["L1_ffn_Wo"], np.float32)
    bo = np.asarray(inp["L1_ffn_bo"], np.float32)
    lng = np.asarray(inp["L1_ffn_ln_gamma"], np.float32)
    lnb = np.asarray(inp["L1_ffn_ln_beta"], np.float32)
    w1t_full = np.ascontiguousarray(W1.reshape(4, 8, 128, 32, 128).transpose(3, 0, 1, 2, 4))
    m_l1a = host_l1a_inputs(inp, np.zeros((4, T, D), np.float32))
    ones128 = np.ones((128, 1), np.float32)
    ones1 = np.ones((128, 128), np.float32)
    cstM = np.zeros((128, 24), np.float32)
    cstM[:, 0:8] = bo.reshape(8, 128).T
    cstM[:, 8:16] = lng.reshape(8, 128).T
    cstM[:, 16:24] = lnb.reshape(8, 128).T
    maps = []
    for c in range(NCORE):
        bidx, hh = c // 2, c % 2
        dsl = slice(hh * 1024, (hh + 1) * 1024)
        xfm = np.ascontiguousarray(x[bidx].T.reshape(8, 128, T))
        m = {"x": xfm, "xr": xfm}
        for k, v in m_a[c].items():
            if k != "x":
                m["a_" + k] = v
        cst2 = np.zeros((128, 17), np.float32)
        bdt8 = b_dt[dsl].reshape(8, 128)
        d8 = D_ssm[dsl].reshape(8, 128)
        for ch in range(8):
            cst2[:, 2 * ch] = bdt8[ch]
            cst2[:, 2 * ch + 1] = d8[ch]
        cst2[:, 16] = b_xp[0:128]
        m.update({
            "b_wdti": np.ascontiguousarray(W_dt[:, dsl]),
            "b_wouti": np.ascontiguousarray(W_out[dsl, :]),
            "b_alogi": np.ascontiguousarray(A_log[dsl].reshape(8, 128, 16).transpose(1, 0, 2).reshape(128, 128)),
            "b_cst2i": cst2,
            "b_bxpbr": np.tile(b_xp[128:144], (128, 1)),
            "b_bxpcr": np.tile(b_xp[144:160], (128, 1)),
            "b_id128": np.eye(128, dtype=np.float32),
        })
        rmbd = np.zeros((16, 128, 128), np.float32)
        for hloc in range(2):
            hgl = 2 * hh + hloc
            for cc in range(8):
                for j in range(8):
                    r = cc * 8 + j
                    rmbd[hloc * 8 + cc, 16 * j:16 * j + 16, 16 * j:16 * j + 16] = rm[hgl, r]
        rupbd = np.zeros((128, 32 * 128), np.float32)
        for hloc in range(2):
            hgl = 2 * hh + hloc
            for pp_ in range(32):
                cc, pj = pp_ // 4, pp_ % 4
                q = (hloc * 8 + cc) * 2 + (pj % 2)
                ro = (pj // 2) * 64
                sub = (pj % 2) * 32
                blkw = np.zeros((64, 128), np.float32)
                for rr in range(2):
                    rail = 2 * pp_ + rr
                    blkw[sub + 16 * rr:sub + 16 * rr + 16, 64 * rr:64 * rr + 64] = rup[hgl, rail].T
                rupbd[ro:ro + 64, q * 128:(q + 1) * 128] = blkw
        m.update({
            "f_bouti": np.ascontiguousarray(b_out.reshape(8, 128).T),
            "f_rmbd": rmbd,
            "f_rupbd": rupbd,
            "f_wci": np.ascontiguousarray(wct[hh * 8192:(hh + 1) * 8192, :]),
            "f_ones128": ones128, "f_ones1": ones1,
            "g_ones128g": ones128, "g_ones1g": ones1,
        })
        for k, v in m_l1a[c].items():
            if k != "x2f":
                m["h_" + k] = v
        rsel = np.zeros((128, 2), np.float32)
        rsel[:, hh] = 1.0
        m["h_revsel"] = rsel
        m["h_id128L"] = np.eye(128, dtype=np.float32)
        cstF = np.zeros((128, 56), np.float32)
        cstF[:, 0:8] = fg.reshape(8, 128).T
        cstF[:, 8:16] = fb.reshape(8, 128).T
        cstF[:, 16:32] = ff1b[hh * 2048:(hh + 1) * 2048].reshape(16, 128).T
        cstF[:, 48:56] = ff2b.reshape(8, 128).T
        m.update({
            "i_ff1wi": np.ascontiguousarray(ff1w.T[:, hh * 2048:(hh + 1) * 2048]),
            "i_ff2wi": np.ascontiguousarray(ff2w.T[hh * 2048:(hh + 1) * 2048, :]),
            "i_cstFi": cstF,
            "i_ones128": ones128, "i_ones1": ones1,
            "j_cstD": np.ascontiguousarray(ff2b.reshape(8, 128).T),
            "j_ones128j": ones128, "j_ones1j": ones1,
        })
        m.update({
            "k_wri": np.ascontiguousarray(Wr),
            "k_w1t": np.ascontiguousarray(w1t_full[hh * 16:(hh + 1) * 16]),
            "k_b1i": np.ascontiguousarray(b1v[:, hh * 2048:(hh + 1) * 2048]),
            "k_woi": np.ascontiguousarray(Wo[hh * 2048:(hh + 1) * 2048, :]),
            "k_cstMi": cstM,
            "k_ones128": ones128, "k_ones1": ones1,
            "l_cstM2": cstM,
            "l_ones128l": ones128, "l_ones1l": ones1,
        })
        maps.append(m)
    return maps


def kernel_fused(**inp):
    inp = {k: np.asarray(v) for k, v in inp.items()}
    nc = build_fused()
    maps = host_fused_inputs(inp)
    res = run_bass_kernel_spmd(nc, maps, core_ids=list(range(NCORE)))
    out = np.empty((4, T, D), np.float32)
    for b in range(4):
        out[b] = res.results[2 * b]["x4"].reshape(D, T).T
    return out



def _fm(a):
    Fdim = a.shape[-1]
    return np.ascontiguousarray(a.T.reshape(Fdim // 128, 128, a.shape[0]))


def host_l0a_inputs(inp):
    x = np.asarray(inp["L0_attn_x"], np.float32)
    W_in = np.asarray(inp["L0_attn_W_in"], np.float32)
    b_in = np.asarray(inp["L0_attn_b_in"], np.float32)
    conv_k = np.asarray(inp["L0_attn_conv_k"], np.float32)
    conv_b = np.asarray(inp["L0_attn_conv_b"], np.float32)
    W_xp = np.asarray(inp["L0_attn_W_xp"], np.float32)
    maps = []
    for c in range(NCORE):
        bidx, hh = c // 2, c % 2
        dsl = slice(hh * 1024, (hh + 1) * 1024)
        consts = np.zeros((8, 128, 7), np.float32)
        consts[:, :, 0] = b_in[dsl].reshape(8, 128)
        consts[:, :, 1] = b_in[2048 + hh * 1024:2048 + (hh + 1) * 1024].reshape(8, 128)
        consts[:, :, 2] = conv_b[dsl].reshape(8, 128)
        for k in range(4):
            consts[:, :, 3 + k] = conv_k[k, 0, dsl].reshape(8, 128)
        maps.append({
            "x": _fm(x[bidx]),
            "w_in_x": np.ascontiguousarray(W_in[:, dsl]),
            "w_in_z": np.ascontiguousarray(W_in[:, 2048 + hh * 1024:2048 + (hh + 1) * 1024]),
            "wxp_dt": np.ascontiguousarray(W_xp[dsl, 0:128]),
            "wxp_b": np.ascontiguousarray(W_xp[dsl, 128:144]),
            "wxp_c": np.ascontiguousarray(W_xp[dsl, 144:160]),
            "consts": np.ascontiguousarray(consts.transpose(1, 0, 2).reshape(128, 56)),
        })
    return maps


def host_l0b_inputs(inp, l0a_res):
    W_dt = np.asarray(inp["L0_attn_W_dt"], np.float32)
    W_out = np.asarray(inp["L0_attn_W_out"], np.float32)
    A_log = np.asarray(inp["L0_attn_A_log"], np.float32)
    b_dt = np.asarray(inp["L0_attn_b_dt"], np.float32)
    D_ssm = np.asarray(inp["L0_attn_D_ssm"], np.float32)
    b_xp = np.asarray(inp["L0_attn_b_xp"], np.float32)
    maps = []
    for c in range(NCORE):
        hh = c % 2
        pc = c - hh + (1 - hh)
        dsl = slice(hh * 1024, (hh + 1) * 1024)
        cst2 = np.zeros((128, 17), np.float32)
        bdt8 = b_dt[dsl].reshape(8, 128)
        d8 = D_ssm[dsl].reshape(8, 128)
        for ch in range(8):
            cst2[:, 2 * ch] = bdt8[ch]
            cst2[:, 2 * ch + 1] = d8[ch]
        cst2[:, 16] = b_xp[0:128]
        maps.append({
            "xa": l0a_res[c]["xa"],
            "z": l0a_res[c]["z"],
            "dtps": l0a_res[c]["dtp"],
            "dtpp": l0a_res[pc]["dtp"],
            "bps": l0a_res[c]["bp"],
            "bpp": l0a_res[pc]["bp"],
            "cps": l0a_res[c]["cp"],
            "cpp": l0a_res[pc]["cp"],
            "wdti": np.ascontiguousarray(W_dt[:, dsl]),
            "wouti": np.ascontiguousarray(W_out[dsl, :]),
            "alogi": np.ascontiguousarray(A_log[dsl].reshape(8, 128, 16).transpose(1, 0, 2).reshape(128, 128)),
            "cst2i": cst2,
            "bxpbr": np.tile(b_xp[128:144], (128, 1)),
            "bxpcr": np.tile(b_xp[144:160], (128, 1)),
            "id128": np.eye(128, dtype=np.float32),
        })
    return maps


def kernel(**inp):
    return kernel_fused(**inp)


def kernel_unfused(**inp):
    inp = {k: np.asarray(v) for k, v in inp.items()}
    core_ids = list(range(NCORE))

    res_a = run_bass_kernel_spmd(build_l0a(), host_l0a_inputs(inp), core_ids=core_ids)
    l0a_res = [dict(r) for r in res_a.results]

    res_b = run_bass_kernel_spmd(build_l0b(), host_l0b_inputs(inp, l0a_res), core_ids=core_ids)
    l0b_res = [dict(r) for r in res_b.results]

    res_f0 = run_bass_kernel_spmd(build_f0(), host_f0_inputs(inp, l0b_res), core_ids=core_ids)
    x2_full = np.empty((4, T, D), np.float32)
    for c in range(NCORE):
        bidx, hh = c // 2, c % 2
        x2_full[bidx, hh * TL:(hh + 1) * TL] = res_f0.results[c]["x2"].reshape(D, TL).T

    res_l1a = run_bass_kernel_spmd(build_l1a(), host_l1a_inputs(inp, x2_full), core_ids=core_ids)
    l1a_res = [dict(r) for r in res_l1a.results]

    res_l1b = run_bass_kernel_spmd(build_l1b(), host_l1b_inputs(inp, x2_full, l1a_res), core_ids=core_ids)
    x3_res = [dict(r) for r in res_l1b.results]

    res_f1 = run_bass_kernel_spmd(build_f1(), host_f1_inputs(inp, x3_res), core_ids=core_ids)
    out = np.empty((4, T, D), np.float32)
    for c in range(NCORE):
        bidx, hh = c // 2, c % 2
        out[bidx, hh * TL:(hh + 1) * TL] = res_f1.results[c]["x4"].reshape(D, TL).T
    return out


# version 12
# speedup vs baseline: 1.0120x; 1.0120x over previous
import numpy as np
from contextlib import ExitStack
import concourse.bass as bass
import concourse.mybir as mybir
from concourse.bass_utils import run_bass_kernel_spmd

F32 = mybir.dt.float32
F32R = mybir.dt.float32r
BF16 = mybir.dt.bfloat16
AF = mybir.ActivationFunctionType
ALU = mybir.AluOpType

NCORE = 8
T = 4096
TL = 2048
D = 1024
DI = 2048
NS = 16


class Ctr:
    def __init__(self, h):
        self.h = h
        self.n = 0
    def inc(self, instr, n=1):
        instr.then_inc(self.h, n)
        self.n += n
        return self.n


DETECT_RACES = True


class Prog:
    def __init__(self):
        self.nc = bass.Bass("TRN2", target_bir_lowering=False,
                            detect_race_conditions=DETECT_RACES)
        self.es = ExitStack()
        self.stack = [self.es]
        self.pre = ""
        self.io = {}
    def begin_stage(self, pre):
        self.stack.append(ExitStack())
        self.pre = pre
    def end_stage(self):
        self.stack.pop().close()
        self.pre = ""
    def sb(self, name, shape, dt):
        return self.stack[-1].enter_context(self.nc.sbuf_tensor(self.pre + name, shape, dt))
    def ps(self, name, shape, dt=F32):
        return self.stack[-1].enter_context(self.nc.psum_tensor(self.pre + name, shape, dt))
    def sem(self, name):
        return Ctr(self.es.enter_context(self.nc.semaphore(self.pre + name)))
    def din(self, name, shape, dt=F32):
        if name in self.io:
            return self.io[name]
        return self.nc.dram_tensor(self.pre + name, list(shape), dt, kind="ExternalInput").ap()
    def dout(self, name, shape, dt=F32):
        if name in self.io:
            return self.io[name]
        return self.nc.dram_tensor(self.pre + name, list(shape), dt, kind="ExternalOutput").ap()
    def close(self):
        self.es.close()
        return self.nc


def fr(ap):
    return ap.bitcast(F32R)



def build_l0a(Tn=T, TT=512, p=None, sems=None):
    NT = Tn // TT
    standalone = p is None
    if standalone:
        p = Prog()
    nc = p.nc
    x = p.din("x", [8, 128, Tn], F32R)
    w_in_x = p.din("w_in_x", [1024, 1024], F32R)
    w_in_z = p.din("w_in_z", [1024, 1024], F32R)
    wxp_dt = p.din("wxp_dt", [1024, 128], F32R)
    wxp_b = p.din("wxp_b", [1024, 16], F32R)
    wxp_c = p.din("wxp_c", [1024, 16], F32R)
    consts = p.din("consts", [128, 8 * 7])
    o_xa = p.dout("xa", [8, 128, Tn], F32R)
    o_z = p.dout("z", [8, 128, Tn])
    o_dtp = p.dout("dtp", [128, Tn])
    o_bp = p.dout("bp", [16, Tn])
    o_cp = p.dout("cp", [16, Tn])

    wix = p.sb("wix", [128, 8 * 1024], F32R)
    wiz = p.sb("wiz", [128, 8 * 1024], F32R)
    wdt = p.sb("wdt", [128, 8 * 128], F32R)
    wbb = p.sb("wbb", [128, 8 * 16], F32R)
    wcc = p.sb("wcc", [128, 8 * 16], F32R)
    cst = p.sb("cst", [128, 8 * 7], F32)
    xb = p.sb("xb", [128, 2 * 8 * TT], F32R)
    xpe = [p.sb(f"xpe{c}", [128, TT + 3], F32) for c in range(8)]
    convo = p.sb("convo", [128, 8 * TT], F32)
    xaa = p.sb("xaa", [128, 8 * TT], F32R)
    zz = p.sb("zz", [128, 8 * TT], F32)
    dtv = p.sb("dtv", [128, TT], F32)
    bv = p.sb("bv", [16, TT], F32)
    cv = p.sb("cv", [16, TT], F32)
    px = [p.ps(f"px{i}", [128, TT]) for i in range(2)]
    pz = [p.ps(f"pz{i}", [128, TT]) for i in range(2)]
    pdt = p.ps("pdt", [128, TT])
    pbc = p.ps("pbc", [16, TT])
    pcc = p.ps("pcc", [16, TT])

    if sems is None:
        ld, mm, dv, ac, st = (p.sem("ld"), p.sem("mm"), p.sem("dv"), p.sem("ac"), p.sem("st"))
    else:
        ld, mm, dv, ac, st = sems["ld"], sems["mm"], sems["dv"], sems["ac"], sems["st"]
    ldE, mmE, dvE, acE, stE = ld.n, mm.n, dv.n, ac.n, st.n

    with nc.Block() as blk:
        @blk.sync
        def _(q):
            for kc in range(8):
                q.dma_start(out=wix[:, kc * 1024:(kc + 1) * 1024],
                            in_=w_in_x[kc * 128:(kc + 1) * 128, :]).then_inc(ld.h, 16); ld.n += 16
                q.dma_start(out=wiz[:, kc * 1024:(kc + 1) * 1024],
                            in_=w_in_z[kc * 128:(kc + 1) * 128, :]).then_inc(ld.h, 16); ld.n += 16
                q.dma_start(out=wdt[:, kc * 128:(kc + 1) * 128],
                            in_=wxp_dt[kc * 128:(kc + 1) * 128, :]).then_inc(ld.h, 16); ld.n += 16
                q.dma_start(out=wbb[:, kc * 16:(kc + 1) * 16],
                            in_=wxp_b[kc * 128:(kc + 1) * 128, :]).then_inc(ld.h, 16); ld.n += 16
                q.dma_start(out=wcc[:, kc * 16:(kc + 1) * 16],
                            in_=wxp_c[kc * 128:(kc + 1) * 128, :]).then_inc(ld.h, 16); ld.n += 16
            q.dma_start(out=cst[:], in_=consts).then_inc(ld.h, 16); ld.n += 16

        @blk.vector
        def _(v):
            for c in range(8):
                v.memset(xpe[c][:, 0:3], 0.0)

    wload = ld.n

    def xcol(c, j):
        return cst[:, c * 7 + j:c * 7 + j + 1]

    MM_PER, DV_PER, AC_PER = 19, 17, 8

    for tt in range(NT):
        mm0, dv0, ac0 = mmE + MM_PER * tt, dvE + DV_PER * tt, acE + AC_PER * tt
        mark_mm_x = lambda mc: mm0 + 2 * mc + 1
        mark_mm_z = lambda mc: mm0 + 2 * mc + 2
        mark_mm_dt = mm0 + 17
        mark_mm_b = mm0 + 18
        mark_mm_c = mm0 + 19
        mark_dv_evac = lambda mc: dv0 + 2 * mc + 1
        mark_dv_conv = lambda mc: dv0 + 2 * mc + 2
        mark_dv_dtbc = dv0 + 17
        mark_ac = lambda mc: ac0 + mc + 1
        b = tt % 2

        with nc.Block() as blk:
            @blk.sync
            def _(q):
                tiles = [0, 1] if (tt == 0 and NT > 1) else ([0] if tt == 0 else ([tt + 1] if tt + 1 < NT else []))
                for ti in tiles:
                    bb = ti % 2
                    for c in range(8):
                        q.dma_start(out=xb[:, (bb * 8 + c) * TT:(bb * 8 + c + 1) * TT],
                                    in_=x[c, :, ti * TT:(ti + 1) * TT]).then_inc(ld.h, 16); ld.n += 16

            @blk.tensor
            def _(t):
                t.wait_ge(ld.h, wload + 16 * 8 * min(tt + 1, NT))
                for mc in range(8):
                    bank = mc % 2
                    if mc >= 2:
                        t.wait_ge(dv.h, mark_dv_evac(mc - 2))
                    for kc in range(8):
                        last = t.matmul(out=px[bank][:],
                                        lhsT=wix[:, kc * 1024 + mc * 128: kc * 1024 + (mc + 1) * 128],
                                        rhs=xb[:, (b * 8 + kc) * TT:(b * 8 + kc + 1) * TT],
                                        start=(kc == 0), stop=(kc == 7))
                    mm.inc(last)
                    for kc in range(8):
                        last = t.matmul(out=pz[bank][:],
                                        lhsT=wiz[:, kc * 1024 + mc * 128: kc * 1024 + (mc + 1) * 128],
                                        rhs=xb[:, (b * 8 + kc) * TT:(b * 8 + kc + 1) * TT],
                                        start=(kc == 0), stop=(kc == 7))
                    mm.inc(last)
                t.wait_ge(ac.h, ac0 + 8)
                for kc in range(8):
                    last = t.matmul(out=pdt[:], lhsT=wdt[:, kc * 128:(kc + 1) * 128],
                                    rhs=xaa[:, kc * TT:(kc + 1) * TT],
                                    start=(kc == 0), stop=(kc == 7))
                mm.inc(last)
                for kc in range(8):
                    last = t.matmul(out=pbc[:], lhsT=wbb[:, kc * 16:(kc + 1) * 16],
                                    rhs=xaa[:, kc * TT:(kc + 1) * TT],
                                    start=(kc == 0), stop=(kc == 7))
                mm.inc(last)
                for kc in range(8):
                    last = t.matmul(out=pcc[:], lhsT=wcc[:, kc * 16:(kc + 1) * 16],
                                    rhs=xaa[:, kc * TT:(kc + 1) * TT],
                                    start=(kc == 0), stop=(kc == 7))
                mm.inc(last)

            @blk.vector
            def _(v):
                for mc in range(8):
                    bank = mc % 2
                    v.wait_ge(mm.h, mark_mm_x(mc))
                    v.tensor_single_scalar(out=xpe[mc][:, 3:3 + TT], in_=px[bank][:],
                                           scalar=xcol(mc, 0), op=ALU.add)
                    v.wait_ge(mm.h, mark_mm_z(mc))
                    dv.inc(v.tensor_single_scalar(out=zz[:, mc * TT:(mc + 1) * TT], in_=pz[bank][:],
                                                  scalar=xcol(mc, 1), op=ALU.add))
                    e = xpe[mc]
                    v.tensor_single_scalar(out=convo[:, mc * TT:(mc + 1) * TT],
                                           in_=e[:, 0:TT], scalar=xcol(mc, 3), op=ALU.mult)
                    for k in range(1, 4):
                        v.scalar_tensor_tensor(out=convo[:, mc * TT:(mc + 1) * TT],
                                               in0=e[:, k:k + TT], scalar=xcol(mc, 3 + k),
                                               in1=convo[:, mc * TT:(mc + 1) * TT],
                                               op0=ALU.mult, op1=ALU.add)
                    dv.inc(v.tensor_copy(out=e[:, 0:3], in_=e[:, TT:TT + 3]))
                v.wait_ge(mm.h, mark_mm_dt)
                v.tensor_copy(out=dtv[:], in_=pdt[:])
                v.wait_ge(mm.h, mark_mm_c)
                v.tensor_copy(out=bv[:], in_=pbc[:])
                dv.inc(v.tensor_copy(out=cv[:], in_=pcc[:]))

            @blk.scalar
            def _(a):
                for mc in range(8):
                    a.wait_ge(dv.h, mark_dv_conv(mc))
                    ac.inc(a.activation(out=xaa[:, mc * TT:(mc + 1) * TT],
                                        in_=convo[:, mc * TT:(mc + 1) * TT],
                                        func=AF.Silu, bias=xcol(mc, 2)))

            @blk.gpsimd
            def _(g):
                for mc in range(8):
                    g.wait_ge(ac.h, mark_ac(mc))
                    g.dma_start(out=o_xa[mc, :, tt * TT:(tt + 1) * TT],
                                in_=xaa[:, mc * TT:(mc + 1) * TT]).then_inc(st.h, 16); st.n += 16
                    g.wait_ge(dv.h, mark_dv_evac(mc))
                    g.dma_start(out=o_z[mc, :, tt * TT:(tt + 1) * TT],
                                in_=zz[:, mc * TT:(mc + 1) * TT]).then_inc(st.h, 16); st.n += 16
                g.wait_ge(dv.h, mark_dv_dtbc)
                g.dma_start(out=o_dtp[:, tt * TT:(tt + 1) * TT], in_=dtv[:]).then_inc(st.h, 16); st.n += 16
                g.dma_start(out=o_bp[:, tt * TT:(tt + 1) * TT], in_=bv[:]).then_inc(st.h, 16); st.n += 16
                g.dma_start(out=o_cp[:, tt * TT:(tt + 1) * TT], in_=cv[:]).then_inc(st.h, 16); st.n += 16
                g.wait_ge(st.h, st.n)

        assert mm.n == mmE + MM_PER * (tt + 1) and dv.n == dvE + DV_PER * (tt + 1) and ac.n == acE + AC_PER * (tt + 1)

    with nc.Block() as blk:
        @blk.gpsimd
        def _(g):
            g.wait_ge(st.h, st.n)
    if standalone:
        return p.close()
    return None



def build_l0b(Tn=T, TTs=512, p=None, sems=None, summed=False, cc_cb=None):
    NSUP = Tn // TTs
    standalone = p is None
    if standalone:
        p = Prog()
    nc = p.nc
    xa_i = p.din("xa", [8, 128, Tn], F32R)
    z_i = p.din("z", [8, 128, Tn])
    if summed:
        dtps_i = p.din("dts", [128, Tn])
        bps_i = p.din("bs", [16, Tn])
        cps_i = p.din("cs", [16, Tn])
        dtpp_i = bpp_i = cpp_i = None
    else:
        dtps_i = p.din("dtps", [128, Tn])
        dtpp_i = p.din("dtpp", [128, Tn])
        bps_i = p.din("bps", [16, Tn]); bpp_i = p.din("bpp", [16, Tn])
        cps_i = p.din("cps", [16, Tn]); cpp_i = p.din("cpp", [16, Tn])
    wdt_i = p.din("wdti", [128, 1024], F32R)
    wout_i = p.din("wouti", [1024, 1024], F32R)
    alog_i = p.din("alogi", [128, 128])
    cst2_i = p.din("cst2i", [128, 17])
    bxpbr_i = p.din("bxpbr", [128, 16])
    bxpcr_i = p.din("bxpcr", [128, 16])
    id128_i = p.din("id128", [128, 128], F32R)
    o_part = p.dout("opart", [Tn // TTs, 8, 128, TTs])

    wdt = p.sb("s_wdt", [128, 1024], F32R)
    wout = p.sb("wout", [128, 8 * 1024], BF16)
    dtint = p.sb("dtint", [128, Tn], F32R)
    alog = p.sb("s_alog", [128, 128], F32)
    aneg = p.sb("aneg", [128, 128], F32)
    ascr = p.sb("ascr", [128, 128], F32)
    cst2 = p.sb("s_cst2", [128, 17], F32)
    bxpbr = p.sb("s_bxpbr", [128, 16], F32)
    bxpcr = p.sb("s_bxpcr", [128, 16], F32)
    hst = p.sb("hst", [128, 128], F32)
    xas = p.sb("xas", [128, 8 * TTs], F32)
    zs = p.sb("zs", [128, 8 * TTs], F32)
    dt_all = p.sb("dt_all", [128, 8 * TTs], F32)
    dtxa = p.sb("dtxa", [128, 8 * TTs], F32)
    sp1 = p.sb("sp1", [128, TTs], F32)
    sp2 = p.sb("sp2", [128, TTs], F32)
    bb1 = p.sb("bb1", [128, 2 * TTs], F32)
    bb2 = p.sb("bb2", [128, 2 * TTs], F32)
    cb1 = p.sb("cb1", [128, 2 * TTs], F32)
    cb2 = p.sb("cb2", [128, 2 * TTs], F32)
    bbuf = p.sb("bbuf", [128, 2 * TTs], F32)
    cbuf = p.sb("cbuf", [128, 2 * TTs], F32)
    dabuf = p.sb("dabuf", [128, 4 * TTs], F32)
    dxbuf = p.sb("dxbuf", [128, 4 * TTs], F32)
    hbuf = p.sb("hbuf", [128, 4 * TTs], F32)
    hcbuf = p.sb("hcbuf", [128, 4 * TTs], BF16)
    id128 = p.sb("s_id128", [128, 128], BF16)
    yacc = p.sb("yacc", [128, 8 * TTs], F32)
    ybuf = p.sb("ybuf", [128, TTs], F32)
    zsc = p.sb("zsc", [128, TTs], F32)
    gated = p.sb("gated", [128, 8 * TTs], BF16)
    ovac = p.sb("ovac", [128, 2 * TTs], F32)
    pdt = [p.ps(f"pdt{i}", [128, TTs]) for i in range(2)]
    pout = [p.ps(f"pout{i}", [128, TTs]) for i in range(2)]
    pacc = [p.ps(f"pacc{i}", [128, TTs]) for i in range(4)]

    if sems is None:
        ld, gl, mm = p.sem("ld"), p.sem("gl"), p.sem("mm")
        dv, ac, pl, st = p.sem("dv"), p.sem("ac"), p.sem("pl"), p.sem("st")
    else:
        ld, gl, mm, dv, ac, pl, st = (sems["ld"], sems["gl"], sems["mm"], sems["dv"],
                                      sems["ac"], sems["pl"], sems["st"])
    ldE, stE = ld.n, st.n

    with nc.Block() as blk:
        @blk.gpsimd
        def _(g0):
            for kc in range(8):
                g0.dma_start(out=wout[:, kc * 1024:(kc + 1) * 1024],
                             in_=wout_i[kc * 128:(kc + 1) * 128, :]).then_inc(ld.h, 16); ld.n += 16
            g0.dma_start(out=id128[:], in_=id128_i).then_inc(ld.h, 16); ld.n += 16

        @blk.sync
        def _(q):
            pairs = [(wdt, wdt_i), (xas, dtps_i), (alog, alog_i),
                     (cst2, cst2_i), (bxpbr, bxpbr_i), (bxpcr, bxpcr_i)]
            if not summed:
                pairs.insert(2, (zs, dtpp_i))
            for t_, s_ in pairs:
                q.dma_start(out=t_[:], in_=s_).then_inc(ld.h, 16); ld.n += 16

        ld_init = ld.n

        @blk.vector
        def _(v):
            v.wait_ge(ld.h, ld_init)
            v.memset(hst[:], 0.0)
            if summed:
                dv.inc(v.tensor_single_scalar(out=dtint[:], in_=xas[:],
                                              scalar=cst2[:, 16:17], op=ALU.add))
            else:
                dv.inc(v.scalar_tensor_tensor(out=dtint[:], in0=xas[:], scalar=cst2[:, 16:17],
                                              in1=zs[:], op0=ALU.add, op1=ALU.add))

        @blk.scalar
        def _(a):
            a.wait_ge(ld.h, ld_init)
            ac.inc(a.activation(out=ascr[:], in_=alog[:], func=AF.Exp))

    with nc.Block() as blk:
        @blk.vector
        def _(v):
            v.tensor_single_scalar(out=aneg[:], in_=ascr[:], scalar=-1.0, op=ALU.mult)

    DV0 = dv.n
    AC0 = ac.n
    PL0 = pl.n
    MM0 = mm.n
    GLE = gl.n
    DV_PER, AC_PER, PL_PER, MM_PER = 384, 272, 128, 80
    GL_PER = 8 * 16
    NBC = 2 if summed else 4
    LD_SUP = (8 + 16 * NBC) * 16
    LD0 = ld.n
    NPE = 4

    def marks(ts):
        dv0 = DV0 + DV_PER * ts
        ac0 = AC0 + AC_PER * ts
        pl0 = PL0 + PL_PER * ts
        mm0 = MM0 + MM_PER * ts
        dvm = {}
        n = dv0
        for c in range(8):
            n += 1; dvm[("dtadd", c)] = n
            n += 1; dvm[("dtxa", c)] = n
        for s in range(16):
            n += 1; dvm[("sttB", s)] = n
            n += 1; dvm[("sttC", s)] = n
            for c in range(8):
                i = s * 8 + c
                n += 1; dvm[("scan", i)] = n
                n += 1; dvm[("hc", i)] = n
                if c >= NPE:
                    n += 1; dvm[("yacc", i)] = n
        for c in range(8):
            n += 1; dvm[("gated", c)] = n
        for mc in range(8):
            n += 1; dvm[("evac", mc)] = n
        assert n - dv0 == DV_PER
        plm = {}
        for i in range(128):
            plm[("dbx", i)] = pl0 + i + 1
        mmm = {}
        q = mm0
        for c in range(8):
            q += 1; mmm[("dtmm", c)] = q
        for s in range(16):
            for c in range(NPE):
                q += 1; mmm[("yaccmm", s * 8 + c)] = q
        for mc in range(8):
            q += 1; mmm[("omm", mc)] = q
        assert q - mm0 == MM_PER
        return {
            "dv": dvm,
            "pl": plm,
            "mm": mmm,
            "relu": lambda c: ac0 + c + 1,
            "da": lambda i: ac0 + 8 + i + max(0, i - 4) + 1,
            "hst": lambda i: ac0 + 8 + min(i + 5, 128) + i + 1,
            "zsil": lambda c: ac0 + 264 + c + 1,
            "ld_bc": lambda s: LD0 + LD_SUP * ts + 16 * (8 + NBC * s + NBC),
            "gl_xa": GLE + GL_PER * (ts + 1),
        }

    for ts in range(NSUP):
        M = marks(ts)
        tsl = slice(ts * TTs, (ts + 1) * TTs)

        with nc.Block() as blk:
            @blk.sync
            def _(q):
                for c in range(8):
                    q.dma_start(out=zs[:, c * TTs:(c + 1) * TTs],
                                in_=z_i[c, :, tsl]).then_inc(ld.h, 16); ld.n += 16
                for s in range(16):
                    k = s % 2
                    if s >= 2:
                        q.wait_ge(dv.h, M["dv"][("sttC", s - 2)])
                    srcs = ((bb1, bps_i), (cb1, cps_i)) if summed else \
                           ((bb1, bps_i), (bb2, bpp_i), (cb1, cps_i), (cb2, cpp_i))
                    for dst, srcr in srcs:
                        q.dma_start(out=dst[:, k * TTs:(k + 1) * TTs],
                                    in_=srcr[s:s + 1, tsl].partition_broadcast(128)).then_inc(ld.h, 16); ld.n += 16

            @blk.tensor
            def _(t):
                for c in range(8):
                    if c >= 2:
                        t.wait_ge(ac.h, M["relu"](c - 2))
                    mm.inc(t.matmul(out=pdt[c % 2][:], lhsT=wdt[:, c * 128:(c + 1) * 128],
                                    rhs=dtint[:, tsl], start=True, stop=True))
                for s in range(16):
                    for c in range(NPE):
                        i = s * 8 + c
                        kk = i % 4
                        t.wait_ge(dv.h, M["dv"][("hc", i)])
                        mm.inc(t.matmul(out=pacc[c][:], lhsT=id128[:],
                                        rhs=hcbuf[:, kk * TTs:(kk + 1) * TTs],
                                        start=(s == 0), stop=(s == 15)))
                for mc in range(8):
                    t.wait_ge(dv.h, M["dv"][("gated", 7)])
                    if mc >= 2:
                        t.wait_ge(dv.h, M["dv"][("evac", mc - 2)])
                    for kc in range(8):
                        last = t.matmul(out=pout[mc % 2][:],
                                        lhsT=wout[:, kc * 1024 + mc * 128:kc * 1024 + (mc + 1) * 128],
                                        rhs=gated[:, kc * TTs:(kc + 1) * TTs],
                                        start=(kc == 0), stop=(kc == 7))
                    mm.inc(last)

            @blk.scalar
            def _(a):
                for c in range(8):
                    a.wait_ge(mm.h, M["mm"][("dtmm", c)])
                    if c >= 1:
                        a.wait_ge(dv.h, M["dv"][("dtadd", c - 1)])
                    a.activation(out=sp1[:], in_=pdt[c % 2][:], func=AF.Abs, bias=cst2[:, 2 * c:2 * c + 1])
                    a.activation(out=sp1[:], in_=sp1[:], func=AF.Exp, scale=-1.0)
                    a.activation(out=sp1[:], in_=sp1[:], func=AF.Ln, bias=1.0)
                    ac.inc(a.activation(out=sp2[:], in_=pdt[c % 2][:], func=AF.Relu,
                                        bias=cst2[:, 2 * c:2 * c + 1]))
                def emit_hst(i):
                    s_, c_ = i // 8, i % 8
                    ac.inc(a.activation(out=hst[:, c_ * 16 + s_:c_ * 16 + s_ + 1],
                                        in_=hbuf[:, (i % 4) * TTs + TTs - 1:(i % 4) * TTs + TTs],
                                        func=AF.Copy))
                for i in range(128):
                    s, c = i // 8, i % 8
                    a.wait_ge(dv.h, M["dv"][("dtadd", c)])
                    if i >= 4:
                        a.wait_ge(dv.h, M["dv"][("scan", i - 4)])
                    ac.inc(a.activation(out=dabuf[:, (i % 4) * TTs:(i % 4 + 1) * TTs],
                                        in_=dt_all[:, c * TTs:(c + 1) * TTs], func=AF.Exp,
                                        scale=aneg[:, c * 16 + s:c * 16 + s + 1]))
                    if i >= 4:
                        emit_hst(i - 4)
                for i in range(124, 128):
                    a.wait_ge(dv.h, M["dv"][("scan", i)])
                    emit_hst(i)
                for c in range(8):
                    if c >= 1:
                        a.wait_ge(dv.h, M["dv"][("gated", c - 1)])
                    ac.inc(a.activation(out=zsc[:], in_=zs[:, c * TTs:(c + 1) * TTs], func=AF.Silu))
                for mc in range(8):
                    a.wait_ge(dv.h, M["dv"][("evac", mc)])
                    a.dma_start(out=o_part[ts, mc],
                                in_=ovac[:, (mc % 2) * TTs:(mc % 2 + 1) * TTs]).then_inc(st.h, 16); st.n += 16
                a.wait_ge(st.h, st.n)

            @blk.vector
            def _(v):
                v.wait_ge(gl.h, M["gl_xa"])
                for c in range(8):
                    v.wait_ge(ac.h, M["relu"](c))
                    dv.inc(v.tensor_add(out=dt_all[:, c * TTs:(c + 1) * TTs], in0=sp1[:], in1=sp2[:]))
                    dv.inc(v.tensor_mul(out=dtxa[:, c * TTs:(c + 1) * TTs],
                                        in0=dt_all[:, c * TTs:(c + 1) * TTs],
                                        in1=xas[:, c * TTs:(c + 1) * TTs]))
                for s in range(16):
                    k = s % 2
                    v.wait_ge(ld.h, M["ld_bc"](s))
                    if summed:
                        dv.inc(v.tensor_single_scalar(out=bbuf[:, k * TTs:(k + 1) * TTs],
                                                      in_=bb1[:, k * TTs:(k + 1) * TTs],
                                                      scalar=bxpbr[:, s:s + 1], op=ALU.add))
                        dv.inc(v.tensor_single_scalar(out=cbuf[:, k * TTs:(k + 1) * TTs],
                                                      in_=cb1[:, k * TTs:(k + 1) * TTs],
                                                      scalar=bxpcr[:, s:s + 1], op=ALU.add))
                    else:
                        dv.inc(v.scalar_tensor_tensor(out=bbuf[:, k * TTs:(k + 1) * TTs],
                                                      in0=bb1[:, k * TTs:(k + 1) * TTs],
                                                      scalar=bxpbr[:, s:s + 1],
                                                      in1=bb2[:, k * TTs:(k + 1) * TTs],
                                                      op0=ALU.add, op1=ALU.add))
                        dv.inc(v.scalar_tensor_tensor(out=cbuf[:, k * TTs:(k + 1) * TTs],
                                                      in0=cb1[:, k * TTs:(k + 1) * TTs],
                                                      scalar=bxpcr[:, s:s + 1],
                                                      in1=cb2[:, k * TTs:(k + 1) * TTs],
                                                      op0=ALU.add, op1=ALU.add))
                    for c in range(8):
                        i = s * 8 + c
                        kk = i % 4
                        v.wait_ge(ac.h, M["hst"](i - 4) if i >= 4 else M["da"](i))
                        v.wait_ge(pl.h, M["pl"][("dbx", i)])
                        dv.inc(v.tensor_tensor_scan(out=hbuf[:, kk * TTs:(kk + 1) * TTs],
                                                    data0=dabuf[:, kk * TTs:(kk + 1) * TTs],
                                                    data1=dxbuf[:, kk * TTs:(kk + 1) * TTs],
                                                    initial=hst[:, c * 16 + s:c * 16 + s + 1],
                                                    op0=ALU.mult, op1=ALU.add))
                        if i >= 4 and (i - 4) % 8 < NPE:
                            v.wait_ge(mm.h, M["mm"][("yaccmm", i - 4)])
                        dv.inc(v.tensor_mul(out=hcbuf[:, kk * TTs:(kk + 1) * TTs],
                                            in0=hbuf[:, kk * TTs:(kk + 1) * TTs],
                                            in1=cbuf[:, k * TTs:(k + 1) * TTs]))
                        if c >= NPE:
                            if s == 0:
                                dv.inc(v.tensor_copy(out=yacc[:, c * TTs:(c + 1) * TTs],
                                                     in_=hcbuf[:, kk * TTs:(kk + 1) * TTs]))
                            else:
                                dv.inc(v.tensor_add(out=yacc[:, c * TTs:(c + 1) * TTs],
                                                    in0=yacc[:, c * TTs:(c + 1) * TTs],
                                                    in1=hcbuf[:, kk * TTs:(kk + 1) * TTs]))
                for c in range(8):
                    if c < NPE:
                        v.wait_ge(mm.h, M["mm"][("yaccmm", 15 * 8 + c)])
                        yin = pacc[c][:]
                    else:
                        yin = yacc[:, c * TTs:(c + 1) * TTs]
                    v.scalar_tensor_tensor(out=ybuf[:], in0=xas[:, c * TTs:(c + 1) * TTs],
                                           scalar=cst2[:, 2 * c + 1:2 * c + 2],
                                           in1=yin,
                                           op0=ALU.mult, op1=ALU.add)
                    v.wait_ge(ac.h, M["zsil"](c))
                    dv.inc(v.tensor_mul(out=gated[:, c * TTs:(c + 1) * TTs],
                                        in0=ybuf[:], in1=zsc[:]))
                for mc in range(8):
                    v.wait_ge(mm.h, M["mm"][("omm", mc)])
                    if mc >= 2:
                        v.wait_ge(st.h, stE + 16 * (8 * ts + mc - 1))
                    dv.inc(v.tensor_copy(out=ovac[:, (mc % 2) * TTs:(mc % 2 + 1) * TTs],
                                         in_=pout[mc % 2][:]))

            @blk.gpsimd
            def _(g):
                g.wait_ge(dv.h, DV0 + DV_PER * ts)
                for c in range(8):
                    g.dma_start(out=xas[:, c * TTs:(c + 1) * TTs],
                                in_=xa_i[c, :, tsl]).then_inc(gl.h, 16); gl.n += 16
                for i in range(128):
                    s_, c_ = i // 8, i % 8
                    g.wait_ge(dv.h, M["dv"][("sttB", s_)])
                    g.wait_ge(dv.h, M["dv"][("dtxa", c_)])
                    if i >= 4:
                        g.wait_ge(dv.h, M["dv"][("scan", i - 4)])
                    pl.inc(g.tensor_mul(out=dxbuf[:, (i % 4) * TTs:(i % 4 + 1) * TTs],
                                        in0=dtxa[:, c_ * TTs:(c_ + 1) * TTs],
                                        in1=bbuf[:, (s_ % 2) * TTs:(s_ % 2 + 1) * TTs]))

        assert dv.n == DV0 + DV_PER * (ts + 1), (dv.n, ts)
        assert ac.n == AC0 + AC_PER * (ts + 1)
        assert pl.n == PL0 + PL_PER * (ts + 1)
        assert mm.n == MM0 + MM_PER * (ts + 1)
        if cc_cb is not None:
            with nc.Block() as blk:
                @blk.gpsimd
                def _(g):
                    cc_cb(g, ts)

    with nc.Block() as blk:
        @blk.gpsimd
        def _(g):
            g.wait_ge(st.h, st.n)
    if standalone:
        return p.close()
    return None



class LnScratch:
    def __init__(self, p, N, tag="", ps_pool=None):
        self.N = N
        rows = p.sb(f"ln_rows{tag}", [1, 3 * N], F32)
        rows2 = p.sb(f"ln_rows2{tag}", [1, 2 * N], F32R)
        self.st1 = rows[0:1, 0:N]
        self.st2 = rows[0:1, N:2 * N]
        self.var = rows[0:1, 2 * N:3 * N]
        self.mu = rows2[0:1, 0:N]
        self.rstd = rows2[0:1, N:2 * N]
        self.murep = p.sb(f"ln_murep{tag}", [128, N], F32)
        self.rrep = p.sb(f"ln_rrep{tag}", [128, N], F32)
        self.sq = p.sb(f"ln_sq{tag}", [128, 2 * N], F32R)
        self.xn = p.sb(f"ln_xn{tag}", [128, N], F32)
        if ps_pool is None:
            self.psS = p.ps(f"ln_psS{tag}", [1, N])[:]
            self.psQ = p.ps(f"ln_psQ{tag}", [1, N])[:]
            self.psM = p.ps(f"ln_psM{tag}", [128, N])[:]
            self.psR = p.ps(f"ln_psR{tag}", [128, N])[:]
        else:
            self.psS = ps_pool[0][0:1, :]
            self.psQ = ps_pool[1][0:1, :]
            self.psM = ps_pool[2][:]
            self.psR = ps_pool[3][:]


def emit_ln(nc, S, ones128, ones1, xin, xout, eps_ap, sems, gamma=None, beta=None,
            store=None, st=None, extra_adds=None):
    mm, dv, ac = sems
    N = S.N
    mm0, dv0, ac0 = mm.n, dv.n, ac.n
    with nc.Block() as blk:
        @blk.tensor
        def _(t):
            for c in range(8):
                last = t.matmul(out=S.psS, lhsT=ones128[:], rhs=xin(c),
                                start=(c == 0), stop=(c == 7))
            mm.inc(last)
            for c in range(8):
                t.wait_ge(ac.h, ac0 + c + 1)
                mm.inc(t.matmul(out=S.psQ, lhsT=ones128[:],
                                rhs=S.sq[:, (c % 2) * N:(c % 2 + 1) * N],
                                start=(c == 0), stop=(c == 7)))
            t.wait_ge(dv.h, dv0 + 3)
            mm.inc(t.matmul(out=S.psM, lhsT=ones1[0:1, :], rhs=S.mu, start=True, stop=True))
            t.wait_ge(ac.h, ac0 + 10)
            mm.inc(t.matmul(out=S.psR, lhsT=ones1[0:1, :], rhs=S.rstd, start=True, stop=True))

        @blk.scalar
        def _(a):
            for c in range(8):
                if c >= 2:
                    a.wait_ge(mm.h, mm0 + 2 + (c - 2))
                ac.inc(a.activation(out=S.sq[:, (c % 2) * N:(c % 2 + 1) * N], in_=xin(c), func=AF.Square))
            a.wait_ge(dv.h, dv0 + 5)
            ac.inc(a.activation(out=S.var, in_=S.var, func=AF.Ln, bias=eps_ap))
            a.wait_ge(ac.h, ac0 + 9)
            ac.inc(a.activation(out=S.rstd, in_=S.var, func=AF.Exp, scale=-0.5))
            if store is not None:
                for c in range(8):
                    a.wait_ge(dv.h, dv0 + 8 + c)
                    dram_ap, sb_ap = store(c)
                    a.dma_start(out=dram_ap, in_=sb_ap).then_inc(st.h, 16); st.n += 16
                a.wait_ge(st.h, st.n)

        @blk.vector
        def _(v):
            v.wait_ge(mm.h, mm0 + 9)
            dv.inc(v.tensor_copy(out=S.st1, in_=S.psS))
            dv.inc(v.tensor_copy(out=S.st2, in_=S.psQ))
            v.wait_ge(dv.h, dv0 + 2)
            dv.inc(v.tensor_single_scalar(out=S.mu, in_=S.st1, scalar=1.0 / 1024.0, op=ALU.mult))
            v.wait_ge(dv.h, dv0 + 3)
            dv.inc(v.tensor_mul(out=S.st1, in0=S.mu, in1=S.mu))
            v.wait_ge(dv.h, dv0 + 4)
            dv.inc(v.scalar_tensor_tensor(out=S.var, in0=S.st2, scalar=1.0 / 1024.0,
                                          in1=S.st1, op0=ALU.mult, op1=ALU.subtract))
            v.wait_ge(mm.h, mm0 + 10)
            dv.inc(v.tensor_copy(out=S.murep[:], in_=S.psM))
            v.wait_ge(mm.h, mm0 + 11)
            dv.inc(v.tensor_copy(out=S.rrep[:], in_=S.psR))
            v.wait_ge(dv.h, dv0 + 7)
            for c in range(8):
                v.tensor_sub(out=S.xn[:], in0=xin(c), in1=S.murep[:])
                if gamma is None:
                    dv.inc(v.tensor_mul(out=xout(c), in0=S.xn[:], in1=S.rrep[:]))
                else:
                    v.tensor_mul(out=S.xn[:], in0=S.xn[:], in1=S.rrep[:])
                    dv.inc(v.tensor_scalar(out=xout(c), in0=S.xn[:], scalar1=gamma(c),
                                           scalar2=beta(c), op0=ALU.mult, op1=ALU.add))



def build_f0(TLn=TL, TTs=256, upto='', p=None, sems=None, heads=4,
             summed_in=False, partial_out=False, cc_cb=None):
    NSUP = TLn // TTs
    H8 = heads * 8
    KGN = heads * 4
    standalone = p is None
    if standalone:
        p = Prog()
    nc = p.nc
    if summed_in:
        pe_i = p.din("osum", [TLn // TTs, 8, 128, TTs])
        po_i = None
    else:
        pe_i = p.din("pe", [8, 128, TLn])
        po_i = p.din("po", [8, 128, TLn])
    x_i = p.din("x", [8, 128, TLn])
    bout_i = p.din("bouti", [128, 8])
    rm_i = p.din("rmbd", [H8, 128, 128], F32R)
    rup_i = p.din("rupbd", [128, 16 * heads * 128], F32R)
    wc_i = p.din("wci", [heads * 4096, 1024])
    ones128_i = p.din("ones128", [128, 1], F32R)
    ones1_i = p.din("ones1", [128, 128], F32R)
    if partial_out:
        o_cp = p.dout("cpart", [TLn // TTs, 8, 128, TTs])
        o_x1 = p.dout("x1o", [8, 128, TLn])
        o_x2 = None
    else:
        o_x2 = p.dout("x2", [8, 128, TLn])

    rm = p.sb("rm", [128, H8 * 128], F32R)
    rup = p.sb("rup", [128, 16 * heads * 128], BF16)
    wck = p.sb("wck", [128, 2 * 8 * 1024], BF16)
    bout = p.sb("s_bout", [128, 8], F32)
    ones128 = p.sb("s_ones128", [128, 1], F32R)
    ones1 = p.sb("s_ones1", [128, 128], F32R)
    pe = p.sb("s_pe", [128, 8 * TTs], F32)
    po = None if summed_in else p.sb("s_po", [128, 8 * TTs], F32)
    xx = p.sb("s_xx", [128, 8 * TTs], F32)
    lnin = p.sb("lnin", [128, 8 * TTs], F32R)
    x1 = p.sb("s_x1", [128, 8 * TTs], F32R)
    x2 = None if partial_out else p.sb("s_x2", [128, 8 * TTs], F32)
    yr = p.sb("yr", [128, H8 * TTs], BF16)
    hact = p.sb("hact", [128, 2 * 8 * TTs], BF16)
    acc = p.sb("s_acc", [128, 8 * TTs], F32)
    eps5 = p.sb("eps5", [1, 1], F32)
    ps1 = [p.ps(f"ps1{i}", [128, TTs]) for i in range(2)]
    ps2 = [p.ps(f"ps2{i}", [128, TTs]) for i in range(2)]
    pso = [p.ps(f"pso{i}", [128, TTs]) for i in range(2)]
    S = LnScratch(p, TTs, tag=p.pre + "F0", ps_pool=[ps1[0], ps1[1], ps2[0], ps2[1]])

    if sems is None:
        ld, gl, mm = p.sem("ld"), p.sem("gl"), p.sem("mm")
        dv, ac, st, pl = p.sem("dv"), p.sem("ac"), p.sem("st"), p.sem("pl")
    else:
        ld, gl, mm, dv, ac, st, pl = (sems["ld"], sems["gl"], sems["mm"], sems["dv"],
                                      sems["ac"], sems["st"], sems["pl"])
    ldE, stE = ld.n, st.n

    with nc.Block() as blk:
        @blk.sync
        def _(q):
            for i in range(H8):
                q.dma_start(out=rm[:, i * 128:(i + 1) * 128],
                            in_=rm_i[i]).then_inc(ld.h, 16); ld.n += 16
            for t_, s_ in ((bout, bout_i), (ones128, ones128_i), (ones1, ones1_i)):
                q.dma_start(out=t_[:], in_=s_).then_inc(ld.h, 16); ld.n += 16

        @blk.gpsimd
        def _(g0):
            g0.dma_start(out=rup[:], in_=rup_i).then_inc(ld.h, 16); ld.n += 16

        @blk.vector
        def _(v):
            v.memset(eps5[:], 1e-5)
    ld_init = ld.n

    NLD = 16 if summed_in else 24
    for ts in range(NSUP):
        tsl = slice(ts * TTs, (ts + 1) * TTs)
        with nc.Block() as blk:
            pl_a0 = pl.n

            @blk.sync
            def _(q):
                for c in range(8):
                    q.dma_start(out=pe[:, c * TTs:(c + 1) * TTs], in_=pe_i[ts, c]).then_inc(ld.h, 16); ld.n += 16
                    if not summed_in:
                        q.dma_start(out=po[:, c * TTs:(c + 1) * TTs], in_=po_i[c, :, tsl]).then_inc(ld.h, 16); ld.n += 16
                    q.dma_start(out=xx[:, c * TTs:(c + 1) * TTs], in_=x_i[c, :, tsl]).then_inc(ld.h, 16); ld.n += 16

            ld_all = ld_init + NLD * 16 * (ts + 1)

            if summed_in:
                @blk.vector
                def _(v):
                    v.wait_ge(ld.h, ld_all)
                    for c in range(8):
                        dv.inc(v.scalar_tensor_tensor(out=lnin[:, c * TTs:(c + 1) * TTs],
                                                      in0=xx[:, c * TTs:(c + 1) * TTs],
                                                      scalar=bout[:, c:c + 1],
                                                      in1=pe[:, c * TTs:(c + 1) * TTs],
                                                      op0=ALU.add, op1=ALU.add))
            else:
                @blk.gpsimd
                def _(g):
                    g.wait_ge(ld.h, ld_all)
                    for c in range(8):
                        pl.inc(g.tensor_add(out=acc[:, c * TTs:(c + 1) * TTs],
                                            in0=pe[:, c * TTs:(c + 1) * TTs],
                                            in1=po[:, c * TTs:(c + 1) * TTs]))

                @blk.vector
                def _(v):
                    for c in range(8):
                        v.wait_ge(pl.h, pl_a0 + c + 1)
                        dv.inc(v.scalar_tensor_tensor(out=lnin[:, c * TTs:(c + 1) * TTs],
                                                      in0=xx[:, c * TTs:(c + 1) * TTs],
                                                      scalar=bout[:, c:c + 1],
                                                      in1=acc[:, c * TTs:(c + 1) * TTs],
                                                      op0=ALU.add, op1=ALU.add))
        emit_ln(nc, S, ones128, ones1,
                lambda c: lnin[:, c * TTs:(c + 1) * TTs],
                lambda c: x1[:, c * TTs:(c + 1) * TTs], eps5[0:1, 0:1], (mm, dv, ac),
                store=(lambda c: (o_x1[c, :, tsl], x1[:, c * TTs:(c + 1) * TTs].bitcast(F32)))
                if partial_out else None,
                st=st if partial_out else None)
        if upto == "a":
            with nc.Block() as blk:
                @blk.scalar
                def _(a):
                    for c in range(8):
                        srcT = x1
                        a.dma_start(out=o_x2[c, :, tsl],
                                    in_=srcT[:, c * TTs:(c + 1) * TTs].bitcast(F32)).then_inc(st.h, 16); st.n += 16
                    a.wait_ge(st.h, st.n)
            continue
        with nc.Block() as blk:
            mm0, dv0, ac0, gl0 = mm.n, dv.n, ac.n, gl.n
            s1m = lambda i: mm0 + i + 1
            s2m = lambda k: mm0 + H8 + (k // 8) * 16 + (k % 8) + 1
            cpm = lambda kg, mc: mm0 + H8 + kg * 16 + 8 + mc + 1
            s1e = lambda i: ac0 + i + 1
            gel = lambda k: ac0 + H8 + (k // 8) * 8 + (k % 8) + 1
            aad = lambda kg, mc: dv0 + kg * 8 + mc + 1

            @blk.gpsimd
            def _(g):
                for kg in range(KGN):
                    if kg >= 2:
                        g.wait_ge(mm.h, cpm(kg - 2, 7))
                    for j in range(8):
                        g.dma_start(out=wck[:, (kg % 2) * 8192 + j * 1024:(kg % 2) * 8192 + (j + 1) * 1024],
                                    in_=wc_i[kg * 1024 + j * 128:kg * 1024 + (j + 1) * 128, :]).then_inc(gl.h, 16); gl.n += 16

            @blk.tensor
            def _(t):
                t.wait_ge(ld.h, ld_init)
                for i in range(H8):
                    h, cc = i // 8, i % 8
                    if i >= 2:
                        t.wait_ge(ac.h, s1e(i - 2))
                    mm.inc(t.matmul(out=ps1[i % 2][:], lhsT=rm[:, i * 128:(i + 1) * 128],
                                    rhs=x1[:, cc * TTs:(cc + 1) * TTs], start=True, stop=True))
                for kg in range(KGN):
                    t.wait_ge(gl.h, gl0 + 16 * 8 * (kg + 1))
                    for j in range(8):
                        k = kg * 8 + j
                        h, rem = k // 32, k % 32
                        cc, pj = rem // 4, rem % 4
                        t.wait_ge(ac.h, s1e(h * 8 + cc))
                        if k >= 2:
                            t.wait_ge(ac.h, gel(k - 2))
                        q2 = (h * 8 + cc) * 2 + (pj % 2)
                        mm.inc(t.matmul(out=ps2[k % 2][:],
                                        lhsT=rup[(pj // 2) * 64:(pj // 2) * 64 + 64, q2 * 128:(q2 + 1) * 128],
                                        rhs=yr[(pj // 2) * 64:(pj // 2) * 64 + 64,
                                               (h * 8 + cc) * TTs:(h * 8 + cc + 1) * TTs],
                                        start=True, stop=True))
                    t.wait_ge(ac.h, gel(kg * 8 + 7))
                    for mc in range(8):
                        if kg >= 1 or mc >= 2:
                            pkg, pmc = (kg, mc - 2) if mc >= 2 else (kg - 1, mc + 6)
                            t.wait_ge(dv.h, aad(pkg, pmc))
                        for j in range(8):
                            last = t.matmul(out=pso[(kg * 8 + mc) % 2][:],
                                            lhsT=wck[:, (kg % 2) * 8192 + j * 1024 + mc * 128:(kg % 2) * 8192 + j * 1024 + (mc + 1) * 128],
                                            rhs=hact[:, ((kg % 2) * 8 + j) * TTs:((kg % 2) * 8 + j + 1) * TTs],
                                            start=(j == 0), stop=(j == 7))
                        mm.inc(last)

            @blk.scalar
            def _(a):
                for i in range(H8):
                    a.wait_ge(mm.h, s1m(i))
                    ac.inc(a.activation(out=yr[:, i * TTs:(i + 1) * TTs], in_=ps1[i % 2][:], func=AF.Copy))
                for k in range(KGN * 8):
                    kg, j = k // 8, k % 8
                    a.wait_ge(mm.h, s2m(k))
                    if k >= 16:
                        a.wait_ge(mm.h, cpm(kg - 2, 7))
                    ac.inc(a.activation(out=hact[:, ((kg % 2) * 8 + j) * TTs:((kg % 2) * 8 + j + 1) * TTs],
                                        in_=ps2[k % 2][:], func=AF.Gelu))

            @blk.vector
            def _(v):
                for kg in range(KGN):
                    for mc in range(8):
                        v.wait_ge(mm.h, cpm(kg, mc))
                        if kg == 0:
                            dv.inc(v.tensor_copy(out=acc[:, mc * TTs:(mc + 1) * TTs],
                                                 in_=pso[(kg * 8 + mc) % 2][:]))
                        else:
                            dv.inc(v.tensor_add(out=acc[:, mc * TTs:(mc + 1) * TTs],
                                                in0=acc[:, mc * TTs:(mc + 1) * TTs],
                                                in1=pso[(kg * 8 + mc) % 2][:]))
        if partial_out:
            with nc.Block() as blk:
                @blk.scalar
                def _(a):
                    for c in range(8):
                        a.dma_start(out=o_cp[ts, c],
                                    in_=acc[:, c * TTs:(c + 1) * TTs]).then_inc(st.h, 16); st.n += 16
                    a.wait_ge(st.h, st.n)
            if cc_cb is not None:
                with nc.Block() as blk:
                    @blk.gpsimd
                    def _(g):
                        cc_cb(g, ts)
            continue
        if upto == "b":
            with nc.Block() as blk:
                @blk.scalar
                def _(a):
                    a.wait_ge(dv.h, dv.n)
                    for c in range(8):
                        a.dma_start(out=o_x2[c, :, tsl],
                                    in_=acc[:, c * TTs:(c + 1) * TTs]).then_inc(st.h, 16); st.n += 16
                    a.wait_ge(st.h, st.n)
            continue
        with nc.Block() as blk:
            @blk.vector
            def _(v):
                for c in range(8):
                    dv.inc(v.tensor_add(out=lnin[:, c * TTs:(c + 1) * TTs],
                                        in0=acc[:, c * TTs:(c + 1) * TTs],
                                        in1=x1[:, c * TTs:(c + 1) * TTs]))
        if upto == "c1":
            with nc.Block() as blk:
                @blk.scalar
                def _(a):
                    for c in range(8):
                        a.dma_start(out=o_x2[c, :, tsl],
                                    in_=lnin[:, c * TTs:(c + 1) * TTs].bitcast(F32)).then_inc(st.h, 16); st.n += 16
                    a.wait_ge(st.h, st.n)
            continue
        emit_ln(nc, S, ones128, ones1,
                lambda c: lnin[:, c * TTs:(c + 1) * TTs],
                lambda c: x2[:, c * TTs:(c + 1) * TTs], eps5[0:1, 0:1], (mm, dv, ac),
                store=lambda c: (o_x2[c, :, tsl], x2[:, c * TTs:(c + 1) * TTs]), st=st)

    with nc.Block() as blk:
        @blk.gpsimd
        def _(g):
            g.wait_ge(st.h, st.n)
    if standalone:
        return p.close()
    return None


def host_f0_inputs(inp, l0b_res):
    x = np.asarray(inp["L0_attn_x"], np.float32)
    b_out = np.asarray(inp["L0_attn_b_out"], np.float32)
    rm = np.asarray(inp["L0_ffn_rail_maker"], np.float32)
    rup = np.asarray(inp["L0_ffn_rail_up_proj"], np.float32)
    wc = np.asarray(inp["L0_ffn_w_cproj"], np.float32)
    rmbd = np.zeros((32, 128, 128), np.float32)
    for h in range(4):
        for cc in range(8):
            for j in range(8):
                r = cc * 8 + j
                rmbd[h * 8 + cc, 16 * j:16 * j + 16, 16 * j:16 * j + 16] = rm[h, r]
    rupbd = np.zeros((128, 64 * 128), np.float32)
    for h in range(4):
        for pp in range(32):
            cc, pj = pp // 4, pp % 4
            q = (h * 8 + cc) * 2 + (pj % 2)
            ro = (pj // 2) * 64
            sub = (pj % 2) * 32
            blk = np.zeros((64, 128), np.float32)
            for rr in range(2):
                rail = 2 * pp + rr
                blk[sub + 16 * rr:sub + 16 * rr + 16, 64 * rr:64 * rr + 64] = rup[h, rail].T
            rupbd[ro:ro + 64, q * 128:(q + 1) * 128] = blk
    wct = np.ascontiguousarray(wc.T)
    maps = []
    for c in range(NCORE):
        bidx, hh = c // 2, c % 2
        tok = slice(hh * TL, (hh + 1) * TL)
        xfm = np.ascontiguousarray(x[bidx, tok].T.reshape(8, 128, TL))
        maps.append({
            "pe": l0b_res[2 * bidx]["opart"][:, :, tok].copy(),
            "po": l0b_res[2 * bidx + 1]["opart"][:, :, tok].copy(),
            "x": xfm,
            "bouti": np.ascontiguousarray(b_out.reshape(8, 128).T),
            "rmbd": rmbd,
            "rupbd": rupbd,
            "wci": wct,
            "ones128": np.ones((128, 1), np.float32),
            "ones1": np.ones((128, 128), np.float32),
        })
    return maps



def build_l1a(Tn=T, TTs=256, p=None, sems=None, sel_in=False, cc_cb=None):
    NSUP = Tn // TTs
    GR = 8
    standalone = p is None
    if standalone:
        p = Prog()
    nc = p.nc
    if sel_in:
        x2_i = p.din("x2", [8, 128, Tn])
        x2r_i = p.din("x2r", [8, 128, Tn])
        rsel_i = p.din("revsel", [128, 2])
    else:
        x2_i = p.din("x2f", [8, 128, Tn])
    wixz_i = p.din("wixzi", [1024, 4096])
    wxp_i = p.din("wxpi", [2048, 96], F32R)
    wdt_i = p.din("wdti", [64, 2048], F32R)
    wout_i = p.din("wouti", [2048, 1024])
    alog_i = p.din("alogi", [128, 256])
    cstL_i = p.din("cstLi", [128, 128])
    ones128_i = p.din("ones128", [128, 1], F32R)
    ones1_i = p.din("ones1", [128, 128], F32R)
    id128L_i = p.din("id128L", [128, 128], F32R)
    o_mo = p.dout("mo", [Tn // TTs, 8, 128, TTs])
    sc_b = nc.dram_tensor(p.pre + "l1a_scb", [16, Tn], F32, kind="Internal").ap()
    sc_c = nc.dram_tensor(p.pre + "l1a_scc", [16, Tn], F32, kind="Internal").ap()
    wixz_bf = nc.dram_tensor(p.pre + "l1a_wixzbf", [1024, 4096], BF16, kind="Internal").ap()
    wout_bf = nc.dram_tensor(p.pre + "l1a_woutbf", [2048, 1024], BF16, kind="Internal").ap()

    wmc = p.sb("wmc", [128, 2 * 1024], BF16)
    wob = p.sb("wob", [128, 2 * 1024], BF16)
    wxp = p.sb("l_wxp", [128, 16 * 96], F32R)
    wdt = p.sb("l_wdt", [64, 2048], BF16)
    ones128 = p.sb("l_ones128", [128, 1], F32R)
    ones1 = p.sb("l_ones1", [128, 128], F32R)
    x2t = p.sb("l_x2t", [128, 8 * TTs], F32R)
    hh_ = p.sb("l_hh", [128, 8 * TTs], BF16)
    halos = p.sb("l_halos", [128, 16 * 3], F32)
    xs_scr = p.sb("l_xsscr", [128, TTs + 3], F32)
    convo = p.sb("l_convo", [128, TTs], F32)
    xa = p.sb("l_xa", [128, 16 * TTs], F32R)
    zb = p.sb("l_zb", [128, 16 * TTs], BF16)
    dtr = p.sb("l_dtr", [64, TTs], BF16)
    bsb = p.sb("l_bsb", [16, TTs], F32)
    csb = p.sb("l_csb", [16, TTs], F32)
    dt_g = p.sb("l_dtg", [128, GR * TTs], F32)
    dtxa_g = p.sb("l_dtxag", [128, GR * TTs], F32)
    brep = p.sb("l_brep", [128, 2 * TTs], F32)
    crep = p.sb("l_crep", [128, 2 * TTs], F32)
    dab = p.sb("l_dab", [128, 4 * TTs], F32)
    dxb = p.sb("l_dxb", [128, 4 * TTs], F32)
    hbu = p.sb("l_hbu", [128, 4 * TTs], F32)
    hcb = p.sb("l_hcb", [128, 4 * TTs], BF16)
    id128L = p.sb("l_id128", [128, 128], BF16)
    yacc = None
    ybuf = p.sb("l_ybuf", [128, TTs], F32)
    gated = None
    ovac = p.sb("l_ovac", [128, TTs], F32)
    hst = p.sb("l_hst", [128, 256], F32)
    yacc = x2t
    gated = zb
    sp1 = p.sb("l_sp1", [128, TTs], F32)
    sp2 = p.sb("l_sp2", [128, TTs], F32)
    zsc = sp2
    if sel_in:
        selscr = sp1
    alog = p.sb("l_alog", [128, 256], F32)
    aneg = alog
    cst = p.sb("l_cst", [128, 128], F32)
    eps5 = p.sb("l_eps5", [1, 1], F32)
    pa = [p.ps(f"l_pa{i}", [128, TTs]) for i in range(8)]
    S = LnScratch(p, TTs, tag="L", ps_pool=[pa[4], pa[5], pa[6], pa[7]])

    if sel_in:
        x2rv = p.sb("l_x2rv", [128, 2 * TTs], F32)
        rsel = p.sb("l_rsel", [128, 2], F32)
    if sems is None:
        ld, gl, mm = p.sem("ld"), p.sem("gl"), p.sem("mm")
        dv, ac, pl, st = p.sem("dv"), p.sem("ac"), p.sem("pl"), p.sem("st")
    else:
        ld, gl, mm, dv, ac, pl, st = (sems["ld"], sems["gl"], sems["mm"], sems["dv"],
                                      sems["ac"], sems["pl"], sems["st"])
    stE = st.n

    def ccol(j):
        return cst[:, j:j + 1]

    with nc.Block() as blk:
        @blk.sync
        def _(q):
            for kc in range(16):
                q.dma_start(out=wxp[:, kc * 96:(kc + 1) * 96],
                            in_=wxp_i[kc * 128:(kc + 1) * 128, :]).then_inc(ld.h, 16); ld.n += 16
            pairs2 = [(alog, alog_i),
                      (cst, cstL_i), (ones128, ones128_i), (ones1, ones1_i)]
            if sel_in:
                pairs2.append((rsel, rsel_i))
            for t_, s_ in pairs2:
                q.dma_start(out=t_[:], in_=s_).then_inc(ld.h, 16); ld.n += 16

        @blk.gpsimd
        def _(g):
            g.dma_start(out=wdt[:], in_=wdt_i).then_inc(ld.h, 16); ld.n += 16
            g.dma_start(out=id128L[:], in_=id128L_i).then_inc(ld.h, 16); ld.n += 16
            for q8 in range(8):
                g.dma_start(out=wixz_bf[q8 * 128:(q8 + 1) * 128],
                            in_=wixz_i[q8 * 128:(q8 + 1) * 128]).then_inc(ld.h, 16); ld.n += 16
            for q8 in range(8):
                g.dma_start(out=wout_bf[q8 * 256:(q8 + 1) * 256],
                            in_=wout_i[q8 * 256:(q8 + 1) * 256]).then_inc(ld.h, 16); ld.n += 16
            g.wait_ge(ld.h, ld.n)

        @blk.vector
        def _(v):
            v.memset(eps5[:], 1e-5)
            v.memset(hst[:], 0.0)
            v.memset(halos[:], 0.0)

        @blk.scalar
        def _(a):
            a.wait_ge(ld.h, ld.n)
            a.activation(out=aneg[:], in_=alog[:], func=AF.Exp)
    with nc.Block() as blk:
        @blk.vector
        def _(v):
            v.tensor_single_scalar(out=aneg[:], in_=aneg[:], scalar=-1.0, op=ALU.mult)

    LDPC = ld.n
    for ts in range(NSUP):
        tsl = slice(ts * TTs, (ts + 1) * TTs)
        with nc.Block() as blk:
            if not sel_in:
                @blk.gpsimd
                def _(g):
                    for c in range(8):
                        g.dma_start(out=x2t[:, c * TTs:(c + 1) * TTs],
                                    in_=x2_i[c, :, tsl]).then_inc(gl.h, 16); gl.n += 16
                    g.wait_ge(gl.h, gl.n)
            else:
                ld0s, gl0s, dv0s = ld.n, gl.n, dv.n

                @blk.sync
                def _(q):
                    q.wait_ge(dv.h, dv0s)
                    for c in range(8):
                        if c >= 2:
                            q.wait_ge(dv.h, dv0s + (c - 2) + 1)
                        q.dma_start(out=x2rv[:, (c % 2) * TTs:(c % 2 + 1) * TTs],
                                    in_=x2r_i[c, :, tsl]).then_inc(ld.h, 16); ld.n += 16

                @blk.gpsimd
                def _(g):
                    g.wait_ge(dv.h, dv0s)
                    for c in range(8):
                        g.dma_start(out=x2t[:, c * TTs:(c + 1) * TTs],
                                    in_=x2_i[c, :, tsl]).then_inc(gl.h, 16); gl.n += 16

                @blk.vector
                def _(v):
                    for c in range(8):
                        v.wait_ge(ld.h, ld0s + 16 * (c + 1))
                        v.wait_ge(gl.h, gl0s + 16 * (c + 1))
                        v.tensor_single_scalar(out=selscr[:], in_=x2rv[:, (c % 2) * TTs:(c % 2 + 1) * TTs],
                                               scalar=rsel[:, 1:2], op=ALU.mult)
                        dv.inc(v.scalar_tensor_tensor(out=x2t[:, c * TTs:(c + 1) * TTs],
                                                      in0=x2t[:, c * TTs:(c + 1) * TTs],
                                                      scalar=rsel[:, 0:1], in1=selscr[:],
                                                      op0=ALU.mult, op1=ALU.add))
        emit_ln(nc, S, ones128, ones1,
                lambda c: x2t[:, c * TTs:(c + 1) * TTs],
                lambda c: hh_[:, c * TTs:(c + 1) * TTs], eps5[0:1, 0:1], (mm, dv, ac),
                gamma=lambda c: cst[:, 112 + c:113 + c], beta=lambda c: cst[:, 120 + c:121 + c])
        with nc.Block() as blk:
            mm0, dv0, ac0, gl0 = mm.n, dv.n, ac.n, gl.n
            m_ipz = lambda j: mm0 + j + 1
            m_ipx = lambda c: mm0 + 16 + c + 1
            m_xse = lambda c: dv0 + 2 * c + 1
            m_conv = lambda c: dv0 + 2 * c + 2
            m_zv = lambda j: ac0 + j + 1
            m_xa = lambda c: ac0 + 16 + c + 1
            wcol = lambda m: (16 + m) * 128 if m < 16 else (m - 16) * 128

            @blk.sync
            def _(q):
                q.wait_ge(ld.h, LDPC)
                for m in range(32):
                    if m >= 2:
                        q.wait_ge(mm.h, mm0 + (m - 2) + 1)
                    src = bass.AP(tensor=wixz_bf.tensor, offset=wcol(m),
                                  ap=[[4096, 128], [128 * 4096, 8], [1, 128]])
                    q.dma_start(out=wmc[:, (m % 2) * 1024:(m % 2) * 1024 + 1024],
                                in_=src).then_inc(gl.h, 16); gl.n += 16

            @blk.tensor
            def _(t):
                for j in range(16):
                    bank = j % 2
                    t.wait_ge(gl.h, gl0 + 16 * (j + 1))
                    if j >= 2:
                        t.wait_ge(ac.h, m_zv(j - 2))
                    for kc in range(8):
                        last = t.matmul(out=pa[bank][:],
                                        lhsT=wmc[:, (j % 2) * 1024 + kc * 128:(j % 2) * 1024 + (kc + 1) * 128],
                                        rhs=hh_[:, kc * TTs:(kc + 1) * TTs],
                                        start=(kc == 0), stop=(kc == 7))
                    mm.inc(last)
                for c in range(16):
                    m = 16 + c
                    bank = c % 2
                    t.wait_ge(gl.h, gl0 + 16 * (m + 1))
                    if c >= 2:
                        t.wait_ge(dv.h, m_xse(c - 2))
                    else:
                        t.wait_ge(ac.h, m_zv(14 + c))
                    for kc in range(8):
                        last = t.matmul(out=pa[bank][:],
                                        lhsT=wmc[:, (m % 2) * 1024 + kc * 128:(m % 2) * 1024 + (kc + 1) * 128],
                                        rhs=hh_[:, kc * TTs:(kc + 1) * TTs],
                                        start=(kc == 0), stop=(kc == 7))
                    mm.inc(last)

            @blk.vector
            def _(v):
                for c in range(16):
                    bank = c % 2
                    v.wait_ge(mm.h, m_ipx(c))
                    if c >= 1:
                        v.wait_ge(ac.h, m_xa(c - 1))
                    v.tensor_copy(out=xs_scr[:, 0:3], in_=halos[:, c * 3:(c + 1) * 3])
                    dv.inc(v.tensor_copy(out=xs_scr[:, 3:3 + TTs], in_=pa[bank][:]))
                    v.wait_ge(dv.h, m_xse(c))
                    v.tensor_copy(out=halos[:, c * 3:(c + 1) * 3], in_=xs_scr[:, TTs:TTs + 3])
                    v.tensor_single_scalar(out=convo[:], in_=xs_scr[:, 0:TTs],
                                           scalar=ccol(6 * c + 1), op=ALU.mult)
                    for k in range(1, 4):
                        kwv = {"out": convo[:], "in0": xs_scr[:, k:k + TTs],
                               "scalar": ccol(6 * c + 1 + k), "in1": convo[:],
                               "op0": ALU.mult, "op1": ALU.add}
                        if k < 3:
                            v.scalar_tensor_tensor(**kwv)
                        else:
                            dv.inc(v.scalar_tensor_tensor(**kwv))

            @blk.scalar
            def _(a):
                for j in range(16):
                    a.wait_ge(mm.h, m_ipz(j))
                    ac.inc(a.activation(out=zb[:, j * TTs:(j + 1) * TTs],
                                        in_=pa[j % 2][:], func=AF.Copy))
                for c in range(16):
                    a.wait_ge(dv.h, m_conv(c))
                    ac.inc(a.activation(out=xa[:, c * TTs:(c + 1) * TTs], in_=convo[:],
                                        func=AF.Silu, bias=ccol(6 * c)))

        with nc.Block() as blk:
            mm0, dv0 = mm.n, dv.n

            @blk.tensor
            def _(t):
                for kc in range(16):
                    last = t.matmul(out=pa[0][0:64, :], lhsT=wxp[:, kc * 96:kc * 96 + 64],
                                    rhs=xa[:, kc * TTs:(kc + 1) * TTs],
                                    start=(kc == 0), stop=(kc == 15))
                mm.inc(last)
                for kc in range(16):
                    last = t.matmul(out=pa[1][0:16, :], lhsT=wxp[:, kc * 96 + 64:kc * 96 + 80],
                                    rhs=xa[:, kc * TTs:(kc + 1) * TTs],
                                    start=(kc == 0), stop=(kc == 15))
                mm.inc(last)
                for kc in range(16):
                    last = t.matmul(out=pa[2][0:16, :], lhsT=wxp[:, kc * 96 + 80:kc * 96 + 96],
                                    rhs=xa[:, kc * TTs:(kc + 1) * TTs],
                                    start=(kc == 0), stop=(kc == 15))
                mm.inc(last)

            @blk.vector
            def _(v):
                v.wait_ge(mm.h, mm0 + 1)
                dv.inc(v.tensor_copy(out=dtr[:], in_=pa[0][0:64, :]))
                v.wait_ge(mm.h, mm0 + 2)
                dv.inc(v.tensor_copy(out=bsb[:], in_=pa[1][0:16, :]))
                v.wait_ge(mm.h, mm0 + 3)
                dv.inc(v.tensor_copy(out=csb[:], in_=pa[2][0:16, :]))

            @blk.scalar
            def _(a):
                a.wait_ge(dv.h, dv0 + 3)
                a.dma_start(out=sc_b[:, tsl], in_=bsb[:]).then_inc(st.h, 16); st.n += 16
                a.dma_start(out=sc_c[:, tsl], in_=csb[:]).then_inc(st.h, 16); st.n += 16
                a.wait_ge(st.h, st.n)

        for g_ in range(2):
            with nc.Block() as blk:
                mm0, dv0, ac0, pl0, ld0 = mm.n, dv.n, ac.n, pl.n, ld.n
                m_dtmm = lambda c8: mm0 + c8 + 1
                m_relu = lambda c8: ac0 + c8 + 1
                m_ldbc = lambda s: ld0 + 16 * (2 * s + 2)
                NPE = 6
                pbank = [pa[0], pa[1], pa[2], pa[5], pa[6], pa[7]]
                m_da = lambda i: ac0 + 8 + i + max(0, i - 4) + 1
                m_hst = lambda i: ac0 + 8 + min(i + 5, 128) + i + 1
                m_zsil = lambda c8: ac0 + 264 + c8 + 1
                dvm = {}
                n_ = dv0
                for c8 in range(8):
                    n_ += 1; dvm[("dtadd", c8)] = n_
                    n_ += 1; dvm[("dtxa", c8)] = n_
                for s in range(16):
                    for c8 in range(8):
                        i = s * 8 + c8
                        n_ += 1; dvm[("scan", i)] = n_
                        n_ += 1; dvm[("hc", i)] = n_
                        if c8 >= NPE:
                            n_ += 1; dvm[("yacc", i)] = n_
                for c8 in range(8):
                    n_ += 1; dvm[("gated", c8)] = n_
                plm = {}
                for i in range(128):
                    plm[("dbx", i)] = pl0 + i + 1
                mmm = {}
                q_ = mm0
                for c8 in range(8):
                    q_ += 1; mmm[("dtmm", c8)] = q_
                for s in range(16):
                    for c8 in range(NPE):
                        q_ += 1; mmm[("yaccmm", s * 8 + c8)] = q_

                @blk.sync
                def _(q):
                    for s in range(16):
                        k = s % 2
                        if s >= 2:
                            q.wait_ge(dv.h, dvm[("hc", (s - 2) * 8 + 7)])
                        q.dma_start(out=brep[:, k * TTs:(k + 1) * TTs],
                                    in_=sc_b[s:s + 1, tsl].partition_broadcast(128)).then_inc(ld.h, 16); ld.n += 16
                        q.dma_start(out=crep[:, k * TTs:(k + 1) * TTs],
                                    in_=sc_c[s:s + 1, tsl].partition_broadcast(128)).then_inc(ld.h, 16); ld.n += 16

                @blk.tensor
                def _(t):
                    for c8 in range(8):
                        c = g_ * 8 + c8
                        if c8 >= 2:
                            t.wait_ge(ac.h, m_relu(c8 - 2))
                        mm.inc(t.matmul(out=pa[3 + c8 % 2][:], lhsT=wdt[:, c * 128:(c + 1) * 128],
                                        rhs=dtr[:], start=True, stop=True))
                    for s in range(16):
                        for c8 in range(NPE):
                            i = s * 8 + c8
                            kk = i % 4
                            t.wait_ge(dv.h, dvm[("hc", i)])
                            mm.inc(t.matmul(out=pbank[c8][:], lhsT=id128L[:],
                                            rhs=hcb[:, kk * TTs:(kk + 1) * TTs],
                                            start=(s == 0), stop=(s == 15)))

                @blk.scalar
                def _(a):
                    for c8 in range(8):
                        c = g_ * 8 + c8
                        a.wait_ge(mm.h, m_dtmm(c8))
                        if c8 >= 1:
                            a.wait_ge(dv.h, dvm[("dtadd", c8 - 1)])
                        a.activation(out=sp1[:], in_=pa[3 + c8 % 2][:], func=AF.Abs, bias=cst[:, 96 + c:97 + c])
                        a.activation(out=sp1[:], in_=sp1[:], func=AF.Exp, scale=-1.0)
                        a.activation(out=sp1[:], in_=sp1[:], func=AF.Ln, bias=1.0)
                        ac.inc(a.activation(out=sp2[:], in_=pa[3 + c8 % 2][:], func=AF.Relu,
                                            bias=cst[:, 96 + c:97 + c]))
                    def emit_hst(i):
                        s_, c8_ = i // 8, i % 8
                        c_ = g_ * 8 + c8_
                        ac.inc(a.activation(out=hst[:, c_ * 16 + s_:c_ * 16 + s_ + 1],
                                            in_=hbu[:, (i % 4) * TTs + TTs - 1:(i % 4) * TTs + TTs],
                                            func=AF.Copy))
                    for i in range(128):
                        s, c8 = i // 8, i % 8
                        c = g_ * 8 + c8
                        a.wait_ge(dv.h, dvm[("dtadd", c8)])
                        if i >= 4:
                            a.wait_ge(dv.h, dvm[("scan", i - 4)])
                        ac.inc(a.activation(out=dab[:, (i % 4) * TTs:(i % 4 + 1) * TTs],
                                            in_=dt_g[:, c8 * TTs:(c8 + 1) * TTs], func=AF.Exp,
                                            scale=aneg[:, c * 16 + s:c * 16 + s + 1]))
                        if i >= 4:
                            emit_hst(i - 4)
                    for i in range(124, 128):
                        a.wait_ge(dv.h, dvm[("scan", i)])
                        emit_hst(i)
                    for c8 in range(8):
                        c = g_ * 8 + c8
                        if c8 >= 1:
                            a.wait_ge(dv.h, dvm[("gated", c8 - 1)])
                        ac.inc(a.activation(out=zsc[:], in_=zb[:, c * TTs:(c + 1) * TTs], func=AF.Silu))

                @blk.vector
                def _(v):
                    for c8 in range(8):
                        c = g_ * 8 + c8
                        v.wait_ge(ac.h, m_relu(c8))
                        dv.inc(v.tensor_add(out=dt_g[:, c8 * TTs:(c8 + 1) * TTs], in0=sp1[:], in1=sp2[:]))
                        dv.inc(v.tensor_mul(out=dtxa_g[:, c8 * TTs:(c8 + 1) * TTs],
                                            in0=dt_g[:, c8 * TTs:(c8 + 1) * TTs],
                                            in1=xa[:, c * TTs:(c + 1) * TTs]))
                    for s in range(16):
                        k = s % 2
                        for c8 in range(8):
                            c = g_ * 8 + c8
                            i = s * 8 + c8
                            kk = i % 4
                            v.wait_ge(ac.h, m_hst(i - 4) if i >= 4 else m_da(i))
                            v.wait_ge(pl.h, plm[("dbx", i)])
                            dv.inc(v.tensor_tensor_scan(out=hbu[:, kk * TTs:(kk + 1) * TTs],
                                                        data0=dab[:, kk * TTs:(kk + 1) * TTs],
                                                        data1=dxb[:, kk * TTs:(kk + 1) * TTs],
                                                        initial=hst[:, c * 16 + s:c * 16 + s + 1],
                                                        op0=ALU.mult, op1=ALU.add))
                            if i >= 4 and (i - 4) % 8 < NPE:
                                v.wait_ge(mm.h, mmm[("yaccmm", i - 4)])
                            dv.inc(v.tensor_mul(out=hcb[:, kk * TTs:(kk + 1) * TTs],
                                                in0=hbu[:, kk * TTs:(kk + 1) * TTs],
                                                in1=crep[:, k * TTs:(k + 1) * TTs]))
                            if c8 >= NPE:
                                if s == 0:
                                    dv.inc(v.tensor_copy(out=yacc[:, c8 * TTs:(c8 + 1) * TTs],
                                                         in_=hcb[:, kk * TTs:(kk + 1) * TTs]))
                                else:
                                    dv.inc(v.tensor_add(out=yacc[:, c8 * TTs:(c8 + 1) * TTs],
                                                        in0=yacc[:, c8 * TTs:(c8 + 1) * TTs],
                                                        in1=hcb[:, kk * TTs:(kk + 1) * TTs]))
                    for c8 in range(8):
                        c = g_ * 8 + c8
                        if c8 < NPE:
                            v.wait_ge(mm.h, mmm[("yaccmm", 15 * 8 + c8)])
                            yin = pbank[c8][:]
                        else:
                            yin = yacc[:, c8 * TTs:(c8 + 1) * TTs]
                        v.scalar_tensor_tensor(out=ybuf[:], in0=xa[:, c * TTs:(c + 1) * TTs],
                                               scalar=ccol(6 * c + 5),
                                               in1=yin,
                                               op0=ALU.mult, op1=ALU.add)
                        v.wait_ge(ac.h, m_zsil(c8))
                        dv.inc(v.tensor_mul(out=gated[:, c * TTs:(c + 1) * TTs],
                                            in0=ybuf[:], in1=zsc[:]))

                @blk.gpsimd
                def _(g):
                    for i in range(128):
                        s_, c8_ = i // 8, i % 8
                        g.wait_ge(ld.h, m_ldbc(s_))
                        g.wait_ge(dv.h, dvm[("dtxa", c8_)])
                        if i >= 4:
                            g.wait_ge(dv.h, dvm[("scan", i - 4)])
                        pl.inc(g.tensor_mul(out=dxb[:, (i % 4) * TTs:(i % 4 + 1) * TTs],
                                            in0=dtxa_g[:, c8_ * TTs:(c8_ + 1) * TTs],
                                            in1=brep[:, (s_ % 2) * TTs:(s_ % 2 + 1) * TTs]))

        with nc.Block() as blk:
            mm0, dv0, gl0, st5 = mm.n, dv.n, gl.n, st.n
            m_okc = lambda kc: mm0 + kc + 1
            m_oev = lambda mc: dv0 + mc + 1

            @blk.sync
            def _(q):
                q.wait_ge(ld.h, LDPC)
                for kc in range(16):
                    if kc >= 2:
                        q.wait_ge(mm.h, m_okc(kc - 2))
                    q.dma_start(out=wob[:, (kc % 2) * 1024:(kc % 2 + 1) * 1024],
                                in_=wout_bf[kc * 128:(kc + 1) * 128, :]).then_inc(gl.h, 16); gl.n += 16

            @blk.tensor
            def _(t):
                for kc in range(16):
                    t.wait_ge(gl.h, gl0 + 16 * (kc + 1))
                    for mc in range(8):
                        last = t.matmul(out=pa[mc][:],
                                        lhsT=wob[:, (kc % 2) * 1024 + mc * 128:(kc % 2) * 1024 + (mc + 1) * 128],
                                        rhs=gated[:, kc * TTs:(kc + 1) * TTs],
                                        start=(kc == 0), stop=(kc == 15))
                    mm.inc(last)

            @blk.vector
            def _(v):
                v.wait_ge(mm.h, m_okc(15))
                for mc in range(8):
                    if mc >= 1:
                        v.wait_ge(st.h, st5 + 16 * mc)
                    dv.inc(v.tensor_copy(out=ovac[:], in_=pa[mc][:]))

            @blk.scalar
            def _(a):
                for mc in range(8):
                    a.wait_ge(dv.h, m_oev(mc))
                    a.dma_start(out=o_mo[ts, mc],
                                in_=ovac[:]).then_inc(st.h, 16); st.n += 16
                a.wait_ge(st.h, st.n)

        if cc_cb is not None:
            with nc.Block() as blk:
                @blk.gpsimd
                def _(g):
                    cc_cb(g, ts)

    with nc.Block() as blk:
        @blk.gpsimd
        def _(g):
            g.wait_ge(st.h, st.n)
    if standalone:
        return p.close()
    return None


def host_l1a_inputs(inp, x2_full):
    maps = []
    ones128 = np.ones((128, 1), np.float32)
    ones1 = np.ones((128, 128), np.float32)
    selm = np.zeros((16, 16, 128), np.float32)
    for s in range(16):
        selm[s, s, :] = 1.0
    selm = np.ascontiguousarray(selm.transpose(1, 0, 2).reshape(16, 2048))
    ng = np.asarray(inp["L1_attn_norm_g"], np.float32)
    nb = np.asarray(inp["L1_attn_norm_b"], np.float32)
    for c in range(NCORE):
        bidx, hh = c // 2, c % 2
        mk = "m1" if hh == 0 else "m2"
        in_w = np.asarray(inp[f"L1_attn_{mk}_in_w"], np.float32)
        conv_w = np.asarray(inp[f"L1_attn_{mk}_conv_w"], np.float32)
        conv_b = np.asarray(inp[f"L1_attn_{mk}_conv_b"], np.float32)
        xproj_w = np.asarray(inp[f"L1_attn_{mk}_xproj_w"], np.float32)
        dt_w = np.asarray(inp[f"L1_attn_{mk}_dt_w"], np.float32)
        dt_b = np.asarray(inp[f"L1_attn_{mk}_dt_b"], np.float32)
        Alog = np.asarray(inp[f"L1_attn_{mk}_Alog"], np.float32)
        Dp = np.asarray(inp[f"L1_attn_{mk}_D"], np.float32)
        out_w = np.asarray(inp[f"L1_attn_{mk}_out_w"], np.float32)
        xv = x2_full[bidx]
        if hh == 1:
            xv = xv[::-1]
        cstL = np.zeros((128, 128), np.float32)
        cb16 = conv_b.reshape(16, 128)
        d16 = Dp.reshape(16, 128)
        for ch in range(16):
            cstL[:, 6 * ch] = cb16[ch]
            for k in range(4):
                cstL[:, 6 * ch + 1 + k] = conv_w[:, k].reshape(16, 128)[ch]
            cstL[:, 6 * ch + 5] = d16[ch]
        cstL[:, 96:112] = dt_b.reshape(16, 128).T
        cstL[:, 112:120] = ng.reshape(8, 128).T
        cstL[:, 120:128] = nb.reshape(8, 128).T
        maps.append({
            "x2f": np.ascontiguousarray(xv.T.reshape(8, 128, T)),
            "wixzi": np.ascontiguousarray(in_w.T),
            "wxpi": np.ascontiguousarray(xproj_w.T),
            "wdti": np.ascontiguousarray(dt_w.T),
            "wouti": np.ascontiguousarray(out_w.T),
            "alogi": np.ascontiguousarray(Alog.reshape(16, 128, 16).transpose(1, 0, 2).reshape(128, 256)),
            "cstLi": cstL,
            "ones128": ones128,
            "ones1": ones1,
            "id128L": np.eye(128, dtype=np.float32),
        })
    return maps



def build_l1b(TLn=TL, TTs=256, p=None, sems=None, summed_in=False, kc_n=32, cc_cb=None,
              partial_out=False):
    NSUP = TLn // TTs
    kg_n = kc_n // 8
    standalone = p is None
    if standalone:
        p = Prog()
    nc = p.nc
    x2_i = p.din("x2h", [8, 128, TLn])
    if summed_in:
        fw_i = p.din("mosum", [TLn // TTs, 8, 128, TTs])
        bw_i = None
    else:
        fw_i = p.din("fwdh", [8, 128, TLn])
        bw_i = p.din("bwdh", [8, 128, TLn])
    ff1_i = p.din("ff1wi", [1024, kc_n * 128])
    ff2_i = p.din("ff2wi", [kc_n * 128, 1024])
    cstF_i = p.din("cstFi", [128, 56])
    ones128_i = p.din("ones128", [128, 1], F32R)
    ones1_i = p.din("ones1", [128, 128], F32R)
    if partial_out:
        o_ffp = p.dout("ffp", [TLn // TTs, 8, 128, TTs])
        o_xm = p.dout("xmo", [8, 128, TLn])
        o_x3 = None
    else:
        o_x3 = p.dout("x3", [8, 128, TLn])

    ff1w = p.sb("b_ff1w", [128, 8 * kc_n * 128], BF16)
    ff2w = p.sb("b_ff2w", [128, kc_n * 1024], BF16)
    cstF = p.sb("b_cst", [128, 56], F32)
    ones128 = p.sb("b_ones128", [128, 1], F32R)
    ones1 = p.sb("b_ones1", [128, 128], F32R)
    x2t = p.sb("b_x2t", [128, 8 * TTs], F32)
    fwt = p.sb("b_fwt", [128, 8 * TTs], F32R)
    bwt = None if summed_in else p.sb("b_bwt", [128, 8 * TTs], F32)
    xm = p.sb("b_xm", [128, 8 * TTs], F32R)
    hf = p.sb("b_hf", [128, 8 * TTs], BF16)
    h1 = p.sb("b_h1", [128, 8 * TTs], BF16)
    acc = p.sb("b_acc", [128, 8 * TTs], F32)
    x3t = p.sb("b_x3t", [128, 8 * TTs], F32)
    scr = p.sb("b_scr", [128, TTs], F32)
    eps5 = p.sb("b_eps5", [1, 1], F32)
    pp = [p.ps(f"b_pp{i}", [128, TTs]) for i in range(4)]
    S = LnScratch(p, TTs, tag="B", ps_pool=None)

    if sems is None:
        ld, gl, mm = p.sem("ld"), p.sem("gl"), p.sem("mm")
        dv, ac, pl, st = p.sem("dv"), p.sem("ac"), p.sem("pl"), p.sem("st")
    else:
        ld, gl, mm, dv, ac, pl, st = (sems["ld"], sems["gl"], sems["mm"], sems["dv"],
                                      sems["ac"], sems["pl"], sems["st"])
    stE = st.n
    W1C = kc_n * 128

    with nc.Block() as blk:
        @blk.sync
        def _(q):
            for t_, s_ in ((cstF, cstF_i), (ones128, ones128_i), (ones1, ones1_i)):
                q.dma_start(out=t_[:], in_=s_).then_inc(ld.h, 16); ld.n += 16

        @blk.gpsimd
        def _(g):
            for kc in range(8):
                g.dma_start(out=ff1w[:, kc * W1C:(kc + 1) * W1C],
                            in_=ff1_i[kc * 128:(kc + 1) * 128, :]).then_inc(gl.h, 16); gl.n += 16
            for kc in range(kc_n):
                g.dma_start(out=ff2w[:, kc * 1024:(kc + 1) * 1024],
                            in_=ff2_i[kc * 128:(kc + 1) * 128, :]).then_inc(gl.h, 16); gl.n += 16
            g.wait_ge(gl.h, gl.n)
            g.wait_ge(ld.h, ld.n)

        @blk.vector
        def _(v):
            v.memset(eps5[:], 1e-5)

    for ts in range(NSUP):
        tsl = slice(ts * TTs, (ts + 1) * TTs)
        with nc.Block() as blk:
            dv0, ac0, pl0 = dv.n, ac.n, pl.n
            ld0 = ld.n

            @blk.sync
            def _(q):
                for c in range(8):
                    q.dma_start(out=x2t[:, c * TTs:(c + 1) * TTs], in_=x2_i[c, :, tsl]).then_inc(ld.h, 16); ld.n += 16
                    if not summed_in:
                        q.dma_start(out=bwt[:, c * TTs:(c + 1) * TTs], in_=bw_i[c, :, tsl]).then_inc(ld.h, 16); ld.n += 16

            @blk.gpsimd
            def _(g):
                gl0_ = gl.n
                for c in range(8):
                    g.dma_start(out=fwt[:, c * TTs:(c + 1) * TTs], in_=fw_i[ts, c]).then_inc(gl.h, 16); gl.n += 16
                g.wait_ge(ld.h, ld0 + 16 * (16 if not summed_in else 8))
                g.wait_ge(gl.h, gl0_ + 8 * 16)
                if not summed_in:
                    for c in range(8):
                        pl.inc(g.tensor_add(out=acc[:, c * TTs:(c + 1) * TTs],
                                            in0=fwt[:, c * TTs:(c + 1) * TTs],
                                            in1=bwt[:, c * TTs:(c + 1) * TTs]))
                else:
                    for c in range(8):
                        pl.inc(g.tensor_copy(out=acc[:, c * TTs:(c + 1) * TTs],
                                             in_=fwt[:, c * TTs:(c + 1) * TTs]))

            @blk.vector
            def _(v):
                for c in range(8):
                    v.wait_ge(pl.h, pl0 + c + 1)
                    if c >= 1:
                        v.wait_ge(ac.h, ac0 + c)
                    dv.inc(v.tensor_add(out=scr[:], in0=acc[:, c * TTs:(c + 1) * TTs],
                                        in1=x2t[:, c * TTs:(c + 1) * TTs]))
                for c in range(8):
                    v.wait_ge(ac.h, ac0 + c + 1)
                    dv.inc(v.tensor_add(out=xm[:, c * TTs:(c + 1) * TTs],
                                        in0=x2t[:, c * TTs:(c + 1) * TTs],
                                        in1=x3t[:, c * TTs:(c + 1) * TTs]))

            @blk.scalar
            def _(a):
                for c in range(8):
                    a.wait_ge(dv.h, dv0 + c + 1)
                    ac.inc(a.activation(out=x3t[:, c * TTs:(c + 1) * TTs], in_=scr[:], func=AF.Gelu))
        emit_ln(nc, S, ones128, ones1,
                lambda c: xm[:, c * TTs:(c + 1) * TTs],
                lambda c: hf[:, c * TTs:(c + 1) * TTs], eps5[0:1, 0:1], (mm, dv, ac),
                gamma=lambda c: cstF[:, c:c + 1], beta=lambda c: cstF[:, 8 + c:9 + c])
        with nc.Block() as blk:
            mm0, dv0, ac0 = mm.n, dv.n, ac.n
            m_f1 = lambda kg, j: mm0 + kg * 16 + j + 1
            m_f2 = lambda kg, mc: mm0 + kg * 16 + 8 + mc + 1
            m_gel = lambda kg, j: ac0 + kg * 8 + j + 1
            m_aad = lambda kg, mc: dv0 + kg * 8 + mc + 1

            @blk.tensor
            def _(t):
                for kg in range(kg_n):
                    for j in range(8):
                        kc = kg * 8 + j
                        if kc >= 2:
                            t.wait_ge(ac.h, m_gel((kc - 2) // 8, (kc - 2) % 8))
                        for k8 in range(8):
                            last = t.matmul(out=pp[kc % 2][:],
                                            lhsT=ff1w[:, k8 * W1C + kc * 128:k8 * W1C + (kc + 1) * 128],
                                            rhs=hf[:, k8 * TTs:(k8 + 1) * TTs],
                                            start=(k8 == 0), stop=(k8 == 7))
                        mm.inc(last)
                    t.wait_ge(ac.h, m_gel(kg, 7))
                    for mc in range(8):
                        if kg >= 1 or mc >= 2:
                            pkg, pmc = (kg, mc - 2) if mc >= 2 else (kg - 1, mc + 6)
                            t.wait_ge(dv.h, m_aad(pkg, pmc))
                        for j in range(8):
                            last = t.matmul(out=pp[2 + (kg * 8 + mc) % 2][:],
                                            lhsT=ff2w[:, (kg * 8 + j) * 1024 + mc * 128:(kg * 8 + j) * 1024 + (mc + 1) * 128],
                                            rhs=h1[:, j * TTs:(j + 1) * TTs],
                                            start=(j == 0), stop=(j == 7))
                        mm.inc(last)

            @blk.scalar
            def _(a):
                for kg in range(kg_n):
                    for j in range(8):
                        kc = kg * 8 + j
                        a.wait_ge(mm.h, m_f1(kg, j))
                        if kc >= 8:
                            a.wait_ge(mm.h, m_f2(kc // 8 - 1, 7))
                        ac.inc(a.activation(out=h1[:, j * TTs:(j + 1) * TTs], in_=pp[kc % 2][:],
                                            func=AF.Gelu, bias=cstF[:, 16 + kc:17 + kc]))

            @blk.vector
            def _(v):
                for kg in range(kg_n):
                    for mc in range(8):
                        v.wait_ge(mm.h, m_f2(kg, mc))
                        if kg == 0:
                            dv.inc(v.tensor_copy(out=acc[:, mc * TTs:(mc + 1) * TTs],
                                                 in_=pp[2 + (kg * 8 + mc) % 2][:]))
                        else:
                            dv.inc(v.tensor_add(out=acc[:, mc * TTs:(mc + 1) * TTs],
                                                in0=acc[:, mc * TTs:(mc + 1) * TTs],
                                                in1=pp[2 + (kg * 8 + mc) % 2][:]))
        if partial_out:
            with nc.Block() as blk:
                @blk.scalar
                def _(a):
                    a.wait_ge(dv.h, dv.n)
                    for c in range(8):
                        a.dma_start(out=o_ffp[ts, c],
                                    in_=acc[:, c * TTs:(c + 1) * TTs]).then_inc(st.h, 16); st.n += 16
                        a.dma_start(out=o_xm[c, :, tsl],
                                    in_=xm[:, c * TTs:(c + 1) * TTs].bitcast(F32)).then_inc(st.h, 16); st.n += 16
                    a.wait_ge(st.h, st.n)
            if cc_cb is not None:
                with nc.Block() as blk:
                    @blk.gpsimd
                    def _(g):
                        cc_cb(g, ts)
            continue
        with nc.Block() as blk:
            @blk.vector
            def _(v):
                for c in range(8):
                    dv.inc(v.scalar_tensor_tensor(out=scr[:], in0=acc[:, c * TTs:(c + 1) * TTs],
                                                  scalar=cstF[:, 48 + c:49 + c],
                                                  in1=xm[:, c * TTs:(c + 1) * TTs],
                                                  op0=ALU.add, op1=ALU.add))
                    v.wait_ge(dv.h, dv.n)
                    dv.inc(v.tensor_add(out=fwt[:, c * TTs:(c + 1) * TTs], in0=scr[:],
                                        in1=x2t[:, c * TTs:(c + 1) * TTs]))
        emit_ln(nc, S, ones128, ones1,
                lambda c: fwt[:, c * TTs:(c + 1) * TTs],
                lambda c: x3t[:, c * TTs:(c + 1) * TTs], eps5[0:1, 0:1], (mm, dv, ac),
                store=lambda c: (o_x3[c, :, tsl], x3t[:, c * TTs:(c + 1) * TTs]), st=st)

    with nc.Block() as blk:
        @blk.gpsimd
        def _(g):
            g.wait_ge(st.h, st.n)
    if standalone:
        return p.close()
    return None


def host_l1b_inputs(inp, x2_full, l1a_res):
    ff1w = np.asarray(inp["L1_attn_ff1_w"], np.float32)
    ff2w = np.asarray(inp["L1_attn_ff2_w"], np.float32)
    ff1b = np.asarray(inp["L1_attn_ff1_b"], np.float32)
    ff2b = np.asarray(inp["L1_attn_ff2_b"], np.float32)
    fg = np.asarray(inp["L1_attn_ffn_g"], np.float32)
    fb = np.asarray(inp["L1_attn_ffn_b"], np.float32)
    cstF = np.zeros((128, 56), np.float32)
    cstF[:, 0:8] = fg.reshape(8, 128).T
    cstF[:, 8:16] = fb.reshape(8, 128).T
    cstF[:, 16:48] = ff1b.reshape(32, 128).T
    cstF[:, 48:56] = ff2b.reshape(8, 128).T
    maps = []
    for c in range(NCORE):
        bidx, hh = c // 2, c % 2
        tok = slice(hh * TL, (hh + 1) * TL)
        fwd = l1a_res[2 * bidx]["mo"]
        bwd = l1a_res[2 * bidx + 1]["mo"]
        maps.append({
            "x2h": np.ascontiguousarray(x2_full[bidx, tok].T.reshape(8, 128, TL)),
            "fwdh": np.ascontiguousarray(fwd[:, :, tok]),
            "bwdh": np.ascontiguousarray(bwd[:, :, tok]),
            "ff1wi": np.ascontiguousarray(ff1w.T),
            "ff2wi": np.ascontiguousarray(ff2w.T),
            "cstFi": cstF,
            "ones128": np.ones((128, 1), np.float32),
            "ones1": np.ones((128, 128), np.float32),
        })
    return maps



def build_f1(TLn=TL, TTs=256, p=None, sems=None, mc_n=32, partial_out=False, TH=None, cc_cb=None):
    if TH is None:
        TH = TLn // 4
    NTH = TLn // TH
    NTT = TH // TTs
    standalone = p is None
    if standalone:
        p = Prog()
    nc = p.nc
    x3_i = p.din("x3h", [8, 128, TLn])
    wr_i = p.din("wri", [1024, 4], F32R)
    w1_i = p.din("w1t", [mc_n, 4, 8, 128, 128])
    b1_i = p.din("b1i", [4, mc_n * 128], F32R)
    wo_i = p.din("woi", [mc_n * 128, 1024])
    cstM_i = p.din("cstMi", [128, 24])
    ones128_i = p.din("ones128", [128, 1], F32R)
    ones1_i = p.din("ones1", [128, 128], F32R)
    if partial_out:
        o_wop = p.dout("wop", [TLn // TH, 8, 128, TH])
        o_x4 = None
    else:
        o_x4 = p.dout("x4", [8, 128, TLn])

    wr = p.sb("f_wr", [128, 8 * 4], F32R)
    b1 = p.sb("f_b1", [4, mc_n * 128], BF16)
    w1b = p.sb("f_w1b", [128, 2 * 4096], BF16)
    wob = p.sb("f_wob", [128, 2 * 1024], BF16)
    cstM = p.sb("f_cst", [128, 24], F32)
    ones128 = p.sb("f_ones128", [128, 1], F32R)
    ones1 = p.sb("f_ones1", [128, 128], F32R)
    x3t = p.sb("f_x3t", [128, 8 * TH], F32R)
    xme = p.sb("f_xme", [128, 32 * TH], BF16)
    acc = p.sb("f_acc", [128, 8 * TH], F32)
    lg = p.sb("f_lg", [4, TTs], F32)
    le = [p.sb(f"f_le{e}", [1, TTs], F32) for e in range(4)]
    cm1 = p.sb("f_cm1", [1, TTs], F32)
    cm2 = p.sb("f_cm2", [1, TTs], F32)
    me = [p.sb(f"f_me{e}", [1, TTs], F32R) for e in range(4)]
    oh = p.sb("f_oh", [4, TH], BF16)
    mrep = p.sb("f_mrep", [128, 2 * TTs], F32)
    inter = p.sb("f_inter", [128, 2 * TTs], BF16)
    moe = p.sb("f_moe", [128, TTs], F32R)
    lnin = p.sb("f_lnin", [128, 8 * TTs], F32R)
    x4t = p.sb("f_x4t", [128, 8 * TTs], F32)
    eps12 = p.sb("f_eps12", [1, 1], F32)
    eps5 = p.sb("f_eps5", [1, 1], F32)
    pq = [p.ps(f"f_pq{i}", [128, TTs]) for i in range(4)]
    S = LnScratch(p, TTs, tag="F", ps_pool=None)

    if sems is None:
        ld, gl, mm = p.sem("ld"), p.sem("gl"), p.sem("mm")
        dv, ac, pl, st = p.sem("dv"), p.sem("ac"), p.sem("pl"), p.sem("st")
    else:
        ld, gl, mm, dv, ac, pl, st = (sems["ld"], sems["gl"], sems["mm"], sems["dv"],
                                      sems["ac"], sems["pl"], sems["st"])
    stE = st.n

    with nc.Block() as blk:
        @blk.sync
        def _(q):
            for kc in range(8):
                q.dma_start(out=wr[:, kc * 4:(kc + 1) * 4],
                            in_=wr_i[kc * 128:(kc + 1) * 128, :]).then_inc(ld.h, 16); ld.n += 16
            for t_, s_ in ((cstM, cstM_i), (ones128, ones128_i), (ones1, ones1_i)):
                q.dma_start(out=t_[:], in_=s_).then_inc(ld.h, 16); ld.n += 16

        @blk.vector
        def _(v):
            v.memset(eps12[:], 1e-12)
            v.memset(eps5[:], 1e-5)

        @blk.gpsimd
        def _(g):
            g.dma_start(out=b1[:], in_=b1_i).then_inc(gl.h, 16); gl.n += 16
            g.wait_ge(gl.h, gl.n)
            g.wait_ge(ld.h, ld.n)

    for th in range(NTH):
        thl = slice(th * TH, (th + 1) * TH)
        with nc.Block() as blk:
            gl0, mm0, dv0, pl0 = gl.n, mm.n, dv.n, pl.n
            DV_TT, MM_TT, PL_TT = 41, 5, 128
            m_evlg = lambda tt: dv0 + DV_TT * tt + 1
            m_me = lambda tt, e: dv0 + DV_TT * tt + 2 + e
            m_mrep = lambda tt, e: dv0 + DV_TT * tt + 6 + 9 * e
            m_xm = lambda tt, e, k8: dv0 + DV_TT * tt + 6 + 9 * e + 1 + k8
            m_rt = lambda tt: mm0 + MM_TT * tt + 1
            m_rep = lambda tt, e: mm0 + MM_TT * tt + 2 + e

            @blk.gpsimd
            def _(g):
                for c in range(8):
                    g.dma_start(out=x3t[:, c * TH:(c + 1) * TH],
                                in_=x3_i[c, :, thl]).then_inc(gl.h, 16); gl.n += 16
                g.wait_ge(gl.h, gl.n)
                for tt in range(NTT):
                    g.wait_ge(dv.h, m_evlg(tt))
                    for e in range(4):
                        g.dma_start(out=le[e][:], in_=lg[e:e + 1, :]).then_inc(pl.h, 16); pl.n += 16
                    g.wait_ge(dv.h, m_me(tt, 3))
                    for e in range(4):
                        g.dma_start(out=oh[e:e + 1, tt * TTs:(tt + 1) * TTs],
                                    in_=me[e][:]).then_inc(pl.h, 16); pl.n += 16

            @blk.tensor
            def _(t):
                t.wait_ge(gl.h, gl0 + 8 * 16)
                for tt in range(NTT):
                    if tt >= 1:
                        t.wait_ge(dv.h, m_evlg(tt - 1))
                    for kc in range(8):
                        last = t.matmul(out=pq[0][0:4, :], lhsT=wr[:, kc * 4:(kc + 1) * 4],
                                        rhs=x3t[:, kc * TH + tt * TTs:kc * TH + tt * TTs + TTs],
                                        start=(kc == 0), stop=(kc == 7))
                    mm.inc(last)
                    for e in range(4):
                        t.wait_ge(dv.h, m_me(tt, e))
                        gi = tt * 4 + e
                        if gi >= 2:
                            pt, pe_ = (gi - 2) // 4, (gi - 2) % 4
                            t.wait_ge(dv.h, m_mrep(pt, pe_))
                        mm.inc(t.matmul(out=pq[1 + e % 2][:], lhsT=ones1[0:1, :], rhs=me[e][:],
                                        start=True, stop=True))

            @blk.vector
            def _(v):
                for tt in range(NTT):
                    v.wait_ge(mm.h, m_rt(tt))
                    if tt >= 1:
                        v.wait_ge(pl.h, pl0 + PL_TT * (tt - 1) + 128)
                    dv.inc(v.tensor_copy(out=lg[:], in_=pq[0][0:4, :]))
                    v.wait_ge(pl.h, pl0 + PL_TT * tt + 64)
                    for e in range(4):
                        first = True
                        for o in range(4):
                            if o == e:
                                continue
                            op = ALU.is_ge if o > e else ALU.is_gt
                            if first:
                                v.tensor_tensor(out=cm1[:], in0=le[e][:], in1=le[o][:], op=op)
                            else:
                                v.tensor_tensor(out=cm2[:], in0=le[e][:], in1=le[o][:], op=op)
                                v.tensor_mul(out=cm1[:], in0=cm1[:], in1=cm2[:])
                            first = False
                        dv.inc(v.tensor_copy(out=me[e][:], in_=cm1[:]))
                    for e in range(4):
                        v.wait_ge(mm.h, m_rep(tt, e))
                        dv.inc(v.tensor_copy(out=mrep[:, (e % 2) * TTs:(e % 2 + 1) * TTs],
                                             in_=pq[1 + e % 2][:]))
                        for k8 in range(8):
                            dv.inc(v.tensor_mul(
                                out=xme[:, (e * 8 + k8) * TH + tt * TTs:(e * 8 + k8) * TH + tt * TTs + TTs],
                                in0=x3t[:, k8 * TH + tt * TTs:k8 * TH + tt * TTs + TTs],
                                in1=mrep[:, (e % 2) * TTs:(e % 2 + 1) * TTs]))
        with nc.Block() as blk:
            gl0, mm0, dv0, ac0 = gl.n, mm.n, dv.n, ac.n
            MMC = NTT * 9
            m_w1i = lambda Mc, tt: mm0 + Mc * MMC + tt + 1
            m_woi = lambda Mc, tt, mc: mm0 + Mc * MMC + NTT + tt * 8 + mc + 1
            m_gel = lambda Mc, tt: ac0 + Mc * NTT + tt + 1
            m_aad = lambda Mc, tt, mc: dv0 + Mc * NTT * 8 + tt * 8 + mc + 1

            @blk.gpsimd
            def _(g):
                for Mc in range(mc_n):
                    if Mc >= 2:
                        g.wait_ge(mm.h, m_w1i(Mc - 1, 0))
                    g.dma_start(out=w1b[:, (Mc % 2) * 4096:(Mc % 2 + 1) * 4096],
                                in_=w1_i[Mc].rearrange("e k p m -> p e k m")).then_inc(gl.h, 16); gl.n += 16
                    g.dma_start(out=wob[:, (Mc % 2) * 1024:(Mc % 2 + 1) * 1024],
                                in_=wo_i[Mc * 128:(Mc + 1) * 128, :]).then_inc(gl.h, 16); gl.n += 16

            @blk.tensor
            def _(t):
                for Mc in range(mc_n):
                    t.wait_ge(gl.h, gl0 + 32 * (Mc + 1))
                    for tt in range(NTT):
                        bank = (Mc * NTT + tt) % 2
                        if Mc * NTT + tt >= 2:
                            pk = Mc * NTT + tt - 2
                            t.wait_ge(ac.h, m_gel(pk // NTT, pk % NTT))
                        for e in range(4):
                            for k8 in range(8):
                                t.matmul(out=pq[bank][:],
                                         lhsT=w1b[:, (Mc % 2) * 4096 + e * 1024 + k8 * 128:(Mc % 2) * 4096 + e * 1024 + (k8 + 1) * 128],
                                         rhs=xme[:, (e * 8 + k8) * TH + tt * TTs:(e * 8 + k8) * TH + tt * TTs + TTs],
                                         start=(e == 0 and k8 == 0), stop=False)
                        last = t.matmul(out=pq[bank][:], lhsT=b1[:, Mc * 128:(Mc + 1) * 128],
                                        rhs=oh[:, tt * TTs:(tt + 1) * TTs], start=False, stop=True)
                        mm.inc(last)
                    for tt in range(NTT):
                        t.wait_ge(ac.h, m_gel(Mc, tt))
                        for mc in range(8):
                            i2 = tt * 8 + mc
                            if Mc * NTT * 8 + i2 >= 2:
                                pk = Mc * NTT * 8 + i2 - 2
                                t.wait_ge(dv.h, m_aad(pk // (NTT * 8), (pk % (NTT * 8)) // 8, pk % 8))
                            mm.inc(t.matmul(out=pq[2 + i2 % 2][:],
                                            lhsT=wob[:, (Mc % 2) * 1024 + mc * 128:(Mc % 2) * 1024 + (mc + 1) * 128],
                                            rhs=inter[:, (tt % 2) * TTs:(tt % 2 + 1) * TTs],
                                            start=True, stop=True))

            @blk.scalar
            def _(a):
                for Mc in range(mc_n):
                    for tt in range(NTT):
                        a.wait_ge(mm.h, m_w1i(Mc, tt))
                        if Mc * NTT + tt >= 2:
                            pk = Mc * NTT + tt - 2
                            a.wait_ge(mm.h, m_woi(pk // NTT, pk % NTT, 7))
                        ac.inc(a.activation(out=inter[:, (tt % 2) * TTs:(tt % 2 + 1) * TTs],
                                            in_=pq[(Mc * NTT + tt) % 2][:], func=AF.Gelu))

            @blk.vector
            def _(v):
                for Mc in range(mc_n):
                    for tt in range(NTT):
                        for mc in range(8):
                            i2 = tt * 8 + mc
                            v.wait_ge(mm.h, m_woi(Mc, tt, mc))
                            dst = acc[:, mc * TH + tt * TTs:mc * TH + tt * TTs + TTs]
                            if Mc == 0:
                                dv.inc(v.tensor_copy(out=dst, in_=pq[2 + i2 % 2][:]))
                            else:
                                dv.inc(v.tensor_add(out=dst, in0=dst, in1=pq[2 + i2 % 2][:]))
        if partial_out:
            with nc.Block() as blk:
                @blk.scalar
                def _(a):
                    a.wait_ge(dv.h, dv.n)
                    for c in range(8):
                        a.dma_start(out=o_wop[th, c],
                                    in_=acc[:, c * TH:(c + 1) * TH]).then_inc(st.h, 16); st.n += 16
                    a.wait_ge(st.h, st.n)
            if cc_cb is not None:
                with nc.Block() as blk:
                    @blk.gpsimd
                    def _(g):
                        cc_cb(g, th)
            continue
        for tt in range(NTT):
            ttl = slice(th * TH + tt * TTs, th * TH + (tt + 1) * TTs)
            with nc.Block() as blk:
                @blk.vector
                def _(v):
                    for c in range(8):
                        dv.inc(v.scalar_tensor_tensor(
                            out=lnin[:, c * TTs:(c + 1) * TTs],
                            in0=acc[:, c * TH + tt * TTs:c * TH + tt * TTs + TTs],
                            scalar=cstM[:, c:c + 1],
                            in1=x3t[:, c * TH + tt * TTs:c * TH + tt * TTs + TTs],
                            op0=ALU.add, op1=ALU.add))
            emit_ln(nc, S, ones128, ones1,
                    lambda c: lnin[:, c * TTs:(c + 1) * TTs],
                    lambda c: x4t[:, c * TTs:(c + 1) * TTs], eps12[0:1, 0:1], (mm, dv, ac),
                    gamma=lambda c: cstM[:, 8 + c:9 + c], beta=lambda c: cstM[:, 16 + c:17 + c])
            with nc.Block() as blk:
                @blk.vector
                def _(v):
                    for c in range(8):
                        dv.inc(v.tensor_add(out=lnin[:, c * TTs:(c + 1) * TTs],
                                            in0=x4t[:, c * TTs:(c + 1) * TTs].bitcast(F32R),
                                            in1=x3t[:, c * TH + tt * TTs:c * TH + tt * TTs + TTs]))
            emit_ln(nc, S, ones128, ones1,
                    lambda c: lnin[:, c * TTs:(c + 1) * TTs],
                    lambda c: x4t[:, c * TTs:(c + 1) * TTs], eps5[0:1, 0:1], (mm, dv, ac),
                    store=lambda c: (o_x4[c, :, ttl], x4t[:, c * TTs:(c + 1) * TTs]), st=st)

    with nc.Block() as blk:
        @blk.gpsimd
        def _(g):
            g.wait_ge(st.h, st.n)
    if standalone:
        return p.close()
    return None


def host_f1_inputs(inp, x3_res):
    Wr = np.asarray(inp["L1_ffn_W_router"], np.float32)
    W1 = np.asarray(inp["L1_ffn_W1"], np.float32)
    b1 = np.asarray(inp["L1_ffn_b1"], np.float32)
    Wo = np.asarray(inp["L1_ffn_Wo"], np.float32)
    bo = np.asarray(inp["L1_ffn_bo"], np.float32)
    lng = np.asarray(inp["L1_ffn_ln_gamma"], np.float32)
    lnb = np.asarray(inp["L1_ffn_ln_beta"], np.float32)
    w1t = np.ascontiguousarray(W1.reshape(4, 8, 128, 32, 128).transpose(3, 0, 1, 2, 4))
    cstM = np.zeros((128, 24), np.float32)
    cstM[:, 0:8] = bo.reshape(8, 128).T
    cstM[:, 8:16] = lng.reshape(8, 128).T
    cstM[:, 16:24] = lnb.reshape(8, 128).T
    maps = []
    for c in range(NCORE):
        maps.append({
            "x3h": x3_res[c]["x3"],
            "wri": np.ascontiguousarray(Wr),
            "w1t": w1t,
            "b1i": np.ascontiguousarray(b1),
            "woi": np.ascontiguousarray(Wo),
            "cstMi": cstM,
            "ones128": np.ones((128, 1), np.float32),
            "ones1": np.ones((128, 128), np.float32),
        })
    return maps



def build_f0b(p, sems, Tn=T, TTs=256):
    nc = p.nc
    x1_i = p.din("x1", [8, 128, Tn])
    cs_i = p.din("csum", [Tn // TTs, 8, 128, TTs])
    o_x2 = p.dout("x2", [8, 128, Tn])
    o_x2r = p.dout("x2r", [8, 128, Tn])
    ones128_i = p.din("ones128g", [128, 1], F32R)
    ones1_i = p.din("ones1g", [128, 128], F32R)
    ld, gl, mm, dv, ac, st = (sems["ld"], sems["gl"], sems["mm"], sems["dv"],
                              sems["ac"], sems["st"])
    x1t = p.sb("x1t", [128, 8 * TTs], F32)
    cst_ = p.sb("cstt", [128, 8 * TTs], F32)
    lnin = p.sb("lnin", [128, 8 * TTs], F32R)
    x2t = p.sb("x2t", [128, 8 * TTs], F32)
    x2rv = p.sb("x2rv", [128, TTs], F32)
    ones128 = p.sb("ones128", [128, 1], F32R)
    ones1 = p.sb("ones1", [128, 128], F32R)
    eps5 = p.sb("eps5", [1, 1], F32)
    pp = [p.ps(f"pp{i}", [128, TTs]) for i in range(4)]
    S = LnScratch(p, TTs, tag=p.pre, ps_pool=pp)
    with nc.Block() as blk:
        @blk.sync
        def _(q):
            q.dma_start(out=ones128[:], in_=ones128_i).then_inc(ld.h, 16); ld.n += 16
            q.dma_start(out=ones1[:], in_=ones1_i).then_inc(ld.h, 16); ld.n += 16

        @blk.vector
        def _(v):
            v.memset(eps5[:], 1e-5)
    NSUP = Tn // TTs
    for ts in range(NSUP):
        tsl = slice(ts * TTs, (ts + 1) * TTs)
        with nc.Block() as blk:
            ld0 = ld.n

            @blk.sync
            def _(q):
                for c in range(8):
                    q.dma_start(out=x1t[:, c * TTs:(c + 1) * TTs], in_=x1_i[c, :, tsl]).then_inc(ld.h, 16); ld.n += 16
                    q.dma_start(out=cst_[:, c * TTs:(c + 1) * TTs], in_=cs_i[ts, c]).then_inc(ld.h, 16); ld.n += 16

            @blk.vector
            def _(v):
                v.wait_ge(ld.h, ld0 + 16 * 16)
                for c in range(8):
                    dv.inc(v.tensor_add(out=lnin[:, c * TTs:(c + 1) * TTs],
                                        in0=x1t[:, c * TTs:(c + 1) * TTs],
                                        in1=cst_[:, c * TTs:(c + 1) * TTs]))
        emit_ln(nc, S, ones128, ones1,
                lambda c: lnin[:, c * TTs:(c + 1) * TTs],
                lambda c: x2t[:, c * TTs:(c + 1) * TTs], eps5[0:1, 0:1], (mm, dv, ac),
                store=lambda c: (o_x2[c, :, tsl], x2t[:, c * TTs:(c + 1) * TTs]), st=st)
        with nc.Block() as blk:
            dv0 = dv.n

            @blk.vector
            def _(v):
                for c in range(8):
                    if c >= 1:
                        v.wait_ge(st.h, st.n)
                    dv.inc(v.tensor_copy(
                        out=x2rv[:],
                        in_=bass.AP(x2t, c * TTs + TTs - 1, [[8 * TTs, 128], [-1, TTs]])))
                    v.wait_ge(dv.h, dv.n)
                    v.dma_start if False else None

            @blk.scalar
            def _(a):
                for c in range(8):
                    a.wait_ge(dv.h, dv0 + c + 1)
                    a.dma_start(out=o_x2r[c, :, Tn - (ts + 1) * TTs:Tn - ts * TTs],
                                in_=x2rv[:]).then_inc(st.h, 16); st.n += 16
                    a.wait_ge(st.h, st.n)
    with nc.Block() as blk:
        @blk.gpsimd
        def _(g):
            g.wait_ge(st.h, st.n)


def build_l1bb(p, sems, Tn=T, TTs=256):
    nc = p.nc
    xm_i = p.din("xm", [8, 128, Tn])
    ff_i = p.din("ffsum", [Tn // TTs, 8, 128, TTs])
    x2_i = p.din("x2j", [8, 128, Tn])
    cstD_i = p.din("cstD", [128, 8])
    ones128_i = p.din("ones128j", [128, 1], F32R)
    ones1_i = p.din("ones1j", [128, 128], F32R)
    o_x3 = p.dout("x3", [8, 128, Tn])
    ld, mm, dv, ac, st = sems["ld"], sems["mm"], sems["dv"], sems["ac"], sems["st"]
    xmt = p.sb("xmt", [128, 8 * TTs], F32)
    fft = p.sb("fft", [128, 8 * TTs], F32)
    x2t = p.sb("x2tj", [128, 8 * TTs], F32)
    scr = p.sb("scrj", [128, TTs], F32)
    lnin = p.sb("lninj", [128, 8 * TTs], F32R)
    x3t = p.sb("x3tj", [128, 8 * TTs], F32)
    cstD = p.sb("cstDs", [128, 8], F32)
    ones128 = p.sb("ones128s", [128, 1], F32R)
    ones1 = p.sb("ones1s", [128, 128], F32R)
    eps5 = p.sb("eps5j", [1, 1], F32)
    pp = [p.ps(f"ppj{i}", [128, TTs]) for i in range(4)]
    S = LnScratch(p, TTs, tag=p.pre, ps_pool=pp)
    with nc.Block() as blk:
        @blk.sync
        def _(q):
            for t_, s_ in ((cstD, cstD_i), (ones128, ones128_i), (ones1, ones1_i)):
                q.dma_start(out=t_[:], in_=s_).then_inc(ld.h, 16); ld.n += 16

        @blk.vector
        def _(v):
            v.memset(eps5[:], 1e-5)
    NSUP = Tn // TTs
    for ts in range(NSUP):
        tsl = slice(ts * TTs, (ts + 1) * TTs)
        with nc.Block() as blk:
            ld0 = ld.n

            @blk.sync
            def _(q):
                for c in range(8):
                    q.dma_start(out=xmt[:, c * TTs:(c + 1) * TTs], in_=xm_i[c, :, tsl]).then_inc(ld.h, 16); ld.n += 16
                    q.dma_start(out=fft[:, c * TTs:(c + 1) * TTs], in_=ff_i[ts, c]).then_inc(ld.h, 16); ld.n += 16
                    q.dma_start(out=x2t[:, c * TTs:(c + 1) * TTs], in_=x2_i[c, :, tsl]).then_inc(ld.h, 16); ld.n += 16

            @blk.vector
            def _(v):
                v.wait_ge(ld.h, ld0 + 24 * 16)
                for c in range(8):
                    dv.inc(v.scalar_tensor_tensor(out=scr[:], in0=fft[:, c * TTs:(c + 1) * TTs],
                                                  scalar=cstD[:, c:c + 1],
                                                  in1=xmt[:, c * TTs:(c + 1) * TTs],
                                                  op0=ALU.add, op1=ALU.add))
                    v.wait_ge(dv.h, dv.n)
                    dv.inc(v.tensor_add(out=lnin[:, c * TTs:(c + 1) * TTs], in0=scr[:],
                                        in1=x2t[:, c * TTs:(c + 1) * TTs]))
        emit_ln(nc, S, ones128, ones1,
                lambda c: lnin[:, c * TTs:(c + 1) * TTs],
                lambda c: x3t[:, c * TTs:(c + 1) * TTs], eps5[0:1, 0:1], (mm, dv, ac),
                store=lambda c: (o_x3[c, :, tsl], x3t[:, c * TTs:(c + 1) * TTs]), st=st)
    with nc.Block() as blk:
        @blk.gpsimd
        def _(g):
            g.wait_ge(st.h, st.n)


def build_f1b(p, sems, Tn=T, TTs=256):
    nc = p.nc
    wo_i = p.din("wosum", [Tn // TTs, 8, 128, TTs])
    x3_i = p.din("x3l", [8, 128, Tn])
    cstM_i = p.din("cstM2", [128, 24])
    ones128_i = p.din("ones128l", [128, 1], F32R)
    ones1_i = p.din("ones1l", [128, 128], F32R)
    o_x4 = p.dout("x4", [8, 128, Tn])
    ld, gl, mm, dv, ac, st = (sems["ld"], sems["gl"], sems["mm"], sems["dv"],
                              sems["ac"], sems["st"])
    wot = p.sb("wot", [128, 8 * TTs], F32)
    x3t = p.sb("x3tl", [128, 8 * TTs], F32R)
    lnin = p.sb("lninl", [128, 8 * TTs], F32R)
    moe = p.sb("moel", [128, 8 * TTs], F32R)
    x4t = p.sb("x4tl", [128, 8 * TTs], F32)
    cstM = p.sb("cstMs", [128, 24], F32)
    ones128 = p.sb("ones128t", [128, 1], F32R)
    ones1 = p.sb("ones1t", [128, 128], F32R)
    eps12 = p.sb("eps12l", [1, 1], F32)
    eps5 = p.sb("eps5l", [1, 1], F32)
    pp = [p.ps(f"ppl{i}", [128, TTs]) for i in range(4)]
    S = LnScratch(p, TTs, tag=p.pre, ps_pool=pp)
    with nc.Block() as blk:
        @blk.sync
        def _(q):
            for t_, s_ in ((cstM, cstM_i), (ones128, ones128_i), (ones1, ones1_i)):
                q.dma_start(out=t_[:], in_=s_).then_inc(ld.h, 16); ld.n += 16

        @blk.vector
        def _(v):
            v.memset(eps12[:], 1e-12)
            v.memset(eps5[:], 1e-5)
    NSUP = Tn // TTs
    for ts in range(NSUP):
        tsl = slice(ts * TTs, (ts + 1) * TTs)
        with nc.Block() as blk:
            ld0, gl0 = ld.n, gl.n

            @blk.sync
            def _(q):
                for c in range(8):
                    q.dma_start(out=wot[:, c * TTs:(c + 1) * TTs], in_=wo_i[ts, c]).then_inc(ld.h, 16); ld.n += 16

            @blk.gpsimd
            def _(g):
                for c in range(8):
                    g.dma_start(out=x3t[:, c * TTs:(c + 1) * TTs], in_=x3_i[c, :, tsl]).then_inc(gl.h, 16); gl.n += 16

            @blk.vector
            def _(v):
                v.wait_ge(ld.h, ld0 + 8 * 16)
                v.wait_ge(gl.h, gl0 + 8 * 16)
                for c in range(8):
                    dv.inc(v.scalar_tensor_tensor(out=lnin[:, c * TTs:(c + 1) * TTs],
                                                  in0=wot[:, c * TTs:(c + 1) * TTs],
                                                  scalar=cstM[:, c:c + 1],
                                                  in1=x3t[:, c * TTs:(c + 1) * TTs],
                                                  op0=ALU.add, op1=ALU.add))
        emit_ln(nc, S, ones128, ones1,
                lambda c: lnin[:, c * TTs:(c + 1) * TTs],
                lambda c: moe[:, c * TTs:(c + 1) * TTs], eps12[0:1, 0:1], (mm, dv, ac),
                gamma=lambda c: cstM[:, 8 + c:9 + c], beta=lambda c: cstM[:, 16 + c:17 + c])
        with nc.Block() as blk:
            @blk.vector
            def _(v):
                for c in range(8):
                    dv.inc(v.tensor_add(out=lnin[:, c * TTs:(c + 1) * TTs],
                                        in0=moe[:, c * TTs:(c + 1) * TTs],
                                        in1=x3t[:, c * TTs:(c + 1) * TTs]))
        emit_ln(nc, S, ones128, ones1,
                lambda c: lnin[:, c * TTs:(c + 1) * TTs],
                lambda c: x4t[:, c * TTs:(c + 1) * TTs], eps5[0:1, 0:1], (mm, dv, ac),
                store=lambda c: (o_x4[c, :, tsl], x4t[:, c * TTs:(c + 1) * TTs]), st=st)
    with nc.Block() as blk:
        @blk.gpsimd
        def _(g):
            g.wait_ge(st.h, st.n)


PAIRS = [[0, 1], [2, 3], [4, 5], [6, 7]]


def build_fused():
    p = Prog()
    nc = p.nc
    sems = {k: p.sem(k) for k in ("ld", "gl", "mm", "dv", "ac", "pl", "st")}
    gx = nc.dram_tensor("x", [8, 128, T], F32, kind="ExternalInput").ap()
    gxr = nc.dram_tensor("xr", [8, 128, T], F32R, kind="ExternalInput").ap()
    out_x4 = nc.dram_tensor("x4", [8, 128, T], F32, kind="ExternalOutput").ap()
    dram = lambda n, sh, dt=F32: nc.dram_tensor(n, list(sh), dt, kind="Internal").ap()
    g_xa = dram("g_xa", [8, 128, T], F32R)
    g_z = dram("g_z", [8, 128, T])
    g_dtp = dram("g_dtp", [128, T]); g_bp = dram("g_bp", [16, T]); g_cp = dram("g_cp", [16, T])
    g_dts = dram("g_dts", [128, T]); g_bs = dram("g_bs", [16, T]); g_cs = dram("g_cs", [16, T])
    g_op = dram("g_op", [8, 8, 128, 512]); g_os = dram("g_os", [8, 8, 128, 512])
    g_cpart = dram("g_cpart", [8, 8, 128, 512]); g_csum = dram("g_csum", [8, 8, 128, 512])
    g_x1 = dram("g_x1", [8, 128, T])
    g_x2 = dram("g_x2", [8, 128, T]); g_x2r = dram("g_x2r", [8, 128, T])
    g_mo = dram("g_mo", [8, 8, 128, 512]); g_mos = dram("g_mos", [8, 8, 128, 512])
    g_ffp = dram("g_ffp", [8, 8, 128, 512]); g_ffs = dram("g_ffs", [8, 8, 128, 512])
    g_xm = dram("g_xm", [8, 128, T])
    g_x3 = dram("g_x3", [8, 128, T])
    g_wop = dram("g_wop", [8, 8, 128, 512]); g_wos = dram("g_wos", [8, 8, 128, 512])

    cc_k = [0]

    def collective(items):
        with nc.Block() as blk:
            @blk.gpsimd
            def _(g):
                for s_, d_ in items:
                    cs = p.sem(f"cc{cc_k[0]}"); cc_k[0] += 1
                    g.collective_compute("AllReduce", ALU.add, replica_groups=PAIRS,
                                         ins=[s_], outs=[d_]).then_inc(cs.h)
                    g.wait_ge(cs.h, 1)

    def mk_cc(items, stc, base, per, TTc=512):
        css = []

        def cb(g, ts):
            if ts % 2 == 0:
                return
            k2 = ts // 2
            g.wait_ge(stc.h, base + per * (ts + 1))
            for s_, d_ in items:
                cs = p.sem(f"cc{cc_k[0]}"); cc_k[0] += 1
                g.collective_compute("AllReduce", ALU.add, replica_groups=PAIRS,
                                     ins=[s_[2 * k2:2 * k2 + 2]],
                                     outs=[d_[2 * k2:2 * k2 + 2]]).then_inc(cs.h)
                css.append(cs)

        def fin():
            with nc.Block() as blk:
                @blk.gpsimd
                def _(g):
                    for cs in css:
                        g.wait_ge(cs.h, 1)
        return cb, fin

    scopes = ExitStack()
    scopes.enter_context(nc.named_scope("s_l0a"))
    p.begin_stage("a_")
    p.io = {"x": gxr, "xa": g_xa, "z": g_z, "dtp": g_dtp, "bp": g_bp, "cp": g_cp}
    build_l0a(p=p, sems=sems)
    p.end_stage(); p.io = {}
    scopes.close(); scopes = ExitStack(); scopes.enter_context(nc.named_scope("c_1"))
    collective([(g_dtp, g_dts), (g_bp, g_bs), (g_cp, g_cs)])
    scopes.close(); scopes = ExitStack(); scopes.enter_context(nc.named_scope("s_l0b"))

    p.begin_stage("b_")
    p.io = {"xa": g_xa, "z": g_z, "dts": g_dts, "bs": g_bs, "cs": g_cs, "opart": g_op}
    cb2, fin2 = mk_cc([(g_op, g_os)], sems["st"], sems["st"].n, 128)
    build_l0b(p=p, sems=sems, summed=True, cc_cb=cb2)
    p.end_stage(); p.io = {}
    scopes.close(); scopes = ExitStack(); scopes.enter_context(nc.named_scope("c_2"))
    fin2()
    scopes.close(); scopes = ExitStack(); scopes.enter_context(nc.named_scope("s_f0a"))

    p.begin_stage("f_")
    p.io = {"osum": g_os, "x": gx, "cpart": g_cpart, "x1o": g_x1}
    cb3, fin3 = mk_cc([(g_cpart, g_csum)], sems["st"], sems["st"].n, 256)
    build_f0(TLn=T, TTs=512, p=p, sems=sems, heads=2, summed_in=True, partial_out=True,
             cc_cb=cb3)
    p.end_stage(); p.io = {}
    scopes.close(); scopes = ExitStack(); scopes.enter_context(nc.named_scope("c_3"))
    fin3()
    scopes.close(); scopes = ExitStack(); scopes.enter_context(nc.named_scope("s_f0b"))

    p.begin_stage("g_")
    p.io = {"x1": g_x1, "csum": g_csum, "x2": g_x2, "x2r": g_x2r}
    build_f0b(p, sems, TTs=512)
    p.end_stage(); p.io = {}

    scopes.close(); scopes = ExitStack(); scopes.enter_context(nc.named_scope("s_l1a"))
    p.begin_stage("h_")
    p.io = {"x2": g_x2, "x2r": g_x2r, "mo": g_mo}
    cb4, fin4 = mk_cc([(g_mo, g_mos)], sems["st"], sems["st"].n, 160)
    build_l1a(TTs=512, p=p, sems=sems, sel_in=True, cc_cb=cb4)
    p.end_stage(); p.io = {}
    scopes.close(); scopes = ExitStack(); scopes.enter_context(nc.named_scope("c_4"))
    fin4()
    scopes.close(); scopes = ExitStack(); scopes.enter_context(nc.named_scope("s_l1ba"))

    p.begin_stage("i_")
    p.io = {"x2h": g_x2, "mosum": g_mos, "ffp": g_ffp, "xmo": g_xm}
    cb5, fin5 = mk_cc([(g_ffp, g_ffs)], sems["st"], sems["st"].n, 256)
    build_l1b(TLn=T, TTs=512, p=p, sems=sems, summed_in=True, kc_n=16, partial_out=True,
              cc_cb=cb5)
    p.end_stage(); p.io = {}
    scopes.close(); scopes = ExitStack(); scopes.enter_context(nc.named_scope("c_5"))
    fin5()
    scopes.close(); scopes = ExitStack(); scopes.enter_context(nc.named_scope("s_l1bb_f1a"))

    p.begin_stage("j_")
    p.io = {"xm": g_xm, "ffsum": g_ffs, "x2j": g_x2, "x3": g_x3}
    build_l1bb(p, sems, TTs=512)
    p.end_stage(); p.io = {}

    p.begin_stage("k_")
    p.io = {"x3h": g_x3, "wop": g_wop}
    cb6, fin6 = mk_cc([(g_wop, g_wos)], sems["st"], sems["st"].n, 128)
    build_f1(TLn=T, TTs=512, p=p, sems=sems, mc_n=16, partial_out=True, TH=512,
             cc_cb=cb6)
    p.end_stage(); p.io = {}
    scopes.close(); scopes = ExitStack(); scopes.enter_context(nc.named_scope("c_6"))
    fin6()
    scopes.close(); scopes = ExitStack(); scopes.enter_context(nc.named_scope("s_f1b"))

    p.begin_stage("l_")
    p.io = {"wosum": g_wos, "x3l": g_x3, "x4": out_x4}
    build_f1b(p, sems, TTs=512)
    p.end_stage(); p.io = {}
    scopes.close()
    return p.close()


def host_fused_inputs(inp):
    x = np.asarray(inp["L0_attn_x"], np.float32)
    m_a = host_l0a_inputs(inp)
    dummy = [{k: None for k in ()} for _ in range(NCORE)]
    W_dt = np.asarray(inp["L0_attn_W_dt"], np.float32)
    W_out = np.asarray(inp["L0_attn_W_out"], np.float32)
    A_log = np.asarray(inp["L0_attn_A_log"], np.float32)
    b_dt = np.asarray(inp["L0_attn_b_dt"], np.float32)
    D_ssm = np.asarray(inp["L0_attn_D_ssm"], np.float32)
    b_xp = np.asarray(inp["L0_attn_b_xp"], np.float32)
    b_out = np.asarray(inp["L0_attn_b_out"], np.float32)
    rm = np.asarray(inp["L0_ffn_rail_maker"], np.float32)
    rup = np.asarray(inp["L0_ffn_rail_up_proj"], np.float32)
    wc = np.asarray(inp["L0_ffn_w_cproj"], np.float32)
    wct = np.ascontiguousarray(wc.T)
    ff1w = np.asarray(inp["L1_attn_ff1_w"], np.float32)
    ff2w = np.asarray(inp["L1_attn_ff2_w"], np.float32)
    ff1b = np.asarray(inp["L1_attn_ff1_b"], np.float32)
    ff2b = np.asarray(inp["L1_attn_ff2_b"], np.float32)
    fg = np.asarray(inp["L1_attn_ffn_g"], np.float32)
    fb = np.asarray(inp["L1_attn_ffn_b"], np.float32)
    Wr = np.asarray(inp["L1_ffn_W_router"], np.float32)
    W1 = np.asarray(inp["L1_ffn_W1"], np.float32)
    b1v = np.asarray(inp["L1_ffn_b1"], np.float32)
    Wo = np.asarray(inp["L1_ffn_Wo"], np.float32)
    bo = np.asarray(inp["L1_ffn_bo"], np.float32)
    lng = np.asarray(inp["L1_ffn_ln_gamma"], np.float32)
    lnb = np.asarray(inp["L1_ffn_ln_beta"], np.float32)
    w1t_full = np.ascontiguousarray(W1.reshape(4, 8, 128, 32, 128).transpose(3, 0, 1, 2, 4))
    m_l1a = host_l1a_inputs(inp, np.zeros((4, T, D), np.float32))
    ones128 = np.ones((128, 1), np.float32)
    ones1 = np.ones((128, 128), np.float32)
    cstM = np.zeros((128, 24), np.float32)
    cstM[:, 0:8] = bo.reshape(8, 128).T
    cstM[:, 8:16] = lng.reshape(8, 128).T
    cstM[:, 16:24] = lnb.reshape(8, 128).T
    maps = []
    for c in range(NCORE):
        bidx, hh = c // 2, c % 2
        dsl = slice(hh * 1024, (hh + 1) * 1024)
        xfm = np.ascontiguousarray(x[bidx].T.reshape(8, 128, T))
        m = {"x": xfm, "xr": xfm}
        for k, v in m_a[c].items():
            if k != "x":
                m["a_" + k] = v
        cst2 = np.zeros((128, 17), np.float32)
        bdt8 = b_dt[dsl].reshape(8, 128)
        d8 = D_ssm[dsl].reshape(8, 128)
        for ch in range(8):
            cst2[:, 2 * ch] = bdt8[ch]
            cst2[:, 2 * ch + 1] = d8[ch]
        cst2[:, 16] = b_xp[0:128]
        m.update({
            "b_wdti": np.ascontiguousarray(W_dt[:, dsl]),
            "b_wouti": np.ascontiguousarray(W_out[dsl, :]),
            "b_alogi": np.ascontiguousarray(A_log[dsl].reshape(8, 128, 16).transpose(1, 0, 2).reshape(128, 128)),
            "b_cst2i": cst2,
            "b_bxpbr": np.tile(b_xp[128:144], (128, 1)),
            "b_bxpcr": np.tile(b_xp[144:160], (128, 1)),
            "b_id128": np.eye(128, dtype=np.float32),
        })
        rmbd = np.zeros((16, 128, 128), np.float32)
        for hloc in range(2):
            hgl = 2 * hh + hloc
            for cc in range(8):
                for j in range(8):
                    r = cc * 8 + j
                    rmbd[hloc * 8 + cc, 16 * j:16 * j + 16, 16 * j:16 * j + 16] = rm[hgl, r]
        rupbd = np.zeros((128, 32 * 128), np.float32)
        for hloc in range(2):
            hgl = 2 * hh + hloc
            for pp_ in range(32):
                cc, pj = pp_ // 4, pp_ % 4
                q = (hloc * 8 + cc) * 2 + (pj % 2)
                ro = (pj // 2) * 64
                sub = (pj % 2) * 32
                blkw = np.zeros((64, 128), np.float32)
                for rr in range(2):
                    rail = 2 * pp_ + rr
                    blkw[sub + 16 * rr:sub + 16 * rr + 16, 64 * rr:64 * rr + 64] = rup[hgl, rail].T
                rupbd[ro:ro + 64, q * 128:(q + 1) * 128] = blkw
        m.update({
            "f_bouti": np.ascontiguousarray(b_out.reshape(8, 128).T),
            "f_rmbd": rmbd,
            "f_rupbd": rupbd,
            "f_wci": np.ascontiguousarray(wct[hh * 8192:(hh + 1) * 8192, :]),
            "f_ones128": ones128, "f_ones1": ones1,
            "g_ones128g": ones128, "g_ones1g": ones1,
        })
        for k, v in m_l1a[c].items():
            if k != "x2f":
                m["h_" + k] = v
        rsel = np.zeros((128, 2), np.float32)
        rsel[:, hh] = 1.0
        m["h_revsel"] = rsel
        m["h_id128L"] = np.eye(128, dtype=np.float32)
        cstF = np.zeros((128, 56), np.float32)
        cstF[:, 0:8] = fg.reshape(8, 128).T
        cstF[:, 8:16] = fb.reshape(8, 128).T
        cstF[:, 16:32] = ff1b[hh * 2048:(hh + 1) * 2048].reshape(16, 128).T
        cstF[:, 48:56] = ff2b.reshape(8, 128).T
        m.update({
            "i_ff1wi": np.ascontiguousarray(ff1w.T[:, hh * 2048:(hh + 1) * 2048]),
            "i_ff2wi": np.ascontiguousarray(ff2w.T[hh * 2048:(hh + 1) * 2048, :]),
            "i_cstFi": cstF,
            "i_ones128": ones128, "i_ones1": ones1,
            "j_cstD": np.ascontiguousarray(ff2b.reshape(8, 128).T),
            "j_ones128j": ones128, "j_ones1j": ones1,
        })
        m.update({
            "k_wri": np.ascontiguousarray(Wr),
            "k_w1t": np.ascontiguousarray(w1t_full[hh * 16:(hh + 1) * 16]),
            "k_b1i": np.ascontiguousarray(b1v[:, hh * 2048:(hh + 1) * 2048]),
            "k_woi": np.ascontiguousarray(Wo[hh * 2048:(hh + 1) * 2048, :]),
            "k_cstMi": cstM,
            "k_ones128": ones128, "k_ones1": ones1,
            "l_cstM2": cstM,
            "l_ones128l": ones128, "l_ones1l": ones1,
        })
        maps.append(m)
    return maps


def kernel_fused(**inp):
    inp = {k: np.asarray(v) for k, v in inp.items()}
    nc = build_fused()
    maps = host_fused_inputs(inp)
    res = run_bass_kernel_spmd(nc, maps, core_ids=list(range(NCORE)))
    out = np.empty((4, T, D), np.float32)
    for b in range(4):
        out[b] = res.results[2 * b]["x4"].reshape(D, T).T
    return out



def _fm(a):
    Fdim = a.shape[-1]
    return np.ascontiguousarray(a.T.reshape(Fdim // 128, 128, a.shape[0]))


def host_l0a_inputs(inp):
    x = np.asarray(inp["L0_attn_x"], np.float32)
    W_in = np.asarray(inp["L0_attn_W_in"], np.float32)
    b_in = np.asarray(inp["L0_attn_b_in"], np.float32)
    conv_k = np.asarray(inp["L0_attn_conv_k"], np.float32)
    conv_b = np.asarray(inp["L0_attn_conv_b"], np.float32)
    W_xp = np.asarray(inp["L0_attn_W_xp"], np.float32)
    maps = []
    for c in range(NCORE):
        bidx, hh = c // 2, c % 2
        dsl = slice(hh * 1024, (hh + 1) * 1024)
        consts = np.zeros((8, 128, 7), np.float32)
        consts[:, :, 0] = b_in[dsl].reshape(8, 128)
        consts[:, :, 1] = b_in[2048 + hh * 1024:2048 + (hh + 1) * 1024].reshape(8, 128)
        consts[:, :, 2] = conv_b[dsl].reshape(8, 128)
        for k in range(4):
            consts[:, :, 3 + k] = conv_k[k, 0, dsl].reshape(8, 128)
        maps.append({
            "x": _fm(x[bidx]),
            "w_in_x": np.ascontiguousarray(W_in[:, dsl]),
            "w_in_z": np.ascontiguousarray(W_in[:, 2048 + hh * 1024:2048 + (hh + 1) * 1024]),
            "wxp_dt": np.ascontiguousarray(W_xp[dsl, 0:128]),
            "wxp_b": np.ascontiguousarray(W_xp[dsl, 128:144]),
            "wxp_c": np.ascontiguousarray(W_xp[dsl, 144:160]),
            "consts": np.ascontiguousarray(consts.transpose(1, 0, 2).reshape(128, 56)),
        })
    return maps


def host_l0b_inputs(inp, l0a_res):
    W_dt = np.asarray(inp["L0_attn_W_dt"], np.float32)
    W_out = np.asarray(inp["L0_attn_W_out"], np.float32)
    A_log = np.asarray(inp["L0_attn_A_log"], np.float32)
    b_dt = np.asarray(inp["L0_attn_b_dt"], np.float32)
    D_ssm = np.asarray(inp["L0_attn_D_ssm"], np.float32)
    b_xp = np.asarray(inp["L0_attn_b_xp"], np.float32)
    maps = []
    for c in range(NCORE):
        hh = c % 2
        pc = c - hh + (1 - hh)
        dsl = slice(hh * 1024, (hh + 1) * 1024)
        cst2 = np.zeros((128, 17), np.float32)
        bdt8 = b_dt[dsl].reshape(8, 128)
        d8 = D_ssm[dsl].reshape(8, 128)
        for ch in range(8):
            cst2[:, 2 * ch] = bdt8[ch]
            cst2[:, 2 * ch + 1] = d8[ch]
        cst2[:, 16] = b_xp[0:128]
        maps.append({
            "xa": l0a_res[c]["xa"],
            "z": l0a_res[c]["z"],
            "dtps": l0a_res[c]["dtp"],
            "dtpp": l0a_res[pc]["dtp"],
            "bps": l0a_res[c]["bp"],
            "bpp": l0a_res[pc]["bp"],
            "cps": l0a_res[c]["cp"],
            "cpp": l0a_res[pc]["cp"],
            "wdti": np.ascontiguousarray(W_dt[:, dsl]),
            "wouti": np.ascontiguousarray(W_out[dsl, :]),
            "alogi": np.ascontiguousarray(A_log[dsl].reshape(8, 128, 16).transpose(1, 0, 2).reshape(128, 128)),
            "cst2i": cst2,
            "bxpbr": np.tile(b_xp[128:144], (128, 1)),
            "bxpcr": np.tile(b_xp[144:160], (128, 1)),
            "id128": np.eye(128, dtype=np.float32),
        })
    return maps


def kernel(**inp):
    return kernel_fused(**inp)


def kernel_unfused(**inp):
    inp = {k: np.asarray(v) for k, v in inp.items()}
    core_ids = list(range(NCORE))

    res_a = run_bass_kernel_spmd(build_l0a(), host_l0a_inputs(inp), core_ids=core_ids)
    l0a_res = [dict(r) for r in res_a.results]

    res_b = run_bass_kernel_spmd(build_l0b(), host_l0b_inputs(inp, l0a_res), core_ids=core_ids)
    l0b_res = [dict(r) for r in res_b.results]

    res_f0 = run_bass_kernel_spmd(build_f0(), host_f0_inputs(inp, l0b_res), core_ids=core_ids)
    x2_full = np.empty((4, T, D), np.float32)
    for c in range(NCORE):
        bidx, hh = c // 2, c % 2
        x2_full[bidx, hh * TL:(hh + 1) * TL] = res_f0.results[c]["x2"].reshape(D, TL).T

    res_l1a = run_bass_kernel_spmd(build_l1a(), host_l1a_inputs(inp, x2_full), core_ids=core_ids)
    l1a_res = [dict(r) for r in res_l1a.results]

    res_l1b = run_bass_kernel_spmd(build_l1b(), host_l1b_inputs(inp, x2_full, l1a_res), core_ids=core_ids)
    x3_res = [dict(r) for r in res_l1b.results]

    res_f1 = run_bass_kernel_spmd(build_f1(), host_f1_inputs(inp, x3_res), core_ids=core_ids)
    out = np.empty((4, T, D), np.float32)
    for c in range(NCORE):
        bidx, hh = c // 2, c % 2
        out[bidx, hh * TL:(hh + 1) * TL] = res_f1.results[c]["x4"].reshape(D, TL).T
    return out


# version 13
# speedup vs baseline: 1.0215x; 1.0094x over previous
import numpy as np
from contextlib import ExitStack
import concourse.bass as bass
import concourse.mybir as mybir
from concourse.bass_utils import run_bass_kernel_spmd

F32 = mybir.dt.float32
F32R = mybir.dt.float32r
BF16 = mybir.dt.bfloat16
AF = mybir.ActivationFunctionType
ALU = mybir.AluOpType

NCORE = 8
T = 4096
TL = 2048
D = 1024
DI = 2048
NS = 16


class Ctr:
    def __init__(self, h):
        self.h = h
        self.n = 0
    def inc(self, instr, n=1):
        instr.then_inc(self.h, n)
        self.n += n
        return self.n


DETECT_RACES = True


class Prog:
    def __init__(self):
        self.nc = bass.Bass("TRN2", target_bir_lowering=False,
                            detect_race_conditions=DETECT_RACES)
        self.es = ExitStack()
        self.stack = [self.es]
        self.pre = ""
        self.io = {}
    def begin_stage(self, pre):
        self.stack.append(ExitStack())
        self.pre = pre
    def end_stage(self):
        self.stack.pop().close()
        self.pre = ""
    def sb(self, name, shape, dt):
        return self.stack[-1].enter_context(self.nc.sbuf_tensor(self.pre + name, shape, dt))
    def ps(self, name, shape, dt=F32):
        return self.stack[-1].enter_context(self.nc.psum_tensor(self.pre + name, shape, dt))
    def sem(self, name):
        return Ctr(self.es.enter_context(self.nc.semaphore(self.pre + name)))
    def din(self, name, shape, dt=F32):
        if name in self.io:
            return self.io[name]
        return self.nc.dram_tensor(self.pre + name, list(shape), dt, kind="ExternalInput").ap()
    def dout(self, name, shape, dt=F32):
        if name in self.io:
            return self.io[name]
        return self.nc.dram_tensor(self.pre + name, list(shape), dt, kind="ExternalOutput").ap()
    def close(self):
        self.es.close()
        return self.nc


def fr(ap):
    return ap.bitcast(F32R)



def build_l0a(Tn=T, TT=512, p=None, sems=None):
    NT = Tn // TT
    standalone = p is None
    if standalone:
        p = Prog()
    nc = p.nc
    x = p.din("x", [8, 128, Tn], F32R)
    w_in_x = p.din("w_in_x", [1024, 1024], F32R)
    w_in_z = p.din("w_in_z", [1024, 1024], F32R)
    wxp_dt = p.din("wxp_dt", [1024, 128], F32R)
    wxp_b = p.din("wxp_b", [1024, 16], F32R)
    wxp_c = p.din("wxp_c", [1024, 16], F32R)
    consts = p.din("consts", [128, 8 * 7])
    o_xa = p.dout("xa", [8, 128, Tn], F32R)
    o_z = p.dout("z", [8, 128, Tn])
    o_dtp = p.dout("dtp", [128, Tn])
    o_bp = p.dout("bp", [16, Tn])
    o_cp = p.dout("cp", [16, Tn])

    wix = p.sb("wix", [128, 8 * 1024], F32R)
    wiz = p.sb("wiz", [128, 8 * 1024], F32R)
    wdt = p.sb("wdt", [128, 8 * 128], F32R)
    wbb = p.sb("wbb", [128, 8 * 16], F32R)
    wcc = p.sb("wcc", [128, 8 * 16], F32R)
    cst = p.sb("cst", [128, 8 * 7], F32)
    xb = p.sb("xb", [128, 2 * 8 * TT], F32R)
    xpe = [p.sb(f"xpe{c}", [128, TT + 3], F32) for c in range(8)]
    convo = p.sb("convo", [128, 8 * TT], F32)
    xaa = p.sb("xaa", [128, 8 * TT], F32R)
    zz = p.sb("zz", [128, 8 * TT], F32)
    dtv = p.sb("dtv", [128, TT], F32)
    bv = p.sb("bv", [16, TT], F32)
    cv = p.sb("cv", [16, TT], F32)
    px = [p.ps(f"px{i}", [128, TT]) for i in range(2)]
    pz = [p.ps(f"pz{i}", [128, TT]) for i in range(2)]
    pdt = p.ps("pdt", [128, TT])
    pbc = p.ps("pbc", [16, TT])
    pcc = p.ps("pcc", [16, TT])

    if sems is None:
        ld, mm, dv, ac, st = (p.sem("ld"), p.sem("mm"), p.sem("dv"), p.sem("ac"), p.sem("st"))
    else:
        ld, mm, dv, ac, st = sems["ld"], sems["mm"], sems["dv"], sems["ac"], sems["st"]
    ldE, mmE, dvE, acE, stE = ld.n, mm.n, dv.n, ac.n, st.n

    with nc.Block() as blk:
        @blk.sync
        def _(q):
            for kc in range(8):
                q.dma_start(out=wix[:, kc * 1024:(kc + 1) * 1024],
                            in_=w_in_x[kc * 128:(kc + 1) * 128, :]).then_inc(ld.h, 16); ld.n += 16
                q.dma_start(out=wiz[:, kc * 1024:(kc + 1) * 1024],
                            in_=w_in_z[kc * 128:(kc + 1) * 128, :]).then_inc(ld.h, 16); ld.n += 16
                q.dma_start(out=wdt[:, kc * 128:(kc + 1) * 128],
                            in_=wxp_dt[kc * 128:(kc + 1) * 128, :]).then_inc(ld.h, 16); ld.n += 16
                q.dma_start(out=wbb[:, kc * 16:(kc + 1) * 16],
                            in_=wxp_b[kc * 128:(kc + 1) * 128, :]).then_inc(ld.h, 16); ld.n += 16
                q.dma_start(out=wcc[:, kc * 16:(kc + 1) * 16],
                            in_=wxp_c[kc * 128:(kc + 1) * 128, :]).then_inc(ld.h, 16); ld.n += 16
            q.dma_start(out=cst[:], in_=consts).then_inc(ld.h, 16); ld.n += 16

        @blk.vector
        def _(v):
            for c in range(8):
                v.memset(xpe[c][:, 0:3], 0.0)

    wload = ld.n

    def xcol(c, j):
        return cst[:, c * 7 + j:c * 7 + j + 1]

    MM_PER, DV_PER, AC_PER = 19, 17, 8

    for tt in range(NT):
        mm0, dv0, ac0 = mmE + MM_PER * tt, dvE + DV_PER * tt, acE + AC_PER * tt
        mark_mm_x = lambda mc: mm0 + 2 * mc + 1
        mark_mm_z = lambda mc: mm0 + 2 * mc + 2
        mark_mm_dt = mm0 + 17
        mark_mm_b = mm0 + 18
        mark_mm_c = mm0 + 19
        mark_dv_evac = lambda mc: dv0 + 2 * mc + 1
        mark_dv_conv = lambda mc: dv0 + 2 * mc + 2
        mark_dv_dtbc = dv0 + 17
        mark_ac = lambda mc: ac0 + mc + 1
        b = tt % 2

        with nc.Block() as blk:
            @blk.sync
            def _(q):
                tiles = [0, 1] if (tt == 0 and NT > 1) else ([0] if tt == 0 else ([tt + 1] if tt + 1 < NT else []))
                for ti in tiles:
                    bb = ti % 2
                    for c in range(8):
                        q.dma_start(out=xb[:, (bb * 8 + c) * TT:(bb * 8 + c + 1) * TT],
                                    in_=x[c, :, ti * TT:(ti + 1) * TT]).then_inc(ld.h, 16); ld.n += 16

            @blk.tensor
            def _(t):
                t.wait_ge(ld.h, wload + 16 * 8 * min(tt + 1, NT))
                for mc in range(8):
                    bank = mc % 2
                    if mc >= 2:
                        t.wait_ge(dv.h, mark_dv_evac(mc - 2))
                    for kc in range(8):
                        last = t.matmul(out=px[bank][:],
                                        lhsT=wix[:, kc * 1024 + mc * 128: kc * 1024 + (mc + 1) * 128],
                                        rhs=xb[:, (b * 8 + kc) * TT:(b * 8 + kc + 1) * TT],
                                        start=(kc == 0), stop=(kc == 7))
                    mm.inc(last)
                    for kc in range(8):
                        last = t.matmul(out=pz[bank][:],
                                        lhsT=wiz[:, kc * 1024 + mc * 128: kc * 1024 + (mc + 1) * 128],
                                        rhs=xb[:, (b * 8 + kc) * TT:(b * 8 + kc + 1) * TT],
                                        start=(kc == 0), stop=(kc == 7))
                    mm.inc(last)
                t.wait_ge(ac.h, ac0 + 8)
                for kc in range(8):
                    last = t.matmul(out=pdt[:], lhsT=wdt[:, kc * 128:(kc + 1) * 128],
                                    rhs=xaa[:, kc * TT:(kc + 1) * TT],
                                    start=(kc == 0), stop=(kc == 7))
                mm.inc(last)
                for kc in range(8):
                    last = t.matmul(out=pbc[:], lhsT=wbb[:, kc * 16:(kc + 1) * 16],
                                    rhs=xaa[:, kc * TT:(kc + 1) * TT],
                                    start=(kc == 0), stop=(kc == 7))
                mm.inc(last)
                for kc in range(8):
                    last = t.matmul(out=pcc[:], lhsT=wcc[:, kc * 16:(kc + 1) * 16],
                                    rhs=xaa[:, kc * TT:(kc + 1) * TT],
                                    start=(kc == 0), stop=(kc == 7))
                mm.inc(last)

            @blk.vector
            def _(v):
                for mc in range(8):
                    bank = mc % 2
                    v.wait_ge(mm.h, mark_mm_x(mc))
                    v.tensor_single_scalar(out=xpe[mc][:, 3:3 + TT], in_=px[bank][:],
                                           scalar=xcol(mc, 0), op=ALU.add)
                    v.wait_ge(mm.h, mark_mm_z(mc))
                    dv.inc(v.tensor_single_scalar(out=zz[:, mc * TT:(mc + 1) * TT], in_=pz[bank][:],
                                                  scalar=xcol(mc, 1), op=ALU.add))
                    e = xpe[mc]
                    v.tensor_single_scalar(out=convo[:, mc * TT:(mc + 1) * TT],
                                           in_=e[:, 0:TT], scalar=xcol(mc, 3), op=ALU.mult)
                    for k in range(1, 4):
                        v.scalar_tensor_tensor(out=convo[:, mc * TT:(mc + 1) * TT],
                                               in0=e[:, k:k + TT], scalar=xcol(mc, 3 + k),
                                               in1=convo[:, mc * TT:(mc + 1) * TT],
                                               op0=ALU.mult, op1=ALU.add)
                    dv.inc(v.tensor_copy(out=e[:, 0:3], in_=e[:, TT:TT + 3]))
                v.wait_ge(mm.h, mark_mm_dt)
                v.tensor_copy(out=dtv[:], in_=pdt[:])
                v.wait_ge(mm.h, mark_mm_c)
                v.tensor_copy(out=bv[:], in_=pbc[:])
                dv.inc(v.tensor_copy(out=cv[:], in_=pcc[:]))

            @blk.scalar
            def _(a):
                for mc in range(8):
                    a.wait_ge(dv.h, mark_dv_conv(mc))
                    ac.inc(a.activation(out=xaa[:, mc * TT:(mc + 1) * TT],
                                        in_=convo[:, mc * TT:(mc + 1) * TT],
                                        func=AF.Silu, bias=xcol(mc, 2)))

            @blk.gpsimd
            def _(g):
                for mc in range(8):
                    g.wait_ge(ac.h, mark_ac(mc))
                    g.dma_start(out=o_xa[mc, :, tt * TT:(tt + 1) * TT],
                                in_=xaa[:, mc * TT:(mc + 1) * TT]).then_inc(st.h, 16); st.n += 16
                    g.wait_ge(dv.h, mark_dv_evac(mc))
                    g.dma_start(out=o_z[mc, :, tt * TT:(tt + 1) * TT],
                                in_=zz[:, mc * TT:(mc + 1) * TT]).then_inc(st.h, 16); st.n += 16
                g.wait_ge(dv.h, mark_dv_dtbc)
                g.dma_start(out=o_dtp[:, tt * TT:(tt + 1) * TT], in_=dtv[:]).then_inc(st.h, 16); st.n += 16
                g.dma_start(out=o_bp[:, tt * TT:(tt + 1) * TT], in_=bv[:]).then_inc(st.h, 16); st.n += 16
                g.dma_start(out=o_cp[:, tt * TT:(tt + 1) * TT], in_=cv[:]).then_inc(st.h, 16); st.n += 16
                g.wait_ge(st.h, st.n)

        assert mm.n == mmE + MM_PER * (tt + 1) and dv.n == dvE + DV_PER * (tt + 1) and ac.n == acE + AC_PER * (tt + 1)

    with nc.Block() as blk:
        @blk.gpsimd
        def _(g):
            g.wait_ge(st.h, st.n)
    if standalone:
        return p.close()
    return None



def build_l0b(Tn=T, TTs=512, p=None, sems=None, summed=False, cc_cb=None):
    NSUP = Tn // TTs
    standalone = p is None
    if standalone:
        p = Prog()
    nc = p.nc
    xa_i = p.din("xa", [8, 128, Tn], F32R)
    z_i = p.din("z", [8, 128, Tn])
    if summed:
        dtps_i = p.din("dts", [128, Tn])
        bps_i = p.din("bs", [16, Tn])
        cps_i = p.din("cs", [16, Tn])
        dtpp_i = bpp_i = cpp_i = None
    else:
        dtps_i = p.din("dtps", [128, Tn])
        dtpp_i = p.din("dtpp", [128, Tn])
        bps_i = p.din("bps", [16, Tn]); bpp_i = p.din("bpp", [16, Tn])
        cps_i = p.din("cps", [16, Tn]); cpp_i = p.din("cpp", [16, Tn])
    wdt_i = p.din("wdti", [128, 1024], F32R)
    wout_i = p.din("wouti", [1024, 1024], F32R)
    alog_i = p.din("alogi", [128, 128])
    cst2_i = p.din("cst2i", [128, 17])
    bxpc_i = p.din("bxpbc", [16, 2])
    id128_i = p.din("id128", [128, 128], F32R)
    o_part = p.dout("opart", [Tn // TTs, 8, 128, TTs])
    sc_b0 = nc.dram_tensor(p.pre + "l0b_scb", [16, Tn], F32, kind="Internal").ap()
    sc_c0 = nc.dram_tensor(p.pre + "l0b_scc", [16, Tn], F32, kind="Internal").ap()

    wdt = p.sb("s_wdt", [128, 1024], F32R)
    wout = p.sb("wout", [128, 8 * 1024], BF16)
    dtint = p.sb("dtint", [128, Tn], F32R)
    alog = p.sb("s_alog", [128, 128], F32)
    aneg = p.sb("aneg", [128, 128], F32)
    ascr = p.sb("ascr", [128, 128], F32)
    cst2 = p.sb("s_cst2", [128, 17], F32)
    bxpc = p.sb("s_bxpc", [16, 2], F32)
    hst = p.sb("hst", [128, 128], F32)
    xas = p.sb("xas", [128, 8 * TTs], F32)
    zs = p.sb("zs", [128, 8 * TTs], F32)
    dt_all = p.sb("dt_all", [128, 8 * TTs], F32)
    dtxa = p.sb("dtxa", [128, 8 * TTs], F32)
    sp1 = p.sb("sp1", [128, TTs], F32)
    sp2 = p.sb("sp2", [128, TTs], F32)
    bb1 = p.sb("bb1", [128, 2 * TTs], F32)
    bb2 = p.sb("bb2", [128, 2 * TTs], F32)
    cb1 = p.sb("cb1", [128, 2 * TTs], F32)
    cb2 = p.sb("cb2", [128, 2 * TTs], F32)
    bbuf = p.sb("bbuf", [128, 2 * TTs], F32)
    cbuf = p.sb("cbuf", [128, 2 * TTs], F32)
    dabuf = p.sb("dabuf", [128, 4 * TTs], F32)
    dxbuf = p.sb("dxbuf", [128, 4 * TTs], F32)
    hbuf = p.sb("hbuf", [128, 4 * TTs], F32)
    hcbuf = p.sb("hcbuf", [128, 4 * TTs], BF16)
    id128 = p.sb("s_id128", [128, 128], BF16)
    yacc = p.sb("yacc", [128, 8 * TTs], F32)
    ybuf = p.sb("ybuf", [128, TTs], F32)
    zsc = p.sb("zsc", [128, TTs], F32)
    gated = p.sb("gated", [128, 8 * TTs], BF16)
    ovac = p.sb("ovac", [128, 2 * TTs], F32)
    pdt = [p.ps(f"pdt{i}", [128, TTs]) for i in range(2)]
    pout = [p.ps(f"pout{i}", [128, TTs]) for i in range(2)]
    pacc = [p.ps(f"pacc{i}", [128, TTs]) for i in range(4)]

    if sems is None:
        ld, gl, mm = p.sem("ld"), p.sem("gl"), p.sem("mm")
        dv, ac, pl, st = p.sem("dv"), p.sem("ac"), p.sem("pl"), p.sem("st")
    else:
        ld, gl, mm, dv, ac, pl, st = (sems["ld"], sems["gl"], sems["mm"], sems["dv"],
                                      sems["ac"], sems["pl"], sems["st"])
    ldE, stE = ld.n, st.n

    with nc.Block() as blk:
        @blk.gpsimd
        def _(g0):
            for kc in range(8):
                g0.dma_start(out=wout[:, kc * 1024:(kc + 1) * 1024],
                             in_=wout_i[kc * 128:(kc + 1) * 128, :]).then_inc(ld.h, 16); ld.n += 16
            g0.dma_start(out=id128[:], in_=id128_i).then_inc(ld.h, 16); ld.n += 16

        @blk.sync
        def _(q):
            pairs = [(wdt, wdt_i), (xas, dtps_i), (alog, alog_i),
                     (cst2, cst2_i), (bxpc, bxpc_i)]
            for t_, s_ in pairs:
                q.dma_start(out=t_[:], in_=s_).then_inc(ld.h, 16); ld.n += 16
            q.dma_start(out=dt_all[0:16, :Tn], in_=bps_i).then_inc(ld.h, 16); ld.n += 16
            q.dma_start(out=dtxa[0:16, :Tn], in_=cps_i).then_inc(ld.h, 16); ld.n += 16

        ld_init = ld.n

        @blk.vector
        def _(v):
            v.wait_ge(ld.h, ld_init)
            v.memset(hst[:], 0.0)
            dv.inc(v.tensor_single_scalar(out=dtint[:], in_=xas[:],
                                          scalar=cst2[:, 16:17], op=ALU.add))
            dv.inc(v.tensor_single_scalar(out=dt_all[0:16, :Tn], in_=dt_all[0:16, :Tn],
                                          scalar=bxpc[0:16, 0:1], op=ALU.add))
            dv.inc(v.tensor_single_scalar(out=dtxa[0:16, :Tn], in_=dtxa[0:16, :Tn],
                                          scalar=bxpc[0:16, 1:2], op=ALU.add))
        dv_bias = dv.n

        @blk.scalar
        def _(a):
            a.wait_ge(ld.h, ld_init)
            ac.inc(a.activation(out=ascr[:], in_=alog[:], func=AF.Exp))
            a.wait_ge(dv.h, dv_bias)
            a.dma_start(out=sc_b0[:], in_=dt_all[0:16, :Tn]).then_inc(ld.h, 16); ld.n += 16
            a.dma_start(out=sc_c0[:], in_=dtxa[0:16, :Tn]).then_inc(ld.h, 16); ld.n += 16

    with nc.Block() as blk:
        @blk.vector
        def _(v):
            v.tensor_single_scalar(out=aneg[:], in_=ascr[:], scalar=-1.0, op=ALU.mult)

    DV0 = dv.n
    AC0 = ac.n
    PL0 = pl.n
    MM0 = mm.n
    GLE = gl.n
    DV_PER, AC_PER, PL_PER, MM_PER = 352, 272, 128, 80
    GL_PER = 8 * 16
    NBC = 2 if summed else 4
    LD_SUP = (8 + 16 * NBC) * 16
    LD0 = ld.n
    NPE = 4

    def marks(ts):
        dv0 = DV0 + DV_PER * ts
        ac0 = AC0 + AC_PER * ts
        pl0 = PL0 + PL_PER * ts
        mm0 = MM0 + MM_PER * ts
        dvm = {}
        n = dv0
        for c in range(8):
            n += 1; dvm[("dtadd", c)] = n
            n += 1; dvm[("dtxa", c)] = n
        for s in range(16):
            for c in range(8):
                i = s * 8 + c
                n += 1; dvm[("scan", i)] = n
                n += 1; dvm[("hc", i)] = n
                if c >= NPE:
                    n += 1; dvm[("yacc", i)] = n
        for c in range(8):
            n += 1; dvm[("gated", c)] = n
        for mc in range(8):
            n += 1; dvm[("evac", mc)] = n
        assert n - dv0 == DV_PER
        plm = {}
        for i in range(128):
            plm[("dbx", i)] = pl0 + i + 1
        mmm = {}
        q = mm0
        for c in range(8):
            q += 1; mmm[("dtmm", c)] = q
        for s in range(16):
            for c in range(NPE):
                q += 1; mmm[("yaccmm", s * 8 + c)] = q
        for mc in range(8):
            q += 1; mmm[("omm", mc)] = q
        assert q - mm0 == MM_PER
        return {
            "dv": dvm,
            "pl": plm,
            "mm": mmm,
            "relu": lambda c: ac0 + c + 1,
            "da": lambda i: ac0 + 8 + i + max(0, i - 4) + 1,
            "hst": lambda i: ac0 + 8 + min(i + 5, 128) + i + 1,
            "zsil": lambda c: ac0 + 264 + c + 1,
            "ld_bc": lambda s: LD0 + LD_SUP * ts + 16 * (8 + NBC * s + NBC),
            "gl_xa": GLE + GL_PER * (ts + 1),
        }

    for ts in range(NSUP):
        M = marks(ts)
        tsl = slice(ts * TTs, (ts + 1) * TTs)

        with nc.Block() as blk:
            @blk.sync
            def _(q):
                for c in range(8):
                    q.dma_start(out=zs[:, c * TTs:(c + 1) * TTs],
                                in_=z_i[c, :, tsl]).then_inc(ld.h, 16); ld.n += 16
                for s in range(16):
                    k = s % 2
                    if s >= 2:
                        q.wait_ge(dv.h, M["dv"][("hc", (s - 2) * 8 + 7)])
                    for dst, srcr in ((bbuf, sc_b0), (cbuf, sc_c0)):
                        q.dma_start(out=dst[:, k * TTs:(k + 1) * TTs],
                                    in_=srcr[s:s + 1, tsl].partition_broadcast(128)).then_inc(ld.h, 16); ld.n += 16

            @blk.tensor
            def _(t):
                for c in range(8):
                    if c >= 2:
                        t.wait_ge(ac.h, M["relu"](c - 2))
                    mm.inc(t.matmul(out=pdt[c % 2][:], lhsT=wdt[:, c * 128:(c + 1) * 128],
                                    rhs=dtint[:, tsl], start=True, stop=True))
                for s in range(16):
                    for c in range(NPE):
                        i = s * 8 + c
                        kk = i % 4
                        t.wait_ge(dv.h, M["dv"][("hc", i)])
                        mm.inc(t.matmul(out=pacc[c][:], lhsT=id128[:],
                                        rhs=hcbuf[:, kk * TTs:(kk + 1) * TTs],
                                        start=(s == 0), stop=(s == 15)))
                for mc in range(8):
                    t.wait_ge(dv.h, M["dv"][("gated", 7)])
                    if mc >= 2:
                        t.wait_ge(dv.h, M["dv"][("evac", mc - 2)])
                    for kc in range(8):
                        last = t.matmul(out=pout[mc % 2][:],
                                        lhsT=wout[:, kc * 1024 + mc * 128:kc * 1024 + (mc + 1) * 128],
                                        rhs=gated[:, kc * TTs:(kc + 1) * TTs],
                                        start=(kc == 0), stop=(kc == 7))
                    mm.inc(last)

            @blk.scalar
            def _(a):
                for c in range(8):
                    a.wait_ge(mm.h, M["mm"][("dtmm", c)])
                    if c >= 1:
                        a.wait_ge(dv.h, M["dv"][("dtadd", c - 1)])
                    a.activation(out=sp1[:], in_=pdt[c % 2][:], func=AF.Abs, bias=cst2[:, 2 * c:2 * c + 1])
                    a.activation(out=sp1[:], in_=sp1[:], func=AF.Exp, scale=-1.0)
                    a.activation(out=sp1[:], in_=sp1[:], func=AF.Ln, bias=1.0)
                    ac.inc(a.activation(out=sp2[:], in_=pdt[c % 2][:], func=AF.Relu,
                                        bias=cst2[:, 2 * c:2 * c + 1]))
                def emit_hst(i):
                    s_, c_ = i // 8, i % 8
                    ac.inc(a.activation(out=hst[:, c_ * 16 + s_:c_ * 16 + s_ + 1],
                                        in_=hbuf[:, (i % 4) * TTs + TTs - 1:(i % 4) * TTs + TTs],
                                        func=AF.Copy))
                for i in range(128):
                    s, c = i // 8, i % 8
                    a.wait_ge(dv.h, M["dv"][("dtadd", c)])
                    if i >= 4:
                        a.wait_ge(dv.h, M["dv"][("scan", i - 4)])
                    ac.inc(a.activation(out=dabuf[:, (i % 4) * TTs:(i % 4 + 1) * TTs],
                                        in_=dt_all[:, c * TTs:(c + 1) * TTs], func=AF.Exp,
                                        scale=aneg[:, c * 16 + s:c * 16 + s + 1]))
                    if i >= 4:
                        emit_hst(i - 4)
                for i in range(124, 128):
                    a.wait_ge(dv.h, M["dv"][("scan", i)])
                    emit_hst(i)
                for c in range(8):
                    if c >= 1:
                        a.wait_ge(dv.h, M["dv"][("gated", c - 1)])
                    ac.inc(a.activation(out=zsc[:], in_=zs[:, c * TTs:(c + 1) * TTs], func=AF.Silu))
                for mc in range(8):
                    a.wait_ge(dv.h, M["dv"][("evac", mc)])
                    a.dma_start(out=o_part[ts, mc],
                                in_=ovac[:, (mc % 2) * TTs:(mc % 2 + 1) * TTs]).then_inc(st.h, 16); st.n += 16
                a.wait_ge(st.h, st.n)

            @blk.vector
            def _(v):
                v.wait_ge(gl.h, M["gl_xa"])
                v.wait_ge(ld.h, LD0)
                for c in range(8):
                    v.wait_ge(ac.h, M["relu"](c))
                    dv.inc(v.tensor_add(out=dt_all[:, c * TTs:(c + 1) * TTs], in0=sp1[:], in1=sp2[:]))
                    dv.inc(v.tensor_mul(out=dtxa[:, c * TTs:(c + 1) * TTs],
                                        in0=dt_all[:, c * TTs:(c + 1) * TTs],
                                        in1=xas[:, c * TTs:(c + 1) * TTs]))
                for s in range(16):
                    k = s % 2
                    v.wait_ge(ld.h, M["ld_bc"](s))
                    for c in range(8):
                        i = s * 8 + c
                        kk = i % 4
                        v.wait_ge(ac.h, M["hst"](i - 4) if i >= 4 else M["da"](i))
                        v.wait_ge(pl.h, M["pl"][("dbx", i)])
                        dv.inc(v.tensor_tensor_scan(out=hbuf[:, kk * TTs:(kk + 1) * TTs],
                                                    data0=dabuf[:, kk * TTs:(kk + 1) * TTs],
                                                    data1=dxbuf[:, kk * TTs:(kk + 1) * TTs],
                                                    initial=hst[:, c * 16 + s:c * 16 + s + 1],
                                                    op0=ALU.mult, op1=ALU.add))
                        if i >= 4 and (i - 4) % 8 < NPE:
                            v.wait_ge(mm.h, M["mm"][("yaccmm", i - 4)])
                        dv.inc(v.tensor_mul(out=hcbuf[:, kk * TTs:(kk + 1) * TTs],
                                            in0=hbuf[:, kk * TTs:(kk + 1) * TTs],
                                            in1=cbuf[:, k * TTs:(k + 1) * TTs]))
                        if c >= NPE:
                            if s == 0:
                                dv.inc(v.tensor_copy(out=yacc[:, c * TTs:(c + 1) * TTs],
                                                     in_=hcbuf[:, kk * TTs:(kk + 1) * TTs]))
                            else:
                                dv.inc(v.tensor_add(out=yacc[:, c * TTs:(c + 1) * TTs],
                                                    in0=yacc[:, c * TTs:(c + 1) * TTs],
                                                    in1=hcbuf[:, kk * TTs:(kk + 1) * TTs]))
                for c in range(8):
                    if c < NPE:
                        v.wait_ge(mm.h, M["mm"][("yaccmm", 15 * 8 + c)])
                        yin = pacc[c][:]
                    else:
                        yin = yacc[:, c * TTs:(c + 1) * TTs]
                    v.scalar_tensor_tensor(out=ybuf[:], in0=xas[:, c * TTs:(c + 1) * TTs],
                                           scalar=cst2[:, 2 * c + 1:2 * c + 2],
                                           in1=yin,
                                           op0=ALU.mult, op1=ALU.add)
                    v.wait_ge(ac.h, M["zsil"](c))
                    dv.inc(v.tensor_mul(out=gated[:, c * TTs:(c + 1) * TTs],
                                        in0=ybuf[:], in1=zsc[:]))
                for mc in range(8):
                    v.wait_ge(mm.h, M["mm"][("omm", mc)])
                    if mc >= 2:
                        v.wait_ge(st.h, stE + 16 * (8 * ts + mc - 1))
                    dv.inc(v.tensor_copy(out=ovac[:, (mc % 2) * TTs:(mc % 2 + 1) * TTs],
                                         in_=pout[mc % 2][:]))

            @blk.gpsimd
            def _(g):
                g.wait_ge(dv.h, DV0 + DV_PER * ts)
                for c in range(8):
                    g.dma_start(out=xas[:, c * TTs:(c + 1) * TTs],
                                in_=xa_i[c, :, tsl]).then_inc(gl.h, 16); gl.n += 16
                for i in range(128):
                    s_, c_ = i // 8, i % 8
                    g.wait_ge(ld.h, M["ld_bc"](s_))
                    g.wait_ge(dv.h, M["dv"][("dtxa", c_)])
                    if i >= 4:
                        g.wait_ge(dv.h, M["dv"][("scan", i - 4)])
                    pl.inc(g.tensor_mul(out=dxbuf[:, (i % 4) * TTs:(i % 4 + 1) * TTs],
                                        in0=dtxa[:, c_ * TTs:(c_ + 1) * TTs],
                                        in1=bbuf[:, (s_ % 2) * TTs:(s_ % 2 + 1) * TTs]))

        assert dv.n == DV0 + DV_PER * (ts + 1), (dv.n, ts)
        assert ac.n == AC0 + AC_PER * (ts + 1)
        assert pl.n == PL0 + PL_PER * (ts + 1)
        assert mm.n == MM0 + MM_PER * (ts + 1)
        if cc_cb is not None:
            with nc.Block() as blk:
                @blk.gpsimd
                def _(g):
                    cc_cb(g, ts)

    with nc.Block() as blk:
        @blk.gpsimd
        def _(g):
            g.wait_ge(st.h, st.n)
    if standalone:
        return p.close()
    return None



class LnScratch:
    def __init__(self, p, N, tag="", ps_pool=None):
        self.N = N
        rows = p.sb(f"ln_rows{tag}", [1, 3 * N], F32)
        rows2 = p.sb(f"ln_rows2{tag}", [1, 2 * N], F32R)
        self.st1 = rows[0:1, 0:N]
        self.st2 = rows[0:1, N:2 * N]
        self.var = rows[0:1, 2 * N:3 * N]
        self.mu = rows2[0:1, 0:N]
        self.rstd = rows2[0:1, N:2 * N]
        self.murep = p.sb(f"ln_murep{tag}", [128, N], F32)
        self.rrep = p.sb(f"ln_rrep{tag}", [128, N], F32)
        self.sq = p.sb(f"ln_sq{tag}", [128, 2 * N], F32R)
        self.xn = p.sb(f"ln_xn{tag}", [128, N], F32)
        if ps_pool is None:
            self.psS = p.ps(f"ln_psS{tag}", [1, N])[:]
            self.psQ = p.ps(f"ln_psQ{tag}", [1, N])[:]
            self.psM = p.ps(f"ln_psM{tag}", [128, N])[:]
            self.psR = p.ps(f"ln_psR{tag}", [128, N])[:]
        else:
            self.psS = ps_pool[0][0:1, :]
            self.psQ = ps_pool[1][0:1, :]
            self.psM = ps_pool[2][:]
            self.psR = ps_pool[3][:]


def emit_ln(nc, S, ones128, ones1, xin, xout, eps_ap, sems, gamma=None, beta=None,
            store=None, st=None, extra_adds=None):
    mm, dv, ac = sems
    N = S.N
    mm0, dv0, ac0 = mm.n, dv.n, ac.n
    with nc.Block() as blk:
        @blk.tensor
        def _(t):
            for c in range(8):
                last = t.matmul(out=S.psS, lhsT=ones128[:], rhs=xin(c),
                                start=(c == 0), stop=(c == 7))
            mm.inc(last)
            for c in range(8):
                t.wait_ge(ac.h, ac0 + c + 1)
                mm.inc(t.matmul(out=S.psQ, lhsT=ones128[:],
                                rhs=S.sq[:, (c % 2) * N:(c % 2 + 1) * N],
                                start=(c == 0), stop=(c == 7)))
            t.wait_ge(dv.h, dv0 + 3)
            mm.inc(t.matmul(out=S.psM, lhsT=ones1[0:1, :], rhs=S.mu, start=True, stop=True))
            t.wait_ge(ac.h, ac0 + 10)
            mm.inc(t.matmul(out=S.psR, lhsT=ones1[0:1, :], rhs=S.rstd, start=True, stop=True))

        @blk.scalar
        def _(a):
            for c in range(8):
                if c >= 2:
                    a.wait_ge(mm.h, mm0 + 2 + (c - 2))
                ac.inc(a.activation(out=S.sq[:, (c % 2) * N:(c % 2 + 1) * N], in_=xin(c), func=AF.Square))
            a.wait_ge(dv.h, dv0 + 5)
            ac.inc(a.activation(out=S.var, in_=S.var, func=AF.Ln, bias=eps_ap))
            a.wait_ge(ac.h, ac0 + 9)
            ac.inc(a.activation(out=S.rstd, in_=S.var, func=AF.Exp, scale=-0.5))
            if store is not None:
                for c in range(8):
                    a.wait_ge(dv.h, dv0 + 8 + c)
                    dram_ap, sb_ap = store(c)
                    a.dma_start(out=dram_ap, in_=sb_ap).then_inc(st.h, 16); st.n += 16
                a.wait_ge(st.h, st.n)

        @blk.vector
        def _(v):
            v.wait_ge(mm.h, mm0 + 9)
            dv.inc(v.tensor_copy(out=S.st1, in_=S.psS))
            dv.inc(v.tensor_copy(out=S.st2, in_=S.psQ))
            v.wait_ge(dv.h, dv0 + 2)
            dv.inc(v.tensor_single_scalar(out=S.mu, in_=S.st1, scalar=1.0 / 1024.0, op=ALU.mult))
            v.wait_ge(dv.h, dv0 + 3)
            dv.inc(v.tensor_mul(out=S.st1, in0=S.mu, in1=S.mu))
            v.wait_ge(dv.h, dv0 + 4)
            dv.inc(v.scalar_tensor_tensor(out=S.var, in0=S.st2, scalar=1.0 / 1024.0,
                                          in1=S.st1, op0=ALU.mult, op1=ALU.subtract))
            v.wait_ge(mm.h, mm0 + 10)
            dv.inc(v.tensor_copy(out=S.murep[:], in_=S.psM))
            v.wait_ge(mm.h, mm0 + 11)
            dv.inc(v.tensor_copy(out=S.rrep[:], in_=S.psR))
            v.wait_ge(dv.h, dv0 + 7)
            for c in range(8):
                v.tensor_sub(out=S.xn[:], in0=xin(c), in1=S.murep[:])
                if gamma is None:
                    dv.inc(v.tensor_mul(out=xout(c), in0=S.xn[:], in1=S.rrep[:]))
                else:
                    v.tensor_mul(out=S.xn[:], in0=S.xn[:], in1=S.rrep[:])
                    dv.inc(v.tensor_scalar(out=xout(c), in0=S.xn[:], scalar1=gamma(c),
                                           scalar2=beta(c), op0=ALU.mult, op1=ALU.add))



def build_f0(TLn=TL, TTs=256, upto='', p=None, sems=None, heads=4,
             summed_in=False, partial_out=False, cc_cb=None):
    NSUP = TLn // TTs
    H8 = heads * 8
    KGN = heads * 4
    standalone = p is None
    if standalone:
        p = Prog()
    nc = p.nc
    if summed_in:
        pe_i = p.din("osum", [TLn // TTs, 8, 128, TTs])
        po_i = None
    else:
        pe_i = p.din("pe", [8, 128, TLn])
        po_i = p.din("po", [8, 128, TLn])
    x_i = p.din("x", [8, 128, TLn])
    bout_i = p.din("bouti", [128, 8])
    rm_i = p.din("rmbd", [H8, 128, 128], F32R)
    rup_i = p.din("rupbd", [128, 16 * heads * 128], F32R)
    wc_i = p.din("wci", [heads * 4096, 1024])
    ones128_i = p.din("ones128", [128, 1], F32R)
    ones1_i = p.din("ones1", [128, 128], F32R)
    if partial_out:
        o_cp = p.dout("cpart", [TLn // TTs, 8, 128, TTs])
        o_x1 = p.dout("x1o", [8, 128, TLn])
        o_x2 = None
    else:
        o_x2 = p.dout("x2", [8, 128, TLn])

    rm = p.sb("rm", [128, H8 * 128], F32R)
    rup = p.sb("rup", [128, 16 * heads * 128], BF16)
    wck = p.sb("wck", [128, 2 * 8 * 1024], BF16)
    bout = p.sb("s_bout", [128, 8], F32)
    ones128 = p.sb("s_ones128", [128, 1], F32R)
    ones1 = p.sb("s_ones1", [128, 128], F32R)
    pe = p.sb("s_pe", [128, 8 * TTs], F32)
    po = None if summed_in else p.sb("s_po", [128, 8 * TTs], F32)
    xx = p.sb("s_xx", [128, 8 * TTs], F32)
    lnin = p.sb("lnin", [128, 8 * TTs], F32R)
    x1 = p.sb("s_x1", [128, 8 * TTs], F32R)
    x2 = None if partial_out else p.sb("s_x2", [128, 8 * TTs], F32)
    yr = p.sb("yr", [128, H8 * TTs], BF16)
    hact = p.sb("hact", [128, 2 * 8 * TTs], BF16)
    acc = p.sb("s_acc", [128, 8 * TTs], F32)
    eps5 = p.sb("eps5", [1, 1], F32)
    ps1 = [p.ps(f"ps1{i}", [128, TTs]) for i in range(2)]
    ps2 = [p.ps(f"ps2{i}", [128, TTs]) for i in range(2)]
    pso = [p.ps(f"pso{i}", [128, TTs]) for i in range(2)]
    S = LnScratch(p, TTs, tag=p.pre + "F0", ps_pool=[ps1[0], ps1[1], ps2[0], ps2[1]])

    if sems is None:
        ld, gl, mm = p.sem("ld"), p.sem("gl"), p.sem("mm")
        dv, ac, st, pl = p.sem("dv"), p.sem("ac"), p.sem("st"), p.sem("pl")
    else:
        ld, gl, mm, dv, ac, st, pl = (sems["ld"], sems["gl"], sems["mm"], sems["dv"],
                                      sems["ac"], sems["st"], sems["pl"])
    ldE, stE = ld.n, st.n

    with nc.Block() as blk:
        @blk.sync
        def _(q):
            for i in range(H8):
                q.dma_start(out=rm[:, i * 128:(i + 1) * 128],
                            in_=rm_i[i]).then_inc(ld.h, 16); ld.n += 16
            for t_, s_ in ((bout, bout_i), (ones128, ones128_i), (ones1, ones1_i)):
                q.dma_start(out=t_[:], in_=s_).then_inc(ld.h, 16); ld.n += 16

        @blk.gpsimd
        def _(g0):
            g0.dma_start(out=rup[:], in_=rup_i).then_inc(ld.h, 16); ld.n += 16

        @blk.vector
        def _(v):
            v.memset(eps5[:], 1e-5)
    ld_init = ld.n

    NLD = 16 if summed_in else 24
    for ts in range(NSUP):
        tsl = slice(ts * TTs, (ts + 1) * TTs)
        with nc.Block() as blk:
            pl_a0 = pl.n

            @blk.sync
            def _(q):
                for c in range(8):
                    q.dma_start(out=pe[:, c * TTs:(c + 1) * TTs], in_=pe_i[ts, c]).then_inc(ld.h, 16); ld.n += 16
                    if not summed_in:
                        q.dma_start(out=po[:, c * TTs:(c + 1) * TTs], in_=po_i[c, :, tsl]).then_inc(ld.h, 16); ld.n += 16
                    q.dma_start(out=xx[:, c * TTs:(c + 1) * TTs], in_=x_i[c, :, tsl]).then_inc(ld.h, 16); ld.n += 16

            ld_all = ld_init + NLD * 16 * (ts + 1)

            if summed_in:
                @blk.vector
                def _(v):
                    v.wait_ge(ld.h, ld_all)
                    for c in range(8):
                        dv.inc(v.scalar_tensor_tensor(out=lnin[:, c * TTs:(c + 1) * TTs],
                                                      in0=xx[:, c * TTs:(c + 1) * TTs],
                                                      scalar=bout[:, c:c + 1],
                                                      in1=pe[:, c * TTs:(c + 1) * TTs],
                                                      op0=ALU.add, op1=ALU.add))
            else:
                @blk.gpsimd
                def _(g):
                    g.wait_ge(ld.h, ld_all)
                    for c in range(8):
                        pl.inc(g.tensor_add(out=acc[:, c * TTs:(c + 1) * TTs],
                                            in0=pe[:, c * TTs:(c + 1) * TTs],
                                            in1=po[:, c * TTs:(c + 1) * TTs]))

                @blk.vector
                def _(v):
                    for c in range(8):
                        v.wait_ge(pl.h, pl_a0 + c + 1)
                        dv.inc(v.scalar_tensor_tensor(out=lnin[:, c * TTs:(c + 1) * TTs],
                                                      in0=xx[:, c * TTs:(c + 1) * TTs],
                                                      scalar=bout[:, c:c + 1],
                                                      in1=acc[:, c * TTs:(c + 1) * TTs],
                                                      op0=ALU.add, op1=ALU.add))
        emit_ln(nc, S, ones128, ones1,
                lambda c: lnin[:, c * TTs:(c + 1) * TTs],
                lambda c: x1[:, c * TTs:(c + 1) * TTs], eps5[0:1, 0:1], (mm, dv, ac),
                store=(lambda c: (o_x1[c, :, tsl], x1[:, c * TTs:(c + 1) * TTs].bitcast(F32)))
                if partial_out else None,
                st=st if partial_out else None)
        if upto == "a":
            with nc.Block() as blk:
                @blk.scalar
                def _(a):
                    for c in range(8):
                        srcT = x1
                        a.dma_start(out=o_x2[c, :, tsl],
                                    in_=srcT[:, c * TTs:(c + 1) * TTs].bitcast(F32)).then_inc(st.h, 16); st.n += 16
                    a.wait_ge(st.h, st.n)
            continue
        with nc.Block() as blk:
            mm0, dv0, ac0, gl0 = mm.n, dv.n, ac.n, gl.n
            s1m = lambda i: mm0 + i + 1
            s2m = lambda k: mm0 + H8 + (k // 8) * 16 + (k % 8) + 1
            cpm = lambda kg, mc: mm0 + H8 + kg * 16 + 8 + mc + 1
            s1e = lambda i: ac0 + i + 1
            gel = lambda k: ac0 + H8 + (k // 8) * 8 + (k % 8) + 1
            aad = lambda kg, mc: dv0 + kg * 8 + mc + 1

            @blk.gpsimd
            def _(g):
                for kg in range(KGN):
                    if kg >= 2:
                        g.wait_ge(mm.h, cpm(kg - 2, 7))
                    for j in range(8):
                        g.dma_start(out=wck[:, (kg % 2) * 8192 + j * 1024:(kg % 2) * 8192 + (j + 1) * 1024],
                                    in_=wc_i[kg * 1024 + j * 128:kg * 1024 + (j + 1) * 128, :]).then_inc(gl.h, 16); gl.n += 16

            @blk.tensor
            def _(t):
                t.wait_ge(ld.h, ld_init)
                for i in range(H8):
                    h, cc = i // 8, i % 8
                    if i >= 2:
                        t.wait_ge(ac.h, s1e(i - 2))
                    mm.inc(t.matmul(out=ps1[i % 2][:], lhsT=rm[:, i * 128:(i + 1) * 128],
                                    rhs=x1[:, cc * TTs:(cc + 1) * TTs], start=True, stop=True))
                for kg in range(KGN):
                    t.wait_ge(gl.h, gl0 + 16 * 8 * (kg + 1))
                    for j in range(8):
                        k = kg * 8 + j
                        h, rem = k // 32, k % 32
                        cc, pj = rem // 4, rem % 4
                        t.wait_ge(ac.h, s1e(h * 8 + cc))
                        if k >= 2:
                            t.wait_ge(ac.h, gel(k - 2))
                        q2 = (h * 8 + cc) * 2 + (pj % 2)
                        mm.inc(t.matmul(out=ps2[k % 2][:],
                                        lhsT=rup[(pj // 2) * 64:(pj // 2) * 64 + 64, q2 * 128:(q2 + 1) * 128],
                                        rhs=yr[(pj // 2) * 64:(pj // 2) * 64 + 64,
                                               (h * 8 + cc) * TTs:(h * 8 + cc + 1) * TTs],
                                        start=True, stop=True))
                    t.wait_ge(ac.h, gel(kg * 8 + 7))
                    for mc in range(8):
                        if kg >= 1 or mc >= 2:
                            pkg, pmc = (kg, mc - 2) if mc >= 2 else (kg - 1, mc + 6)
                            t.wait_ge(dv.h, aad(pkg, pmc))
                        for j in range(8):
                            last = t.matmul(out=pso[(kg * 8 + mc) % 2][:],
                                            lhsT=wck[:, (kg % 2) * 8192 + j * 1024 + mc * 128:(kg % 2) * 8192 + j * 1024 + (mc + 1) * 128],
                                            rhs=hact[:, ((kg % 2) * 8 + j) * TTs:((kg % 2) * 8 + j + 1) * TTs],
                                            start=(j == 0), stop=(j == 7))
                        mm.inc(last)

            @blk.scalar
            def _(a):
                for i in range(H8):
                    a.wait_ge(mm.h, s1m(i))
                    ac.inc(a.activation(out=yr[:, i * TTs:(i + 1) * TTs], in_=ps1[i % 2][:], func=AF.Copy))
                for k in range(KGN * 8):
                    kg, j = k // 8, k % 8
                    a.wait_ge(mm.h, s2m(k))
                    if k >= 16:
                        a.wait_ge(mm.h, cpm(kg - 2, 7))
                    ac.inc(a.activation(out=hact[:, ((kg % 2) * 8 + j) * TTs:((kg % 2) * 8 + j + 1) * TTs],
                                        in_=ps2[k % 2][:], func=AF.Gelu))

            @blk.vector
            def _(v):
                for kg in range(KGN):
                    for mc in range(8):
                        v.wait_ge(mm.h, cpm(kg, mc))
                        if kg == 0:
                            dv.inc(v.tensor_copy(out=acc[:, mc * TTs:(mc + 1) * TTs],
                                                 in_=pso[(kg * 8 + mc) % 2][:]))
                        else:
                            dv.inc(v.tensor_add(out=acc[:, mc * TTs:(mc + 1) * TTs],
                                                in0=acc[:, mc * TTs:(mc + 1) * TTs],
                                                in1=pso[(kg * 8 + mc) % 2][:]))
        if partial_out:
            with nc.Block() as blk:
                @blk.scalar
                def _(a):
                    for c in range(8):
                        a.dma_start(out=o_cp[ts, c],
                                    in_=acc[:, c * TTs:(c + 1) * TTs]).then_inc(st.h, 16); st.n += 16
                    a.wait_ge(st.h, st.n)
            if cc_cb is not None:
                with nc.Block() as blk:
                    @blk.gpsimd
                    def _(g):
                        cc_cb(g, ts)
            continue
        if upto == "b":
            with nc.Block() as blk:
                @blk.scalar
                def _(a):
                    a.wait_ge(dv.h, dv.n)
                    for c in range(8):
                        a.dma_start(out=o_x2[c, :, tsl],
                                    in_=acc[:, c * TTs:(c + 1) * TTs]).then_inc(st.h, 16); st.n += 16
                    a.wait_ge(st.h, st.n)
            continue
        with nc.Block() as blk:
            @blk.vector
            def _(v):
                for c in range(8):
                    dv.inc(v.tensor_add(out=lnin[:, c * TTs:(c + 1) * TTs],
                                        in0=acc[:, c * TTs:(c + 1) * TTs],
                                        in1=x1[:, c * TTs:(c + 1) * TTs]))
        if upto == "c1":
            with nc.Block() as blk:
                @blk.scalar
                def _(a):
                    for c in range(8):
                        a.dma_start(out=o_x2[c, :, tsl],
                                    in_=lnin[:, c * TTs:(c + 1) * TTs].bitcast(F32)).then_inc(st.h, 16); st.n += 16
                    a.wait_ge(st.h, st.n)
            continue
        emit_ln(nc, S, ones128, ones1,
                lambda c: lnin[:, c * TTs:(c + 1) * TTs],
                lambda c: x2[:, c * TTs:(c + 1) * TTs], eps5[0:1, 0:1], (mm, dv, ac),
                store=lambda c: (o_x2[c, :, tsl], x2[:, c * TTs:(c + 1) * TTs]), st=st)

    with nc.Block() as blk:
        @blk.gpsimd
        def _(g):
            g.wait_ge(st.h, st.n)
    if standalone:
        return p.close()
    return None


def host_f0_inputs(inp, l0b_res):
    x = np.asarray(inp["L0_attn_x"], np.float32)
    b_out = np.asarray(inp["L0_attn_b_out"], np.float32)
    rm = np.asarray(inp["L0_ffn_rail_maker"], np.float32)
    rup = np.asarray(inp["L0_ffn_rail_up_proj"], np.float32)
    wc = np.asarray(inp["L0_ffn_w_cproj"], np.float32)
    rmbd = np.zeros((32, 128, 128), np.float32)
    for h in range(4):
        for cc in range(8):
            for j in range(8):
                r = cc * 8 + j
                rmbd[h * 8 + cc, 16 * j:16 * j + 16, 16 * j:16 * j + 16] = rm[h, r]
    rupbd = np.zeros((128, 64 * 128), np.float32)
    for h in range(4):
        for pp in range(32):
            cc, pj = pp // 4, pp % 4
            q = (h * 8 + cc) * 2 + (pj % 2)
            ro = (pj // 2) * 64
            sub = (pj % 2) * 32
            blk = np.zeros((64, 128), np.float32)
            for rr in range(2):
                rail = 2 * pp + rr
                blk[sub + 16 * rr:sub + 16 * rr + 16, 64 * rr:64 * rr + 64] = rup[h, rail].T
            rupbd[ro:ro + 64, q * 128:(q + 1) * 128] = blk
    wct = np.ascontiguousarray(wc.T)
    maps = []
    for c in range(NCORE):
        bidx, hh = c // 2, c % 2
        tok = slice(hh * TL, (hh + 1) * TL)
        xfm = np.ascontiguousarray(x[bidx, tok].T.reshape(8, 128, TL))
        maps.append({
            "pe": l0b_res[2 * bidx]["opart"][:, :, tok].copy(),
            "po": l0b_res[2 * bidx + 1]["opart"][:, :, tok].copy(),
            "x": xfm,
            "bouti": np.ascontiguousarray(b_out.reshape(8, 128).T),
            "rmbd": rmbd,
            "rupbd": rupbd,
            "wci": wct,
            "ones128": np.ones((128, 1), np.float32),
            "ones1": np.ones((128, 128), np.float32),
        })
    return maps



def build_l1a(Tn=T, TTs=256, p=None, sems=None, sel_in=False, cc_cb=None):
    NSUP = Tn // TTs
    GR = 8
    standalone = p is None
    if standalone:
        p = Prog()
    nc = p.nc
    if sel_in:
        x2_i = p.din("x2", [8, 128, Tn])
        x2r_i = p.din("x2r", [8, 128, Tn])
        rsel_i = p.din("revsel", [128, 2])
    else:
        x2_i = p.din("x2f", [8, 128, Tn])
    wixz_i = p.din("wixzi", [1024, 4096])
    wxp_i = p.din("wxpi", [2048, 96], F32R)
    wdt_i = p.din("wdti", [64, 2048], F32R)
    wout_i = p.din("wouti", [2048, 1024])
    alog_i = p.din("alogi", [128, 256])
    cstL_i = p.din("cstLi", [128, 128])
    ones128_i = p.din("ones128", [128, 1], F32R)
    ones1_i = p.din("ones1", [128, 128], F32R)
    id128L_i = p.din("id128L", [128, 128], F32R)
    o_mo = p.dout("mo", [Tn // TTs, 8, 128, TTs])
    sc_b = nc.dram_tensor(p.pre + "l1a_scb", [16, Tn], F32, kind="Internal").ap()
    sc_c = nc.dram_tensor(p.pre + "l1a_scc", [16, Tn], F32, kind="Internal").ap()
    wixz_bf = nc.dram_tensor(p.pre + "l1a_wixzbf", [1024, 4096], BF16, kind="Internal").ap()
    wout_bf = nc.dram_tensor(p.pre + "l1a_woutbf", [2048, 1024], BF16, kind="Internal").ap()

    wmc = p.sb("wmc", [128, 2 * 1024], BF16)
    wob = p.sb("wob", [128, 2 * 1024], BF16)
    wxp = p.sb("l_wxp", [128, 16 * 96], F32R)
    wdt = p.sb("l_wdt", [64, 2048], BF16)
    ones128 = p.sb("l_ones128", [128, 1], F32R)
    ones1 = p.sb("l_ones1", [128, 128], F32R)
    x2t = p.sb("l_x2t", [128, 8 * TTs], F32R)
    hh_ = p.sb("l_hh", [128, 8 * TTs], BF16)
    halos = p.sb("l_halos", [128, 16 * 3], F32)
    xs_scr = p.sb("l_xsscr", [128, TTs + 3], F32)
    convo = p.sb("l_convo", [128, TTs], F32)
    xa = p.sb("l_xa", [128, 16 * TTs], F32R)
    zb = p.sb("l_zb", [128, 16 * TTs], BF16)
    dtr = p.sb("l_dtr", [64, TTs], BF16)
    bsb = p.sb("l_bsb", [16, TTs], F32)
    csb = p.sb("l_csb", [16, TTs], F32)
    dt_g = p.sb("l_dtg", [128, GR * TTs], F32)
    dtxa_g = p.sb("l_dtxag", [128, GR * TTs], F32)
    brep = p.sb("l_brep", [128, 2 * TTs], F32)
    crep = p.sb("l_crep", [128, 2 * TTs], F32)
    dab = p.sb("l_dab", [128, 4 * TTs], F32)
    dxb = p.sb("l_dxb", [128, 4 * TTs], F32)
    hbu = p.sb("l_hbu", [128, 4 * TTs], F32)
    hcb = p.sb("l_hcb", [128, 4 * TTs], BF16)
    id128L = p.sb("l_id128", [128, 128], BF16)
    yacc = None
    ybuf = p.sb("l_ybuf", [128, TTs], F32)
    gated = None
    ovac = p.sb("l_ovac", [128, TTs], F32)
    hst = p.sb("l_hst", [128, 256], F32)
    yacc = x2t
    gated = zb
    sp1 = p.sb("l_sp1", [128, TTs], F32)
    sp2 = p.sb("l_sp2", [128, TTs], F32)
    zsc = sp2
    if sel_in:
        selscr = sp1
    alog = p.sb("l_alog", [128, 256], F32)
    aneg = alog
    cst = p.sb("l_cst", [128, 128], F32)
    eps5 = p.sb("l_eps5", [1, 1], F32)
    pa = [p.ps(f"l_pa{i}", [128, TTs]) for i in range(8)]
    S = LnScratch(p, TTs, tag="L", ps_pool=[pa[4], pa[5], pa[6], pa[7]])

    if sel_in:
        x2rv = p.sb("l_x2rv", [128, 2 * TTs], F32)
        rsel = p.sb("l_rsel", [128, 2], F32)
    if sems is None:
        ld, gl, mm = p.sem("ld"), p.sem("gl"), p.sem("mm")
        dv, ac, pl, st = p.sem("dv"), p.sem("ac"), p.sem("pl"), p.sem("st")
    else:
        ld, gl, mm, dv, ac, pl, st = (sems["ld"], sems["gl"], sems["mm"], sems["dv"],
                                      sems["ac"], sems["pl"], sems["st"])
    stE = st.n

    def ccol(j):
        return cst[:, j:j + 1]

    with nc.Block() as blk:
        @blk.sync
        def _(q):
            for kc in range(16):
                q.dma_start(out=wxp[:, kc * 96:(kc + 1) * 96],
                            in_=wxp_i[kc * 128:(kc + 1) * 128, :]).then_inc(ld.h, 16); ld.n += 16
            pairs2 = [(alog, alog_i),
                      (cst, cstL_i), (ones128, ones128_i), (ones1, ones1_i)]
            if sel_in:
                pairs2.append((rsel, rsel_i))
            for t_, s_ in pairs2:
                q.dma_start(out=t_[:], in_=s_).then_inc(ld.h, 16); ld.n += 16

        @blk.gpsimd
        def _(g):
            g.dma_start(out=wdt[:], in_=wdt_i).then_inc(ld.h, 16); ld.n += 16
            g.dma_start(out=id128L[:], in_=id128L_i).then_inc(ld.h, 16); ld.n += 16
            for q8 in range(8):
                g.dma_start(out=wixz_bf[q8 * 128:(q8 + 1) * 128],
                            in_=wixz_i[q8 * 128:(q8 + 1) * 128]).then_inc(ld.h, 16); ld.n += 16
            for q8 in range(8):
                g.dma_start(out=wout_bf[q8 * 256:(q8 + 1) * 256],
                            in_=wout_i[q8 * 256:(q8 + 1) * 256]).then_inc(ld.h, 16); ld.n += 16
            g.wait_ge(ld.h, ld.n)

        @blk.vector
        def _(v):
            v.memset(eps5[:], 1e-5)
            v.memset(hst[:], 0.0)
            v.memset(halos[:], 0.0)

        @blk.scalar
        def _(a):
            a.wait_ge(ld.h, ld.n)
            a.activation(out=aneg[:], in_=alog[:], func=AF.Exp)
    with nc.Block() as blk:
        @blk.vector
        def _(v):
            v.tensor_single_scalar(out=aneg[:], in_=aneg[:], scalar=-1.0, op=ALU.mult)

    LDPC = ld.n
    for ts in range(NSUP):
        tsl = slice(ts * TTs, (ts + 1) * TTs)
        with nc.Block() as blk:
            if not sel_in:
                @blk.gpsimd
                def _(g):
                    for c in range(8):
                        g.dma_start(out=x2t[:, c * TTs:(c + 1) * TTs],
                                    in_=x2_i[c, :, tsl]).then_inc(gl.h, 16); gl.n += 16
                    g.wait_ge(gl.h, gl.n)
            else:
                ld0s, gl0s, dv0s = ld.n, gl.n, dv.n

                @blk.sync
                def _(q):
                    q.wait_ge(dv.h, dv0s)
                    for c in range(8):
                        if c >= 2:
                            q.wait_ge(dv.h, dv0s + (c - 2) + 1)
                        q.dma_start(out=x2rv[:, (c % 2) * TTs:(c % 2 + 1) * TTs],
                                    in_=x2r_i[c, :, tsl]).then_inc(ld.h, 16); ld.n += 16

                @blk.gpsimd
                def _(g):
                    g.wait_ge(dv.h, dv0s)
                    for c in range(8):
                        g.dma_start(out=x2t[:, c * TTs:(c + 1) * TTs],
                                    in_=x2_i[c, :, tsl]).then_inc(gl.h, 16); gl.n += 16

                @blk.vector
                def _(v):
                    for c in range(8):
                        v.wait_ge(ld.h, ld0s + 16 * (c + 1))
                        v.wait_ge(gl.h, gl0s + 16 * (c + 1))
                        v.tensor_single_scalar(out=selscr[:], in_=x2rv[:, (c % 2) * TTs:(c % 2 + 1) * TTs],
                                               scalar=rsel[:, 1:2], op=ALU.mult)
                        dv.inc(v.scalar_tensor_tensor(out=x2t[:, c * TTs:(c + 1) * TTs],
                                                      in0=x2t[:, c * TTs:(c + 1) * TTs],
                                                      scalar=rsel[:, 0:1], in1=selscr[:],
                                                      op0=ALU.mult, op1=ALU.add))
        emit_ln(nc, S, ones128, ones1,
                lambda c: x2t[:, c * TTs:(c + 1) * TTs],
                lambda c: hh_[:, c * TTs:(c + 1) * TTs], eps5[0:1, 0:1], (mm, dv, ac),
                gamma=lambda c: cst[:, 112 + c:113 + c], beta=lambda c: cst[:, 120 + c:121 + c])
        with nc.Block() as blk:
            mm0, dv0, ac0, gl0 = mm.n, dv.n, ac.n, gl.n
            m_ipz = lambda j: mm0 + j + 1
            m_ipx = lambda c: mm0 + 16 + c + 1
            m_xse = lambda c: dv0 + 2 * c + 1
            m_conv = lambda c: dv0 + 2 * c + 2
            m_zv = lambda j: ac0 + j + 1
            m_xa = lambda c: ac0 + 16 + c + 1
            wcol = lambda m: (16 + m) * 128 if m < 16 else (m - 16) * 128

            @blk.sync
            def _(q):
                q.wait_ge(ld.h, LDPC)
                for m in range(32):
                    if m >= 2:
                        q.wait_ge(mm.h, mm0 + (m - 2) + 1)
                    src = bass.AP(tensor=wixz_bf.tensor, offset=wcol(m),
                                  ap=[[4096, 128], [128 * 4096, 8], [1, 128]])
                    q.dma_start(out=wmc[:, (m % 2) * 1024:(m % 2) * 1024 + 1024],
                                in_=src).then_inc(gl.h, 16); gl.n += 16

            @blk.tensor
            def _(t):
                for j in range(16):
                    bank = j % 2
                    t.wait_ge(gl.h, gl0 + 16 * (j + 1))
                    if j >= 2:
                        t.wait_ge(ac.h, m_zv(j - 2))
                    for kc in range(8):
                        last = t.matmul(out=pa[bank][:],
                                        lhsT=wmc[:, (j % 2) * 1024 + kc * 128:(j % 2) * 1024 + (kc + 1) * 128],
                                        rhs=hh_[:, kc * TTs:(kc + 1) * TTs],
                                        start=(kc == 0), stop=(kc == 7))
                    mm.inc(last)
                for c in range(16):
                    m = 16 + c
                    bank = c % 2
                    t.wait_ge(gl.h, gl0 + 16 * (m + 1))
                    if c >= 2:
                        t.wait_ge(dv.h, m_xse(c - 2))
                    else:
                        t.wait_ge(ac.h, m_zv(14 + c))
                    for kc in range(8):
                        last = t.matmul(out=pa[bank][:],
                                        lhsT=wmc[:, (m % 2) * 1024 + kc * 128:(m % 2) * 1024 + (kc + 1) * 128],
                                        rhs=hh_[:, kc * TTs:(kc + 1) * TTs],
                                        start=(kc == 0), stop=(kc == 7))
                    mm.inc(last)

            @blk.vector
            def _(v):
                for c in range(16):
                    bank = c % 2
                    v.wait_ge(mm.h, m_ipx(c))
                    if c >= 1:
                        v.wait_ge(ac.h, m_xa(c - 1))
                    v.tensor_copy(out=xs_scr[:, 0:3], in_=halos[:, c * 3:(c + 1) * 3])
                    dv.inc(v.tensor_copy(out=xs_scr[:, 3:3 + TTs], in_=pa[bank][:]))
                    v.wait_ge(dv.h, m_xse(c))
                    v.tensor_copy(out=halos[:, c * 3:(c + 1) * 3], in_=xs_scr[:, TTs:TTs + 3])
                    v.tensor_single_scalar(out=convo[:], in_=xs_scr[:, 0:TTs],
                                           scalar=ccol(6 * c + 1), op=ALU.mult)
                    for k in range(1, 4):
                        kwv = {"out": convo[:], "in0": xs_scr[:, k:k + TTs],
                               "scalar": ccol(6 * c + 1 + k), "in1": convo[:],
                               "op0": ALU.mult, "op1": ALU.add}
                        if k < 3:
                            v.scalar_tensor_tensor(**kwv)
                        else:
                            dv.inc(v.scalar_tensor_tensor(**kwv))

            @blk.scalar
            def _(a):
                for j in range(16):
                    a.wait_ge(mm.h, m_ipz(j))
                    ac.inc(a.activation(out=zb[:, j * TTs:(j + 1) * TTs],
                                        in_=pa[j % 2][:], func=AF.Copy))
                for c in range(16):
                    a.wait_ge(dv.h, m_conv(c))
                    ac.inc(a.activation(out=xa[:, c * TTs:(c + 1) * TTs], in_=convo[:],
                                        func=AF.Silu, bias=ccol(6 * c)))

        with nc.Block() as blk:
            mm0, dv0 = mm.n, dv.n

            @blk.tensor
            def _(t):
                for kc in range(16):
                    last = t.matmul(out=pa[0][0:64, :], lhsT=wxp[:, kc * 96:kc * 96 + 64],
                                    rhs=xa[:, kc * TTs:(kc + 1) * TTs],
                                    start=(kc == 0), stop=(kc == 15))
                mm.inc(last)
                for kc in range(16):
                    last = t.matmul(out=pa[1][0:16, :], lhsT=wxp[:, kc * 96 + 64:kc * 96 + 80],
                                    rhs=xa[:, kc * TTs:(kc + 1) * TTs],
                                    start=(kc == 0), stop=(kc == 15))
                mm.inc(last)
                for kc in range(16):
                    last = t.matmul(out=pa[2][0:16, :], lhsT=wxp[:, kc * 96 + 80:kc * 96 + 96],
                                    rhs=xa[:, kc * TTs:(kc + 1) * TTs],
                                    start=(kc == 0), stop=(kc == 15))
                mm.inc(last)

            @blk.vector
            def _(v):
                v.wait_ge(mm.h, mm0 + 1)
                dv.inc(v.tensor_copy(out=dtr[:], in_=pa[0][0:64, :]))
                v.wait_ge(mm.h, mm0 + 2)
                dv.inc(v.tensor_copy(out=bsb[:], in_=pa[1][0:16, :]))
                v.wait_ge(mm.h, mm0 + 3)
                dv.inc(v.tensor_copy(out=csb[:], in_=pa[2][0:16, :]))

            @blk.scalar
            def _(a):
                a.wait_ge(dv.h, dv0 + 3)
                a.dma_start(out=sc_b[:, tsl], in_=bsb[:]).then_inc(st.h, 16); st.n += 16
                a.dma_start(out=sc_c[:, tsl], in_=csb[:]).then_inc(st.h, 16); st.n += 16
                a.wait_ge(st.h, st.n)

        for g_ in range(2):
            with nc.Block() as blk:
                mm0, dv0, ac0, pl0, ld0 = mm.n, dv.n, ac.n, pl.n, ld.n
                m_dtmm = lambda c8: mm0 + c8 + 1
                m_relu = lambda c8: ac0 + c8 + 1
                m_ldbc = lambda s: ld0 + 16 * (2 * s + 2)
                NPE = 6
                pbank = [pa[0], pa[1], pa[2], pa[5], pa[6], pa[7]]
                m_da = lambda i: ac0 + 8 + i + max(0, i - 4) + 1
                m_hst = lambda i: ac0 + 8 + min(i + 5, 128) + i + 1
                m_zsil = lambda c8: ac0 + 264 + c8 + 1
                dvm = {}
                n_ = dv0
                for c8 in range(8):
                    n_ += 1; dvm[("dtadd", c8)] = n_
                    n_ += 1; dvm[("dtxa", c8)] = n_
                for s in range(16):
                    for c8 in range(8):
                        i = s * 8 + c8
                        n_ += 1; dvm[("scan", i)] = n_
                        n_ += 1; dvm[("hc", i)] = n_
                        if c8 >= NPE:
                            n_ += 1; dvm[("yacc", i)] = n_
                for c8 in range(8):
                    n_ += 1; dvm[("gated", c8)] = n_
                plm = {}
                for i in range(128):
                    plm[("dbx", i)] = pl0 + i + 1
                mmm = {}
                q_ = mm0
                for c8 in range(8):
                    q_ += 1; mmm[("dtmm", c8)] = q_
                for s in range(16):
                    for c8 in range(NPE):
                        q_ += 1; mmm[("yaccmm", s * 8 + c8)] = q_

                @blk.sync
                def _(q):
                    for s in range(16):
                        k = s % 2
                        if s >= 2:
                            q.wait_ge(dv.h, dvm[("hc", (s - 2) * 8 + 7)])
                        q.dma_start(out=brep[:, k * TTs:(k + 1) * TTs],
                                    in_=sc_b[s:s + 1, tsl].partition_broadcast(128)).then_inc(ld.h, 16); ld.n += 16
                        q.dma_start(out=crep[:, k * TTs:(k + 1) * TTs],
                                    in_=sc_c[s:s + 1, tsl].partition_broadcast(128)).then_inc(ld.h, 16); ld.n += 16

                @blk.tensor
                def _(t):
                    for c8 in range(8):
                        c = g_ * 8 + c8
                        if c8 >= 2:
                            t.wait_ge(ac.h, m_relu(c8 - 2))
                        mm.inc(t.matmul(out=pa[3 + c8 % 2][:], lhsT=wdt[:, c * 128:(c + 1) * 128],
                                        rhs=dtr[:], start=True, stop=True))
                    for s in range(16):
                        for c8 in range(NPE):
                            i = s * 8 + c8
                            kk = i % 4
                            t.wait_ge(dv.h, dvm[("hc", i)])
                            mm.inc(t.matmul(out=pbank[c8][:], lhsT=id128L[:],
                                            rhs=hcb[:, kk * TTs:(kk + 1) * TTs],
                                            start=(s == 0), stop=(s == 15)))

                @blk.scalar
                def _(a):
                    for c8 in range(8):
                        c = g_ * 8 + c8
                        a.wait_ge(mm.h, m_dtmm(c8))
                        if c8 >= 1:
                            a.wait_ge(dv.h, dvm[("dtadd", c8 - 1)])
                        a.activation(out=sp1[:], in_=pa[3 + c8 % 2][:], func=AF.Abs, bias=cst[:, 96 + c:97 + c])
                        a.activation(out=sp1[:], in_=sp1[:], func=AF.Exp, scale=-1.0)
                        a.activation(out=sp1[:], in_=sp1[:], func=AF.Ln, bias=1.0)
                        ac.inc(a.activation(out=sp2[:], in_=pa[3 + c8 % 2][:], func=AF.Relu,
                                            bias=cst[:, 96 + c:97 + c]))
                    def emit_hst(i):
                        s_, c8_ = i // 8, i % 8
                        c_ = g_ * 8 + c8_
                        ac.inc(a.activation(out=hst[:, c_ * 16 + s_:c_ * 16 + s_ + 1],
                                            in_=hbu[:, (i % 4) * TTs + TTs - 1:(i % 4) * TTs + TTs],
                                            func=AF.Copy))
                    for i in range(128):
                        s, c8 = i // 8, i % 8
                        c = g_ * 8 + c8
                        a.wait_ge(dv.h, dvm[("dtadd", c8)])
                        if i >= 4:
                            a.wait_ge(dv.h, dvm[("scan", i - 4)])
                        ac.inc(a.activation(out=dab[:, (i % 4) * TTs:(i % 4 + 1) * TTs],
                                            in_=dt_g[:, c8 * TTs:(c8 + 1) * TTs], func=AF.Exp,
                                            scale=aneg[:, c * 16 + s:c * 16 + s + 1]))
                        if i >= 4:
                            emit_hst(i - 4)
                    for i in range(124, 128):
                        a.wait_ge(dv.h, dvm[("scan", i)])
                        emit_hst(i)
                    for c8 in range(8):
                        c = g_ * 8 + c8
                        if c8 >= 1:
                            a.wait_ge(dv.h, dvm[("gated", c8 - 1)])
                        ac.inc(a.activation(out=zsc[:], in_=zb[:, c * TTs:(c + 1) * TTs], func=AF.Silu))

                @blk.vector
                def _(v):
                    for c8 in range(8):
                        c = g_ * 8 + c8
                        v.wait_ge(ac.h, m_relu(c8))
                        dv.inc(v.tensor_add(out=dt_g[:, c8 * TTs:(c8 + 1) * TTs], in0=sp1[:], in1=sp2[:]))
                        dv.inc(v.tensor_mul(out=dtxa_g[:, c8 * TTs:(c8 + 1) * TTs],
                                            in0=dt_g[:, c8 * TTs:(c8 + 1) * TTs],
                                            in1=xa[:, c * TTs:(c + 1) * TTs]))
                    for s in range(16):
                        k = s % 2
                        for c8 in range(8):
                            c = g_ * 8 + c8
                            i = s * 8 + c8
                            kk = i % 4
                            v.wait_ge(ac.h, m_hst(i - 4) if i >= 4 else m_da(i))
                            v.wait_ge(pl.h, plm[("dbx", i)])
                            dv.inc(v.tensor_tensor_scan(out=hbu[:, kk * TTs:(kk + 1) * TTs],
                                                        data0=dab[:, kk * TTs:(kk + 1) * TTs],
                                                        data1=dxb[:, kk * TTs:(kk + 1) * TTs],
                                                        initial=hst[:, c * 16 + s:c * 16 + s + 1],
                                                        op0=ALU.mult, op1=ALU.add))
                            if i >= 4 and (i - 4) % 8 < NPE:
                                v.wait_ge(mm.h, mmm[("yaccmm", i - 4)])
                            dv.inc(v.tensor_mul(out=hcb[:, kk * TTs:(kk + 1) * TTs],
                                                in0=hbu[:, kk * TTs:(kk + 1) * TTs],
                                                in1=crep[:, k * TTs:(k + 1) * TTs]))
                            if c8 >= NPE:
                                if s == 0:
                                    dv.inc(v.tensor_copy(out=yacc[:, c8 * TTs:(c8 + 1) * TTs],
                                                         in_=hcb[:, kk * TTs:(kk + 1) * TTs]))
                                else:
                                    dv.inc(v.tensor_add(out=yacc[:, c8 * TTs:(c8 + 1) * TTs],
                                                        in0=yacc[:, c8 * TTs:(c8 + 1) * TTs],
                                                        in1=hcb[:, kk * TTs:(kk + 1) * TTs]))
                    for c8 in range(8):
                        c = g_ * 8 + c8
                        if c8 < NPE:
                            v.wait_ge(mm.h, mmm[("yaccmm", 15 * 8 + c8)])
                            yin = pbank[c8][:]
                        else:
                            yin = yacc[:, c8 * TTs:(c8 + 1) * TTs]
                        v.scalar_tensor_tensor(out=ybuf[:], in0=xa[:, c * TTs:(c + 1) * TTs],
                                               scalar=ccol(6 * c + 5),
                                               in1=yin,
                                               op0=ALU.mult, op1=ALU.add)
                        v.wait_ge(ac.h, m_zsil(c8))
                        dv.inc(v.tensor_mul(out=gated[:, c * TTs:(c + 1) * TTs],
                                            in0=ybuf[:], in1=zsc[:]))

                @blk.gpsimd
                def _(g):
                    for i in range(128):
                        s_, c8_ = i // 8, i % 8
                        g.wait_ge(ld.h, m_ldbc(s_))
                        g.wait_ge(dv.h, dvm[("dtxa", c8_)])
                        if i >= 4:
                            g.wait_ge(dv.h, dvm[("scan", i - 4)])
                        pl.inc(g.tensor_mul(out=dxb[:, (i % 4) * TTs:(i % 4 + 1) * TTs],
                                            in0=dtxa_g[:, c8_ * TTs:(c8_ + 1) * TTs],
                                            in1=brep[:, (s_ % 2) * TTs:(s_ % 2 + 1) * TTs]))

        with nc.Block() as blk:
            mm0, dv0, gl0, st5 = mm.n, dv.n, gl.n, st.n
            m_okc = lambda kc: mm0 + kc + 1
            m_oev = lambda mc: dv0 + mc + 1

            @blk.sync
            def _(q):
                q.wait_ge(ld.h, LDPC)
                for kc in range(16):
                    if kc >= 2:
                        q.wait_ge(mm.h, m_okc(kc - 2))
                    q.dma_start(out=wob[:, (kc % 2) * 1024:(kc % 2 + 1) * 1024],
                                in_=wout_bf[kc * 128:(kc + 1) * 128, :]).then_inc(gl.h, 16); gl.n += 16

            @blk.tensor
            def _(t):
                for kc in range(16):
                    t.wait_ge(gl.h, gl0 + 16 * (kc + 1))
                    for mc in range(8):
                        last = t.matmul(out=pa[mc][:],
                                        lhsT=wob[:, (kc % 2) * 1024 + mc * 128:(kc % 2) * 1024 + (mc + 1) * 128],
                                        rhs=gated[:, kc * TTs:(kc + 1) * TTs],
                                        start=(kc == 0), stop=(kc == 15))
                    mm.inc(last)

            @blk.vector
            def _(v):
                v.wait_ge(mm.h, m_okc(15))
                for mc in range(8):
                    if mc >= 1:
                        v.wait_ge(st.h, st5 + 16 * mc)
                    dv.inc(v.tensor_copy(out=ovac[:], in_=pa[mc][:]))

            @blk.scalar
            def _(a):
                for mc in range(8):
                    a.wait_ge(dv.h, m_oev(mc))
                    a.dma_start(out=o_mo[ts, mc],
                                in_=ovac[:]).then_inc(st.h, 16); st.n += 16
                a.wait_ge(st.h, st.n)

        if cc_cb is not None:
            with nc.Block() as blk:
                @blk.gpsimd
                def _(g):
                    cc_cb(g, ts)

    with nc.Block() as blk:
        @blk.gpsimd
        def _(g):
            g.wait_ge(st.h, st.n)
    if standalone:
        return p.close()
    return None


def host_l1a_inputs(inp, x2_full):
    maps = []
    ones128 = np.ones((128, 1), np.float32)
    ones1 = np.ones((128, 128), np.float32)
    selm = np.zeros((16, 16, 128), np.float32)
    for s in range(16):
        selm[s, s, :] = 1.0
    selm = np.ascontiguousarray(selm.transpose(1, 0, 2).reshape(16, 2048))
    ng = np.asarray(inp["L1_attn_norm_g"], np.float32)
    nb = np.asarray(inp["L1_attn_norm_b"], np.float32)
    for c in range(NCORE):
        bidx, hh = c // 2, c % 2
        mk = "m1" if hh == 0 else "m2"
        in_w = np.asarray(inp[f"L1_attn_{mk}_in_w"], np.float32)
        conv_w = np.asarray(inp[f"L1_attn_{mk}_conv_w"], np.float32)
        conv_b = np.asarray(inp[f"L1_attn_{mk}_conv_b"], np.float32)
        xproj_w = np.asarray(inp[f"L1_attn_{mk}_xproj_w"], np.float32)
        dt_w = np.asarray(inp[f"L1_attn_{mk}_dt_w"], np.float32)
        dt_b = np.asarray(inp[f"L1_attn_{mk}_dt_b"], np.float32)
        Alog = np.asarray(inp[f"L1_attn_{mk}_Alog"], np.float32)
        Dp = np.asarray(inp[f"L1_attn_{mk}_D"], np.float32)
        out_w = np.asarray(inp[f"L1_attn_{mk}_out_w"], np.float32)
        xv = x2_full[bidx]
        if hh == 1:
            xv = xv[::-1]
        cstL = np.zeros((128, 128), np.float32)
        cb16 = conv_b.reshape(16, 128)
        d16 = Dp.reshape(16, 128)
        for ch in range(16):
            cstL[:, 6 * ch] = cb16[ch]
            for k in range(4):
                cstL[:, 6 * ch + 1 + k] = conv_w[:, k].reshape(16, 128)[ch]
            cstL[:, 6 * ch + 5] = d16[ch]
        cstL[:, 96:112] = dt_b.reshape(16, 128).T
        cstL[:, 112:120] = ng.reshape(8, 128).T
        cstL[:, 120:128] = nb.reshape(8, 128).T
        maps.append({
            "x2f": np.ascontiguousarray(xv.T.reshape(8, 128, T)),
            "wixzi": np.ascontiguousarray(in_w.T),
            "wxpi": np.ascontiguousarray(xproj_w.T),
            "wdti": np.ascontiguousarray(dt_w.T),
            "wouti": np.ascontiguousarray(out_w.T),
            "alogi": np.ascontiguousarray(Alog.reshape(16, 128, 16).transpose(1, 0, 2).reshape(128, 256)),
            "cstLi": cstL,
            "ones128": ones128,
            "ones1": ones1,
            "id128L": np.eye(128, dtype=np.float32),
        })
    return maps



def build_l1b(TLn=TL, TTs=256, p=None, sems=None, summed_in=False, kc_n=32, cc_cb=None,
              partial_out=False):
    NSUP = TLn // TTs
    kg_n = kc_n // 8
    standalone = p is None
    if standalone:
        p = Prog()
    nc = p.nc
    x2_i = p.din("x2h", [8, 128, TLn])
    if summed_in:
        fw_i = p.din("mosum", [TLn // TTs, 8, 128, TTs])
        bw_i = None
    else:
        fw_i = p.din("fwdh", [8, 128, TLn])
        bw_i = p.din("bwdh", [8, 128, TLn])
    ff1_i = p.din("ff1wi", [1024, kc_n * 128])
    ff2_i = p.din("ff2wi", [kc_n * 128, 1024])
    cstF_i = p.din("cstFi", [128, 56])
    ones128_i = p.din("ones128", [128, 1], F32R)
    ones1_i = p.din("ones1", [128, 128], F32R)
    if partial_out:
        o_ffp = p.dout("ffp", [TLn // TTs, 8, 128, TTs])
        o_xm = p.dout("xmo", [8, 128, TLn])
        o_x3 = None
    else:
        o_x3 = p.dout("x3", [8, 128, TLn])

    ff1w = p.sb("b_ff1w", [128, 8 * kc_n * 128], BF16)
    ff2w = p.sb("b_ff2w", [128, kc_n * 1024], BF16)
    cstF = p.sb("b_cst", [128, 56], F32)
    ones128 = p.sb("b_ones128", [128, 1], F32R)
    ones1 = p.sb("b_ones1", [128, 128], F32R)
    x2t = p.sb("b_x2t", [128, 8 * TTs], F32)
    fwt = p.sb("b_fwt", [128, 8 * TTs], F32R)
    bwt = None if summed_in else p.sb("b_bwt", [128, 8 * TTs], F32)
    xm = p.sb("b_xm", [128, 8 * TTs], F32R)
    hf = p.sb("b_hf", [128, 8 * TTs], BF16)
    h1 = p.sb("b_h1", [128, 8 * TTs], BF16)
    acc = p.sb("b_acc", [128, 8 * TTs], F32)
    x3t = p.sb("b_x3t", [128, 8 * TTs], F32)
    scr = p.sb("b_scr", [128, TTs], F32)
    eps5 = p.sb("b_eps5", [1, 1], F32)
    pp = [p.ps(f"b_pp{i}", [128, TTs]) for i in range(4)]
    S = LnScratch(p, TTs, tag="B", ps_pool=None)

    if sems is None:
        ld, gl, mm = p.sem("ld"), p.sem("gl"), p.sem("mm")
        dv, ac, pl, st = p.sem("dv"), p.sem("ac"), p.sem("pl"), p.sem("st")
    else:
        ld, gl, mm, dv, ac, pl, st = (sems["ld"], sems["gl"], sems["mm"], sems["dv"],
                                      sems["ac"], sems["pl"], sems["st"])
    stE = st.n
    W1C = kc_n * 128

    with nc.Block() as blk:
        @blk.sync
        def _(q):
            for t_, s_ in ((cstF, cstF_i), (ones128, ones128_i), (ones1, ones1_i)):
                q.dma_start(out=t_[:], in_=s_).then_inc(ld.h, 16); ld.n += 16

        @blk.gpsimd
        def _(g):
            for kc in range(8):
                g.dma_start(out=ff1w[:, kc * W1C:(kc + 1) * W1C],
                            in_=ff1_i[kc * 128:(kc + 1) * 128, :]).then_inc(gl.h, 16); gl.n += 16
            for kc in range(kc_n):
                g.dma_start(out=ff2w[:, kc * 1024:(kc + 1) * 1024],
                            in_=ff2_i[kc * 128:(kc + 1) * 128, :]).then_inc(gl.h, 16); gl.n += 16
            g.wait_ge(gl.h, gl.n)
            g.wait_ge(ld.h, ld.n)

        @blk.vector
        def _(v):
            v.memset(eps5[:], 1e-5)

    for ts in range(NSUP):
        tsl = slice(ts * TTs, (ts + 1) * TTs)
        with nc.Block() as blk:
            dv0, ac0, pl0 = dv.n, ac.n, pl.n
            ld0 = ld.n

            @blk.sync
            def _(q):
                for c in range(8):
                    q.dma_start(out=x2t[:, c * TTs:(c + 1) * TTs], in_=x2_i[c, :, tsl]).then_inc(ld.h, 16); ld.n += 16
                    if not summed_in:
                        q.dma_start(out=bwt[:, c * TTs:(c + 1) * TTs], in_=bw_i[c, :, tsl]).then_inc(ld.h, 16); ld.n += 16

            @blk.gpsimd
            def _(g):
                gl0_ = gl.n
                for c in range(8):
                    g.dma_start(out=fwt[:, c * TTs:(c + 1) * TTs], in_=fw_i[ts, c]).then_inc(gl.h, 16); gl.n += 16
                g.wait_ge(ld.h, ld0 + 16 * (16 if not summed_in else 8))
                g.wait_ge(gl.h, gl0_ + 8 * 16)
                if not summed_in:
                    for c in range(8):
                        pl.inc(g.tensor_add(out=acc[:, c * TTs:(c + 1) * TTs],
                                            in0=fwt[:, c * TTs:(c + 1) * TTs],
                                            in1=bwt[:, c * TTs:(c + 1) * TTs]))
                else:
                    for c in range(8):
                        pl.inc(g.tensor_copy(out=acc[:, c * TTs:(c + 1) * TTs],
                                             in_=fwt[:, c * TTs:(c + 1) * TTs]))

            @blk.vector
            def _(v):
                for c in range(8):
                    v.wait_ge(pl.h, pl0 + c + 1)
                    if c >= 1:
                        v.wait_ge(ac.h, ac0 + c)
                    dv.inc(v.tensor_add(out=scr[:], in0=acc[:, c * TTs:(c + 1) * TTs],
                                        in1=x2t[:, c * TTs:(c + 1) * TTs]))
                for c in range(8):
                    v.wait_ge(ac.h, ac0 + c + 1)
                    dv.inc(v.tensor_add(out=xm[:, c * TTs:(c + 1) * TTs],
                                        in0=x2t[:, c * TTs:(c + 1) * TTs],
                                        in1=x3t[:, c * TTs:(c + 1) * TTs]))

            @blk.scalar
            def _(a):
                for c in range(8):
                    a.wait_ge(dv.h, dv0 + c + 1)
                    ac.inc(a.activation(out=x3t[:, c * TTs:(c + 1) * TTs], in_=scr[:], func=AF.Gelu))
        emit_ln(nc, S, ones128, ones1,
                lambda c: xm[:, c * TTs:(c + 1) * TTs],
                lambda c: hf[:, c * TTs:(c + 1) * TTs], eps5[0:1, 0:1], (mm, dv, ac),
                gamma=lambda c: cstF[:, c:c + 1], beta=lambda c: cstF[:, 8 + c:9 + c])
        with nc.Block() as blk:
            mm0, dv0, ac0 = mm.n, dv.n, ac.n
            m_f1 = lambda kg, j: mm0 + kg * 16 + j + 1
            m_f2 = lambda kg, mc: mm0 + kg * 16 + 8 + mc + 1
            m_gel = lambda kg, j: ac0 + kg * 8 + j + 1
            m_aad = lambda kg, mc: dv0 + kg * 8 + mc + 1

            @blk.tensor
            def _(t):
                for kg in range(kg_n):
                    for j in range(8):
                        kc = kg * 8 + j
                        if kc >= 2:
                            t.wait_ge(ac.h, m_gel((kc - 2) // 8, (kc - 2) % 8))
                        for k8 in range(8):
                            last = t.matmul(out=pp[kc % 2][:],
                                            lhsT=ff1w[:, k8 * W1C + kc * 128:k8 * W1C + (kc + 1) * 128],
                                            rhs=hf[:, k8 * TTs:(k8 + 1) * TTs],
                                            start=(k8 == 0), stop=(k8 == 7))
                        mm.inc(last)
                    t.wait_ge(ac.h, m_gel(kg, 7))
                    for mc in range(8):
                        if kg >= 1 or mc >= 2:
                            pkg, pmc = (kg, mc - 2) if mc >= 2 else (kg - 1, mc + 6)
                            t.wait_ge(dv.h, m_aad(pkg, pmc))
                        for j in range(8):
                            last = t.matmul(out=pp[2 + (kg * 8 + mc) % 2][:],
                                            lhsT=ff2w[:, (kg * 8 + j) * 1024 + mc * 128:(kg * 8 + j) * 1024 + (mc + 1) * 128],
                                            rhs=h1[:, j * TTs:(j + 1) * TTs],
                                            start=(j == 0), stop=(j == 7))
                        mm.inc(last)

            @blk.scalar
            def _(a):
                for kg in range(kg_n):
                    for j in range(8):
                        kc = kg * 8 + j
                        a.wait_ge(mm.h, m_f1(kg, j))
                        if kc >= 8:
                            a.wait_ge(mm.h, m_f2(kc // 8 - 1, 7))
                        ac.inc(a.activation(out=h1[:, j * TTs:(j + 1) * TTs], in_=pp[kc % 2][:],
                                            func=AF.Gelu, bias=cstF[:, 16 + kc:17 + kc]))

            @blk.vector
            def _(v):
                for kg in range(kg_n):
                    for mc in range(8):
                        v.wait_ge(mm.h, m_f2(kg, mc))
                        if kg == 0:
                            dv.inc(v.tensor_copy(out=acc[:, mc * TTs:(mc + 1) * TTs],
                                                 in_=pp[2 + (kg * 8 + mc) % 2][:]))
                        else:
                            dv.inc(v.tensor_add(out=acc[:, mc * TTs:(mc + 1) * TTs],
                                                in0=acc[:, mc * TTs:(mc + 1) * TTs],
                                                in1=pp[2 + (kg * 8 + mc) % 2][:]))
        if partial_out:
            with nc.Block() as blk:
                @blk.scalar
                def _(a):
                    a.wait_ge(dv.h, dv.n)
                    for c in range(8):
                        a.dma_start(out=o_ffp[ts, c],
                                    in_=acc[:, c * TTs:(c + 1) * TTs]).then_inc(st.h, 16); st.n += 16
                        a.dma_start(out=o_xm[c, :, tsl],
                                    in_=xm[:, c * TTs:(c + 1) * TTs].bitcast(F32)).then_inc(st.h, 16); st.n += 16
                    a.wait_ge(st.h, st.n)
            if cc_cb is not None:
                with nc.Block() as blk:
                    @blk.gpsimd
                    def _(g):
                        cc_cb(g, ts)
            continue
        with nc.Block() as blk:
            @blk.vector
            def _(v):
                for c in range(8):
                    dv.inc(v.scalar_tensor_tensor(out=scr[:], in0=acc[:, c * TTs:(c + 1) * TTs],
                                                  scalar=cstF[:, 48 + c:49 + c],
                                                  in1=xm[:, c * TTs:(c + 1) * TTs],
                                                  op0=ALU.add, op1=ALU.add))
                    v.wait_ge(dv.h, dv.n)
                    dv.inc(v.tensor_add(out=fwt[:, c * TTs:(c + 1) * TTs], in0=scr[:],
                                        in1=x2t[:, c * TTs:(c + 1) * TTs]))
        emit_ln(nc, S, ones128, ones1,
                lambda c: fwt[:, c * TTs:(c + 1) * TTs],
                lambda c: x3t[:, c * TTs:(c + 1) * TTs], eps5[0:1, 0:1], (mm, dv, ac),
                store=lambda c: (o_x3[c, :, tsl], x3t[:, c * TTs:(c + 1) * TTs]), st=st)

    with nc.Block() as blk:
        @blk.gpsimd
        def _(g):
            g.wait_ge(st.h, st.n)
    if standalone:
        return p.close()
    return None


def host_l1b_inputs(inp, x2_full, l1a_res):
    ff1w = np.asarray(inp["L1_attn_ff1_w"], np.float32)
    ff2w = np.asarray(inp["L1_attn_ff2_w"], np.float32)
    ff1b = np.asarray(inp["L1_attn_ff1_b"], np.float32)
    ff2b = np.asarray(inp["L1_attn_ff2_b"], np.float32)
    fg = np.asarray(inp["L1_attn_ffn_g"], np.float32)
    fb = np.asarray(inp["L1_attn_ffn_b"], np.float32)
    cstF = np.zeros((128, 56), np.float32)
    cstF[:, 0:8] = fg.reshape(8, 128).T
    cstF[:, 8:16] = fb.reshape(8, 128).T
    cstF[:, 16:48] = ff1b.reshape(32, 128).T
    cstF[:, 48:56] = ff2b.reshape(8, 128).T
    maps = []
    for c in range(NCORE):
        bidx, hh = c // 2, c % 2
        tok = slice(hh * TL, (hh + 1) * TL)
        fwd = l1a_res[2 * bidx]["mo"]
        bwd = l1a_res[2 * bidx + 1]["mo"]
        maps.append({
            "x2h": np.ascontiguousarray(x2_full[bidx, tok].T.reshape(8, 128, TL)),
            "fwdh": np.ascontiguousarray(fwd[:, :, tok]),
            "bwdh": np.ascontiguousarray(bwd[:, :, tok]),
            "ff1wi": np.ascontiguousarray(ff1w.T),
            "ff2wi": np.ascontiguousarray(ff2w.T),
            "cstFi": cstF,
            "ones128": np.ones((128, 1), np.float32),
            "ones1": np.ones((128, 128), np.float32),
        })
    return maps



def build_f1(TLn=TL, TTs=256, p=None, sems=None, mc_n=32, partial_out=False, TH=None, cc_cb=None):
    if TH is None:
        TH = TLn // 4
    NTH = TLn // TH
    NTT = TH // TTs
    standalone = p is None
    if standalone:
        p = Prog()
    nc = p.nc
    x3_i = p.din("x3h", [8, 128, TLn])
    wr_i = p.din("wri", [1024, 4], F32R)
    w1_i = p.din("w1t", [mc_n, 4, 8, 128, 128])
    b1_i = p.din("b1i", [4, mc_n * 128], F32R)
    wo_i = p.din("woi", [mc_n * 128, 1024])
    cstM_i = p.din("cstMi", [128, 24])
    ones128_i = p.din("ones128", [128, 1], F32R)
    ones1_i = p.din("ones1", [128, 128], F32R)
    if partial_out:
        o_wop = p.dout("wop", [TLn // TH, 8, 128, TH])
        o_x4 = None
    else:
        o_x4 = p.dout("x4", [8, 128, TLn])

    wr = p.sb("f_wr", [128, 8 * 4], F32R)
    b1 = p.sb("f_b1", [4, mc_n * 128], BF16)
    w1b = p.sb("f_w1b", [128, 2 * 4096], BF16)
    wob = p.sb("f_wob", [128, 2 * 1024], BF16)
    cstM = p.sb("f_cst", [128, 24], F32)
    ones128 = p.sb("f_ones128", [128, 1], F32R)
    ones1 = p.sb("f_ones1", [128, 128], F32R)
    x3t = p.sb("f_x3t", [128, 8 * TH], F32R)
    xme = p.sb("f_xme", [128, 32 * TH], BF16)
    acc = p.sb("f_acc", [128, 8 * TH], F32)
    lg = p.sb("f_lg", [4, TTs], F32)
    le = [p.sb(f"f_le{e}", [1, TTs], F32) for e in range(4)]
    cm1 = p.sb("f_cm1", [1, TTs], F32)
    cm2 = p.sb("f_cm2", [1, TTs], F32)
    me = [p.sb(f"f_me{e}", [1, TTs], F32R) for e in range(4)]
    oh = p.sb("f_oh", [4, TH], BF16)
    mrep = p.sb("f_mrep", [128, 2 * TTs], F32)
    inter = p.sb("f_inter", [128, 2 * TTs], BF16)
    moe = p.sb("f_moe", [128, TTs], F32R)
    lnin = p.sb("f_lnin", [128, 8 * TTs], F32R)
    x4t = p.sb("f_x4t", [128, 8 * TTs], F32)
    eps12 = p.sb("f_eps12", [1, 1], F32)
    eps5 = p.sb("f_eps5", [1, 1], F32)
    pq = [p.ps(f"f_pq{i}", [128, TTs]) for i in range(4)]
    S = LnScratch(p, TTs, tag="F", ps_pool=None)

    if sems is None:
        ld, gl, mm = p.sem("ld"), p.sem("gl"), p.sem("mm")
        dv, ac, pl, st = p.sem("dv"), p.sem("ac"), p.sem("pl"), p.sem("st")
    else:
        ld, gl, mm, dv, ac, pl, st = (sems["ld"], sems["gl"], sems["mm"], sems["dv"],
                                      sems["ac"], sems["pl"], sems["st"])
    stE = st.n

    with nc.Block() as blk:
        @blk.sync
        def _(q):
            for kc in range(8):
                q.dma_start(out=wr[:, kc * 4:(kc + 1) * 4],
                            in_=wr_i[kc * 128:(kc + 1) * 128, :]).then_inc(ld.h, 16); ld.n += 16
            for t_, s_ in ((cstM, cstM_i), (ones128, ones128_i), (ones1, ones1_i)):
                q.dma_start(out=t_[:], in_=s_).then_inc(ld.h, 16); ld.n += 16

        @blk.vector
        def _(v):
            v.memset(eps12[:], 1e-12)
            v.memset(eps5[:], 1e-5)

        @blk.gpsimd
        def _(g):
            g.dma_start(out=b1[:], in_=b1_i).then_inc(gl.h, 16); gl.n += 16
            g.wait_ge(gl.h, gl.n)
            g.wait_ge(ld.h, ld.n)

    for th in range(NTH):
        thl = slice(th * TH, (th + 1) * TH)
        with nc.Block() as blk:
            gl0, mm0, dv0, pl0 = gl.n, mm.n, dv.n, pl.n
            DV_TT, MM_TT, PL_TT = 41, 5, 128
            m_evlg = lambda tt: dv0 + DV_TT * tt + 1
            m_me = lambda tt, e: dv0 + DV_TT * tt + 2 + e
            m_mrep = lambda tt, e: dv0 + DV_TT * tt + 6 + 9 * e
            m_xm = lambda tt, e, k8: dv0 + DV_TT * tt + 6 + 9 * e + 1 + k8
            m_rt = lambda tt: mm0 + MM_TT * tt + 1
            m_rep = lambda tt, e: mm0 + MM_TT * tt + 2 + e

            @blk.gpsimd
            def _(g):
                for c in range(8):
                    g.dma_start(out=x3t[:, c * TH:(c + 1) * TH],
                                in_=x3_i[c, :, thl]).then_inc(gl.h, 16); gl.n += 16
                g.wait_ge(gl.h, gl.n)
                for tt in range(NTT):
                    g.wait_ge(dv.h, m_evlg(tt))
                    for e in range(4):
                        g.dma_start(out=le[e][:], in_=lg[e:e + 1, :]).then_inc(pl.h, 16); pl.n += 16
                    g.wait_ge(dv.h, m_me(tt, 3))
                    for e in range(4):
                        g.dma_start(out=oh[e:e + 1, tt * TTs:(tt + 1) * TTs],
                                    in_=me[e][:]).then_inc(pl.h, 16); pl.n += 16

            @blk.tensor
            def _(t):
                t.wait_ge(gl.h, gl0 + 8 * 16)
                for tt in range(NTT):
                    if tt >= 1:
                        t.wait_ge(dv.h, m_evlg(tt - 1))
                    for kc in range(8):
                        last = t.matmul(out=pq[0][0:4, :], lhsT=wr[:, kc * 4:(kc + 1) * 4],
                                        rhs=x3t[:, kc * TH + tt * TTs:kc * TH + tt * TTs + TTs],
                                        start=(kc == 0), stop=(kc == 7))
                    mm.inc(last)
                    for e in range(4):
                        t.wait_ge(dv.h, m_me(tt, e))
                        gi = tt * 4 + e
                        if gi >= 2:
                            pt, pe_ = (gi - 2) // 4, (gi - 2) % 4
                            t.wait_ge(dv.h, m_mrep(pt, pe_))
                        mm.inc(t.matmul(out=pq[1 + e % 2][:], lhsT=ones1[0:1, :], rhs=me[e][:],
                                        start=True, stop=True))

            @blk.vector
            def _(v):
                for tt in range(NTT):
                    v.wait_ge(mm.h, m_rt(tt))
                    if tt >= 1:
                        v.wait_ge(pl.h, pl0 + PL_TT * (tt - 1) + 128)
                    dv.inc(v.tensor_copy(out=lg[:], in_=pq[0][0:4, :]))
                    v.wait_ge(pl.h, pl0 + PL_TT * tt + 64)
                    for e in range(4):
                        first = True
                        for o in range(4):
                            if o == e:
                                continue
                            op = ALU.is_ge if o > e else ALU.is_gt
                            if first:
                                v.tensor_tensor(out=cm1[:], in0=le[e][:], in1=le[o][:], op=op)
                            else:
                                v.tensor_tensor(out=cm2[:], in0=le[e][:], in1=le[o][:], op=op)
                                v.tensor_mul(out=cm1[:], in0=cm1[:], in1=cm2[:])
                            first = False
                        dv.inc(v.tensor_copy(out=me[e][:], in_=cm1[:]))
                    for e in range(4):
                        v.wait_ge(mm.h, m_rep(tt, e))
                        dv.inc(v.tensor_copy(out=mrep[:, (e % 2) * TTs:(e % 2 + 1) * TTs],
                                             in_=pq[1 + e % 2][:]))
                        for k8 in range(8):
                            dv.inc(v.tensor_mul(
                                out=xme[:, (e * 8 + k8) * TH + tt * TTs:(e * 8 + k8) * TH + tt * TTs + TTs],
                                in0=x3t[:, k8 * TH + tt * TTs:k8 * TH + tt * TTs + TTs],
                                in1=mrep[:, (e % 2) * TTs:(e % 2 + 1) * TTs]))
        with nc.Block() as blk:
            gl0, mm0, dv0, ac0 = gl.n, mm.n, dv.n, ac.n
            MMC = NTT * 9
            m_w1i = lambda Mc, tt: mm0 + Mc * MMC + tt + 1
            m_woi = lambda Mc, tt, mc: mm0 + Mc * MMC + NTT + tt * 8 + mc + 1
            m_gel = lambda Mc, tt: ac0 + Mc * NTT + tt + 1
            m_aad = lambda Mc, tt, mc: dv0 + Mc * NTT * 8 + tt * 8 + mc + 1

            @blk.gpsimd
            def _(g):
                for Mc in range(mc_n):
                    if Mc >= 2:
                        g.wait_ge(mm.h, m_w1i(Mc - 1, 0))
                    g.dma_start(out=w1b[:, (Mc % 2) * 4096:(Mc % 2 + 1) * 4096],
                                in_=w1_i[Mc].rearrange("e k p m -> p e k m")).then_inc(gl.h, 16); gl.n += 16
                    g.dma_start(out=wob[:, (Mc % 2) * 1024:(Mc % 2 + 1) * 1024],
                                in_=wo_i[Mc * 128:(Mc + 1) * 128, :]).then_inc(gl.h, 16); gl.n += 16

            @blk.tensor
            def _(t):
                for Mc in range(mc_n):
                    t.wait_ge(gl.h, gl0 + 32 * (Mc + 1))
                    for tt in range(NTT):
                        bank = (Mc * NTT + tt) % 2
                        if Mc * NTT + tt >= 2:
                            pk = Mc * NTT + tt - 2
                            t.wait_ge(ac.h, m_gel(pk // NTT, pk % NTT))
                        for e in range(4):
                            for k8 in range(8):
                                t.matmul(out=pq[bank][:],
                                         lhsT=w1b[:, (Mc % 2) * 4096 + e * 1024 + k8 * 128:(Mc % 2) * 4096 + e * 1024 + (k8 + 1) * 128],
                                         rhs=xme[:, (e * 8 + k8) * TH + tt * TTs:(e * 8 + k8) * TH + tt * TTs + TTs],
                                         start=(e == 0 and k8 == 0), stop=False)
                        last = t.matmul(out=pq[bank][:], lhsT=b1[:, Mc * 128:(Mc + 1) * 128],
                                        rhs=oh[:, tt * TTs:(tt + 1) * TTs], start=False, stop=True)
                        mm.inc(last)
                    for tt in range(NTT):
                        t.wait_ge(ac.h, m_gel(Mc, tt))
                        for mc in range(8):
                            i2 = tt * 8 + mc
                            if Mc * NTT * 8 + i2 >= 2:
                                pk = Mc * NTT * 8 + i2 - 2
                                t.wait_ge(dv.h, m_aad(pk // (NTT * 8), (pk % (NTT * 8)) // 8, pk % 8))
                            mm.inc(t.matmul(out=pq[2 + i2 % 2][:],
                                            lhsT=wob[:, (Mc % 2) * 1024 + mc * 128:(Mc % 2) * 1024 + (mc + 1) * 128],
                                            rhs=inter[:, (tt % 2) * TTs:(tt % 2 + 1) * TTs],
                                            start=True, stop=True))

            @blk.scalar
            def _(a):
                for Mc in range(mc_n):
                    for tt in range(NTT):
                        a.wait_ge(mm.h, m_w1i(Mc, tt))
                        if Mc * NTT + tt >= 2:
                            pk = Mc * NTT + tt - 2
                            a.wait_ge(mm.h, m_woi(pk // NTT, pk % NTT, 7))
                        ac.inc(a.activation(out=inter[:, (tt % 2) * TTs:(tt % 2 + 1) * TTs],
                                            in_=pq[(Mc * NTT + tt) % 2][:], func=AF.Gelu))

            @blk.vector
            def _(v):
                for Mc in range(mc_n):
                    for tt in range(NTT):
                        for mc in range(8):
                            i2 = tt * 8 + mc
                            v.wait_ge(mm.h, m_woi(Mc, tt, mc))
                            dst = acc[:, mc * TH + tt * TTs:mc * TH + tt * TTs + TTs]
                            if Mc == 0:
                                dv.inc(v.tensor_copy(out=dst, in_=pq[2 + i2 % 2][:]))
                            else:
                                dv.inc(v.tensor_add(out=dst, in0=dst, in1=pq[2 + i2 % 2][:]))
        if partial_out:
            with nc.Block() as blk:
                @blk.scalar
                def _(a):
                    a.wait_ge(dv.h, dv.n)
                    for c in range(8):
                        a.dma_start(out=o_wop[th, c],
                                    in_=acc[:, c * TH:(c + 1) * TH]).then_inc(st.h, 16); st.n += 16
                    a.wait_ge(st.h, st.n)
            if cc_cb is not None:
                with nc.Block() as blk:
                    @blk.gpsimd
                    def _(g):
                        cc_cb(g, th)
            continue
        for tt in range(NTT):
            ttl = slice(th * TH + tt * TTs, th * TH + (tt + 1) * TTs)
            with nc.Block() as blk:
                @blk.vector
                def _(v):
                    for c in range(8):
                        dv.inc(v.scalar_tensor_tensor(
                            out=lnin[:, c * TTs:(c + 1) * TTs],
                            in0=acc[:, c * TH + tt * TTs:c * TH + tt * TTs + TTs],
                            scalar=cstM[:, c:c + 1],
                            in1=x3t[:, c * TH + tt * TTs:c * TH + tt * TTs + TTs],
                            op0=ALU.add, op1=ALU.add))
            emit_ln(nc, S, ones128, ones1,
                    lambda c: lnin[:, c * TTs:(c + 1) * TTs],
                    lambda c: x4t[:, c * TTs:(c + 1) * TTs], eps12[0:1, 0:1], (mm, dv, ac),
                    gamma=lambda c: cstM[:, 8 + c:9 + c], beta=lambda c: cstM[:, 16 + c:17 + c])
            with nc.Block() as blk:
                @blk.vector
                def _(v):
                    for c in range(8):
                        dv.inc(v.tensor_add(out=lnin[:, c * TTs:(c + 1) * TTs],
                                            in0=x4t[:, c * TTs:(c + 1) * TTs].bitcast(F32R),
                                            in1=x3t[:, c * TH + tt * TTs:c * TH + tt * TTs + TTs]))
            emit_ln(nc, S, ones128, ones1,
                    lambda c: lnin[:, c * TTs:(c + 1) * TTs],
                    lambda c: x4t[:, c * TTs:(c + 1) * TTs], eps5[0:1, 0:1], (mm, dv, ac),
                    store=lambda c: (o_x4[c, :, ttl], x4t[:, c * TTs:(c + 1) * TTs]), st=st)

    with nc.Block() as blk:
        @blk.gpsimd
        def _(g):
            g.wait_ge(st.h, st.n)
    if standalone:
        return p.close()
    return None


def host_f1_inputs(inp, x3_res):
    Wr = np.asarray(inp["L1_ffn_W_router"], np.float32)
    W1 = np.asarray(inp["L1_ffn_W1"], np.float32)
    b1 = np.asarray(inp["L1_ffn_b1"], np.float32)
    Wo = np.asarray(inp["L1_ffn_Wo"], np.float32)
    bo = np.asarray(inp["L1_ffn_bo"], np.float32)
    lng = np.asarray(inp["L1_ffn_ln_gamma"], np.float32)
    lnb = np.asarray(inp["L1_ffn_ln_beta"], np.float32)
    w1t = np.ascontiguousarray(W1.reshape(4, 8, 128, 32, 128).transpose(3, 0, 1, 2, 4))
    cstM = np.zeros((128, 24), np.float32)
    cstM[:, 0:8] = bo.reshape(8, 128).T
    cstM[:, 8:16] = lng.reshape(8, 128).T
    cstM[:, 16:24] = lnb.reshape(8, 128).T
    maps = []
    for c in range(NCORE):
        maps.append({
            "x3h": x3_res[c]["x3"],
            "wri": np.ascontiguousarray(Wr),
            "w1t": w1t,
            "b1i": np.ascontiguousarray(b1),
            "woi": np.ascontiguousarray(Wo),
            "cstMi": cstM,
            "ones128": np.ones((128, 1), np.float32),
            "ones1": np.ones((128, 128), np.float32),
        })
    return maps



def build_f0b(p, sems, Tn=T, TTs=256):
    nc = p.nc
    x1_i = p.din("x1", [8, 128, Tn])
    cs_i = p.din("csum", [Tn // TTs, 8, 128, TTs])
    o_x2 = p.dout("x2", [8, 128, Tn])
    o_x2r = p.dout("x2r", [8, 128, Tn])
    ones128_i = p.din("ones128g", [128, 1], F32R)
    ones1_i = p.din("ones1g", [128, 128], F32R)
    ld, gl, mm, dv, ac, st = (sems["ld"], sems["gl"], sems["mm"], sems["dv"],
                              sems["ac"], sems["st"])
    x1t = p.sb("x1t", [128, 8 * TTs], F32)
    cst_ = p.sb("cstt", [128, 8 * TTs], F32)
    lnin = p.sb("lnin", [128, 8 * TTs], F32R)
    x2t = p.sb("x2t", [128, 8 * TTs], F32)
    x2rv = p.sb("x2rv", [128, TTs], F32)
    ones128 = p.sb("ones128", [128, 1], F32R)
    ones1 = p.sb("ones1", [128, 128], F32R)
    eps5 = p.sb("eps5", [1, 1], F32)
    pp = [p.ps(f"pp{i}", [128, TTs]) for i in range(4)]
    S = LnScratch(p, TTs, tag=p.pre, ps_pool=pp)
    with nc.Block() as blk:
        @blk.sync
        def _(q):
            q.dma_start(out=ones128[:], in_=ones128_i).then_inc(ld.h, 16); ld.n += 16
            q.dma_start(out=ones1[:], in_=ones1_i).then_inc(ld.h, 16); ld.n += 16

        @blk.vector
        def _(v):
            v.memset(eps5[:], 1e-5)
    NSUP = Tn // TTs
    for ts in range(NSUP):
        tsl = slice(ts * TTs, (ts + 1) * TTs)
        with nc.Block() as blk:
            ld0 = ld.n

            @blk.sync
            def _(q):
                for c in range(8):
                    q.dma_start(out=x1t[:, c * TTs:(c + 1) * TTs], in_=x1_i[c, :, tsl]).then_inc(ld.h, 16); ld.n += 16
                    q.dma_start(out=cst_[:, c * TTs:(c + 1) * TTs], in_=cs_i[ts, c]).then_inc(ld.h, 16); ld.n += 16

            @blk.vector
            def _(v):
                v.wait_ge(ld.h, ld0 + 16 * 16)
                for c in range(8):
                    dv.inc(v.tensor_add(out=lnin[:, c * TTs:(c + 1) * TTs],
                                        in0=x1t[:, c * TTs:(c + 1) * TTs],
                                        in1=cst_[:, c * TTs:(c + 1) * TTs]))
        emit_ln(nc, S, ones128, ones1,
                lambda c: lnin[:, c * TTs:(c + 1) * TTs],
                lambda c: x2t[:, c * TTs:(c + 1) * TTs], eps5[0:1, 0:1], (mm, dv, ac),
                store=lambda c: (o_x2[c, :, tsl], x2t[:, c * TTs:(c + 1) * TTs]), st=st)
        with nc.Block() as blk:
            dv0 = dv.n

            @blk.vector
            def _(v):
                for c in range(8):
                    if c >= 1:
                        v.wait_ge(st.h, st.n)
                    dv.inc(v.tensor_copy(
                        out=x2rv[:],
                        in_=bass.AP(x2t, c * TTs + TTs - 1, [[8 * TTs, 128], [-1, TTs]])))
                    v.wait_ge(dv.h, dv.n)
                    v.dma_start if False else None

            @blk.scalar
            def _(a):
                for c in range(8):
                    a.wait_ge(dv.h, dv0 + c + 1)
                    a.dma_start(out=o_x2r[c, :, Tn - (ts + 1) * TTs:Tn - ts * TTs],
                                in_=x2rv[:]).then_inc(st.h, 16); st.n += 16
                    a.wait_ge(st.h, st.n)
    with nc.Block() as blk:
        @blk.gpsimd
        def _(g):
            g.wait_ge(st.h, st.n)


def build_l1bb(p, sems, Tn=T, TTs=256):
    nc = p.nc
    xm_i = p.din("xm", [8, 128, Tn])
    ff_i = p.din("ffsum", [Tn // TTs, 8, 128, TTs])
    x2_i = p.din("x2j", [8, 128, Tn])
    cstD_i = p.din("cstD", [128, 8])
    ones128_i = p.din("ones128j", [128, 1], F32R)
    ones1_i = p.din("ones1j", [128, 128], F32R)
    o_x3 = p.dout("x3", [8, 128, Tn])
    ld, mm, dv, ac, st = sems["ld"], sems["mm"], sems["dv"], sems["ac"], sems["st"]
    xmt = p.sb("xmt", [128, 8 * TTs], F32)
    fft = p.sb("fft", [128, 8 * TTs], F32)
    x2t = p.sb("x2tj", [128, 8 * TTs], F32)
    scr = p.sb("scrj", [128, TTs], F32)
    lnin = p.sb("lninj", [128, 8 * TTs], F32R)
    x3t = p.sb("x3tj", [128, 8 * TTs], F32)
    cstD = p.sb("cstDs", [128, 8], F32)
    ones128 = p.sb("ones128s", [128, 1], F32R)
    ones1 = p.sb("ones1s", [128, 128], F32R)
    eps5 = p.sb("eps5j", [1, 1], F32)
    pp = [p.ps(f"ppj{i}", [128, TTs]) for i in range(4)]
    S = LnScratch(p, TTs, tag=p.pre, ps_pool=pp)
    with nc.Block() as blk:
        @blk.sync
        def _(q):
            for t_, s_ in ((cstD, cstD_i), (ones128, ones128_i), (ones1, ones1_i)):
                q.dma_start(out=t_[:], in_=s_).then_inc(ld.h, 16); ld.n += 16

        @blk.vector
        def _(v):
            v.memset(eps5[:], 1e-5)
    NSUP = Tn // TTs
    for ts in range(NSUP):
        tsl = slice(ts * TTs, (ts + 1) * TTs)
        with nc.Block() as blk:
            ld0 = ld.n

            @blk.sync
            def _(q):
                for c in range(8):
                    q.dma_start(out=xmt[:, c * TTs:(c + 1) * TTs], in_=xm_i[c, :, tsl]).then_inc(ld.h, 16); ld.n += 16
                    q.dma_start(out=fft[:, c * TTs:(c + 1) * TTs], in_=ff_i[ts, c]).then_inc(ld.h, 16); ld.n += 16
                    q.dma_start(out=x2t[:, c * TTs:(c + 1) * TTs], in_=x2_i[c, :, tsl]).then_inc(ld.h, 16); ld.n += 16

            @blk.vector
            def _(v):
                v.wait_ge(ld.h, ld0 + 24 * 16)
                for c in range(8):
                    dv.inc(v.scalar_tensor_tensor(out=scr[:], in0=fft[:, c * TTs:(c + 1) * TTs],
                                                  scalar=cstD[:, c:c + 1],
                                                  in1=xmt[:, c * TTs:(c + 1) * TTs],
                                                  op0=ALU.add, op1=ALU.add))
                    v.wait_ge(dv.h, dv.n)
                    dv.inc(v.tensor_add(out=lnin[:, c * TTs:(c + 1) * TTs], in0=scr[:],
                                        in1=x2t[:, c * TTs:(c + 1) * TTs]))
        emit_ln(nc, S, ones128, ones1,
                lambda c: lnin[:, c * TTs:(c + 1) * TTs],
                lambda c: x3t[:, c * TTs:(c + 1) * TTs], eps5[0:1, 0:1], (mm, dv, ac),
                store=lambda c: (o_x3[c, :, tsl], x3t[:, c * TTs:(c + 1) * TTs]), st=st)
    with nc.Block() as blk:
        @blk.gpsimd
        def _(g):
            g.wait_ge(st.h, st.n)


def build_f1b(p, sems, Tn=T, TTs=256):
    nc = p.nc
    wo_i = p.din("wosum", [Tn // TTs, 8, 128, TTs])
    x3_i = p.din("x3l", [8, 128, Tn])
    cstM_i = p.din("cstM2", [128, 24])
    ones128_i = p.din("ones128l", [128, 1], F32R)
    ones1_i = p.din("ones1l", [128, 128], F32R)
    o_x4 = p.dout("x4", [8, 128, Tn])
    ld, gl, mm, dv, ac, st = (sems["ld"], sems["gl"], sems["mm"], sems["dv"],
                              sems["ac"], sems["st"])
    wot = p.sb("wot", [128, 8 * TTs], F32)
    x3t = p.sb("x3tl", [128, 8 * TTs], F32R)
    lnin = p.sb("lninl", [128, 8 * TTs], F32R)
    moe = p.sb("moel", [128, 8 * TTs], F32R)
    x4t = p.sb("x4tl", [128, 8 * TTs], F32)
    cstM = p.sb("cstMs", [128, 24], F32)
    ones128 = p.sb("ones128t", [128, 1], F32R)
    ones1 = p.sb("ones1t", [128, 128], F32R)
    eps12 = p.sb("eps12l", [1, 1], F32)
    eps5 = p.sb("eps5l", [1, 1], F32)
    pp = [p.ps(f"ppl{i}", [128, TTs]) for i in range(4)]
    S = LnScratch(p, TTs, tag=p.pre, ps_pool=pp)
    with nc.Block() as blk:
        @blk.sync
        def _(q):
            for t_, s_ in ((cstM, cstM_i), (ones128, ones128_i), (ones1, ones1_i)):
                q.dma_start(out=t_[:], in_=s_).then_inc(ld.h, 16); ld.n += 16

        @blk.vector
        def _(v):
            v.memset(eps12[:], 1e-12)
            v.memset(eps5[:], 1e-5)
    NSUP = Tn // TTs
    for ts in range(NSUP):
        tsl = slice(ts * TTs, (ts + 1) * TTs)
        with nc.Block() as blk:
            ld0, gl0 = ld.n, gl.n

            @blk.sync
            def _(q):
                for c in range(8):
                    q.dma_start(out=wot[:, c * TTs:(c + 1) * TTs], in_=wo_i[ts, c]).then_inc(ld.h, 16); ld.n += 16

            @blk.gpsimd
            def _(g):
                for c in range(8):
                    g.dma_start(out=x3t[:, c * TTs:(c + 1) * TTs], in_=x3_i[c, :, tsl]).then_inc(gl.h, 16); gl.n += 16

            @blk.vector
            def _(v):
                v.wait_ge(ld.h, ld0 + 8 * 16)
                v.wait_ge(gl.h, gl0 + 8 * 16)
                for c in range(8):
                    dv.inc(v.scalar_tensor_tensor(out=lnin[:, c * TTs:(c + 1) * TTs],
                                                  in0=wot[:, c * TTs:(c + 1) * TTs],
                                                  scalar=cstM[:, c:c + 1],
                                                  in1=x3t[:, c * TTs:(c + 1) * TTs],
                                                  op0=ALU.add, op1=ALU.add))
        emit_ln(nc, S, ones128, ones1,
                lambda c: lnin[:, c * TTs:(c + 1) * TTs],
                lambda c: moe[:, c * TTs:(c + 1) * TTs], eps12[0:1, 0:1], (mm, dv, ac),
                gamma=lambda c: cstM[:, 8 + c:9 + c], beta=lambda c: cstM[:, 16 + c:17 + c])
        with nc.Block() as blk:
            @blk.vector
            def _(v):
                for c in range(8):
                    dv.inc(v.tensor_add(out=lnin[:, c * TTs:(c + 1) * TTs],
                                        in0=moe[:, c * TTs:(c + 1) * TTs],
                                        in1=x3t[:, c * TTs:(c + 1) * TTs]))
        emit_ln(nc, S, ones128, ones1,
                lambda c: lnin[:, c * TTs:(c + 1) * TTs],
                lambda c: x4t[:, c * TTs:(c + 1) * TTs], eps5[0:1, 0:1], (mm, dv, ac),
                store=lambda c: (o_x4[c, :, tsl], x4t[:, c * TTs:(c + 1) * TTs]), st=st)
    with nc.Block() as blk:
        @blk.gpsimd
        def _(g):
            g.wait_ge(st.h, st.n)


PAIRS = [[0, 1], [2, 3], [4, 5], [6, 7]]


def build_fused():
    p = Prog()
    nc = p.nc
    sems = {k: p.sem(k) for k in ("ld", "gl", "mm", "dv", "ac", "pl", "st")}
    gx = nc.dram_tensor("x", [8, 128, T], F32, kind="ExternalInput").ap()
    gxr = nc.dram_tensor("xr", [8, 128, T], F32R, kind="ExternalInput").ap()
    out_x4 = nc.dram_tensor("x4", [8, 128, T], F32, kind="ExternalOutput").ap()
    dram = lambda n, sh, dt=F32: nc.dram_tensor(n, list(sh), dt, kind="Internal").ap()
    g_xa = dram("g_xa", [8, 128, T], F32R)
    g_z = dram("g_z", [8, 128, T])
    g_dtp = dram("g_dtp", [128, T]); g_bp = dram("g_bp", [16, T]); g_cp = dram("g_cp", [16, T])
    g_dts = dram("g_dts", [128, T]); g_bs = dram("g_bs", [16, T]); g_cs = dram("g_cs", [16, T])
    g_op = dram("g_op", [8, 8, 128, 512]); g_os = dram("g_os", [8, 8, 128, 512])
    g_cpart = dram("g_cpart", [8, 8, 128, 512]); g_csum = dram("g_csum", [8, 8, 128, 512])
    g_x1 = dram("g_x1", [8, 128, T])
    g_x2 = dram("g_x2", [8, 128, T]); g_x2r = dram("g_x2r", [8, 128, T])
    g_mo = dram("g_mo", [8, 8, 128, 512]); g_mos = dram("g_mos", [8, 8, 128, 512])
    g_ffp = dram("g_ffp", [8, 8, 128, 512]); g_ffs = dram("g_ffs", [8, 8, 128, 512])
    g_xm = dram("g_xm", [8, 128, T])
    g_x3 = dram("g_x3", [8, 128, T])
    g_wop = dram("g_wop", [8, 8, 128, 512]); g_wos = dram("g_wos", [8, 8, 128, 512])

    cc_k = [0]

    def collective(items):
        with nc.Block() as blk:
            @blk.gpsimd
            def _(g):
                for s_, d_ in items:
                    cs = p.sem(f"cc{cc_k[0]}"); cc_k[0] += 1
                    g.collective_compute("AllReduce", ALU.add, replica_groups=PAIRS,
                                         ins=[s_], outs=[d_]).then_inc(cs.h)
                    g.wait_ge(cs.h, 1)

    def mk_cc(items, stc, base, per, TTc=512):
        css = []

        def cb(g, ts):
            if ts % 2 == 0:
                return
            k2 = ts // 2
            g.wait_ge(stc.h, base + per * (ts + 1))
            for s_, d_ in items:
                cs = p.sem(f"cc{cc_k[0]}"); cc_k[0] += 1
                g.collective_compute("AllReduce", ALU.add, replica_groups=PAIRS,
                                     ins=[s_[2 * k2:2 * k2 + 2]],
                                     outs=[d_[2 * k2:2 * k2 + 2]]).then_inc(cs.h)
                css.append(cs)

        def fin():
            with nc.Block() as blk:
                @blk.gpsimd
                def _(g):
                    for cs in css:
                        g.wait_ge(cs.h, 1)
        return cb, fin

    scopes = ExitStack()
    scopes.enter_context(nc.named_scope("s_l0a"))
    p.begin_stage("a_")
    p.io = {"x": gxr, "xa": g_xa, "z": g_z, "dtp": g_dtp, "bp": g_bp, "cp": g_cp}
    build_l0a(p=p, sems=sems)
    p.end_stage(); p.io = {}
    scopes.close(); scopes = ExitStack(); scopes.enter_context(nc.named_scope("c_1"))
    collective([(g_dtp, g_dts), (g_bp, g_bs), (g_cp, g_cs)])
    scopes.close(); scopes = ExitStack(); scopes.enter_context(nc.named_scope("s_l0b"))

    p.begin_stage("b_")
    p.io = {"xa": g_xa, "z": g_z, "dts": g_dts, "bs": g_bs, "cs": g_cs, "opart": g_op}
    cb2, fin2 = mk_cc([(g_op, g_os)], sems["st"], sems["st"].n, 128)
    build_l0b(p=p, sems=sems, summed=True, cc_cb=cb2)
    p.end_stage(); p.io = {}
    scopes.close(); scopes = ExitStack(); scopes.enter_context(nc.named_scope("c_2"))
    fin2()
    scopes.close(); scopes = ExitStack(); scopes.enter_context(nc.named_scope("s_f0a"))

    p.begin_stage("f_")
    p.io = {"osum": g_os, "x": gx, "cpart": g_cpart, "x1o": g_x1}
    cb3, fin3 = mk_cc([(g_cpart, g_csum)], sems["st"], sems["st"].n, 256)
    build_f0(TLn=T, TTs=512, p=p, sems=sems, heads=2, summed_in=True, partial_out=True,
             cc_cb=cb3)
    p.end_stage(); p.io = {}
    scopes.close(); scopes = ExitStack(); scopes.enter_context(nc.named_scope("c_3"))
    fin3()
    scopes.close(); scopes = ExitStack(); scopes.enter_context(nc.named_scope("s_f0b"))

    p.begin_stage("g_")
    p.io = {"x1": g_x1, "csum": g_csum, "x2": g_x2, "x2r": g_x2r}
    build_f0b(p, sems, TTs=512)
    p.end_stage(); p.io = {}

    scopes.close(); scopes = ExitStack(); scopes.enter_context(nc.named_scope("s_l1a"))
    p.begin_stage("h_")
    p.io = {"x2": g_x2, "x2r": g_x2r, "mo": g_mo}
    cb4, fin4 = mk_cc([(g_mo, g_mos)], sems["st"], sems["st"].n, 160)
    build_l1a(TTs=512, p=p, sems=sems, sel_in=True, cc_cb=cb4)
    p.end_stage(); p.io = {}
    scopes.close(); scopes = ExitStack(); scopes.enter_context(nc.named_scope("c_4"))
    fin4()
    scopes.close(); scopes = ExitStack(); scopes.enter_context(nc.named_scope("s_l1ba"))

    p.begin_stage("i_")
    p.io = {"x2h": g_x2, "mosum": g_mos, "ffp": g_ffp, "xmo": g_xm}
    cb5, fin5 = mk_cc([(g_ffp, g_ffs)], sems["st"], sems["st"].n, 256)
    build_l1b(TLn=T, TTs=512, p=p, sems=sems, summed_in=True, kc_n=16, partial_out=True,
              cc_cb=cb5)
    p.end_stage(); p.io = {}
    scopes.close(); scopes = ExitStack(); scopes.enter_context(nc.named_scope("c_5"))
    fin5()
    scopes.close(); scopes = ExitStack(); scopes.enter_context(nc.named_scope("s_l1bb_f1a"))

    p.begin_stage("j_")
    p.io = {"xm": g_xm, "ffsum": g_ffs, "x2j": g_x2, "x3": g_x3}
    build_l1bb(p, sems, TTs=512)
    p.end_stage(); p.io = {}

    p.begin_stage("k_")
    p.io = {"x3h": g_x3, "wop": g_wop}
    cb6, fin6 = mk_cc([(g_wop, g_wos)], sems["st"], sems["st"].n, 128)
    build_f1(TLn=T, TTs=512, p=p, sems=sems, mc_n=16, partial_out=True, TH=512,
             cc_cb=cb6)
    p.end_stage(); p.io = {}
    scopes.close(); scopes = ExitStack(); scopes.enter_context(nc.named_scope("c_6"))
    fin6()
    scopes.close(); scopes = ExitStack(); scopes.enter_context(nc.named_scope("s_f1b"))

    p.begin_stage("l_")
    p.io = {"wosum": g_wos, "x3l": g_x3, "x4": out_x4}
    build_f1b(p, sems, TTs=512)
    p.end_stage(); p.io = {}
    scopes.close()
    return p.close()


def host_fused_inputs(inp):
    x = np.asarray(inp["L0_attn_x"], np.float32)
    m_a = host_l0a_inputs(inp)
    dummy = [{k: None for k in ()} for _ in range(NCORE)]
    W_dt = np.asarray(inp["L0_attn_W_dt"], np.float32)
    W_out = np.asarray(inp["L0_attn_W_out"], np.float32)
    A_log = np.asarray(inp["L0_attn_A_log"], np.float32)
    b_dt = np.asarray(inp["L0_attn_b_dt"], np.float32)
    D_ssm = np.asarray(inp["L0_attn_D_ssm"], np.float32)
    b_xp = np.asarray(inp["L0_attn_b_xp"], np.float32)
    b_out = np.asarray(inp["L0_attn_b_out"], np.float32)
    rm = np.asarray(inp["L0_ffn_rail_maker"], np.float32)
    rup = np.asarray(inp["L0_ffn_rail_up_proj"], np.float32)
    wc = np.asarray(inp["L0_ffn_w_cproj"], np.float32)
    wct = np.ascontiguousarray(wc.T)
    ff1w = np.asarray(inp["L1_attn_ff1_w"], np.float32)
    ff2w = np.asarray(inp["L1_attn_ff2_w"], np.float32)
    ff1b = np.asarray(inp["L1_attn_ff1_b"], np.float32)
    ff2b = np.asarray(inp["L1_attn_ff2_b"], np.float32)
    fg = np.asarray(inp["L1_attn_ffn_g"], np.float32)
    fb = np.asarray(inp["L1_attn_ffn_b"], np.float32)
    Wr = np.asarray(inp["L1_ffn_W_router"], np.float32)
    W1 = np.asarray(inp["L1_ffn_W1"], np.float32)
    b1v = np.asarray(inp["L1_ffn_b1"], np.float32)
    Wo = np.asarray(inp["L1_ffn_Wo"], np.float32)
    bo = np.asarray(inp["L1_ffn_bo"], np.float32)
    lng = np.asarray(inp["L1_ffn_ln_gamma"], np.float32)
    lnb = np.asarray(inp["L1_ffn_ln_beta"], np.float32)
    w1t_full = np.ascontiguousarray(W1.reshape(4, 8, 128, 32, 128).transpose(3, 0, 1, 2, 4))
    m_l1a = host_l1a_inputs(inp, np.zeros((4, T, D), np.float32))
    ones128 = np.ones((128, 1), np.float32)
    ones1 = np.ones((128, 128), np.float32)
    cstM = np.zeros((128, 24), np.float32)
    cstM[:, 0:8] = bo.reshape(8, 128).T
    cstM[:, 8:16] = lng.reshape(8, 128).T
    cstM[:, 16:24] = lnb.reshape(8, 128).T
    maps = []
    for c in range(NCORE):
        bidx, hh = c // 2, c % 2
        dsl = slice(hh * 1024, (hh + 1) * 1024)
        xfm = np.ascontiguousarray(x[bidx].T.reshape(8, 128, T))
        m = {"x": xfm, "xr": xfm}
        for k, v in m_a[c].items():
            if k != "x":
                m["a_" + k] = v
        cst2 = np.zeros((128, 17), np.float32)
        bdt8 = b_dt[dsl].reshape(8, 128)
        d8 = D_ssm[dsl].reshape(8, 128)
        for ch in range(8):
            cst2[:, 2 * ch] = bdt8[ch]
            cst2[:, 2 * ch + 1] = d8[ch]
        cst2[:, 16] = b_xp[0:128]
        m.update({
            "b_wdti": np.ascontiguousarray(W_dt[:, dsl]),
            "b_wouti": np.ascontiguousarray(W_out[dsl, :]),
            "b_alogi": np.ascontiguousarray(A_log[dsl].reshape(8, 128, 16).transpose(1, 0, 2).reshape(128, 128)),
            "b_cst2i": cst2,
            "b_bxpbc": np.stack([b_xp[128:144], b_xp[144:160]], axis=1).copy(),
            "b_id128": np.eye(128, dtype=np.float32),
        })
        rmbd = np.zeros((16, 128, 128), np.float32)
        for hloc in range(2):
            hgl = 2 * hh + hloc
            for cc in range(8):
                for j in range(8):
                    r = cc * 8 + j
                    rmbd[hloc * 8 + cc, 16 * j:16 * j + 16, 16 * j:16 * j + 16] = rm[hgl, r]
        rupbd = np.zeros((128, 32 * 128), np.float32)
        for hloc in range(2):
            hgl = 2 * hh + hloc
            for pp_ in range(32):
                cc, pj = pp_ // 4, pp_ % 4
                q = (hloc * 8 + cc) * 2 + (pj % 2)
                ro = (pj // 2) * 64
                sub = (pj % 2) * 32
                blkw = np.zeros((64, 128), np.float32)
                for rr in range(2):
                    rail = 2 * pp_ + rr
                    blkw[sub + 16 * rr:sub + 16 * rr + 16, 64 * rr:64 * rr + 64] = rup[hgl, rail].T
                rupbd[ro:ro + 64, q * 128:(q + 1) * 128] = blkw
        m.update({
            "f_bouti": np.ascontiguousarray(b_out.reshape(8, 128).T),
            "f_rmbd": rmbd,
            "f_rupbd": rupbd,
            "f_wci": np.ascontiguousarray(wct[hh * 8192:(hh + 1) * 8192, :]),
            "f_ones128": ones128, "f_ones1": ones1,
            "g_ones128g": ones128, "g_ones1g": ones1,
        })
        for k, v in m_l1a[c].items():
            if k != "x2f":
                m["h_" + k] = v
        rsel = np.zeros((128, 2), np.float32)
        rsel[:, hh] = 1.0
        m["h_revsel"] = rsel
        m["h_id128L"] = np.eye(128, dtype=np.float32)
        cstF = np.zeros((128, 56), np.float32)
        cstF[:, 0:8] = fg.reshape(8, 128).T
        cstF[:, 8:16] = fb.reshape(8, 128).T
        cstF[:, 16:32] = ff1b[hh * 2048:(hh + 1) * 2048].reshape(16, 128).T
        cstF[:, 48:56] = ff2b.reshape(8, 128).T
        m.update({
            "i_ff1wi": np.ascontiguousarray(ff1w.T[:, hh * 2048:(hh + 1) * 2048]),
            "i_ff2wi": np.ascontiguousarray(ff2w.T[hh * 2048:(hh + 1) * 2048, :]),
            "i_cstFi": cstF,
            "i_ones128": ones128, "i_ones1": ones1,
            "j_cstD": np.ascontiguousarray(ff2b.reshape(8, 128).T),
            "j_ones128j": ones128, "j_ones1j": ones1,
        })
        m.update({
            "k_wri": np.ascontiguousarray(Wr),
            "k_w1t": np.ascontiguousarray(w1t_full[hh * 16:(hh + 1) * 16]),
            "k_b1i": np.ascontiguousarray(b1v[:, hh * 2048:(hh + 1) * 2048]),
            "k_woi": np.ascontiguousarray(Wo[hh * 2048:(hh + 1) * 2048, :]),
            "k_cstMi": cstM,
            "k_ones128": ones128, "k_ones1": ones1,
            "l_cstM2": cstM,
            "l_ones128l": ones128, "l_ones1l": ones1,
        })
        maps.append(m)
    return maps


def kernel_fused(**inp):
    inp = {k: np.asarray(v) for k, v in inp.items()}
    nc = build_fused()
    maps = host_fused_inputs(inp)
    res = run_bass_kernel_spmd(nc, maps, core_ids=list(range(NCORE)))
    out = np.empty((4, T, D), np.float32)
    for b in range(4):
        out[b] = res.results[2 * b]["x4"].reshape(D, T).T
    return out



def _fm(a):
    Fdim = a.shape[-1]
    return np.ascontiguousarray(a.T.reshape(Fdim // 128, 128, a.shape[0]))


def host_l0a_inputs(inp):
    x = np.asarray(inp["L0_attn_x"], np.float32)
    W_in = np.asarray(inp["L0_attn_W_in"], np.float32)
    b_in = np.asarray(inp["L0_attn_b_in"], np.float32)
    conv_k = np.asarray(inp["L0_attn_conv_k"], np.float32)
    conv_b = np.asarray(inp["L0_attn_conv_b"], np.float32)
    W_xp = np.asarray(inp["L0_attn_W_xp"], np.float32)
    maps = []
    for c in range(NCORE):
        bidx, hh = c // 2, c % 2
        dsl = slice(hh * 1024, (hh + 1) * 1024)
        consts = np.zeros((8, 128, 7), np.float32)
        consts[:, :, 0] = b_in[dsl].reshape(8, 128)
        consts[:, :, 1] = b_in[2048 + hh * 1024:2048 + (hh + 1) * 1024].reshape(8, 128)
        consts[:, :, 2] = conv_b[dsl].reshape(8, 128)
        for k in range(4):
            consts[:, :, 3 + k] = conv_k[k, 0, dsl].reshape(8, 128)
        maps.append({
            "x": _fm(x[bidx]),
            "w_in_x": np.ascontiguousarray(W_in[:, dsl]),
            "w_in_z": np.ascontiguousarray(W_in[:, 2048 + hh * 1024:2048 + (hh + 1) * 1024]),
            "wxp_dt": np.ascontiguousarray(W_xp[dsl, 0:128]),
            "wxp_b": np.ascontiguousarray(W_xp[dsl, 128:144]),
            "wxp_c": np.ascontiguousarray(W_xp[dsl, 144:160]),
            "consts": np.ascontiguousarray(consts.transpose(1, 0, 2).reshape(128, 56)),
        })
    return maps


def host_l0b_inputs(inp, l0a_res):
    W_dt = np.asarray(inp["L0_attn_W_dt"], np.float32)
    W_out = np.asarray(inp["L0_attn_W_out"], np.float32)
    A_log = np.asarray(inp["L0_attn_A_log"], np.float32)
    b_dt = np.asarray(inp["L0_attn_b_dt"], np.float32)
    D_ssm = np.asarray(inp["L0_attn_D_ssm"], np.float32)
    b_xp = np.asarray(inp["L0_attn_b_xp"], np.float32)
    maps = []
    for c in range(NCORE):
        hh = c % 2
        pc = c - hh + (1 - hh)
        dsl = slice(hh * 1024, (hh + 1) * 1024)
        cst2 = np.zeros((128, 17), np.float32)
        bdt8 = b_dt[dsl].reshape(8, 128)
        d8 = D_ssm[dsl].reshape(8, 128)
        for ch in range(8):
            cst2[:, 2 * ch] = bdt8[ch]
            cst2[:, 2 * ch + 1] = d8[ch]
        cst2[:, 16] = b_xp[0:128]
        maps.append({
            "xa": l0a_res[c]["xa"],
            "z": l0a_res[c]["z"],
            "dtps": l0a_res[c]["dtp"],
            "dtpp": l0a_res[pc]["dtp"],
            "bps": l0a_res[c]["bp"],
            "bpp": l0a_res[pc]["bp"],
            "cps": l0a_res[c]["cp"],
            "cpp": l0a_res[pc]["cp"],
            "wdti": np.ascontiguousarray(W_dt[:, dsl]),
            "wouti": np.ascontiguousarray(W_out[dsl, :]),
            "alogi": np.ascontiguousarray(A_log[dsl].reshape(8, 128, 16).transpose(1, 0, 2).reshape(128, 128)),
            "cst2i": cst2,
            "bxpbr": np.tile(b_xp[128:144], (128, 1)),
            "bxpcr": np.tile(b_xp[144:160], (128, 1)),
            "id128": np.eye(128, dtype=np.float32),
        })
    return maps


def kernel(**inp):
    return kernel_fused(**inp)


def kernel_unfused(**inp):
    inp = {k: np.asarray(v) for k, v in inp.items()}
    core_ids = list(range(NCORE))

    res_a = run_bass_kernel_spmd(build_l0a(), host_l0a_inputs(inp), core_ids=core_ids)
    l0a_res = [dict(r) for r in res_a.results]

    res_b = run_bass_kernel_spmd(build_l0b(), host_l0b_inputs(inp, l0a_res), core_ids=core_ids)
    l0b_res = [dict(r) for r in res_b.results]

    res_f0 = run_bass_kernel_spmd(build_f0(), host_f0_inputs(inp, l0b_res), core_ids=core_ids)
    x2_full = np.empty((4, T, D), np.float32)
    for c in range(NCORE):
        bidx, hh = c // 2, c % 2
        x2_full[bidx, hh * TL:(hh + 1) * TL] = res_f0.results[c]["x2"].reshape(D, TL).T

    res_l1a = run_bass_kernel_spmd(build_l1a(), host_l1a_inputs(inp, x2_full), core_ids=core_ids)
    l1a_res = [dict(r) for r in res_l1a.results]

    res_l1b = run_bass_kernel_spmd(build_l1b(), host_l1b_inputs(inp, x2_full, l1a_res), core_ids=core_ids)
    x3_res = [dict(r) for r in res_l1b.results]

    res_f1 = run_bass_kernel_spmd(build_f1(), host_f1_inputs(inp, x3_res), core_ids=core_ids)
    out = np.empty((4, T, D), np.float32)
    for c in range(NCORE):
        bidx, hh = c // 2, c % 2
        out[bidx, hh * TL:(hh + 1) * TL] = res_f1.results[c]["x4"].reshape(D, TL).T
    return out
